# version 01
# baseline (speedup 1.0000x reference)
import contextlib
import math
import numpy as np

import concourse.bass as bass
import concourse.mybir as mybir
from concourse.bass_utils import run_bass_kernel_spmd

F32 = mybir.dt.float32
BF16 = mybir.dt.bfloat16
AF = mybir.ActivationFunctionType
ALU = mybir.AluOpType
AX = mybir.AxisListType

NCORES = 8
B, S, D = 4, 4096, 1024
T_ALL = B * S
T_C = T_ALL // NCORES
DT = D // 128

ENGS = ("tensor", "vector", "scalar", "sync", "gpsimd")


class Prog:

    def __init__(self, nc):
        self.nc = nc
        self.stages = []
        self.cur = None
        self.new_stage()

    def new_stage(self, name=""):
        self.cur = {e: [] for e in ENGS}
        self.cur["_name"] = name
        self.stages.append(self.cur)

    def on(self, eng, fn):
        self.cur[eng].append(("op", fn))

    def dma(self, queue, out, in_, **kw):
        def fn(e, out=out, in_=in_, kw=kw):
            return e.dma_start(out=out, in_=in_, **kw)
        self.cur[queue].append(("dma", fn))

    def cc(self, fn):
        self.cur["gpsimd"].append(("cc", fn))

    def finalize(self):
        nc = self.nc
        nstage = len(self.stages)
        incs = {e: [0] * nstage for e in ENGS}
        cc_events = []
        for k, st in enumerate(self.stages):
            for e in ENGS:
                kinds = {kind for kind, _ in st[e]}
                if "op" in kinds and len(kinds) > 1:
                    raise AssertionError(f"stage {k} engine {e}: mixing op and dma/cc unsupported")
                if "cc" in kinds and "dma" in kinds:
                    raise AssertionError(f"stage {k} engine {e}: mixing cc and dma unsupported")
                for kind, _ in st[e]:
                    if kind == "dma":
                        incs[e][k] += 16
                    elif kind == "cc":
                        cc_events.append((k, len(cc_events)))
                if st[e] and all(kind == "op" for kind, _ in st[e]):
                    incs[e][k] = 1
        cum = {e: np.cumsum([0] + incs[e]).tolist() for e in ENGS}

        with contextlib.ExitStack() as ctx:
            sems = {e: ctx.enter_context(nc.semaphore(f"sem_{e}")) for e in ENGS}
            cc_sems = [ctx.enter_context(nc.semaphore(f"cc_{i}")) for _, i in cc_events]
            cc_stage = [k for k, _ in cc_events]
            block = ctx.enter_context(nc.Block())
            stages = self.stages
            for eng_name in ENGS:
                def body(eng, eng_name=eng_name):
                    last_wait = {e2: 0 for e2 in ENGS}
                    cc_waited = [False] * len(cc_sems)
                    cc_idx = 0

                    def do_waits(k):
                        for e2 in ENGS:
                            if e2 == eng_name:
                                continue
                            tgt = cum[e2][k]
                            if tgt > last_wait[e2]:
                                eng.wait_ge(sems[e2], tgt)
                                last_wait[e2] = tgt
                        for j in range(len(cc_sems)):
                            if cc_stage[j] < k and not cc_waited[j] and eng_name != "gpsimd":
                                eng.wait_ge(cc_sems[j], 1)
                                cc_waited[j] = True

                    for k in range(nstage):
                        items = stages[k][eng_name]
                        if not items:
                            continue
                        do_waits(k)
                        n = len(items)
                        for i, (kind, fn) in enumerate(items):
                            inst = fn(eng)
                            if kind == "dma":
                                inst.then_inc(sems[eng_name], 16)
                            elif kind == "cc":
                                inst.then_inc(cc_sems[cc_idx])
                                cc_idx += 1
                            elif i == n - 1:
                                inst.then_inc(sems[eng_name], 1)
                            if kind == "cc":
                                eng.wait_ge(cc_sems[cc_idx - 1], 1)
                    do_waits(nstage)
                getattr(block, eng_name)(body)



def ceil_div(a, b):
    return (a + b - 1) // b


def chunk_starts(total, size):
    if total <= size:
        return [0]
    starts = list(range(0, total - size + 1, size))
    if starts[-1] + size < total:
        starts.append(total - size)
    return starts


def emit_mm_group(P, psum, lhsT_tiles, rhs_tiles, n0, nn, accumulate=False, pn0=None):
    nk = len(lhsT_tiles)
    p0 = n0 if pn0 is None else pn0
    for ki in range(nk):
        def fn(t, ki=ki):
            return t.matmul(psum[:, p0:p0 + nn], lhsT_tiles[ki],
                            rhs_tiles[ki][:, n0:n0 + nn],
                            start=(ki == 0 and not accumulate),
                            stop=(ki == nk - 1))
        P.on("tensor", fn)


def act(P, out, in_, func, bias=None, scale=None, accum_out=None):
    def fn(sc, out=out, in_=in_, func=func, bias=bias, scale=scale, accum_out=accum_out):
        kw = {}
        if bias is not None:
            kw["bias"] = bias
        if scale is not None:
            kw["scale"] = scale
        if accum_out is not None:
            kw["accum_out"] = accum_out
        return sc.activation(out, in_, func, **kw)
    P.on("scalar", fn)


def vec_tt(P, out, a, b_, op):
    P.on("vector", lambda v, out=out, a=a, b_=b_, op=op: v.tensor_tensor(out, a, b_, op=op))


def vec_ts(P, out, a, s1, s2, op0, op1=None):
    def fn(v, out=out, a=a, s1=s1, s2=s2, op0=op0, op1=op1):
        if op1 is None:
            return v.tensor_scalar(out, a, s1, None, op0=op0)
        return v.tensor_scalar(out, a, s1, s2, op0=op0, op1=op1)
    P.on("vector", fn)


def vec_stt(P, out, in0, scalar, in1, op0, op1):
    P.on("vector", lambda v, out=out, in0=in0, s=scalar, in1=in1, op0=op0, op1=op1:
         v.scalar_tensor_tensor(out, in0, s, in1, op0=op0, op1=op1))


def vec_copy(P, out, in_):
    P.on("vector", lambda v, out=out, in_=in_: v.tensor_copy(out, in_))


def vec_red(P, out, in_, op, negate=False):
    def fn(v, out=out, in_=in_, op=op, negate=negate):
        kw = {"negate": True} if negate else {}
        return v.tensor_reduce(out, in_, axis=AX.X, op=op, **kw)
    P.on("vector", fn)


def vec_memset(P, ap, val):
    P.on("vector", lambda v, ap=ap, val=val: v.memset(ap, val))


def vec_scan(P, out, a, b_, init):
    P.on("vector", lambda v, out=out, a=a, b_=b_, init=init:
         v.tensor_tensor_scan(out, a, b_, init, op0=ALU.mult, op1=ALU.add))



def emit_ln(P, xt, xnt, sq, mu_b, rstd_b, ps_rows, ps_bc, mu_row, sq_row,
            ones_col_f32, ones_row_f32, Tn, eps=1e-5, gam=None, bet=None,
            out_f32=None, nt=DT, dim=None, rms=False):
    dim = dim or (nt * 128)
    P.new_stage("ln_sq")
    for i in range(nt):
        act(P, sq[i][:], xt[i][:], AF.Square)
    for j in chunk_starts(Tn, 512):
        nn = min(512, Tn - j)
        P.new_stage("ln_sums")
        if not rms:
            emit_mm_group(P, ps_rows[0], [ones_col_f32] * nt,
                          [xt[i][:, j:j + nn] for i in range(nt)], 0, nn)
        emit_mm_group(P, ps_rows[1], [ones_col_f32] * nt,
                      [sq[i][:, j:j + nn] for i in range(nt)], 0, nn)
        P.new_stage("ln_rm1")
        if not rms:
            vec_ts(P, mu_row[:, j:j + nn], ps_rows[0][0:1, 0:nn], 1.0 / dim, None, ALU.mult)
        else:
            vec_ts(P, sq_row[:, j:j + nn], ps_rows[1][0:1, 0:nn], 1.0 / dim, eps, ALU.mult, ALU.add)
        if not rms:
            P.new_stage("ln_rm2")
            vec_stt(P, sq_row[:, j:j + nn], ps_rows[0][0:1, 0:nn], 1.0 / dim,
                    mu_row[:, j:j + nn], ALU.mult, ALU.mult)
            P.new_stage("ln_rm3")
            vec_stt(P, sq_row[:, j:j + nn], ps_rows[1][0:1, 0:nn], 1.0 / dim,
                    sq_row[:, j:j + nn], ALU.mult, ALU.subtract)
            P.new_stage("ln_rm4")
            vec_ts(P, sq_row[:, j:j + nn], sq_row[:, j:j + nn], eps, None, ALU.add)
        P.new_stage("ln_rm5")
        act(P, sq_row[:, j:j + nn], sq_row[:, j:j + nn], AF.Ln)
        P.new_stage("ln_rm6")
        act(P, sq_row[:, j:j + nn], sq_row[:, j:j + nn], AF.Exp, scale=-0.5)
        P.new_stage("ln_bc")
        if not rms:
            emit_mm_group(P, ps_bc[:, 0:512], [ones_row_f32], [mu_row[:]], j, nn, pn0=0)
        emit_mm_group(P, ps_bc[:, 512:1024], [ones_row_f32], [sq_row[:]], j, nn, pn0=0)
        P.new_stage("ln_bcev")
        if not rms:
            vec_copy(P, mu_b[:, j:j + nn], ps_bc[:, 0:nn])
        vec_copy(P, rstd_b[:, j:j + nn], ps_bc[:, 512:512 + nn])
    if not rms:
        P.new_stage("ln_apply")
        for i in range(nt):
            vec_tt(P, sq[i][:], xt[i][:], mu_b[:], ALU.subtract)
    src0 = sq if not rms else xt
    P.new_stage("ln_apply2")
    for i in range(nt):
        dst = out_f32[i][:] if out_f32 is not None else sq[i][:]
        vec_tt(P, dst, src0[i][:], rstd_b[:], ALU.mult)
    src = out_f32 if out_f32 is not None else sq
    if gam is not None:
        P.new_stage("ln_affine")
        for i in range(nt):
            if bet is not None:
                vec_ts(P, xnt[i][:], src[i][:], gam[i], bet[i], ALU.mult, ALU.add)
            else:
                vec_ts(P, xnt[i][:], src[i][:], gam[i], None, ALU.mult)
    elif xnt is not src:
        P.new_stage("ln_cast")
        for i in range(nt):
            vec_copy(P, xnt[i][:], src[i][:])



def fm(a):
    a = np.asarray(a)
    a2 = a.reshape(-1, a.shape[-1])
    return np.ascontiguousarray(a2.T)


ONES_COL = np.ones((128, 1), np.float32)
ONES_ROW = np.ones((1, 128), np.float32)
IDENT = np.eye(128, dtype=np.float32)



def build_T0():
    nc = bass.Bass("TRN2", target_bir_lowering=False)
    xin = nc.dram_tensor("xfm", [D, T_C], F32, kind="ExternalInput").ap()
    onc = nc.dram_tensor("ones_col", [128, 1], F32, kind="ExternalInput").ap()
    onr = nc.dram_tensor("ones_row", [1, 128], F32, kind="ExternalInput").ap()
    xno = nc.dram_tensor("xnfm", [D, T_C], F32, kind="ExternalOutput").ap()
    Tn = T_C
    with contextlib.ExitStack() as ctx:
        E = ctx.enter_context
        xt = [E(nc.sbuf_tensor(f"x{i}", [128, Tn], F32)) for i in range(DT)]
        sq = [E(nc.sbuf_tensor(f"sq{i}", [128, Tn], F32)) for i in range(DT)]
        xnt = sq
        mu_b = E(nc.sbuf_tensor("mu_b", [128, Tn], F32))
        rstd_b = E(nc.sbuf_tensor("rstd_b", [128, Tn], F32))
        mu_row = E(nc.sbuf_tensor("mu_row", [1, Tn], F32))
        sq_row = E(nc.sbuf_tensor("sq_row", [1, Tn], F32))
        onescol = E(nc.sbuf_tensor("onescol", [128, 1], F32))
        onesrow = E(nc.sbuf_tensor("onesrow", [1, 128], F32))
        pr_mu = E(nc.psum_tensor("pr_mu", [1, 512], F32))
        pr_sq = E(nc.psum_tensor("pr_sq", [1, 512], F32))
        ps_bc = E(nc.psum_tensor("ps_bc", [128, 1024], F32))

        P = Prog(nc)
        P.new_stage("loads")
        for i in range(DT):
            P.dma("sync", xt[i][:], xin[i * 128:(i + 1) * 128, :])
        P.dma("sync", onescol[:], onc)
        P.dma("sync", onesrow[:], onr)
        emit_ln(P, xt, xnt, sq, mu_b, rstd_b, (pr_mu, pr_sq), ps_bc, mu_row, sq_row,
                onescol[:], onesrow[:], Tn)
        P.new_stage("store")
        for i in range(DT):
            P.dma("sync", xno[i * 128:(i + 1) * 128, :], xnt[i][:])
        P.finalize()
    return nc


def run_T0(x_fm_full):
    nc = build_T0()
    in_maps = []
    for c in range(NCORES):
        in_maps.append({
            "xfm": np.ascontiguousarray(x_fm_full[:, c * T_C:(c + 1) * T_C]),
            "ones_col": ONES_COL, "ones_row": ONES_ROW,
        })
    res = run_bass_kernel_spmd(nc, in_maps, core_ids=list(range(NCORES)))
    return np.concatenate([r["xnfm"] for r in res.results], axis=1)



M0_DEBUG = False
M0_TC = 512
M0_NCH = S // M0_TC
XB = 2048
XBT = XB // 128
HT = 1024
HTT = HT // 128
STATE = 16


def build_M0():
    nc = bass.Bass("TRN2", target_bir_lowering=False)
    ap = lambda n, s, d=F32, k="ExternalInput": nc.dram_tensor(n, s, d, kind=k).ap()
    xraw = ap("xraw", [D, S])
    wi_xb = ap("wi_xb", [D, XB])
    wi_z = ap("wi_z", [D, HT])
    wdt = ap("wdt", [XB, HT])
    wbc = ap("wbc", [XB, 32])
    alog = ap("alog", [128, HTT * STATE])
    convrows = ap("convrows", [1, 3 * XB])
    convb = ap("convb", [128, XBT])
    bdt = ap("bdt", [128, HTT])
    gam = ap("gam", [128, DT])
    bet = ap("bet", [128, DT])
    dpar = ap("dpar", [128, HTT])
    sel16 = ap("sel16", [16, 16 * 128])
    onc = ap("ones_col", [128, 1])
    onr = ap("ones_row", [1, 128])
    yout = ap("ygated", [HT, S], F32, "ExternalOutput")
    if M0_DEBUG:
        d_xn2 = ap("d_xn2", [D, S], BF16, "ExternalOutput")
        d_xbc = ap("d_xbc", [XB, S], F32, "ExternalOutput")
        d_dt = ap("d_dt", [HT, S], F32, "ExternalOutput")
        d_zs = ap("d_zs", [HT, S], F32, "ExternalOutput")
        d_ys = ap("d_ys", [HT, S], F32, "ExternalOutput")
        d_bc = ap("d_bc", [32, S], F32, "ExternalOutput")
    wsc = nc.dram_tensor("wsc", [XBT, 128, 3 * DT * 128], BF16)
    wsc_r = wsc.ap()

    W = M0_TC + 2
    TC = M0_TC
    with contextlib.ExitStack() as ctx:
        E = ctx.enter_context
        P = Prog(nc)

        wiz_sb = E(nc.sbuf_tensor("wiz_sb", [128, DT * HT], BF16))
        wdt_sb = E(nc.sbuf_tensor("wdt_sb", [128, XBT * HT], BF16))
        wbc_sb = E(nc.sbuf_tensor("wbc_sb", [128, XBT * 32], BF16))
        A_sb = E(nc.sbuf_tensor("A_sb", [128, HTT * STATE], F32))
        sel_sb = E(nc.sbuf_tensor("sel_sb", [16, 16 * 128], F32))
        cols = E(nc.sbuf_tensor("cols", [128, 2 * XBT + 4 * HTT + 2 * DT], F32))
        convb_c = cols[:, 0:XBT]
        bdt_c = cols[:, XBT:XBT + HTT]
        gam_c = cols[:, XBT + HTT:XBT + HTT + DT]
        bet_c = cols[:, XBT + HTT + DT:XBT + HTT + 2 * DT]
        dpar_c = cols[:, XBT + HTT + 2 * DT:XBT + 2 * HTT + 2 * DT]
        onescol = E(nc.sbuf_tensor("onescol", [128, 1], F32))
        onesrow = E(nc.sbuf_tensor("onesrow", [1, 128], F32))

        PS_A = E(nc.psum_tensor("PS_A", [128, 1024], F32))
        PS_B = E(nc.psum_tensor("PS_B", [128, 1024], F32))
        PS_C = E(nc.psum_tensor("PS_C", [128, 1024], F32))
        PS_R0 = E(nc.psum_tensor("PS_R0", [1, 512], F32))
        PS_R1 = E(nc.psum_tensor("PS_R1", [1, 512], F32))

        P.new_stage("prep_loads")
        P.dma("sync", onescol[:], onc)
        P.dma("sync", onesrow[:], onr)
        P.dma("sync", sel_sb[:], sel16)
        P.dma("sync", cols[:, 0:XBT], convb)
        P.dma("sync", cols[:, XBT:XBT + HTT], bdt)
        P.dma("sync", cols[:, XBT + HTT:XBT + HTT + DT], gam)
        P.dma("sync", cols[:, XBT + HTT + DT:XBT + HTT + 2 * DT], bet)
        P.dma("sync", cols[:, XBT + HTT + 2 * DT:XBT + 2 * HTT + 2 * DT], dpar)
        for k in range(DT):
            P.dma("gpsimd", wiz_sb[:, k * HT:(k + 1) * HT], wi_z[k * 128:(k + 1) * 128, :])
        for k in range(XBT):
            P.dma("gpsimd", wdt_sb[:, k * HT:(k + 1) * HT], wdt[k * 128:(k + 1) * 128, :])
            P.dma("gpsimd", wbc_sb[:, k * 32:(k + 1) * 32], wbc[k * 128:(k + 1) * 128, :])
        with contextlib.ExitStack() as pctx:
            Ep = pctx.enter_context
            convrows_sb = Ep(nc.sbuf_tensor("convrows_sb", [1, 3 * XB], F32))
            alog_sb = Ep(nc.sbuf_tensor("alog_sb", [128, HTT * STATE], F32))
            wjb = Ep(nc.sbuf_tensor("wjb", [128, XB], F32))
            wtmp = Ep(nc.sbuf_tensor("wtmp", [128, XB], F32))
            wbf = Ep(nc.sbuf_tensor("wbf", [128, XB], BF16))
            P.dma("sync", convrows_sb[:], convrows)
            P.dma("sync", alog_sb[:], alog)
            P.new_stage("prep_A")
            act(P, A_sb[:], alog_sb[:], AF.Exp)
            P.new_stage("prep_A2")
            P.on("scalar", lambda sc: sc.mul(A_sb[:], A_sb[:], -1.0))
            for j in range(3):
                for q in range(4):
                    P.new_stage("prep_bc_mm")
                    emit_mm_group(P, PS_A[:, (q % 2) * 512:(q % 2) * 512 + 512],
                                  [onesrow[:]], [convrows_sb[:, j * XB:(j + 1) * XB]],
                                  q * 512, 512, pn0=0)
                    P.new_stage("prep_bc_ev")
                    vec_copy(P, wjb[:, q * 512:(q + 1) * 512], PS_A[:, (q % 2) * 512:(q % 2) * 512 + 512])
                for k in range(DT):
                    P.new_stage("prep_wld")
                    P.dma("sync", wtmp[:], wi_xb[k * 128:(k + 1) * 128, :])
                    P.new_stage("prep_wsc")
                    vec_tt(P, wbf[:], wtmp[:], wjb[:], ALU.mult)
                    P.new_stage("prep_wst")
                    cs = (j * DT + k) * 128
                    P.dma("sync",
                          wsc_r[:, :, cs:cs + 128].rearrange("g p c -> p g c"),
                          wbf[:].rearrange("p (g c) -> p g c", c=128))

        xnt = [E(nc.sbuf_tensor(f"xn{i}", [128, W], F32)) for i in range(DT)]
        sqt = [E(nc.sbuf_tensor(f"sq{i}", [128, W], F32)) for i in range(DT)]
        xn2 = [E(nc.sbuf_tensor(f"xn2_{i}", [128, W], BF16)) for i in range(DT)]
        mu_b = E(nc.sbuf_tensor("mu_b", [128, W], F32))
        rstd_b = E(nc.sbuf_tensor("rstd_b", [128, W], F32))
        mu_row = E(nc.sbuf_tensor("mu_row", [1, W], F32))
        sq_row = E(nc.sbuf_tensor("sq_row", [1, W], F32))
        xbc_f = [E(nc.sbuf_tensor(f"xbcf{i}", [128, TC], F32)) for i in range(XBT)]
        xbc_b = [E(nc.sbuf_tensor(f"xbcb{i}", [128, TC], BF16)) for i in range(XBT)]
        dt_f = [E(nc.sbuf_tensor(f"dtf{i}", [128, TC], F32)) for i in range(HTT)]
        zs_f = [E(nc.sbuf_tensor(f"zsf{i}", [128, TC], F32)) for i in range(HTT)]
        brow = E(nc.sbuf_tensor("brow", [16, TC], F32))
        crow = E(nc.sbuf_tensor("crow", [16, TC], F32))
        hstate = E(nc.sbuf_tensor("hstate", [128, HTT * STATE], F32))
        hout = E(nc.sbuf_tensor("hout", [128, TC], F32))
        abar = E(nc.sbuf_tensor("abar", [128, TC], F32))
        utmp = E(nc.sbuf_tensor("utmp", [128, TC], F32))
        ytmp2 = E(nc.sbuf_tensor("ytmp2", [128, TC], F32))
        wg_buf = E(nc.sbuf_tensor("wg_buf", [128, 3 * DT * 128], BF16))
        yacc = sqt
        dtx = xnt

        P.new_stage("zero_state")
        vec_memset(P, hstate[:], 0.0)

        for ch in range(M0_NCH):
            t0 = ch * TC
            first, last = ch == 0, ch == M0_NCH - 1
            P.new_stage("xn_load")
            lo = 0 if first else t0 - 1
            hi = S if last else t0 + TC + 1
            off = 1 if first else 0
            for i in range(DT):
                P.dma("sync", xnt[i][:, off:off + (hi - lo)], xraw[i * 128:(i + 1) * 128, lo:hi])
                if first:
                    vec_memset(P, xnt[i][:, 0:1], 0.0)
                if last:
                    vec_memset(P, xnt[i][:, W - 1:W], 0.0)
            emit_ln(P, xnt, xnt, sqt, mu_b, rstd_b, (PS_R0, PS_R1), PS_A,
                    mu_row, sq_row, onescol[:], onesrow[:], W, out_f32=xnt)
            emit_ln(P, xnt, xn2, sqt, mu_b, rstd_b, (PS_R0, PS_R1), PS_A,
                    mu_row, sq_row, onescol[:], onesrow[:], W,
                    gam=[gam_c[:, i:i + 1] for i in range(DT)],
                    bet=[bet_c[:, i:i + 1] for i in range(DT)])
            if first or last:
                P.new_stage("halo_zero")
                col = 0 if first else W - 1
                for i in range(DT):
                    vec_memset(P, xn2[i][:, col:col + 1], 0.0)
            P.new_stage("wg_load0")
            P.dma("sync", wg_buf[:], wsc_r[0])
            for g in range(XBT):
                P.new_stage(f"xb_mm{g}")
                pp = PS_B[:, (g % 2) * 512:(g % 2) * 512 + 512]
                for j in range(3):
                    for k in range(DT):
                        def fn(t, j=j, k=k, pp=pp, g=g):
                            return t.matmul(pp[:, 0:TC],
                                            wg_buf[:, (j * DT + k) * 128:(j * DT + k) * 128 + 128],
                                            xn2[k][:, j:j + TC],
                                            start=(j == 0 and k == 0), stop=(j == 2 and k == DT - 1))
                        P.on("tensor", fn)
                P.new_stage(f"xb_ev{g}")
                act(P, xbc_f[g][:], pp[:, 0:TC], AF.Silu, bias=convb_c[:, g:g + 1])
                if g + 1 < XBT:
                    P.dma("sync", wg_buf[:], wsc_r[g + 1])
            P.new_stage("xb_cast")
            for g in range(XBT):
                vec_copy(P, xbc_b[g][:], xbc_f[g][:])
            for m in range(HTT):
                P.new_stage(f"z_mm{m}")
                pp = PS_B[:, (m % 2) * 512:(m % 2) * 512 + 512]
                emit_mm_group(P, pp, [wiz_sb[:, k * HT + m * 128:k * HT + m * 128 + 128] for k in range(DT)],
                              [xn2[k][:, 1:1 + TC] for k in range(DT)], 0, TC)
                P.new_stage(f"z_ev{m}")
                act(P, zs_f[m][:], pp[:, 0:TC], AF.Silu)
            for m in range(HTT):
                P.new_stage(f"dt_mm{m}")
                pp = PS_B[:, (m % 2) * 512:(m % 2) * 512 + 512]
                emit_mm_group(P, pp, [wdt_sb[:, k * HT + m * 128:k * HT + m * 128 + 128] for k in range(XBT)],
                              [xbc_b[k][:] for k in range(XBT)], 0, TC)
                P.new_stage(f"dt_abs{m}")
                act(P, abar[:], pp[:, 0:TC], AF.Abs, bias=bdt_c[:, m:m + 1])
                P.new_stage(f"dt_exp{m}")
                act(P, abar[:], abar[:], AF.Exp, scale=-1.0)
                P.new_stage(f"dt_ln{m}")
                act(P, abar[:], abar[:], AF.Ln, bias=1.0)
                P.new_stage(f"dt_relu{m}")
                act(P, dt_f[m][:], pp[:, 0:TC], AF.Relu, bias=bdt_c[:, m:m + 1])
                P.new_stage(f"dt_add{m}")
                vec_tt(P, dt_f[m][:], dt_f[m][:], abar[:], ALU.add)
            P.new_stage("bc_mm")
            emit_mm_group(P, PS_C[0:16, 0:512], [wbc_sb[:, k * 32:k * 32 + 16] for k in range(XBT)],
                          [xbc_b[k][:] for k in range(XBT)], 0, TC, pn0=0)
            emit_mm_group(P, PS_C[0:16, 512:1024], [wbc_sb[:, k * 32 + 16:k * 32 + 32] for k in range(XBT)],
                          [xbc_b[k][:] for k in range(XBT)], 0, TC, pn0=0)
            P.new_stage("bc_ev")
            vec_copy(P, brow[:], PS_C[0:16, 0:TC])
            vec_copy(P, crow[:], PS_C[0:16, 512:512 + TC])
            P.new_stage("dtx")
            for g in range(HTT):
                vec_tt(P, dtx[g][:, 0:TC], dt_f[g][:], xbc_f[g][:], ALU.mult)
            P.new_stage("yacc0")
            for g in range(HTT):
                vec_memset(P, yacc[g][:, 0:TC], 0.0)
            for n in range(STATE):
                P.new_stage(f"bbc_mm{n}")
                emit_mm_group(P, PS_C[:, 0:512], [sel_sb[:, n * 128:(n + 1) * 128]], [brow[:]], 0, TC, pn0=0)
                emit_mm_group(P, PS_C[:, 512:1024], [sel_sb[:, n * 128:(n + 1) * 128]], [crow[:]], 0, TC, pn0=0)
                for g in range(HTT):
                    col = g * STATE + n
                    P.new_stage(f"scan{n}_{g}")
                    vec_ts(P, abar[:], dt_f[g][:], A_sb[:, col:col + 1], None, ALU.mult)
                    if g > 0 or n > 0:
                        pc = (g * STATE + n) - 1 if g > 0 else (g + HTT - 1) * 0
                    P.new_stage(f"scan{n}_{g}b")
                    act(P, abar[:], abar[:], AF.Exp)
                    P.new_stage(f"scan{n}_{g}c")
                    vec_tt(P, utmp[:], dtx[g][:, 0:TC], PS_C[:, 0:TC], ALU.mult)
                    vec_scan(P, hout[:], abar[:], utmp[:], hstate[:, col:col + 1])
                    vec_tt(P, ytmp2[:], hout[:], PS_C[:, 512:512 + TC], ALU.mult)
                    vec_tt(P, yacc[g][:, 0:TC], yacc[g][:, 0:TC], ytmp2[:], ALU.add)
                    P.new_stage(f"scan{n}_{g}d")
                    act(P, hstate[:, col:col + 1], hout[:, TC - 1:TC], AF.Copy)
            if M0_DEBUG:
                P.new_stage("dbg_store")
                for i in range(DT):
                    P.dma("sync", d_xn2[i * 128:(i + 1) * 128, t0:t0 + TC], xn2[i][:, 1:1 + TC])
                for g in range(XBT):
                    P.dma("sync", d_xbc[g * 128:(g + 1) * 128, t0:t0 + TC], xbc_f[g][:])
                for g in range(HTT):
                    P.dma("sync", d_dt[g * 128:(g + 1) * 128, t0:t0 + TC], dt_f[g][:])
                    P.dma("sync", d_zs[g * 128:(g + 1) * 128, t0:t0 + TC], zs_f[g][:])
                    P.dma("sync", d_ys[g * 128:(g + 1) * 128, t0:t0 + TC], yacc[g][:, 0:TC])
                P.dma("sync", d_bc[0:16, t0:t0 + TC], brow[:])
                P.dma("sync", d_bc[16:32, t0:t0 + TC], crow[:])
            P.new_stage("yg1")
            for g in range(HTT):
                vec_ts(P, hout[:], xbc_f[g][:], dpar_c[:, g:g + 1], None, ALU.mult)
                vec_tt(P, yacc[g][:, 0:TC], yacc[g][:, 0:TC], hout[:], ALU.add)
                vec_tt(P, dt_f[g][:], yacc[g][:, 0:TC], zs_f[g][:], ALU.mult)
            P.new_stage("yg_store")
            for g in range(HTT):
                P.dma("sync", yout[g * 128:(g + 1) * 128, t0:t0 + TC], dt_f[g][:])
        P.finalize()
    return nc


def m0_inputs(x_fm, ins):
    W_in = np.asarray(ins["L0_attn_W_in"])
    W_dt = np.asarray(ins["L0_attn_W_dt"])
    conv_w = np.asarray(ins["L0_attn_conv_w"])
    A_log = np.asarray(ins["L0_attn_A_log"])
    sel = np.zeros((16, 16 * 128), np.float32)
    for n in range(16):
        sel[n, n * 128:(n + 1) * 128] = 1.0
    maps = []
    for c in range(NCORES):
        b, dh = c // 2, c % 2
        hsl = slice(dh * HT, (dh + 1) * HT)
        xbperm = np.concatenate([np.arange(dh * HT, (dh + 1) * HT),
                                 np.arange((1 - dh) * HT, (2 - dh) * HT)])
        alog_r = np.ascontiguousarray(
            A_log[hsl].reshape(HTT, 128, STATE).transpose(1, 0, 2).reshape(128, HTT * STATE))
        maps.append({
            "xraw": np.ascontiguousarray(x_fm[:, b * S:(b + 1) * S]),
            "wi_xb": np.ascontiguousarray(W_in[:, :XB][:, xbperm]),
            "wi_z": np.ascontiguousarray(W_in[:, XB + dh * HT:XB + (dh + 1) * HT]),
            "wdt": np.ascontiguousarray(W_dt[xbperm][:, hsl]),
            "wbc": np.ascontiguousarray(np.concatenate(
                [ins["L0_attn_W_B"], ins["L0_attn_W_C"]], axis=1)[xbperm]),
            "alog": alog_r,
            "convrows": np.ascontiguousarray(conv_w[xbperm].T.reshape(1, 3 * XB)),
            "convb": np.ascontiguousarray(np.asarray(ins["L0_attn_conv_b"])[xbperm].reshape(XBT, 128).T),
            "bdt": np.ascontiguousarray(np.asarray(ins["L0_attn_b_dt"])[hsl].reshape(HTT, 128).T),
            "gam": np.ascontiguousarray(np.asarray(ins["L0_attn_norm_scale"]).reshape(DT, 128).T),
            "bet": np.ascontiguousarray(np.asarray(ins["L0_attn_norm_bias"]).reshape(DT, 128).T),
            "dpar": np.ascontiguousarray(np.asarray(ins["L0_attn_D_param"])[hsl].reshape(HTT, 128).T),
            "sel16": sel,
            "ones_col": ONES_COL, "ones_row": ONES_ROW,
        })
    return maps


def run_M0(x_fm, ins):
    nc = build_M0()
    res = run_bass_kernel_spmd(nc, m0_inputs(x_fm, ins), core_ids=list(range(NCORES)))
    out = np.zeros((XB, T_ALL), np.float32)
    for c in range(NCORES):
        b, dh = c // 2, c % 2
        out[dh * HT:(dh + 1) * HT, b * S:(b + 1) * S] = res.results[c]["ygated"]
    return out



T1_STOP = "full"
T1_DEBUG = False
T1_TH = 1024
T1_TC = 512
NEXP = 8
FF = 4096
FFT = FF // 128


def scol(P, dst, src):
    act(P, dst, src, AF.Copy)


def vdrain(P):
    P.on("vector", lambda v: v.drain())


def sdrain(P):
    P.on("scalar", lambda sc: sc.drain())


def build_T1():
    nc = bass.Bass("TRN2", target_bir_lowering=False)
    ap = lambda n, s, d=F32, k="ExternalInput": nc.dram_tensor(n, s, d, kind=k).ap()
    xin = ap("xin", [D, T_C])
    ygin = ap("ygin", [XB, T_C])
    wout = ap("wout", [XB, D])
    wr = ap("wr", [D, NEXP])
    w1 = ap("w1", [NEXP, D, FF])
    w2 = ap("w2", [NEXP, FF, D])
    b1c_in = ap("b1c", [128, NEXP * FFT])
    b2_in = ap("b2", [NEXP, D])
    sel8 = ap("sel8", [NEXP, NEXP * 128])
    ident = ap("ident", [128, 128])
    onc = ap("ones_col", [128, 1])
    onr = ap("ones_row", [1, 128])
    xout = ap("xout", [D, T_C], F32, "ExternalOutput")
    d_wrow = ap("d_wrow", [NEXP, T_C], F32, "ExternalOutput") if T1_DEBUG else None
    d_lg = ap("d_lg", [128, 8], F32, "ExternalOutput") if T1_DEBUG else None
    d_pt = ap("d_pt", [128, 8], F32, "ExternalOutput") if T1_DEBUG else None
    d_tr = ap("d_tr", [8, 128], F32, "ExternalOutput") if T1_DEBUG else None
    d_h = ap("d_h", [128, 512], BF16, "ExternalOutput") if T1_DEBUG else None
    d_wb = ap("d_wb", [128, 512], F32, "ExternalOutput") if T1_DEBUG else None

    TH, TC = T1_TH, T1_TC
    with contextlib.ExitStack() as ctx:
        E = ctx.enter_context
        P = Prog(nc)
        out_sb = [E(nc.sbuf_tensor(f"out{i}", [128, TH], F32)) for i in range(DT)]
        xn_bf = [E(nc.sbuf_tensor(f"xnb{i}", [128, TH], BF16)) for i in range(DT)]
        wrow = E(nc.sbuf_tensor("wrow", [NEXP, TH], F32))
        wr_sb = E(nc.sbuf_tensor("wr_sb", [128, DT * NEXP], BF16))
        b1c = E(nc.sbuf_tensor("b1c_sb", [128, NEXP * FFT], F32))
        b2_sb = E(nc.sbuf_tensor("b2_sb", [NEXP, D], F32))
        sel_sb = E(nc.sbuf_tensor("sel_sb", [NEXP, NEXP * 128], F32))
        id_sb = E(nc.sbuf_tensor("id_sb", [128, 128], F32))
        onescol = E(nc.sbuf_tensor("onescol", [128, 1], F32))
        onesrow = E(nc.sbuf_tensor("onesrow", [1, 128], F32))
        ptile = E(nc.sbuf_tensor("ptile", [128, NEXP], F32))
        ptmp = E(nc.sbuf_tensor("ptmp", [128, NEXP], F32))
        t4c = E(nc.sbuf_tensor("t4c", [128, 4], F32))
        t2c = E(nc.sbuf_tensor("t2c", [128, 4], F32))
        w8 = E(nc.sbuf_tensor("w8", [128, NEXP], F32))
        PS_A = E(nc.psum_tensor("PS_A", [128, 1024], F32))
        PS_O = E(nc.psum_tensor("PS_O", [128, 2048], F32))
        PS_W = E(nc.psum_tensor("PS_W", [128, 512], F32))
        PS_X = E(nc.psum_tensor("PS_X", [128, 512], F32))

        P.new_stage("const_loads")
        P.dma("sync", b1c[:], b1c_in)
        P.dma("sync", b2_sb[:], b2_in)
        P.dma("sync", sel_sb[:], sel8)
        P.dma("sync", id_sb[:], ident)
        P.dma("sync", onescol[:], onc)
        P.dma("sync", onesrow[:], onr)
        for k in range(DT):
            P.dma("gpsimd", wr_sb[:, k * NEXP:(k + 1) * NEXP], wr[k * 128:(k + 1) * 128, :])

        for hh in range(2):
            t0 = hh * TH
            with contextlib.ExitStack() as sctx:
                Es = sctx.enter_context
                xt = [Es(nc.sbuf_tensor(f"xt{hh}_{i}", [128, TH], F32)) for i in range(DT)]
                sq = [Es(nc.sbuf_tensor(f"sq{hh}_{i}", [128, TH], F32)) for i in range(DT)]
                mu_b = Es(nc.sbuf_tensor(f"mu_b{hh}", [128, TH], F32))
                rstd_b = Es(nc.sbuf_tensor(f"rstd_b{hh}", [128, TH], F32))
                mu_row = Es(nc.sbuf_tensor(f"mu_row{hh}", [1, TH], F32))
                sq_row = Es(nc.sbuf_tensor(f"sq_row{hh}", [1, TH], F32))
                P.new_stage("x_load")
                for i in range(DT):
                    P.dma("sync", xt[i][:], xin[i * 128:(i + 1) * 128, t0:t0 + TH])
                emit_ln(P, xt, xn_bf, sq, mu_b, rstd_b,
                        (PS_A[0:1, 0:512], PS_A[0:1, 512:1024]), PS_O[:, 0:1024],
                        mu_row, sq_row, onescol[:], onesrow[:], TH, out_f32=out_sb)
                P.new_stage("base_addx")
                for i in range(DT):
                    vec_tt(P, out_sb[i][:], out_sb[i][:], xt[i][:], ALU.add)
            with contextlib.ExitStack() as sctx:
                Es = sctx.enter_context
                yg_bf = [Es(nc.sbuf_tensor(f"yg{hh}_{k}", [128, TH], BF16)) for k in range(XBT)]
                wo_sb = Es(nc.sbuf_tensor(f"wo_sb{hh}", [128, XBT * 1024], BF16))
                P.new_stage("yg_load")
                for k in range(XBT):
                    P.dma("gpsimd", yg_bf[k][:], ygin[k * 128:(k + 1) * 128, t0:t0 + TH])
                    P.dma("gpsimd", wo_sb[:, k * 1024:(k + 1) * 1024], wout[k * 128:(k + 1) * 128, :])
                for m in range(DT):
                    for j in range(2):
                        P.new_stage(f"wo_mm{m}_{j}")
                        pp = PS_A[:, (j % 2) * 512:(j % 2) * 512 + 512]
                        emit_mm_group(P, pp, [wo_sb[:, k * 1024 + m * 128:k * 1024 + m * 128 + 128] for k in range(XBT)],
                                      [yg_bf[k][:, j * 512:(j + 1) * 512] for k in range(XBT)], 0, 512)
                        P.new_stage(f"wo_ev{m}_{j}")
                        vec_tt(P, out_sb[m][:, j * 512:(j + 1) * 512],
                               out_sb[m][:, j * 512:(j + 1) * 512], pp[:, 0:512], ALU.add)
            def tourn8(src_t, op):
                P.new_stage("tn_a")
                vec_tt(P, t4c[:], src_t[:, 0:4], src_t[:, 4:8], op)
                vdrain(P)
                P.new_stage("tn_b")
                vec_tt(P, t2c[:, 0:2], t4c[:, 0:2], t4c[:, 2:4], op)
                vdrain(P)
                P.new_stage("tn_c")
                vec_tt(P, t2c[:, 2:3], t2c[:, 0:1], t2c[:, 1:2], op)
                vdrain(P)
                return t2c[:, 2:3]
            for tt in range(TH // 128 if T1_STOP != "base" else 0):
                P.new_stage(f"rt_mm{tt}")
                emit_mm_group(P, PS_W[:, 0:NEXP], [xn_bf[k][:, tt * 128:(tt + 1) * 128] for k in range(DT)],
                              [wr_sb[:, k * NEXP:(k + 1) * NEXP] for k in range(DT)], 0, NEXP, pn0=0)
                P.new_stage(f"rt_ev{tt}")
                vec_copy(P, ptile[:], PS_W[:, 0:NEXP])
                vdrain(P)
                if T1_DEBUG and hh == 0 and tt == 0:
                    P.new_stage("dbg_lg")
                    P.dma("sync", d_lg, ptile[:])
                mc = tourn8(ptile, ALU.max)
                P.new_stage(f"rt_sub{tt}")
                vec_tt(P, ptile[:], ptile[:], mc.broadcast_to([128, NEXP]), ALU.subtract)
                vdrain(P)
                P.new_stage(f"rt_exp{tt}")
                act(P, ptile[:], ptile[:], AF.Exp)
                sdrain(P)
                sc_ = tourn8(ptile, ALU.add)
                P.new_stage(f"rt_rc{tt}")
                P.on("vector", lambda v, sc_=sc_: v.reciprocal(t2c[:, 3:4], sc_))
                vdrain(P)
                P.new_stage(f"rt_nm{tt}")
                vec_tt(P, ptile[:], ptile[:], t2c[:, 3:4].broadcast_to([128, NEXP]), ALU.mult)
                vdrain(P)
                mc = tourn8(ptile, ALU.max)
                P.new_stage(f"rt_t{tt}d")
                vec_tt(P, ptmp[:], ptile[:], mc.broadcast_to([128, NEXP]), ALU.is_ge)
                vdrain(P)
                P.new_stage(f"rt_t{tt}e")
                vec_stt(P, ptmp[:], ptmp[:], -1e30, ptile[:], ALU.mult, ALU.add)
                vdrain(P)
                mc = tourn8(ptmp, ALU.max)
                P.new_stage(f"rt_t{tt}i")
                vec_tt(P, ptmp[:], ptile[:], mc.broadcast_to([128, NEXP]), ALU.is_ge)
                vdrain(P)
                P.new_stage(f"rt_t{tt}j")
                vec_tt(P, ptile[:], ptile[:], ptmp[:], ALU.mult)
                vdrain(P)
                if T1_DEBUG and hh == 0 and tt == 0:
                    P.new_stage("dbg_pt")
                    P.dma("sync", d_pt, ptile[:])
                P.new_stage(f"rt_tr{tt}")
                P.on("tensor", lambda t, tt=tt: t.transpose(PS_W[0:NEXP, 0:128], ptile[:], id_sb[:]))
                P.new_stage(f"rt_trev{tt}")
                vec_copy(P, wrow[:, tt * 128:(tt + 1) * 128], PS_W[0:NEXP, 0:128])
                vdrain(P)
                if T1_DEBUG and hh == 0 and tt in (0, 1, 8, 15):
                    P.new_stage(f"dbg_tr{tt}")
                    P.dma("sync", d_tr[:, (0 if tt == 0 else 1) * 0:128], wrow[:, 0:128])
            with contextlib.ExitStack() as sctx:
                Es = sctx.enter_context
                poolA = Es(nc.sbuf_tensor(f"poolA{hh}", [128, DT * FF], BF16))
                poolB = Es(nc.sbuf_tensor(f"poolB{hh}", [128, 2 * 8192], BF16))
                h_bf = Es(nc.sbuf_tensor(f"h_bf{hh}", [128, FFT * TC], BF16))
                wbce = Es(nc.sbuf_tensor(f"wbce{hh}", [128, TC], F32))
                opsum = [PS_O[:, 0:512], PS_O[:, 512:1024], PS_O[:, 1024:1536], PS_O[:, 1536:2048],
                         PS_A[:, 0:512], PS_A[:, 512:1024], PS_W[:, 0:512], PS_X[:, 0:512]]
                for e in range(NEXP if T1_STOP == "full" else 0):
                    P.new_stage(f"w1_load{e}")
                    for k in range(DT):
                        P.dma("gpsimd", poolA[:, k * FF:(k + 1) * FF], w1[e, k * 128:(k + 1) * 128, :])
                    for tci, tcs in enumerate(range(0, TH, TC)):
                        for ht in range(FFT):
                            P.new_stage(f"moe_h{e}_{tci}_{ht}")
                            pp = PS_A[:, (ht % 2) * 512:(ht % 2) * 512 + 512]
                            emit_mm_group(P, pp, [poolA[:, k * FF + ht * 128:k * FF + ht * 128 + 128] for k in range(DT)],
                                          [xn_bf[k][:, tcs:tcs + TC] for k in range(DT)], 0, TC)
                            if ht == 0:
                                P.dma("gpsimd",
                                      poolB[:, 0:8192].rearrange("p (kk d) -> p kk d", d=1024),
                                      w2[e, 0:1024, :].rearrange("(kk p) d -> p kk d", p=128))
                            P.new_stage(f"moe_g{e}_{tci}_{ht}")
                            act(P, h_bf[:, ht * TC:(ht + 1) * TC], pp[:, 0:TC], AF.Gelu_apprx_tanh,
                                bias=b1c[:, e * FFT + ht:e * FFT + ht + 1])
                        P.new_stage(f"moe_w{e}_{tci}")
                        emit_mm_group(P, PS_A[:, 0:512], [sel_sb[:, e * 128:(e + 1) * 128]],
                                      [wrow[:, tcs:tcs + TC]], 0, TC, pn0=0)
                        P.new_stage(f"moe_wev{e}_{tci}")
                        vec_copy(P, wbce[:], PS_A[:, 0:TC])
                        for gi in range(4):
                            P.new_stage(f"moe_o{e}_{tci}_{gi}")
                            for ki in range(8):
                                kk = gi * 8 + ki
                                for mm in range(8):
                                    def fn(t, kk=kk, ki=ki, gi=gi, mm=mm):
                                        return t.matmul(opsum[mm][:, 0:TC],
                                                        poolB[:, (gi % 2) * 8192 + ki * 1024 + mm * 128:
                                                              (gi % 2) * 8192 + ki * 1024 + mm * 128 + 128],
                                                        h_bf[:, kk * TC:(kk + 1) * TC],
                                                        start=(kk == 0), stop=(kk == FFT - 1))
                                    P.on("tensor", fn)
                            if gi + 1 < 4:
                                gl = gi + 1
                                P.dma("gpsimd",
                                      poolB[:, (gl % 2) * 8192:(gl % 2) * 8192 + 8192].rearrange("p (kk d) -> p kk d", d=1024),
                                      w2[e, gl * 1024:(gl + 1) * 1024, :].rearrange("(kk p) d -> p kk d", p=128))
                        P.new_stage(f"moe_acc{e}_{tci}")
                        for mm in range(8):
                            vec_tt(P, opsum[mm][:, 0:TC], opsum[mm][:, 0:TC], wbce[:], ALU.mult)
                        P.new_stage(f"moe_acc2{e}_{tci}")
                        for mm in range(8):
                            vec_tt(P, out_sb[mm][:, tcs:tcs + TC], out_sb[mm][:, tcs:tcs + TC],
                                   opsum[mm][:, 0:TC], ALU.add)
            if T1_DEBUG:
                P.new_stage("dbg_moe")
                P.dma("sync", d_h, h_bf[:, 0:512])
                P.dma("sync", d_wb, wbce[:])
            for m in range(DT if T1_STOP == "full" else 0):
                P.new_stage(f"b2_mm{m}")
                emit_mm_group(P, PS_A[:, 0:512], [b2_sb[:, m * 128:(m + 1) * 128]], [wrow[:]], 0, 512, pn0=0)
                emit_mm_group(P, PS_A[:, 512:1024], [b2_sb[:, m * 128:(m + 1) * 128]], [wrow[:]], 512, 512, pn0=0)
                P.new_stage(f"b2_ev{m}")
                vec_tt(P, out_sb[m][:, 0:512], out_sb[m][:, 0:512], PS_A[:, 0:512], ALU.add)
                vec_tt(P, out_sb[m][:, 512:1024], out_sb[m][:, 512:1024], PS_A[:, 512:1024], ALU.add)
            P.new_stage("store")
            for i in range(DT):
                P.dma("sync", xout[i * 128:(i + 1) * 128, t0:t0 + TH], out_sb[i][:])
        P.finalize()
    return nc


def t1_inputs(x_fm, yg_fm, ins):
    w_router = np.asarray(ins["L0_ffn_w_router"])
    b1 = np.asarray(ins["L0_ffn_b1"])
    b1c = np.ascontiguousarray(b1.reshape(NEXP, FFT, 128).transpose(2, 0, 1).reshape(128, NEXP * FFT))
    sel = np.zeros((NEXP, NEXP * 128), np.float32)
    for e in range(NEXP):
        sel[e, e * 128:(e + 1) * 128] = 1.0
    maps = []
    for c in range(NCORES):
        ts = slice(c * T_C, (c + 1) * T_C)
        maps.append({
            "xin": np.ascontiguousarray(x_fm[:, ts]),
            "ygin": np.ascontiguousarray(yg_fm[:, ts]),
            "wout": np.asarray(ins["L0_attn_W_out"]),
            "wr": w_router,
            "w1": np.asarray(ins["L0_ffn_w1"]),
            "w2": np.asarray(ins["L0_ffn_w2"]),
            "b1c": b1c,
            "b2": np.asarray(ins["L0_ffn_b2"]),
            "sel8": sel,
            "ident": IDENT,
            "ones_col": ONES_COL, "ones_row": ONES_ROW,
        })
    return maps


def run_T1(x_fm, yg_fm, ins):
    nc = build_T1()
    res = run_bass_kernel_spmd(nc, t1_inputs(x_fm, yg_fm, ins), core_ids=list(range(NCORES)))
    return np.concatenate([r["xout"] for r in res.results], axis=1)



SHIFT_C = 20.0
M1_DEBUG = False
NH_C = 8
HD = 64


def nc_psum(nc, E, name, shape):
    return E(nc.psum_tensor(name, shape, F32))


def build_M1():
    nc = bass.Bass("TRN2", target_bir_lowering=False)
    ap = lambda n, s, d=F32, k="ExternalInput": nc.dram_tensor(n, s, d, kind=k).ap()
    xb_in = ap("xb_in", [D, S])
    wq = ap("wq", [D, 512])
    wk = ap("wk", [D, 512])
    wv = ap("wv", [D, 512])
    bq = ap("bq", [128, 4])
    bk = ap("bk", [128, 4])
    bv_r = ap("bv_r", [1, 512])
    cosm = ap("cosm", [32, S])
    sinm = ap("sinm", [32, S])
    maskd = ap("maskd", [128, 128])
    ident = ap("ident", [128, 128])
    attn_o = ap("attn_o", [512, S], BF16, "ExternalOutput")
    if M1_DEBUG:
        d_q = ap("d_q", [128, 512], BF16, "ExternalOutput")
        d_k = ap("d_k", [128, 512], BF16, "ExternalOutput")
        d_v = ap("d_v", [128, 65], BF16, "ExternalOutput")
        d_st = ap("d_st", [128, 512], BF16, "ExternalOutput")
    TCn = 512

    with contextlib.ExitStack() as ctx:
        E = ctx.enter_context
        P = Prog(nc)
        q_bf = [E(nc.sbuf_tensor(f"q{m}", [128, S], BF16)) for m in range(4)]
        k_bf = [E(nc.sbuf_tensor(f"k{m}", [128, S], BF16)) for m in range(4)]
        v_tm = [E(nc.sbuf_tensor(f"v{t}", [128, NH_C * 65], BF16)) for t in range(S // 128)]
        mask_sb = E(nc.sbuf_tensor("mask_sb", [128, 128], BF16))
        id_sb = E(nc.sbuf_tensor("id_sb", [128, 128], F32))
        bcols = E(nc.sbuf_tensor("bcols", [128, 8], F32))
        bvrow = E(nc.sbuf_tensor("bvrow", [1, 512], F32))
        st_bf = E(nc.sbuf_tensor("st_bf", [128, 512], BF16))
        ot_tm = E(nc.sbuf_tensor("ot_tm", [128, 512], F32))
        rec_c = E(nc.sbuf_tensor("rec_c", [128, 4], F32))
        oh_fm = E(nc.sbuf_tensor("oh_fm", [64, S], BF16))

        wq_sb = E(nc.sbuf_tensor("wq_sb", [128, DT * 512], BF16))
        wk_sb = E(nc.sbuf_tensor("wk_sb", [128, DT * 512], BF16))
        wv_sb = E(nc.sbuf_tensor("wv_sb", [128, DT * 512], BF16))
        P.new_stage("m1_const")
        for k in range(DT):
            P.dma("gpsimd", wq_sb[:, k * 512:(k + 1) * 512], wq[k * 128:(k + 1) * 128, :])
            P.dma("gpsimd", wk_sb[:, k * 512:(k + 1) * 512], wk[k * 128:(k + 1) * 128, :])
            P.dma("gpsimd", wv_sb[:, k * 512:(k + 1) * 512], wv[k * 128:(k + 1) * 128, :])
        P.dma("gpsimd", mask_sb[:], maskd)
        P.dma("sync", id_sb[:], ident)
        P.dma("sync", bcols[:, 0:4], bq)
        P.dma("sync", bcols[:, 4:8], bk)
        P.dma("sync", bvrow[:], bv_r)
        P.new_stage("m1_vones")
        for t in range(S // 128):
            P.on("vector", lambda v, t=t: v.memset(v_tm[t][:], 0.0))

        with contextlib.ExitStack() as sctx:
            Es = sctx.enter_context
            xt = [Es(nc.sbuf_tensor(f"pjx{i}", [128, TCn], F32)) for i in range(DT)]
            sq = [Es(nc.sbuf_tensor(f"pjs{i}", [128, TCn], F32)) for i in range(DT)]
            xn2 = [Es(nc.sbuf_tensor(f"pjn{i}", [128, TCn], BF16)) for i in range(DT)]
            mu_b = Es(nc.sbuf_tensor("pjmu", [128, TCn], F32))
            rstd_b = Es(nc.sbuf_tensor("pjrs", [128, TCn], F32))
            mu_row = Es(nc.sbuf_tensor("pjmr", [1, TCn], F32))
            sq_row = Es(nc.sbuf_tensor("pjsr", [1, TCn], F32))
            qf4 = [Es(nc.sbuf_tensor(f"pjqf{m}", [128, TCn], F32)) for m in range(1)]
            rt_a = Es(nc.sbuf_tensor("pjra", [32, TCn], F32))
            rt_b = Es(nc.sbuf_tensor("pjrb", [32, TCn], F32))
            rt_c = Es(nc.sbuf_tensor("pjrc", [32, TCn], F32))
            rt_d = Es(nc.sbuf_tensor("pjrd", [32, TCn], F32))
            cos_sb = Es(nc.sbuf_tensor("cos_sb", [32, TCn], F32))
            sin_sb = Es(nc.sbuf_tensor("sin_sb", [32, TCn], F32))
            onescol = Es(nc.sbuf_tensor("pjoc", [128, 1], F32))
            onesrow = Es(nc.sbuf_tensor("pjor", [1, 128], F32))
            PS_A = nc_psum(nc, Es, "pjA", [128, 1024])
            PS_B = nc_psum(nc, Es, "pjB", [128, 1024])
            PS_R0 = nc_psum(nc, Es, "pjR0", [1, 512])
            PS_R1 = nc_psum(nc, Es, "pjR1", [1, 512])
            P.new_stage("pj_const")
            P.on("vector", lambda v: v.memset(onescol[:], 1.0))
            P.new_stage("pj_const2")
            vec_memset(P, onesrow[:], 1.0)
            vdrain(P)
            for ci, t0 in enumerate(range(0, S, TCn)):
                P.new_stage("pj_load")
                for i in range(DT):
                    P.dma("sync", xt[i][:], xb_in[i * 128:(i + 1) * 128, t0:t0 + TCn])
                P.dma("sync", cos_sb[:], cosm[:, t0:t0 + TCn])
                P.dma("sync", sin_sb[:], sinm[:, t0:t0 + TCn])
                emit_ln(P, xt, xn2, sq, mu_b, rstd_b, (PS_R0, PS_R1), PS_A,
                        mu_row, sq_row, onescol[:], onesrow[:], TCn)
                for which, w_in, dst, bofs in (("q", wq_sb, q_bf, 0), ("k", wk_sb, k_bf, 4)):
                    for m in range(4):
                        P.new_stage(f"pj_{which}mm{m}")
                        pp = PS_B[:, (m % 2) * 512:(m % 2) * 512 + 512]
                        emit_mm_group(P, pp, [w_in[:, k * 512 + m * 128:k * 512 + (m + 1) * 128] for k in range(DT)],
                                      [xn2[k][:] for k in range(DT)], 0, TCn)
                        P.new_stage(f"pj_{which}ev{m}")
                        vec_ts(P, qf4[0][:], pp[:, 0:TCn], bcols[:, bofs + m:bofs + m + 1], None, ALU.add)
                        for hb in (0, 64):
                            P.new_stage(f"pj_{which}r1{m}{hb}")
                            vec_copy(P, rt_a[:], qf4[0][hb:hb + 32, :])
                            vec_copy(P, rt_b[:], qf4[0][hb + 32:hb + 64, :])
                            P.new_stage(f"pj_{which}r2{m}{hb}")
                            vec_tt(P, rt_c[:], rt_a[:], cos_sb[0:32, :], ALU.mult)
                            vec_tt(P, rt_d[:], rt_b[:], sin_sb[0:32, :], ALU.mult)
                            P.new_stage(f"pj_{which}r3{m}{hb}")
                            vec_tt(P, rt_c[:], rt_c[:], rt_d[:], ALU.subtract)
                            vec_tt(P, rt_d[:], rt_a[:], sin_sb[0:32, :], ALU.mult)
                            P.new_stage(f"pj_{which}r4{m}{hb}")
                            vec_tt(P, rt_b[:], rt_b[:], cos_sb[0:32, :], ALU.mult)
                            P.new_stage(f"pj_{which}r5{m}{hb}")
                            vec_tt(P, rt_d[:], rt_d[:], rt_b[:], ALU.add)
                            P.new_stage(f"pj_{which}r6{m}{hb}")
                            vec_copy(P, qf4[0][hb:hb + 32, :], rt_c[:])
                            vec_copy(P, qf4[0][hb + 32:hb + 64, :], rt_d[:])
                        P.new_stage(f"pj_{which}cp{m}")
                        vec_copy(P, dst[m][:, t0:t0 + TCn], qf4[0][:])
                for tb in range(TCn // 128):
                    tg = (t0 + tb * 128) // 128
                    P.new_stage(f"pj_vmm{tb}")
                    pvof = lambda h: h * 65 if h < 4 else 512 + (h - 4) * 65
                    for h in range(NH_C):
                        for k in range(DT):
                            def fn(t, tb=tb, h=h, k=k, pvof=pvof):
                                return t.matmul(PS_B[:, pvof(h):pvof(h) + 64],
                                                xn2[k][:, tb * 128:(tb + 1) * 128],
                                                wv_sb[:, k * 512 + h * 64:k * 512 + (h + 1) * 64],
                                                start=(k == 0), stop=(k == DT - 1))
                            P.on("tensor", fn)
                    P.new_stage(f"pj_vev{tb}")
                    for h in range(NH_C):
                        act(P, v_tm[tg][:, h * 65:h * 65 + 64], PS_B[:, pvof(h):pvof(h) + 64], AF.Copy)

        with contextlib.ExitStack() as sctx:
            Es = sctx.enter_context
            onesr = Es(nc.sbuf_tensor("m1_onesr", [1, 128], F32))
            bvb = Es(nc.sbuf_tensor("m1_bvb", [128, NH_C * 65], F32))
            PS_V = nc_psum(nc, Es, "m1psv", [128, 1024])
            P.new_stage("m1_fixv0")
            vec_memset(P, onesr[:], 1.0)
            vdrain(P)
            P.new_stage("m1_fixv1")
            pvof2 = lambda h: h * 65 if h < 4 else 512 + (h - 4) * 65
            for h in range(NH_C):
                def fn(t, h=h):
                    return t.matmul(PS_V[:, pvof2(h):pvof2(h) + 64], onesr[0:1, :],
                                    bvrow[0:1, h * 64:(h + 1) * 64], start=True, stop=True)
                P.on("tensor", fn)
            P.new_stage("m1_fixv2")
            for h in range(NH_C):
                vec_copy(P, bvb[:, h * 65:h * 65 + 64], PS_V[:, pvof2(h):pvof2(h) + 64])
            P.new_stage("m1_fixv3")
            for h in range(NH_C):
                vec_memset(P, bvb[:, h * 65 + 64:h * 65 + 65], 1.0)
            P.new_stage("m1_fixv4")
            for t in range(S // 128):
                vec_tt(P, v_tm[t][:], v_tm[t][:], bvb[:], ALU.add)

        if M1_DEBUG:
            P.new_stage("m1_dbg")
            P.dma("sync", d_q[0:128, :], q_bf[0][:, 0:512])
            P.dma("sync", d_k[0:128, :], k_bf[0][:, 0:512])
            P.dma("sync", d_v, v_tm[0][:, 455:520])
        with contextlib.ExitStack() as sctx:
            Es = sctx.enter_context
            PS_S = nc_psum(nc, Es, "m1pss", [128, 1024])
            PS_P = nc_psum(nc, Es, "m1psp", [128, 512])
            PS_T = nc_psum(nc, Es, "m1pst", [128, 512])
            for h in range(NH_C):
                qt = q_bf[h // 2]
                kt_ = k_bf[h // 2]
                hb = (h % 2) * 64
                for band in range(S // 512):
                    nkt = 4 * band + 4
                    for kti in range(nkt):
                        pp = PS_S[:, (kti % 2) * 512:(kti % 2) * 512 + 512]
                        P.new_stage(f"at_s{h}_{band}_{kti}")
                        def fn(t, pp=pp, kti=kti, band=band, qt=qt, kt_=kt_, hb=hb):
                            return t.matmul(pp, kt_[hb:hb + 64, kti * 128:(kti + 1) * 128],
                                            qt[hb:hb + 64, band * 512:(band + 1) * 512],
                                            start=True, stop=True)
                        P.on("tensor", fn)
                        P.new_stage(f"at_e{h}_{band}_{kti}")
                        act(P, st_bf[:], pp, AF.Exp, scale=0.125)
                        dq = kti - 4 * band
                        if dq >= 0:
                            P.new_stage(f"at_m{h}_{band}_{kti}")
                            vec_tt(P, st_bf[:, dq * 128:(dq + 1) * 128],
                                   st_bf[:, dq * 128:(dq + 1) * 128], mask_sb[:], ALU.mult)
                        P.new_stage(f"at_p{h}_{band}_{kti}")
                        for qb in range(4):
                            if 4 * band + qb < kti:
                                continue
                            def fn(t, qb=qb, kti=kti, band=band, h=h, nkt=nkt):
                                return t.matmul(PS_P[:, qb * 128:qb * 128 + 65],
                                                st_bf[:, qb * 128:(qb + 1) * 128],
                                                v_tm[kti][:, h * 65:h * 65 + 65],
                                                start=(kti == 0 and qb == 0), stop=(kti == 4 * band + qb))
                            P.on("tensor", fn)
                    if M1_DEBUG and h == 7 and band == 0:
                        P.new_stage("m1_dbgst")
                        P.dma("sync", d_st, st_bf[:])
                    P.new_stage(f"at_r{h}_{band}")
                    for qb in range(4):
                        P.on("vector", lambda v, qb=qb: v.reciprocal(
                            rec_c[:, qb:qb + 1], PS_P[:, qb * 128 + 64:qb * 128 + 65]))
                    vdrain(P)
                    P.new_stage(f"at_n{h}_{band}")
                    for qb in range(4):
                        vec_tt(P, ot_tm[:, qb * 128:qb * 128 + 64], PS_P[:, qb * 128:qb * 128 + 64],
                               rec_c[:, qb:qb + 1].broadcast_to([128, 64]), ALU.mult)
                    vdrain(P)
                    P.new_stage(f"at_t{h}_{band}")
                    for qb in range(4):
                        def fn(t, qb=qb):
                            return t.transpose(PS_T[0:64, qb * 128:(qb + 1) * 128],
                                               ot_tm[:, qb * 128:qb * 128 + 64], id_sb[:])
                        P.on("tensor", fn)
                    P.new_stage(f"at_o{h}_{band}")
                    vec_copy(P, oh_fm[:, band * 512:(band + 1) * 512], PS_T[0:64, 0:512])
                P.new_stage(f"at_st{h}")
                P.dma("sync", attn_o[h * 64:(h + 1) * 64, :], oh_fm[:])
        P.finalize()
    return nc


def m1_inputs(x_fm, ins):
    wq = np.asarray(ins["L1_attn_wq"])
    wk = np.asarray(ins["L1_attn_wk"])
    wv = np.asarray(ins["L1_attn_wv"])
    bq = np.asarray(ins["L1_attn_bq"])
    bk = np.asarray(ins["L1_attn_bk"])
    bv = np.asarray(ins["L1_attn_bv"])
    pos = np.arange(S, dtype=np.float64)[:, None]
    dim = np.arange(32, dtype=np.float64)[None, :]
    theta = pos / np.exp(2.0 * dim / 64 * math.log(10000.0))
    cosm = np.ascontiguousarray(np.cos(theta).T.astype(np.float32))
    sinm = np.ascontiguousarray(np.sin(theta).T.astype(np.float32))
    maskd = np.triu(np.ones((128, 128), np.float32))
    maps = []
    for c in range(NCORES):
        b, hh = c // 2, c % 2
        hsl = slice(hh * 8 * 64, (hh + 1) * 8 * 64)
        wqT = wq.T[:, hsl]
        wkT = wk.T[:, hsl]
        wvT = wv.T[:, hsl]
        pq = np.concatenate(
            [np.concatenate([np.arange(h * 64, (h + 1) * 64, 2),
                             np.arange(h * 64 + 1, (h + 1) * 64, 2)]) for h in range(8)])
        wq_c = np.ascontiguousarray(wqT[:, pq])
        wk_c = np.ascontiguousarray(wkT[:, pq])
        bq_c = np.ascontiguousarray(bq[hsl][pq].reshape(4, 128).T)
        bk_c = np.ascontiguousarray(bk[hsl][pq].reshape(4, 128).T)
        maps.append({
            "xb_in": np.ascontiguousarray(x_fm[:, b * S:(b + 1) * S]),
            "wq": wq_c, "wk": wk_c, "wv": np.ascontiguousarray(wvT),
            "bq": bq_c, "bk": bk_c,
            "bv_r": np.ascontiguousarray(bv[hsl].reshape(1, 512)),
            "cosm": cosm, "sinm": sinm, "maskd": maskd, "ident": IDENT,
        })
    return maps


def run_M1(x_fm, ins):
    nc = build_M1()
    res = run_bass_kernel_spmd(nc, m1_inputs(x_fm, ins), core_ids=list(range(NCORES)))
    out = np.zeros((D, T_ALL), np.float32)
    import ml_dtypes
    for c in range(NCORES):
        b, hh = c // 2, c % 2
        a = np.asarray(res.results[c]["attn_o"])
        if a.dtype != np.float32:
            a = a.view(ml_dtypes.bfloat16).astype(np.float32) if a.dtype == np.uint16 else a.astype(np.float32)
        out[hh * 512:(hh + 1) * 512, b * S:(b + 1) * S] = a
    return out



def build_T2():
    nc = bass.Bass("TRN2", target_bir_lowering=False)
    ap = lambda n, s, d=F32, k="ExternalInput": nc.dram_tensor(n, s, d, kind=k).ap()
    xin = ap("xin", [D, T_C + 1])
    attn_in = ap("attn_in", [D, T_C], BF16)
    woT = ap("woT", [D, D])
    wrT = ap("wrT", [D, D])
    wkT = ap("wkT", [D, D])
    wvT = ap("wvT", [D, D])
    bo_c = ap("bo_c", [128, DT])
    mur_c = ap("mur_c", [128, DT])
    muk_c = ap("muk_c", [128, DT])
    xout = ap("xout", [D, T_C], F32, "ExternalOutput")
    TH = 1024

    with contextlib.ExitStack() as ctx:
        E = ctx.enter_context
        P = Prog(nc)
        out_sb = [E(nc.sbuf_tensor(f"t2o{i}", [128, TH], F32)) for i in range(DT)]
        xr_bf = [E(nc.sbuf_tensor(f"t2xr{i}", [128, TH], BF16)) for i in range(DT)]
        xk_bf = [E(nc.sbuf_tensor(f"t2xk{i}", [128, TH], BF16)) for i in range(DT)]
        r_bf = [E(nc.sbuf_tensor(f"t2r{i}", [128, TH], BF16)) for i in range(DT)]
        ka_bf = [E(nc.sbuf_tensor(f"t2ka{i}", [128, TH], BF16)) for i in range(DT)]
        cols = E(nc.sbuf_tensor("t2cols", [128, 4 * DT], F32))
        bo_sb = cols[:, 0:DT]
        mur_sb = cols[:, DT:2 * DT]
        muk_sb = cols[:, 2 * DT:3 * DT]
        om_sb = E(nc.sbuf_tensor("t2om", [128, 2 * DT], F32))
        PS_A = E(nc.psum_tensor("t2psA", [128, 1024], F32))
        PS_B = E(nc.psum_tensor("t2psB", [128, 1024], F32))
        PS_R0 = E(nc.psum_tensor("t2psR0", [1, 512], F32))
        PS_R1 = E(nc.psum_tensor("t2psR1", [1, 512], F32))

        P.new_stage("t2_cols")
        P.dma("sync", bo_sb, bo_c)
        P.dma("sync", mur_sb, mur_c)
        P.dma("sync", muk_sb, muk_c)
        P.new_stage("t2_om")
        vec_ts(P, om_sb[:, 0:DT], mur_sb, -1.0, 1.0, ALU.mult, ALU.add)
        vec_ts(P, om_sb[:, DT:2 * DT], muk_sb, -1.0, 1.0, ALU.mult, ALU.add)
        vdrain(P)

        for hh in range(2):
            t0 = hh * TH
            W = TH + 1
            with contextlib.ExitStack() as sctx:
                Es = sctx.enter_context
                xt = [Es(nc.sbuf_tensor(f"t2x{hh}_{i}", [128, W], F32)) for i in range(DT)]
                sq = [Es(nc.sbuf_tensor(f"t2s{hh}_{i}", [128, W], F32)) for i in range(DT)]
                mu_b = Es(nc.sbuf_tensor(f"t2mu{hh}", [128, W], F32))
                rstd_b = Es(nc.sbuf_tensor(f"t2rs{hh}", [128, W], F32))
                mu_row = Es(nc.sbuf_tensor(f"t2mr{hh}", [1, W], F32))
                sq_row = Es(nc.sbuf_tensor(f"t2sr{hh}", [1, W], F32))
                onescol = Es(nc.sbuf_tensor(f"t2oc{hh}", [128, 1], F32))
                onesrow = Es(nc.sbuf_tensor(f"t2or{hh}", [1, 128], F32))
                P.new_stage("t2_c2")
                P.on("vector", lambda v, onescol=onescol: v.memset(onescol[:], 1.0))
                P.new_stage("t2_c3")
                vec_memset(P, onesrow[:], 1.0)
                vdrain(P)
                P.new_stage("t2_xload")
                for i in range(DT):
                    P.dma("sync", xt[i][:], xin[i * 128:(i + 1) * 128, t0:t0 + W])
                emit_ln(P, xt, sq, sq, mu_b, rstd_b, (PS_R0, PS_R1), PS_A,
                        mu_row, sq_row, onescol[:], onesrow[:], W)
                P.new_stage("t2_base")
                for i in range(DT):
                    vec_copy(P, out_sb[i][:], xt[i][:, 1:1 + TH])
                P.new_stage("t2_mix1")
                for i in range(DT):
                    vec_ts(P, mu_b[:, 0:TH], sq[i][:, 1:1 + TH], mur_sb[:, i:i + 1], None, ALU.mult)
                    vec_stt(P, xr_bf[i][:], sq[i][:, 0:TH], om_sb[:, i:i + 1], mu_b[:, 0:TH],
                            ALU.mult, ALU.add)
                    vec_ts(P, mu_b[:, 0:TH], sq[i][:, 1:1 + TH], muk_sb[:, i:i + 1], None, ALU.mult)
                    vec_stt(P, xk_bf[i][:], sq[i][:, 0:TH], om_sb[:, DT + i:DT + i + 1], mu_b[:, 0:TH],
                            ALU.mult, ALU.add)
            with contextlib.ExitStack() as sctx:
                Es = sctx.enter_context
                w_sb = Es(nc.sbuf_tensor(f"t2w{hh}", [128, DT * 1024], BF16))
                attn_bf = [Es(nc.sbuf_tensor(f"t2a{hh}_{i}", [128, TH], BF16)) for i in range(DT)]
                tmp = Es(nc.sbuf_tensor(f"t2tmp{hh}", [128, 512], F32))
                for wname, w_in, rhs_t, dstl, kind in (
                        ("r", wrT, xr_bf, r_bf, "sig"),
                        ("k", wkT, xk_bf, ka_bf, "relsq"),
                        ("v", wvT, ka_bf, None, "vout"),
                        ("ao", woT, attn_bf, None, "attn")):
                    P.new_stage(f"t2_{wname}wl")
                    for k in range(DT):
                        P.dma("gpsimd", w_sb[:, k * 1024:(k + 1) * 1024],
                              w_in[k * 128:(k + 1) * 128, :])
                    if kind == "attn":
                        for i in range(DT):
                            P.dma("gpsimd", attn_bf[i][:], attn_in[i * 128:(i + 1) * 128, t0:t0 + TH])
                    for m in range(DT):
                        for j in range(2):
                            P.new_stage(f"t2_{wname}mm{m}{j}")
                            pp = PS_B[:, (j % 2) * 512:(j % 2) * 512 + 512]
                            emit_mm_group(P, pp, [w_sb[:, k * 1024 + m * 128:k * 1024 + (m + 1) * 128]
                                                  for k in range(DT)],
                                          [rhs_t[k][:, j * 512:(j + 1) * 512] for k in range(DT)], 0, 512)
                            P.new_stage(f"t2_{wname}ev{m}{j}")
                            sl = slice(j * 512, (j + 1) * 512)
                            if kind == "sig":
                                act(P, dstl[m][:, sl], pp[:, 0:512], AF.Sigmoid)
                            elif kind == "relsq":
                                act(P, tmp[:], pp[:, 0:512], AF.Relu)
                                P.new_stage(f"t2_{wname}ev2{m}{j}")
                                act(P, dstl[m][:, sl], tmp[:], AF.Square)
                            elif kind == "vout":
                                vec_tt(P, tmp[:], pp[:, 0:512], r_bf[m][:, sl], ALU.mult)
                                P.new_stage(f"t2_{wname}ac{m}{j}")
                                vec_tt(P, out_sb[m][:, sl], out_sb[m][:, sl], tmp[:], ALU.add)
                            elif kind == "attn":
                                vec_ts(P, tmp[:], pp[:, 0:512], bo_sb[:, m:m + 1], None, ALU.add)
                                P.new_stage(f"t2_{wname}ac{m}{j}")
                                vec_tt(P, out_sb[m][:, sl], out_sb[m][:, sl], tmp[:], ALU.add)
            P.new_stage("t2_store")
            for i in range(DT):
                P.dma("sync", xout[i * 128:(i + 1) * 128, t0:t0 + TH], out_sb[i][:])
        P.finalize()
    return nc


def t2_inputs(x_fm, attn_fm_bf, ins):
    woT = np.ascontiguousarray(np.asarray(ins["L1_attn_wo"]).T)
    wrT = np.ascontiguousarray(np.asarray(ins["L1_ffn_W_r"]).T)
    wkT = np.ascontiguousarray(np.asarray(ins["L1_ffn_W_k"]).T)
    wvT = np.ascontiguousarray(np.asarray(ins["L1_ffn_W_v"]).T)
    cz = lambda a: np.ascontiguousarray(np.asarray(a).reshape(DT, 128).T)
    maps = []
    for c in range(NCORES):
        t0 = c * T_C
        if c % 2 == 0:
            prev = np.zeros((D, 1), np.float32)
        else:
            prev = x_fm[:, t0 - 1:t0]
        xin = np.ascontiguousarray(np.concatenate([prev, x_fm[:, t0:t0 + T_C]], axis=1))
        maps.append({
            "xin": xin,
            "attn_in": np.ascontiguousarray(attn_fm_bf[:, t0:t0 + T_C]),
            "woT": woT, "wrT": wrT, "wkT": wkT, "wvT": wvT,
            "bo_c": cz(ins["L1_attn_bo"]),
            "mur_c": cz(ins["L1_ffn_mu_r"]),
            "muk_c": cz(ins["L1_ffn_mu_k"]),
        })
    return maps


def run_T2(x_fm, attn_fm, ins):
    import ml_dtypes
    nc = build_T2()
    attn_bf = attn_fm.astype(ml_dtypes.bfloat16) if attn_fm.dtype == np.float32 else attn_fm
    res = run_bass_kernel_spmd(nc, t2_inputs(x_fm, attn_bf, ins), core_ids=list(range(NCORES)))
    return np.concatenate([r["xout"] for r in res.results], axis=1)



L2EPS = 0.001


def build_M2():
    nc = bass.Bass("TRN2", target_bir_lowering=False)
    ap = lambda n, s, d=F32, k="ExternalInput": nc.dram_tensor(n, s, d, kind=k).ap()
    xb_in = ap("xb_in", [D, S])
    wq = ap("wq", [D, 512])
    wk = ap("wk", [D, 512])
    wv = ap("wv", [D, 512])
    maskd = ap("maskd", [128, 128])
    ident = ap("ident", [128, 128])
    ident64 = ap("ident64", [128, 64])
    attn_o = ap("attn_o", [512, S], BF16, "ExternalOutput")
    TCn = 512

    with contextlib.ExitStack() as ctx:
        E = ctx.enter_context
        P = Prog(nc)
        q_bf = [E(nc.sbuf_tensor(f"m2q{m}", [128, S], BF16)) for m in range(4)]
        k_bf = [E(nc.sbuf_tensor(f"m2k{m}", [128, S], BF16)) for m in range(4)]
        v_tm = [E(nc.sbuf_tensor(f"m2v{t}", [128, NH_C * 65], BF16)) for t in range(S // 128)]
        ktb = E(nc.sbuf_tensor("m2ktb", [128, 64], BF16))
        mask_sb = E(nc.sbuf_tensor("m2mask", [128, 128], BF16))
        id_sb = E(nc.sbuf_tensor("m2id", [128, 128], F32))
        id_bf = E(nc.sbuf_tensor("m2idb", [128, 64], BF16))
        s_f = E(nc.sbuf_tensor("m2sf", [64, 65], F32))
        s_b = E(nc.sbuf_tensor("m2sb", [128, 65], BF16))
        abuf = E(nc.sbuf_tensor("m2ab", [128, 128], BF16))
        ot_tm = E(nc.sbuf_tensor("m2ot", [128, 64], F32))
        rec_c = E(nc.sbuf_tensor("m2rc", [128, 4], F32))
        oh_fm = E(nc.sbuf_tensor("m2oh", [64, S], BF16))
        wq_sb = E(nc.sbuf_tensor("m2wq", [128, DT * 512], BF16))
        wk_sb = E(nc.sbuf_tensor("m2wk", [128, DT * 512], BF16))
        wv_sb = E(nc.sbuf_tensor("m2wv", [128, DT * 512], BF16))

        P.new_stage("m2_const")
        P.dma("gpsimd", mask_sb[:], maskd)
        P.dma("sync", id_sb[:], ident)
        P.dma("gpsimd", id_bf[:], ident64)
        for k in range(DT):
            P.dma("gpsimd", wq_sb[:, k * 512:(k + 1) * 512], wq[k * 128:(k + 1) * 128, :])
            P.dma("gpsimd", wk_sb[:, k * 512:(k + 1) * 512], wk[k * 128:(k + 1) * 128, :])
            P.dma("gpsimd", wv_sb[:, k * 512:(k + 1) * 512], wv[k * 128:(k + 1) * 128, :])

        with contextlib.ExitStack() as sctx:
            Es = sctx.enter_context
            xt = [Es(nc.sbuf_tensor(f"m2x{i}", [128, TCn], F32)) for i in range(DT)]
            sq = [Es(nc.sbuf_tensor(f"m2s{i}", [128, TCn], F32)) for i in range(DT)]
            xn2 = [Es(nc.sbuf_tensor(f"m2n{i}", [128, TCn], BF16)) for i in range(DT)]
            mu_b = Es(nc.sbuf_tensor("m2mu", [128, TCn], F32))
            rstd_b = Es(nc.sbuf_tensor("m2rs", [128, TCn], F32))
            mu_row = Es(nc.sbuf_tensor("m2mr", [1, TCn], F32))
            sq_row = Es(nc.sbuf_tensor("m2sr", [1, TCn], F32))
            onescol = Es(nc.sbuf_tensor("m2oc", [128, 1], F32))
            onesrow = Es(nc.sbuf_tensor("m2or", [1, 128], F32))
            PS_A = nc_psum(nc, Es, "m2A", [128, 1024])
            PS_B = nc_psum(nc, Es, "m2B", [128, 1024])
            PS_R0 = nc_psum(nc, Es, "m2R0", [1, 512])
            PS_R1 = nc_psum(nc, Es, "m2R1", [1, 512])
            P.new_stage("m2_c1")
            P.on("vector", lambda v: v.memset(onescol[:], 1.0))
            P.new_stage("m2_c2")
            vec_memset(P, onesrow[:], 1.0)
            vdrain(P)
            for ci, t0 in enumerate(range(0, S, TCn)):
                P.new_stage("m2_load")
                for i in range(DT):
                    P.dma("sync", xt[i][:], xb_in[i * 128:(i + 1) * 128, t0:t0 + TCn])
                emit_ln(P, xt, xn2, sq, mu_b, rstd_b, (PS_R0, PS_R1), PS_A,
                        mu_row, sq_row, onescol[:], onesrow[:], TCn)
                for which, w_in, dst in (("q", wq_sb, q_bf), ("k", wk_sb, k_bf)):
                    for m in range(4):
                        P.new_stage(f"m2_{which}mm{m}")
                        pp = PS_B[:, (m % 2) * 512:(m % 2) * 512 + 512]
                        emit_mm_group(P, pp, [w_in[:, k * 512 + m * 128:k * 512 + (m + 1) * 128] for k in range(DT)],
                                      [xn2[k][:] for k in range(DT)], 0, TCn)
                        P.new_stage(f"m2_{which}ev{m}")
                        vec_ts(P, dst[m][:, t0:t0 + TCn], pp[:, 0:TCn], 0.0, L2EPS, ALU.max, ALU.add)
                pvof = lambda h: h * 65 if h < 4 else 512 + (h - 4) * 65
                for tb in range(TCn // 128):
                    tg = (t0 + tb * 128) // 128
                    P.new_stage(f"m2_vt{tb}")
                    for h in range(NH_C):
                        for k in range(DT):
                            def fn(t, tb=tb, h=h, k=k, pvof=pvof):
                                return t.matmul(PS_B[:, pvof(h):pvof(h) + 64],
                                                xn2[k][:, tb * 128:(tb + 1) * 128],
                                                wv_sb[:, k * 512 + h * 64:k * 512 + (h + 1) * 64],
                                                start=(k == 0), stop=(k == DT - 1))
                            P.on("tensor", fn)
                    P.new_stage(f"m2_vtev{tb}")
                    for h in range(NH_C):
                        act(P, v_tm[tg][:, h * 65:h * 65 + 64], PS_B[:, pvof(h):pvof(h) + 64], AF.Copy)
        P.new_stage("m2_vaug")
        for t in range(S // 128):
            for h in range(NH_C):
                vec_memset(P, v_tm[t][:, h * 65 + 64:h * 65 + 65], 1.0)
        vdrain(P)

        with contextlib.ExitStack() as sctx:
            Es = sctx.enter_context
            PS_S = nc_psum(nc, Es, "m2pss", [128, 512])
            PS_P = nc_psum(nc, Es, "m2psp", [128, 512])
            PS_U = nc_psum(nc, Es, "m2psu", [128, 512])
            PS_T = nc_psum(nc, Es, "m2pst", [128, 512])
            PS_KT = Es(nc.psum_tensor("m2pskt", [128, 64], BF16))
            for h in range(NH_C):
                qf = q_bf[h // 2]
                kf = k_bf[h // 2]
                hb = (h % 2) * 64
                P.new_stage(f"m2_z{h}")
                vec_memset(P, s_f[:], 0.0)
                vec_memset(P, s_b[:], 0.0)
                vdrain(P)
                for ci in range(S // 128):
                    csl = slice(ci * 128, (ci + 1) * 128)
                    P.new_stage(f"m2a_{h}_{ci}")
                    def fn(t, qf=qf, kf=kf, hb=hb, csl=csl):
                        return t.matmul(PS_S[:, 0:128], kf[hb:hb + 64, csl], qf[hb:hb + 64, csl],
                                        start=True, stop=True)
                    P.on("tensor", fn)
                    P.new_stage(f"m2b_{h}_{ci}")
                    vec_tt(P, abuf[:], PS_S[:, 0:128], mask_sb[:], ALU.mult)
                    P.new_stage(f"m2c_{h}_{ci}")
                    def fn(t, ci=ci, h=h, qf=qf, hb=hb, csl=csl):
                        t.matmul(PS_P[:, 0:65], abuf[:], v_tm[ci][:, h * 65:h * 65 + 65],
                                 start=True, stop=False)
                        return t.matmul(PS_P[:, 0:65], qf[hb:hb + 64, csl], s_b[hb:hb + 64, :],
                                        start=False, stop=True)
                    P.on("tensor", fn)
                    P.new_stage(f"m2d_{h}_{ci}")
                    P.on("vector", lambda v, ci=ci: v.reciprocal(rec_c[:, 0:1], PS_P[:, 64:65]))
                    vdrain(P)
                    P.new_stage(f"m2e_{h}_{ci}")
                    vec_tt(P, ot_tm[:], PS_P[:, 0:64], rec_c[:, 0:1].broadcast_to([128, 64]), ALU.mult)
                    P.new_stage(f"m2f_{h}_{ci}")
                    def fn(t, ci=ci):
                        return t.transpose(PS_T[0:64, 0:128], ot_tm[:], id_sb[:])
                    P.on("tensor", fn)
                    P.new_stage(f"m2g_{h}_{ci}")
                    vec_copy(P, oh_fm[:, csl], PS_T[0:64, 0:128])
                    P.new_stage(f"m2h0_{h}_{ci}")
                    def fn(t, ci=ci, kf=kf, hb=hb):
                        return t.transpose(PS_KT[:], kf[hb:hb + 64, ci * 128:(ci + 1) * 128], id_bf[hb:hb + 64, :])
                    P.on("tensor", fn)
                    P.new_stage(f"m2h1_{h}_{ci}")
                    vec_copy(P, ktb[:], PS_KT[:])
                    vdrain(P)
                    P.new_stage(f"m2h_{h}_{ci}")
                    def fn(t, ci=ci, h=h):
                        return t.matmul(PS_U[0:64, 0:65], ktb[:],
                                        v_tm[ci][:, h * 65:h * 65 + 65], start=True, stop=True)
                    P.on("tensor", fn)
                    P.new_stage(f"m2i_{h}_{ci}")
                    vec_tt(P, s_f[:], s_f[:], PS_U[0:64, 0:65], ALU.add)
                    P.new_stage(f"m2j_{h}_{ci}")
                    vec_copy(P, s_b[0:64, :], s_f[:])
                    vec_copy(P, s_b[64:128, :], s_f[:])
                    vdrain(P)
                P.new_stage(f"m2st{h}")
                P.dma("sync", attn_o[h * 64:(h + 1) * 64, :], oh_fm[:])
        P.finalize()
    return nc


def m2_inputs(x_fm, ins):
    Wq = np.asarray(ins["L2_attn_Wq"])
    Wk = np.asarray(ins["L2_attn_Wk"])
    Wv = np.asarray(ins["L2_attn_Wv"])
    maskd = np.triu(np.ones((128, 128), np.float32))
    maps = []
    for c in range(NCORES):
        b, hh = c // 2, c % 2
        hsl = slice(hh * 512, (hh + 1) * 512)
        maps.append({
            "xb_in": np.ascontiguousarray(x_fm[:, b * S:(b + 1) * S]),
            "wq": np.ascontiguousarray(Wq[:, hsl]),
            "wk": np.ascontiguousarray(Wk[:, hsl]),
            "wv": np.ascontiguousarray(Wv[:, hsl]),
            "maskd": maskd, "ident": IDENT,
            "ident64": np.ascontiguousarray(np.vstack([np.eye(64, dtype=np.float32)] * 2)),
        })
    return maps


def run_M2(x_fm, ins):
    nc = build_M2()
    res = run_bass_kernel_spmd(nc, m2_inputs(x_fm, ins), core_ids=list(range(NCORES)))
    out = np.zeros((D, T_ALL), np.float32)
    for c in range(NCORES):
        b, hh = c // 2, c % 2
        out[hh * 512:(hh + 1) * 512, b * S:(b + 1) * S] = np.asarray(res.results[c]["attn_o"]).astype(np.float32)
    return out



T3_DEBUG = False


def build_T3():
    nc = bass.Bass("TRN2", target_bir_lowering=False)
    ap = lambda n, s, d=F32, k="ExternalInput": nc.dram_tensor(n, s, d, kind=k).ap()
    xin = ap("xin", [D, T_C])
    attn_in = ap("attn_in", [D, T_C], BF16)
    wo = ap("wo", [D, D])
    w1T = ap("w1T", [D, FF])
    w2T = ap("w2T", [FF, D])
    bo_c = ap("bo_c", [128, DT])
    b1_c = ap("b1_c", [128, FFT])
    b2_c = ap("b2_c", [128, DT])
    hm_col = ap("hm_col", [128, 2])
    onr = ap("ones_row", [1, 128])
    xout = ap("xout", [D, T_C], F32, "ExternalOutput")
    d_sc = ap("d_sc", [128, 8], F32, "ExternalOutput") if T3_DEBUG else None
    d_w1 = ap("d_w1", [128, FF], BF16, "ExternalOutput") if T3_DEBUG else None
    w1t_d = nc.dram_tensor("w1t_d", [D, FF], BF16).ap()
    w2t_d = nc.dram_tensor("w2t_d", [FF, D], BF16).ap()
    TH = 1024

    with contextlib.ExitStack() as ctx:
        E = ctx.enter_context
        P = Prog(nc)
        out_sb = [E(nc.sbuf_tensor(f"t3o{i}", [128, TH], F32)) for i in range(DT)]
        xn_bf = [E(nc.sbuf_tensor(f"t3n{i}", [128, TH], BF16)) for i in range(DT)]
        cols = E(nc.sbuf_tensor("t3cols", [128, 2 * DT + FFT + 8], F32))
        bo_sb = cols[:, 0:DT]
        b1_sb = cols[:, DT:DT + FFT]
        b2_sb = cols[:, DT + FFT:2 * DT + FFT]
        sc_sb = cols[:, 2 * DT + FFT:2 * DT + FFT + 8]
        hmc = E(nc.sbuf_tensor("t3hm", [128, 2], F32))
        onesrow = E(nc.sbuf_tensor("t3or", [1, 128], F32))
        srow = E(nc.sbuf_tensor("t3srow", [1, 2], F32))
        PS_A = E(nc.psum_tensor("t3psA", [128, 1024], F32))
        PS_B = E(nc.psum_tensor("t3psB", [128, 1024], F32))
        PS_R0 = E(nc.psum_tensor("t3psR0", [1, 512], F32))
        PS_R1 = E(nc.psum_tensor("t3psR1", [1, 512], F32))

        P.new_stage("t3_c")
        P.dma("sync", bo_sb, bo_c)
        P.dma("sync", cols[:, DT:DT + FFT], b1_c)
        P.dma("sync", cols[:, DT + FFT:2 * DT + FFT], b2_c)
        P.dma("sync", hmc[:], hm_col)
        P.dma("sync", onesrow[:], onr)

        with contextlib.ExitStack() as sctx:
            Es = sctx.enter_context
            wtmp = Es(nc.sbuf_tensor("t3wt", [128, FF], F32))
            wabs = Es(nc.sbuf_tensor("t3wa", [128, FF], F32))
            wsgn = Es(nc.sbuf_tensor("t3wg", [128, FF], F32))
            wbf = Es(nc.sbuf_tensor("t3wb", [128, FF], BF16))
            acc = Es(nc.sbuf_tensor("t3ac", [128, 64], F32))
            for wi, (w_in, wd, ntile, wid, thr_i) in enumerate(
                    ((w1T, w1t_d, DT, FF, 0), (w2T, w2t_d, FFT, D, 1))):
                for k in range(ntile):
                    P.new_stage(f"t3_s{wi}_{k}")
                    P.dma("sync", wtmp[:, 0:wid], w_in[k * 128:(k + 1) * 128, :])
                    P.new_stage(f"t3_s{wi}_{k}b")
                    act(P, wabs[:, 0:wid], wtmp[:, 0:wid], AF.Abs,
                        accum_out=acc[:, 8 + wi * 16 + k:8 + wi * 16 + k + 1])
                    sdrain(P)
                P.new_stage(f"t3_t{wi}")
                nt = ntile
                base = 8 + wi * 16
                step = 1
                while step < nt:
                    for j in range(0, nt, 2 * step):
                        if j + step < nt:
                            vec_tt(P, acc[:, base + j:base + j + 1], acc[:, base + j:base + j + 1],
                                   acc[:, base + j + step:base + j + step + 1], ALU.add)
                    vdrain(P)
                    P.new_stage(f"t3_t{wi}_{step}")
                    step *= 2
                vec_copy(P, acc[:, wi:wi + 1], acc[:, base:base + 1])
                vdrain(P)
                P.new_stage(f"t3_th{wi}")
                def fn(t, wi=wi, thr_i=thr_i):
                    return t.matmul(PS_R0[0:1, 0:1], hmc[:, thr_i:thr_i + 1],
                                    acc[:, wi:wi + 1], start=True, stop=True)
                P.on("tensor", fn)
                P.new_stage(f"t3_th{wi}b")
                vec_ts(P, srow[:, wi:wi + 1], PS_R0[0:1, 0:1], 5e-9, None, ALU.max)
                vdrain(P)
                P.new_stage(f"t3_th{wi}c")
                def fn(t, wi=wi):
                    return t.matmul(PS_R1[0:1, 0:128].rearrange("a b -> a b") if False else PS_A[:, 0:1],
                                    onesrow[0:1, :], srow[0:1, wi:wi + 1], start=True, stop=True)
                P.on("tensor", fn)
                P.new_stage(f"t3_th{wi}d")
                vec_copy(P, sc_sb[:, wi:wi + 1], PS_A[:, 0:1])
                vec_ts(P, sc_sb[:, 2 + wi:3 + wi], PS_A[:, 0:1], 2.0, None, ALU.mult)
                vdrain(P)
                for k in range(ntile):
                    P.new_stage(f"t3_z{wi}_{k}")
                    P.dma("sync", wtmp[:, 0:wid], w_in[k * 128:(k + 1) * 128, :])
                    P.new_stage(f"t3_z{wi}_{k}b")
                    act(P, wabs[:, 0:wid], wtmp[:, 0:wid], AF.Abs)
                    P.new_stage(f"t3_z{wi}_{k}c")
                    act(P, wsgn[:, 0:wid], wtmp[:, 0:wid], AF.Sign)
                    P.new_stage(f"t3_z{wi}_{k}d")
                    vec_tt(P, wabs[:, 0:wid], wabs[:, 0:wid],
                           sc_sb[:, wi:wi + 1].broadcast_to([128, wid]), ALU.is_gt)
                    P.new_stage(f"t3_z{wi}_{k}e")
                    vec_tt(P, wbf[:, 0:wid], wsgn[:, 0:wid], wabs[:, 0:wid], ALU.mult)
                    P.new_stage(f"t3_z{wi}_{k}f")
                    P.dma("sync", wd[k * 128:(k + 1) * 128, :], wbf[:, 0:wid])

        if T3_DEBUG:
            P.new_stage("t3_dbg")
            P.dma("sync", d_sc, cols[:, 2 * DT + FFT:2 * DT + FFT + 8])
            P.dma("sync", d_w1, w1t_d[0:128, :])
        for hh in range(2):
            t0 = hh * TH
            with contextlib.ExitStack() as sctx:
                Es = sctx.enter_context
                xt = [Es(nc.sbuf_tensor(f"t3x{hh}_{i}", [128, TH], F32)) for i in range(DT)]
                sq = [Es(nc.sbuf_tensor(f"t3s{hh}_{i}", [128, TH], F32)) for i in range(DT)]
                mu_b = Es(nc.sbuf_tensor(f"t3mu{hh}", [128, TH], F32))
                rstd_b = Es(nc.sbuf_tensor(f"t3rs{hh}", [128, TH], F32))
                mu_row = Es(nc.sbuf_tensor(f"t3mr{hh}", [1, TH], F32))
                sq_row = Es(nc.sbuf_tensor(f"t3sr{hh}", [1, TH], F32))
                onescol = Es(nc.sbuf_tensor(f"t3oc{hh}", [128, 1], F32))
                P.new_stage("t3_c2")
                P.on("vector", lambda v, onescol=onescol: v.memset(onescol[:], 1.0))
                P.new_stage("t3_xload")
                for i in range(DT):
                    P.dma("sync", xt[i][:], xin[i * 128:(i + 1) * 128, t0:t0 + TH])
                emit_ln(P, xt, xn_bf, sq, mu_b, rstd_b, (PS_R0, PS_R1), PS_A,
                        mu_row, sq_row, onescol[:], onesrow[:], TH)
                P.new_stage("t3_base")
                for i in range(DT):
                    vec_copy(P, out_sb[i][:], xt[i][:])
            with contextlib.ExitStack() as sctx:
                Es = sctx.enter_context
                w_sb = Es(nc.sbuf_tensor(f"t3w{hh}", [128, DT * 1024], BF16))
                attn_bf = [Es(nc.sbuf_tensor(f"t3a{hh}_{i}", [128, TH], BF16)) for i in range(DT)]
                tmp = Es(nc.sbuf_tensor(f"t3tmp{hh}", [128, 512], F32))
                P.new_stage("t3_wol")
                for k in range(DT):
                    P.dma("gpsimd", w_sb[:, k * 1024:(k + 1) * 1024], wo[k * 128:(k + 1) * 128, :])
                    P.dma("gpsimd", attn_bf[k][:], attn_in[k * 128:(k + 1) * 128, t0:t0 + TH])
                for m in range(DT):
                    for j in range(2):
                        P.new_stage(f"t3_aomm{m}{j}")
                        pp = PS_B[:, (j % 2) * 512:(j % 2) * 512 + 512]
                        emit_mm_group(P, pp, [w_sb[:, k * 1024 + m * 128:k * 1024 + (m + 1) * 128] for k in range(DT)],
                                      [attn_bf[k][:, j * 512:(j + 1) * 512] for k in range(DT)], 0, 512)
                        P.new_stage(f"t3_aoev{m}{j}")
                        sl = slice(j * 512, (j + 1) * 512)
                        vec_ts(P, tmp[:], pp[:, 0:512], bo_sb[:, m:m + 1], None, ALU.add)
                        P.new_stage(f"t3_aoac{m}{j}")
                        vec_tt(P, out_sb[m][:, sl], out_sb[m][:, sl], tmp[:], ALU.add)
            with contextlib.ExitStack() as sctx:
                Es = sctx.enter_context
                poolA = Es(nc.sbuf_tensor(f"t3pA{hh}", [128, DT * FF], BF16))
                poolB = Es(nc.sbuf_tensor(f"t3pB{hh}", [128, 2 * 8192], BF16))
                h_bf = Es(nc.sbuf_tensor(f"t3h{hh}", [128, FFT * 512], BF16))
                tmp = Es(nc.sbuf_tensor(f"t3tm2{hh}", [128, 512], F32))
                opsum = [PS_A[:, 0:512], PS_A[:, 512:1024], PS_B[:, 0:512], PS_B[:, 512:1024]]
                P.new_stage("t3_w1l")
                for k in range(DT):
                    P.dma("sync", poolA[:, k * FF:(k + 1) * FF], w1t_d[k * 128:(k + 1) * 128, :])
                for tci, tcs in enumerate(range(0, TH, 512)):
                    for ht in range(FFT):
                        P.new_stage(f"t3_h{hh}_{tci}_{ht}")
                        pp = PS_B[:, (ht % 2) * 512:(ht % 2) * 512 + 512]
                        emit_mm_group(P, pp, [poolA[:, k * FF + ht * 128:k * FF + ht * 128 + 128] for k in range(DT)],
                                      [xn_bf[k][:, tcs:tcs + 512] for k in range(DT)], 0, 512)
                        if ht == 0:
                            P.dma("gpsimd", poolB[:, 0:8192].rearrange("p (kk d) -> p kk d", d=1024),
                                  w2t_d[0:1024, :].rearrange("(kk p) d -> p kk d", p=128))
                        P.new_stage(f"t3_hb{hh}_{tci}_{ht}")
                        vec_ts(P, tmp[:], pp[:, 0:512], b1_sb[:, ht:ht + 1], None, ALU.add)
                        P.new_stage(f"t3_hg{hh}_{tci}_{ht}")
                        act(P, h_bf[:, ht * 512:(ht + 1) * 512], tmp[:], AF.Gelu)
                    for gi in range(4):
                        P.new_stage(f"t3_o{hh}_{tci}_{gi}")
                        for ki in range(8):
                            kk = gi * 8 + ki
                            for mm in range(4):
                                def fn(t, kk=kk, ki=ki, gi=gi, mm=mm, hh2=hh):
                                    return t.matmul(opsum[mm][:, 0:512],
                                                    poolB[:, (gi % 2) * 8192 + ki * 1024 + (mm + 0) * 128:
                                                          (gi % 2) * 8192 + ki * 1024 + (mm + 0) * 128 + 128],
                                                    h_bf[:, kk * 512:(kk + 1) * 512],
                                                    start=(kk == 0), stop=(kk == FFT - 1))
                                P.on("tensor", fn)
                        if gi + 1 < 4:
                            gl = gi + 1
                            P.dma("gpsimd", poolB[:, (gl % 2) * 8192:(gl % 2) * 8192 + 8192].rearrange("p (kk d) -> p kk d", d=1024),
                                  w2t_d[gl * 1024:(gl + 1) * 1024, :].rearrange("(kk p) d -> p kk d", p=128))
                    P.new_stage(f"t3_oe{hh}_{tci}")
                    for mm in range(4):
                        vec_ts(P, tmp[:], opsum[mm][:, 0:512], b2_sb[:, mm:mm + 1], None, ALU.add)
                        P.new_stage(f"t3_oe3{hh}_{tci}_{mm}")
                        vec_tt(P, out_sb[mm][:, tcs:tcs + 512], out_sb[mm][:, tcs:tcs + 512], tmp[:], ALU.add)
                        if mm < 3:
                            P.new_stage(f"t3_oe4{hh}_{tci}_{mm}")
                    P.new_stage(f"t3_o2pre{hh}_{tci}")
                    P.dma("gpsimd", poolB[:, 0:8192].rearrange("p (kk d) -> p kk d", d=1024),
                          w2t_d[0:1024, :].rearrange("(kk p) d -> p kk d", p=128))
                    for gi in range(4):
                        P.new_stage(f"t3_o2{hh}_{tci}_{gi}")
                        for ki in range(8):
                            kk = gi * 8 + ki
                            for mm in range(4):
                                def fn(t, kk=kk, ki=ki, gi=gi, mm=mm):
                                    return t.matmul(opsum[mm][:, 0:512],
                                                    poolB[:, (gi % 2) * 8192 + ki * 1024 + (mm + 4) * 128:
                                                          (gi % 2) * 8192 + ki * 1024 + (mm + 4) * 128 + 128],
                                                    h_bf[:, kk * 512:(kk + 1) * 512],
                                                    start=(kk == 0), stop=(kk == FFT - 1))
                                P.on("tensor", fn)
                        if gi + 1 < 4:
                            gl = gi + 1
                            P.dma("gpsimd", poolB[:, (gl % 2) * 8192:(gl % 2) * 8192 + 8192].rearrange("p (kk d) -> p kk d", d=1024),
                                  w2t_d[gl * 1024:(gl + 1) * 1024, :].rearrange("(kk p) d -> p kk d", p=128))
                    P.new_stage(f"t3_o2e{hh}_{tci}")
                    for mm in range(4):
                        vec_ts(P, tmp[:], opsum[mm][:, 0:512], b2_sb[:, 4 + mm:5 + mm], None, ALU.add)
                        P.new_stage(f"t3_o2e3{hh}_{tci}_{mm}")
                        vec_tt(P, out_sb[4 + mm][:, tcs:tcs + 512], out_sb[4 + mm][:, tcs:tcs + 512], tmp[:], ALU.add)
                        if mm < 3:
                            P.new_stage(f"t3_o2e4{hh}_{tci}_{mm}")
            P.new_stage("t3_store")
            for i in range(DT):
                P.dma("sync", xout[i * 128:(i + 1) * 128, t0:t0 + TH], out_sb[i][:])
        P.finalize()
    return nc


def t3_inputs(x_fm, attn_fm_bf, ins):
    cz = lambda a, n: np.ascontiguousarray(np.asarray(a).reshape(n, 128).T)
    w1 = np.asarray(ins["L2_ffn_W1"])
    w2 = np.asarray(ins["L2_ffn_W2"])
    hm = np.zeros((128, 2), np.float32)
    hm[:, 0] = 0.5 / w1.size
    hm[:, 1] = 0.5 / w2.size
    maps = []
    for c in range(NCORES):
        t0 = c * T_C
        maps.append({
            "xin": np.ascontiguousarray(x_fm[:, t0:t0 + T_C]),
            "attn_in": np.ascontiguousarray(attn_fm_bf[:, t0:t0 + T_C]),
            "wo": np.asarray(ins["L2_attn_Wo"]),
            "w1T": np.ascontiguousarray(w1.T),
            "w2T": np.ascontiguousarray(w2.T),
            "bo_c": cz(ins["L2_attn_bo"], DT),
            "b1_c": cz(ins["L2_ffn_b1"], FFT),
            "b2_c": cz(ins["L2_ffn_b2"], DT),
            "hm_col": hm,
            "ones_row": ONES_ROW,
        })
    return maps


def run_T3(x_fm, attn_fm, ins):
    import ml_dtypes
    nc = build_T3()
    attn_bf = attn_fm.astype(ml_dtypes.bfloat16) if attn_fm.dtype == np.float32 else attn_fm
    res = run_bass_kernel_spmd(nc, t3_inputs(x_fm, attn_bf, ins), core_ids=list(range(NCORES)))
    return np.concatenate([r["xout"] for r in res.results], axis=1)



def build_M3():
    nc = bass.Bass("TRN2", target_bir_lowering=False)
    ap = lambda n, s, d=F32, k="ExternalInput": nc.dram_tensor(n, s, d, kind=k).ap()
    xb_in = ap("xb_in", [D, S])
    wqT = ap("wqT", [D, 512])
    wdkvT = ap("wdkvT", [D, 512])
    wukT = ap("wukT", [512, 512])
    wuvT = ap("wuvT", [512, 512])
    n1w_c = ap("n1w_c", [128, DT])
    lng_c = ap("lng_c", [128, 4])
    lnb_c = ap("lnb_c", [128, 4])
    cosm = ap("cosm", [32, S])
    sinm = ap("sinm", [32, S])
    maskd = ap("maskd", [128, 128])
    ident = ap("ident", [128, 128])
    attn_o = ap("attn_o", [512, S], BF16, "ExternalOutput")
    TCn = 512

    with contextlib.ExitStack() as ctx:
        E = ctx.enter_context
        P = Prog(nc)
        q_bf = [E(nc.sbuf_tensor(f"m3q{m}", [128, S], BF16)) for m in range(4)]
        k_bf = [E(nc.sbuf_tensor(f"m3k{m}", [128, S], BF16)) for m in range(4)]
        v_tm = [E(nc.sbuf_tensor(f"m3v{t}", [128, NH_C * 65], BF16)) for t in range(S // 128)]
        mask_sb = E(nc.sbuf_tensor("m3mask", [128, 128], BF16))
        id_sb = E(nc.sbuf_tensor("m3id", [128, 128], F32))
        st_bf = E(nc.sbuf_tensor("m3st", [128, 512], BF16))
        ot_tm = E(nc.sbuf_tensor("m3ot", [128, 512], F32))
        rec_c = E(nc.sbuf_tensor("m3rc", [128, 4], F32))
        oh_fm = E(nc.sbuf_tensor("m3oh", [64, S], BF16))
        wq_sb = E(nc.sbuf_tensor("m3wq", [128, DT * 512], BF16))
        wdkv_sb = E(nc.sbuf_tensor("m3wd", [128, DT * 512], BF16))
        wuk_sb = E(nc.sbuf_tensor("m3wu", [128, 4 * 512], BF16))
        wuv_sb = E(nc.sbuf_tensor("m3wv", [128, 4 * 512], BF16))
        ccols = E(nc.sbuf_tensor("m3cc", [128, DT + 8], F32))
        n1w_sb = ccols[:, 0:DT]
        lng_sb = ccols[:, DT:DT + 4]
        lnb_sb = ccols[:, DT + 4:DT + 8]

        P.new_stage("m3_const")
        P.dma("gpsimd", mask_sb[:], maskd)
        P.dma("sync", id_sb[:], ident)
        P.dma("sync", ccols[:, 0:DT], n1w_c)
        P.dma("sync", ccols[:, DT:DT + 4], lng_c)
        P.dma("sync", ccols[:, DT + 4:DT + 8], lnb_c)
        for k in range(DT):
            P.dma("gpsimd", wq_sb[:, k * 512:(k + 1) * 512], wqT[k * 128:(k + 1) * 128, :])
            P.dma("gpsimd", wdkv_sb[:, k * 512:(k + 1) * 512], wdkvT[k * 128:(k + 1) * 128, :])
        for k in range(4):
            P.dma("gpsimd", wuk_sb[:, k * 512:(k + 1) * 512], wukT[k * 128:(k + 1) * 128, :])
            P.dma("gpsimd", wuv_sb[:, k * 512:(k + 1) * 512], wuvT[k * 128:(k + 1) * 128, :])
        P.new_stage("m3_vz")
        for t in range(S // 128):
            P.on("vector", lambda v, t=t: v.memset(v_tm[t][:], 0.0))
        P.new_stage("m3_vone")
        for t in range(S // 128):
            for h in range(NH_C):
                vec_memset(P, v_tm[t][:, h * 65 + 64:h * 65 + 65], 1.0)
        vdrain(P)

        with contextlib.ExitStack() as sctx:
            Es = sctx.enter_context
            xt = [Es(nc.sbuf_tensor(f"m3x{i}", [128, TCn], F32)) for i in range(DT)]
            sq = [Es(nc.sbuf_tensor(f"m3s{i}", [128, TCn], F32)) for i in range(DT)]
            xn2 = [Es(nc.sbuf_tensor(f"m3n{i}", [128, TCn], BF16)) for i in range(DT)]
            lat_f = [Es(nc.sbuf_tensor(f"m3l{i}", [128, TCn], F32)) for i in range(4)]
            lat_bf = [Es(nc.sbuf_tensor(f"m3lb{i}", [128, TCn], BF16)) for i in range(4)]
            mu_b = Es(nc.sbuf_tensor("m3mu", [128, TCn], F32))
            rstd_b = Es(nc.sbuf_tensor("m3rs", [128, TCn], F32))
            mu_row = Es(nc.sbuf_tensor("m3mr", [1, TCn], F32))
            sq_row = Es(nc.sbuf_tensor("m3sr", [1, TCn], F32))
            qf4 = Es(nc.sbuf_tensor("m3qf", [128, TCn], F32))
            rt_a = Es(nc.sbuf_tensor("m3ra", [32, TCn], F32))
            rt_b = Es(nc.sbuf_tensor("m3rb", [32, TCn], F32))
            rt_c = Es(nc.sbuf_tensor("m3rcs", [32, TCn], F32))
            rt_d = Es(nc.sbuf_tensor("m3rd", [32, TCn], F32))
            cos_sb = Es(nc.sbuf_tensor("m3cos", [32, TCn], F32))
            sin_sb = Es(nc.sbuf_tensor("m3sin", [32, TCn], F32))
            onescol = Es(nc.sbuf_tensor("m3oc", [128, 1], F32))
            onesrow = Es(nc.sbuf_tensor("m3or", [1, 128], F32))
            PS_A = nc_psum(nc, Es, "m3A", [128, 1024])
            PS_B = nc_psum(nc, Es, "m3B", [128, 1024])
            PS_R0 = nc_psum(nc, Es, "m3R0", [1, 512])
            PS_R1 = nc_psum(nc, Es, "m3R1", [1, 512])
            P.new_stage("m3_c1")
            P.on("vector", lambda v: v.memset(onescol[:], 1.0))
            P.new_stage("m3_c2")
            vec_memset(P, onesrow[:], 1.0)
            vdrain(P)
            for ci, t0 in enumerate(range(0, S, TCn)):
                P.new_stage("m3_load")
                for i in range(DT):
                    P.dma("sync", xt[i][:], xb_in[i * 128:(i + 1) * 128, t0:t0 + TCn])
                P.dma("sync", cos_sb[:], cosm[:, t0:t0 + TCn])
                P.dma("sync", sin_sb[:], sinm[:, t0:t0 + TCn])
                emit_ln(P, xt, sq, sq, mu_b, rstd_b, (PS_R0, PS_R1), PS_A,
                        mu_row, sq_row, onescol[:], onesrow[:], TCn)
                emit_ln(P, sq, xn2, xt, mu_b, rstd_b, (PS_R0, PS_R1), PS_A,
                        mu_row, sq_row, onescol[:], onesrow[:], TCn, eps=1e-6,
                        gam=[n1w_sb[:, i:i + 1] for i in range(DT)], bet=None, rms=True)
                for m in range(4):
                    P.new_stage(f"m3_qmm{m}")
                    pp = PS_B[:, (m % 2) * 512:(m % 2) * 512 + 512]
                    emit_mm_group(P, pp, [wq_sb[:, k * 512 + m * 128:k * 512 + (m + 1) * 128] for k in range(DT)],
                                  [xn2[k][:] for k in range(DT)], 0, TCn)
                    P.new_stage(f"m3_qev{m}")
                    vec_copy(P, qf4[:], pp[:, 0:TCn])
                    for hb in (0, 64):
                        P.new_stage(f"m3_qr1{m}{hb}")
                        vec_copy(P, rt_a[:], qf4[hb:hb + 32, :])
                        vec_copy(P, rt_b[:], qf4[hb + 32:hb + 64, :])
                        P.new_stage(f"m3_qr2{m}{hb}")
                        vec_tt(P, rt_c[:], rt_a[:], cos_sb[:], ALU.mult)
                        vec_tt(P, rt_d[:], rt_b[:], sin_sb[:], ALU.mult)
                        P.new_stage(f"m3_qr3{m}{hb}")
                        vec_tt(P, rt_c[:], rt_c[:], rt_d[:], ALU.subtract)
                        vec_tt(P, rt_d[:], rt_a[:], sin_sb[:], ALU.mult)
                        P.new_stage(f"m3_qr4{m}{hb}")
                        vec_tt(P, rt_b[:], rt_b[:], cos_sb[:], ALU.mult)
                        P.new_stage(f"m3_qr5{m}{hb}")
                        vec_tt(P, rt_d[:], rt_d[:], rt_b[:], ALU.add)
                        P.new_stage(f"m3_qr6{m}{hb}")
                        vec_copy(P, qf4[hb:hb + 32, :], rt_c[:])
                        vec_copy(P, qf4[hb + 32:hb + 64, :], rt_d[:])
                    P.new_stage(f"m3_qcp{m}")
                    vec_copy(P, q_bf[m][:, t0:t0 + TCn], qf4[:])
                for m in range(4):
                    P.new_stage(f"m3_lmm{m}")
                    pp = PS_B[:, (m % 2) * 512:(m % 2) * 512 + 512]
                    emit_mm_group(P, pp, [wdkv_sb[:, k * 512 + m * 128:k * 512 + (m + 1) * 128] for k in range(DT)],
                                  [xn2[k][:] for k in range(DT)], 0, TCn)
                    P.new_stage(f"m3_lev{m}")
                    vec_copy(P, lat_f[m][:], pp[:, 0:TCn])
                emit_ln(P, lat_f, lat_bf, xt, mu_b, rstd_b, (PS_R0, PS_R1), PS_A,
                        mu_row, sq_row, onescol[:], onesrow[:], TCn, nt=4, dim=512,
                        gam=[lng_sb[:, i:i + 1] for i in range(4)],
                        bet=[lnb_sb[:, i:i + 1] for i in range(4)])
                for m in range(4):
                    P.new_stage(f"m3_kmm{m}")
                    pp = PS_B[:, (m % 2) * 512:(m % 2) * 512 + 512]
                    emit_mm_group(P, pp, [wuk_sb[:, k * 512 + m * 128:k * 512 + (m + 1) * 128] for k in range(4)],
                                  [lat_bf[k][:] for k in range(4)], 0, TCn)
                    P.new_stage(f"m3_kev{m}")
                    vec_copy(P, qf4[:], pp[:, 0:TCn])
                    for hb in (0, 64):
                        P.new_stage(f"m3_kr1{m}{hb}")
                        vec_copy(P, rt_a[:], qf4[hb:hb + 32, :])
                        vec_copy(P, rt_b[:], qf4[hb + 32:hb + 64, :])
                        P.new_stage(f"m3_kr2{m}{hb}")
                        vec_tt(P, rt_c[:], rt_a[:], cos_sb[:], ALU.mult)
                        vec_tt(P, rt_d[:], rt_b[:], sin_sb[:], ALU.mult)
                        P.new_stage(f"m3_kr3{m}{hb}")
                        vec_tt(P, rt_c[:], rt_c[:], rt_d[:], ALU.subtract)
                        vec_tt(P, rt_d[:], rt_a[:], sin_sb[:], ALU.mult)
                        P.new_stage(f"m3_kr4{m}{hb}")
                        vec_tt(P, rt_b[:], rt_b[:], cos_sb[:], ALU.mult)
                        P.new_stage(f"m3_kr5{m}{hb}")
                        vec_tt(P, rt_d[:], rt_d[:], rt_b[:], ALU.add)
                        P.new_stage(f"m3_kr6{m}{hb}")
                        vec_copy(P, qf4[hb:hb + 32, :], rt_c[:])
                        vec_copy(P, qf4[hb + 32:hb + 64, :], rt_d[:])
                    P.new_stage(f"m3_kcp{m}")
                    vec_copy(P, k_bf[m][:, t0:t0 + TCn], qf4[:])
                pvof = lambda h: h * 65 if h < 4 else 512 + (h - 4) * 65
                for tb in range(TCn // 128):
                    tg = (t0 + tb * 128) // 128
                    P.new_stage(f"m3_vt{tb}")
                    for h in range(NH_C):
                        for k in range(4):
                            def fn(t, tb=tb, h=h, k=k, pvof=pvof):
                                return t.matmul(PS_B[:, pvof(h):pvof(h) + 64],
                                                lat_bf[k][:, tb * 128:(tb + 1) * 128],
                                                wuv_sb[:, k * 512 + h * 64:k * 512 + (h + 1) * 64],
                                                start=(k == 0), stop=(k == 3))
                            P.on("tensor", fn)
                    P.new_stage(f"m3_vtev{tb}")
                    for h in range(NH_C):
                        act(P, v_tm[tg][:, h * 65:h * 65 + 64], PS_B[:, pvof(h):pvof(h) + 64], AF.Copy)

        with contextlib.ExitStack() as sctx:
            Es = sctx.enter_context
            PS_S = nc_psum(nc, Es, "m3pss", [128, 1024])
            PS_P = nc_psum(nc, Es, "m3psp", [128, 512])
            PS_T = nc_psum(nc, Es, "m3pst", [128, 512])
            for h in range(NH_C):
                qt = q_bf[h // 2]
                kt_ = k_bf[h // 2]
                hb = (h % 2) * 64
                for band in range(S // 512):
                    nkt = 4 * band + 4
                    for kti in range(nkt):
                        pp = PS_S[:, (kti % 2) * 512:(kti % 2) * 512 + 512]
                        P.new_stage(f"m3at_s{h}_{band}_{kti}")
                        def fn(t, pp=pp, kti=kti, band=band, qt=qt, kt_=kt_, hb=hb):
                            return t.matmul(pp, kt_[hb:hb + 64, kti * 128:(kti + 1) * 128],
                                            qt[hb:hb + 64, band * 512:(band + 1) * 512],
                                            start=True, stop=True)
                        P.on("tensor", fn)
                        P.new_stage(f"m3at_e{h}_{band}_{kti}")
                        act(P, st_bf[:], pp, AF.Exp, scale=0.125)
                        dq = kti - 4 * band
                        if dq >= 0:
                            P.new_stage(f"m3at_m{h}_{band}_{kti}")
                            vec_tt(P, st_bf[:, dq * 128:(dq + 1) * 128],
                                   st_bf[:, dq * 128:(dq + 1) * 128], mask_sb[:], ALU.mult)
                        P.new_stage(f"m3at_p{h}_{band}_{kti}")
                        for qb in range(4):
                            if 4 * band + qb < kti:
                                continue
                            def fn(t, qb=qb, kti=kti, band=band, h=h):
                                return t.matmul(PS_P[:, qb * 128:qb * 128 + 65],
                                                st_bf[:, qb * 128:(qb + 1) * 128],
                                                v_tm[kti][:, h * 65:h * 65 + 65],
                                                start=(kti == 0 and qb == 0), stop=(kti == 4 * band + qb))
                            P.on("tensor", fn)
                    P.new_stage(f"m3at_r{h}_{band}")
                    for qb in range(4):
                        P.on("vector", lambda v, qb=qb: v.reciprocal(
                            rec_c[:, qb:qb + 1], PS_P[:, qb * 128 + 64:qb * 128 + 65]))
                    vdrain(P)
                    P.new_stage(f"m3at_n{h}_{band}")
                    for qb in range(4):
                        vec_tt(P, ot_tm[:, qb * 128:qb * 128 + 64], PS_P[:, qb * 128:qb * 128 + 64],
                               rec_c[:, qb:qb + 1].broadcast_to([128, 64]), ALU.mult)
                    vdrain(P)
                    P.new_stage(f"m3at_t{h}_{band}")
                    for qb in range(4):
                        def fn(t, qb=qb):
                            return t.transpose(PS_T[0:64, qb * 128:(qb + 1) * 128],
                                               ot_tm[:, qb * 128:qb * 128 + 64], id_sb[:])
                        P.on("tensor", fn)
                    P.new_stage(f"m3at_o{h}_{band}")
                    vec_copy(P, oh_fm[:, band * 512:(band + 1) * 512], PS_T[0:64, 0:512])
                P.new_stage(f"m3at_st{h}")
                P.dma("sync", attn_o[h * 64:(h + 1) * 64, :], oh_fm[:])
        P.finalize()
    return nc


def m3_inputs(x_fm, ins):
    WqT = np.ascontiguousarray(np.asarray(ins["L3_ffn_Wq"]).T)
    WdkvT = np.ascontiguousarray(np.asarray(ins["L3_ffn_Wdkv"]).T)
    WukT = np.ascontiguousarray(np.asarray(ins["L3_ffn_Wuk"]).T)
    WuvT = np.ascontiguousarray(np.asarray(ins["L3_ffn_Wuv"]).T)
    pos = np.arange(S, dtype=np.float64)[:, None]
    inv = 1.0 / 10000.0 ** (np.arange(32, dtype=np.float64) / 32.0)
    ang = pos * inv[None, :]
    cosm = np.ascontiguousarray(np.cos(ang).T.astype(np.float32))
    sinm = np.ascontiguousarray(np.sin(ang).T.astype(np.float32))
    maskd = np.triu(np.ones((128, 128), np.float32))
    cz = lambda a, n: np.ascontiguousarray(np.asarray(a).reshape(n, 128).T)
    maps = []
    for c in range(NCORES):
        b, hh = c // 2, c % 2
        hsl = slice(hh * 512, (hh + 1) * 512)
        maps.append({
            "xb_in": np.ascontiguousarray(x_fm[:, b * S:(b + 1) * S]),
            "wqT": np.ascontiguousarray(WqT[:, hsl]),
            "wdkvT": WdkvT,
            "wukT": np.ascontiguousarray(WukT[:, hsl]),
            "wuvT": np.ascontiguousarray(WuvT[:, hsl]),
            "n1w_c": cz(ins["L3_ffn_norm1_w"], DT),
            "lng_c": cz(ins["L3_ffn_ln_g"], 4),
            "lnb_c": cz(ins["L3_ffn_ln_b"], 4),
            "cosm": cosm, "sinm": sinm, "maskd": maskd, "ident": IDENT,
        })
    return maps


def run_M3(x_fm, ins):
    nc = build_M3()
    res = run_bass_kernel_spmd(nc, m3_inputs(x_fm, ins), core_ids=list(range(NCORES)))
    out = np.zeros((D, T_ALL), np.float32)
    for c in range(NCORES):
        b, hh = c // 2, c % 2
        out[hh * 512:(hh + 1) * 512, b * S:(b + 1) * S] = np.asarray(res.results[c]["attn_o"]).astype(np.float32)
    return out



NBANK = 8192
SQRT_NB = float(np.float32(np.sqrt(np.float32(8192.0))))


def build_T4():
    nc = bass.Bass("TRN2", target_bir_lowering=False)
    ap = lambda n, s, d=F32, k="ExternalInput": nc.dram_tensor(n, s, d, kind=k).ap()
    xin = ap("xin", [D, T_C])
    mla_in = ap("mla_in", [D, T_C], BF16)
    kT = ap("kT", [D, NBANK])
    vbank = ap("vbank", [NBANK, D])
    woT = ap("woT", [D, D])
    n2w_c = ap("n2w_c", [128, DT])
    gwT = ap("gwT", [D, 4])
    gb_t = ap("gb_t", [128, 4])
    w1T = ap("w1T", [4, D, FF])
    w2T = ap("w2T", [4, FF, D])
    b1_c = ap("b1_c", [128, 4 * FFT])
    b2_in = ap("b2", [4, D])
    sel4 = ap("sel4", [4, 4 * 128])
    ident = ap("ident", [128, 128])
    onr = ap("ones_row", [1, 128])
    xout = ap("xout", [D, T_C], F32, "ExternalOutput")
    TH = 1024

    with contextlib.ExitStack() as ctx:
        E = ctx.enter_context
        P = Prog(nc)
        out_sb = [E(nc.sbuf_tensor(f"t4o{i}", [128, TH], F32)) for i in range(DT)]
        xn_bf = [E(nc.sbuf_tensor(f"t4n{i}", [128, TH], BF16)) for i in range(DT)]
        rms2_bf = [E(nc.sbuf_tensor(f"t4r{i}", [128, TH], BF16)) for i in range(DT)]
        wrow = E(nc.sbuf_tensor("t4wr", [4, TH], F32))
        cols = E(nc.sbuf_tensor("t4cols", [128, DT + 4 * FFT + 4], F32))
        n2w_sb = cols[:, 0:DT]
        b1_sb = cols[:, DT:DT + 4 * FFT]
        gb_sb = cols[:, DT + 4 * FFT:DT + 4 * FFT + 4]
        b2_sb = E(nc.sbuf_tensor("t4b2", [4, D], F32))
        sel_sb = E(nc.sbuf_tensor("t4sel", [4, 4 * 128], F32))
        id_sb = E(nc.sbuf_tensor("t4id", [128, 128], F32))
        idb_g = E(nc.sbuf_tensor("t4idb", [128, 128], BF16))
        onesrow = E(nc.sbuf_tensor("t4or", [1, 128], F32))
        gw_sb = E(nc.sbuf_tensor("t4gw", [128, DT * 4], BF16))
        lgt = E(nc.sbuf_tensor("t4lg", [128, 4], F32))
        lg2 = E(nc.sbuf_tensor("t4lg2", [128, 4], F32))
        PS_A = E(nc.psum_tensor("t4psA", [128, 1024], F32))
        PS_B = E(nc.psum_tensor("t4psB", [128, 1024], F32))
        PS_O = E(nc.psum_tensor("t4psO", [128, 1024], F32))
        PS_W = E(nc.psum_tensor("t4psW", [128, 512], F32))
        PS_GT = E(nc.psum_tensor("t4psGT", [128, 128], BF16))
        PS_R0 = PS_A[0:1, 0:512]
        PS_R1 = PS_A[0:1, 512:1024]

        P.new_stage("t4_c")
        P.dma("sync", cols[:, 0:DT], n2w_c)
        P.dma("sync", cols[:, DT:DT + 4 * FFT], b1_c)
        P.dma("sync", cols[:, DT + 4 * FFT:DT + 4 * FFT + 4], gb_t)
        P.dma("sync", b2_sb[:], b2_in)
        P.dma("sync", sel_sb[:], sel4)
        P.dma("sync", id_sb[:], ident)
        P.dma("gpsimd", idb_g[:], ident)
        P.dma("sync", onesrow[:], onr)
        for k in range(DT):
            P.dma("gpsimd", gw_sb[:, k * 4:(k + 1) * 4], gwT[k * 128:(k + 1) * 128, :])

        for hh in range(2):
          t0 = hh * TH
          with contextlib.ExitStack() as hctx:
            Eh = hctx.enter_context
            yctx = contextlib.ExitStack()
            xn_f = [yctx.enter_context(nc.sbuf_tensor(f"t4y{hh}_{i}", [128, TH], F32)) for i in range(DT)]
            with contextlib.ExitStack() as sctx:
                Es = sctx.enter_context
                xt = [Es(nc.sbuf_tensor(f"t4x{hh}_{i}", [128, TH], F32)) for i in range(DT)]
                sq0 = [Es(nc.sbuf_tensor(f"t4sq0{hh}_{i}", [128, TH], F32)) for i in range(DT)]
                mu_b = Es(nc.sbuf_tensor(f"t4mu{hh}", [128, TH], F32))
                rstd_b = Es(nc.sbuf_tensor(f"t4rs{hh}", [128, TH], F32))
                mu_row = Es(nc.sbuf_tensor(f"t4mr{hh}", [1, TH], F32))
                sq_row = Es(nc.sbuf_tensor(f"t4sr{hh}", [1, TH], F32))
                onescol = Es(nc.sbuf_tensor(f"t4oc{hh}", [128, 1], F32))
                P.new_stage("t4_c2")
                P.on("vector", lambda v, onescol=onescol: v.memset(onescol[:], 1.0))
                P.new_stage("t4_xload")
                for i in range(DT):
                    P.dma("sync", xt[i][:], xin[i * 128:(i + 1) * 128, t0:t0 + TH])
                emit_ln(P, xt, xn_bf, sq0, mu_b, rstd_b, (PS_R0, PS_R1), PS_O[:, 0:1024],
                        mu_row, sq_row, onescol[:], onesrow[:], TH, out_f32=xn_f)
                P.new_stage("t4_base")
                for i in range(DT):
                    vec_tt(P, out_sb[i][:], xt[i][:], xn_f[i][:], ALU.add)
            with contextlib.ExitStack() as sctx:
                Es = sctx.enter_context
                srow = Es(nc.sbuf_tensor(f"t4sw{hh}", [128, NBANK], F32))
                gbuf = Es(nc.sbuf_tensor(f"t4gb{hh}", [128, NBANK], BF16))
                kT_sb = Es(nc.sbuf_tensor(f"t4kt{hh}", [128, 2 * DT * 512], BF16))
                v_sb = Es(nc.sbuf_tensor(f"t4vs{hh}", [128, 2 * 8192], BF16))
                gT = Es(nc.sbuf_tensor(f"t4gT{hh}", [128, 128], BF16))
                accq = Es(nc.sbuf_tensor(f"t4aq{hh}", [128, 32], F32))
                for tt in range(TH // 128):
                    tsl = slice(tt * 128, (tt + 1) * 128)
                    for nchi in range(16):
                        P.new_stage(f"t4a_l{hh}{tt}{nchi}")
                        buf = (nchi % 2) * DT * 512
                        P.dma("gpsimd", kT_sb[:, buf:buf + DT * 512].rearrange("p (kk d) -> p kk d", d=512),
                              kT[:, nchi * 512:(nchi + 1) * 512].rearrange("(kk p) d -> p kk d", p=128))
                        P.new_stage(f"t4a_s{hh}{tt}{nchi}")
                        pp = PS_A[:, (nchi % 2) * 512:(nchi % 2) * 512 + 512]
                        emit_mm_group(P, pp, [xn_bf[k][:, tt * 128:(tt + 1) * 128] for k in range(DT)],
                                      [kT_sb[:, buf + k * 512:buf + (k + 1) * 512] for k in range(DT)], 0, 512)
                        P.new_stage(f"t4a_e{hh}{tt}{nchi}")
                        act(P, srow[:, nchi * 512:(nchi + 1) * 512], pp[:, 0:512], AF.Square,
                            accum_out=accq[:, nchi:nchi + 1])
                        sdrain(P)
                        P.new_stage(f"t4a_c{hh}{tt}{nchi}")
                        vec_copy(P, srow[:, nchi * 512:(nchi + 1) * 512], pp[:, 0:512])
                    P.new_stage(f"t4a_n{hh}{tt}")
                    step = 1
                    while step < 16:
                        for j in range(0, 16, 2 * step):
                            vec_tt(P, accq[:, j:j + 1], accq[:, j:j + 1], accq[:, j + step:j + step + 1], ALU.add)
                        vdrain(P)
                        P.new_stage(f"t4a_n{hh}{tt}_{step}")
                        step *= 2
                    act(P, accq[:, 16:17], accq[:, 0:1], AF.Ln)
                    sdrain(P)
                    P.new_stage(f"t4a_n2{hh}{tt}")
                    act(P, accq[:, 17:18], accq[:, 16:17], AF.Exp, scale=-0.5)
                    sdrain(P)
                    P.new_stage(f"t4a_n3{hh}{tt}")
                    vec_ts(P, accq[:, 18:19], accq[:, 17:18], SQRT_NB, None, ALU.mult)
                    vdrain(P)
                    P.new_stage(f"t4a_g{hh}{tt}")
                    vec_tt(P, srow[:], srow[:], accq[:, 18:19].broadcast_to([128, NBANK]), ALU.mult)
                    P.new_stage(f"t4a_g2{hh}{tt}")
                    act(P, gbuf[:], srow[:], AF.Gelu)
                    for vg in range(8):
                        P.new_stage(f"t4a_vl{hh}{tt}{vg}")
                        vb = (vg % 2) * 8192
                        P.dma("gpsimd", v_sb[:, vb:vb + 8192].rearrange("p (kk d) -> p kk d", d=1024),
                              vbank[vg * 1024:(vg + 1) * 1024, :].rearrange("(kk p) d -> p kk d", p=128))
                        for ki in range(8):
                            n64 = vg * 8 + ki
                            P.new_stage(f"t4a_tr{hh}{tt}{vg}{ki}")
                            def fn(t, n64=n64):
                                return t.transpose(PS_GT[:], gbuf[:, n64 * 128:(n64 + 1) * 128], idb_g[:])
                            P.on("tensor", fn)
                            P.new_stage(f"t4a_te{hh}{tt}{vg}{ki}")
                            vec_copy(P, gT[:], PS_GT[:])
                            vdrain(P)
                            P.new_stage(f"t4a_vm{hh}{tt}{vg}{ki}")
                            for mm in range(DT):
                                def fn(t, vb=vb, ki=ki, mm=mm, n64=n64):
                                    return t.matmul(PS_O[:, mm * 128:(mm + 1) * 128],
                                                    v_sb[:, vb + ki * 1024 + mm * 128:vb + ki * 1024 + (mm + 1) * 128],
                                                    gT[:], start=(n64 == 0), stop=(n64 == 63))
                                P.on("tensor", fn)
                    P.new_stage(f"t4a_oe{hh}{tt}")
                    for mm in range(DT):
                        vec_tt(P, out_sb[mm][:, tsl], out_sb[mm][:, tsl], PS_O[:, mm * 128:(mm + 1) * 128], ALU.add)
            with contextlib.ExitStack() as sctx:
                Es = sctx.enter_context
                w_sb = Es(nc.sbuf_tensor(f"t4w{hh}", [128, DT * 1024], BF16))
                mla_bf = [Es(nc.sbuf_tensor(f"t4m{hh}_{i}", [128, TH], BF16)) for i in range(DT)]
                P.new_stage("t4_mwl")
                for k in range(DT):
                    P.dma("gpsimd", w_sb[:, k * 1024:(k + 1) * 1024], woT[k * 128:(k + 1) * 128, :])
                    P.dma("gpsimd", mla_bf[k][:], mla_in[k * 128:(k + 1) * 128, t0:t0 + TH])
                for m in range(DT):
                    for j in range(2):
                        P.new_stage(f"t4_mm{m}{j}")
                        pp = PS_B[:, (j % 2) * 512:(j % 2) * 512 + 512]
                        emit_mm_group(P, pp, [w_sb[:, k * 1024 + m * 128:k * 1024 + (m + 1) * 128] for k in range(DT)],
                                      [mla_bf[k][:, j * 512:(j + 1) * 512] for k in range(DT)], 0, 512)
                        P.new_stage(f"t4_mev{m}{j}")
                        sl = slice(j * 512, (j + 1) * 512)
                        vec_tt(P, xn_f[m][:, sl], xn_f[m][:, sl], pp[:, 0:512], ALU.add)
                        P.new_stage(f"t4_mac{m}{j}")
                        vec_tt(P, out_sb[m][:, sl], out_sb[m][:, sl], pp[:, 0:512], ALU.add)
            with contextlib.ExitStack() as sctx:
                Es = sctx.enter_context
                sq2 = [Es(nc.sbuf_tensor(f"t4q{hh}_{i}", [128, TH], F32)) for i in range(DT)]
                mu_b = Es(nc.sbuf_tensor(f"t4mu2{hh}", [128, TH], F32))
                rstd_b = Es(nc.sbuf_tensor(f"t4rs2{hh}", [128, TH], F32))
                mu_row = Es(nc.sbuf_tensor(f"t4mr2{hh}", [1, TH], F32))
                sq_row = Es(nc.sbuf_tensor(f"t4sr2{hh}", [1, TH], F32))
                onescol = Es(nc.sbuf_tensor(f"t4oc2{hh}", [128, 1], F32))
                P.new_stage("t4_c3")
                P.on("vector", lambda v, onescol=onescol: v.memset(onescol[:], 1.0))
                emit_ln(P, xn_f, rms2_bf, sq2, mu_b, rstd_b, (PS_R0, PS_R1), PS_O[:, 0:1024],
                        mu_row, sq_row, onescol[:], onesrow[:], TH, eps=1e-6,
                        gam=[n2w_sb[:, i:i + 1] for i in range(DT)], bet=None, rms=True)
            yctx.close()
            for tt in range(TH // 128):
                P.new_stage(f"t4g_mm{tt}")
                emit_mm_group(P, PS_W[:, 0:4], [rms2_bf[k][:, tt * 128:(tt + 1) * 128] for k in range(DT)],
                              [gw_sb[:, k * 4:(k + 1) * 4] for k in range(DT)], 0, 4, pn0=0)
                P.new_stage(f"t4g_ev{tt}")
                vec_copy(P, lgt[:], PS_W[:, 0:4])
                vdrain(P)
                P.new_stage(f"t4g_b{tt}")
                vec_tt(P, lgt[:], lgt[:], gb_sb[:, 0:4] if False else gb_sb, ALU.add)
                vdrain(P)
                P.new_stage(f"t4g_m1{tt}")
                vec_tt(P, lg2[:, 0:2], lgt[:, 0:2], lgt[:, 2:4], ALU.max)
                vdrain(P)
                P.new_stage(f"t4g_m2{tt}")
                vec_tt(P, lg2[:, 2:3], lg2[:, 0:1], lg2[:, 1:2], ALU.max)
                vdrain(P)
                P.new_stage(f"t4g_oh{tt}")
                vec_tt(P, lgt[:], lgt[:], lg2[:, 2:3].broadcast_to([128, 4]), ALU.is_ge)
                vdrain(P)
                P.new_stage(f"t4g_tr{tt}")
                P.on("tensor", lambda t, tt=tt: t.transpose(PS_W[0:4, 0:128], lgt[:], id_sb[:]))
                P.new_stage(f"t4g_tev{tt}")
                vec_copy(P, wrow[:, tt * 128:(tt + 1) * 128], PS_W[0:4, 0:128])
                vdrain(P)
            with contextlib.ExitStack() as sctx:
                Es = sctx.enter_context
                poolA = Es(nc.sbuf_tensor(f"t4pA{hh}", [128, DT * FF], BF16))
                poolB = Es(nc.sbuf_tensor(f"t4pB{hh}", [128, 2 * 8192], BF16))
                h_bf = Es(nc.sbuf_tensor(f"t4h{hh}", [128, FFT * 512], BF16))
                wbce = Es(nc.sbuf_tensor(f"t4wb{hh}", [128, 512], F32))
                opsum = [PS_O[:, 0:512], PS_O[:, 512:1024], PS_A[:, 0:512], PS_A[:, 512:1024],
                         PS_B[:, 0:512], PS_B[:, 512:1024], PS_W[:, 0:512], None]
                for e in range(4):
                    P.new_stage(f"t4w1l{e}")
                    for k in range(DT):
                        P.dma("gpsimd", poolA[:, k * FF:(k + 1) * FF], w1T[e, k * 128:(k + 1) * 128, :])
                    for tci, tcs in enumerate(range(0, TH, 512)):
                        for ht in range(FFT):
                            P.new_stage(f"t4h{e}_{tci}_{ht}")
                            pp = PS_A[:, (ht % 2) * 512:(ht % 2) * 512 + 512]
                            emit_mm_group(P, pp, [poolA[:, k * FF + ht * 128:k * FF + ht * 128 + 128] for k in range(DT)],
                                          [rms2_bf[k][:, tcs:tcs + 512] for k in range(DT)], 0, 512)
                            if ht == 0:
                                P.dma("gpsimd", poolB[:, 0:8192].rearrange("p (kk d) -> p kk d", d=1024),
                                      w2T[e, 0:1024, :].rearrange("(kk p) d -> p kk d", p=128))
                            P.new_stage(f"t4hg{e}_{tci}_{ht}")
                            act(P, h_bf[:, ht * 512:(ht + 1) * 512], pp[:, 0:512], AF.Relu,
                                bias=b1_sb[:, e * FFT + ht:e * FFT + ht + 1])
                        P.new_stage(f"t4w{e}_{tci}")
                        emit_mm_group(P, PS_W, [sel_sb[:, e * 128:(e + 1) * 128]],
                                      [wrow[:, tcs:tcs + 512]], 0, 512, pn0=0)
                        P.new_stage(f"t4wev{e}_{tci}")
                        vec_copy(P, wbce[:], PS_W[:, 0:512])
                        for half in range(2):
                            if half == 1:
                                P.new_stage(f"t4o2p{e}_{tci}")
                                P.dma("gpsimd", poolB[:, 0:8192].rearrange("p (kk d) -> p kk d", d=1024),
                                      w2T[e, 0:1024, :].rearrange("(kk p) d -> p kk d", p=128))
                            for gi in range(4):
                                P.new_stage(f"t4o{e}_{tci}_{half}_{gi}")
                                for ki in range(8):
                                    kk = gi * 8 + ki
                                    for mm in range(4):
                                        def fn(t, kk=kk, ki=ki, gi=gi, mm=mm, half=half):
                                            return t.matmul(opsum[mm][:, 0:512],
                                                            poolB[:, (gi % 2) * 8192 + ki * 1024 + (mm + 4 * half) * 128:
                                                                  (gi % 2) * 8192 + ki * 1024 + (mm + 4 * half) * 128 + 128],
                                                            h_bf[:, kk * 512:(kk + 1) * 512],
                                                            start=(kk == 0), stop=(kk == FFT - 1))
                                        P.on("tensor", fn)
                                if gi + 1 < 4:
                                    gl = gi + 1
                                    P.dma("gpsimd", poolB[:, (gl % 2) * 8192:(gl % 2) * 8192 + 8192].rearrange("p (kk d) -> p kk d", d=1024),
                                          w2T[e, gl * 1024:(gl + 1) * 1024, :].rearrange("(kk p) d -> p kk d", p=128))
                            P.new_stage(f"t4oa{e}_{tci}_{half}")
                            for mm in range(4):
                                vec_tt(P, opsum[mm][:, 0:512], opsum[mm][:, 0:512], wbce[:], ALU.mult)
                            P.new_stage(f"t4ob{e}_{tci}_{half}")
                            for mm in range(4):
                                vec_tt(P, out_sb[4 * half + mm][:, tcs:tcs + 512],
                                       out_sb[4 * half + mm][:, tcs:tcs + 512], opsum[mm][:, 0:512], ALU.add)
            for m in range(DT):
                P.new_stage(f"t4b2_{m}")
                emit_mm_group(P, PS_A[:, 0:512], [b2_sb[:, m * 128:(m + 1) * 128]], [wrow[:]], 0, 512, pn0=0)
                emit_mm_group(P, PS_A[:, 512:1024], [b2_sb[:, m * 128:(m + 1) * 128]], [wrow[:]], 512, 512, pn0=0)
                P.new_stage(f"t4b2e_{m}")
                vec_tt(P, out_sb[m][:, 0:512], out_sb[m][:, 0:512], PS_A[:, 0:512], ALU.add)
                vec_tt(P, out_sb[m][:, 512:1024], out_sb[m][:, 512:1024], PS_A[:, 512:1024], ALU.add)
            P.new_stage("t4_store")
            for i in range(DT):
                P.dma("sync", xout[i * 128:(i + 1) * 128, t0:t0 + TH], out_sb[i][:])
        P.finalize()
    return nc


def t4_inputs(x_fm, mla_fm_bf, ins):
    cz = lambda a, n: np.ascontiguousarray(np.asarray(a).reshape(n, 128).T)
    b1 = np.asarray(ins["L3_ffn_b1"])
    sel = np.zeros((4, 4 * 128), np.float32)
    for e in range(4):
        sel[e, e * 128:(e + 1) * 128] = 1.0
    w1T = np.ascontiguousarray(np.asarray(ins["L3_ffn_W1"]).transpose(0, 2, 1))
    w2T = np.ascontiguousarray(np.asarray(ins["L3_ffn_W2"]).transpose(0, 2, 1))
    kT = np.ascontiguousarray(np.asarray(ins["L3_attn_k"]).T)
    gb = np.asarray(ins["L3_ffn_gate_b"])
    maps = []
    for c in range(NCORES):
        t0 = c * T_C
        maps.append({
            "xin": np.ascontiguousarray(x_fm[:, t0:t0 + T_C]),
            "mla_in": np.ascontiguousarray(mla_fm_bf[:, t0:t0 + T_C]),
            "kT": kT,
            "vbank": np.asarray(ins["L3_attn_v"]),
            "woT": np.ascontiguousarray(np.asarray(ins["L3_ffn_Wo"]).T),
            "n2w_c": cz(ins["L3_ffn_norm2_w"], DT),
            "gwT": np.ascontiguousarray(np.asarray(ins["L3_ffn_gate_w"]).T),
            "gb_t": np.ascontiguousarray(np.tile(gb[None, :], (128, 1))),
            "w1T": w1T, "w2T": w2T,
            "b1_c": np.ascontiguousarray(b1.reshape(4, FFT, 128).transpose(2, 0, 1).reshape(128, 4 * FFT)),
            "b2": np.asarray(ins["L3_ffn_b2"]),
            "sel4": sel, "ident": IDENT, "ones_row": ONES_ROW,
        })
    return maps


def run_T4(x_fm, mla_fm, ins):
    import ml_dtypes
    nc = build_T4()
    mla_bf = mla_fm.astype(ml_dtypes.bfloat16) if mla_fm.dtype == np.float32 else mla_fm
    res = run_bass_kernel_spmd(nc, t4_inputs(x_fm, mla_bf, ins), core_ids=list(range(NCORES)))
    return np.concatenate([r["xout"] for r in res.results], axis=1)



_NC_CACHE = {}


def _get_nc(name, builder):
    if name not in _NC_CACHE:
        _NC_CACHE[name] = builder()
    return _NC_CACHE[name]


def kernel(**inputs):
    import ml_dtypes
    ins = inputs
    x_fm = fm(ins["L0_attn_x"])
    cores = list(range(NCORES))

    res = run_bass_kernel_spmd(_get_nc("M0", build_M0), m0_inputs(x_fm, ins), core_ids=cores)
    yg = np.zeros((XB, T_ALL), np.float32)
    for c in cores:
        b, dh = c // 2, c % 2
        yg[dh * HT:(dh + 1) * HT, b * S:(b + 1) * S] = res.results[c]["ygated"]
    res = run_bass_kernel_spmd(_get_nc("T1", build_T1), t1_inputs(x_fm, yg, ins), core_ids=cores)
    x_fm = np.concatenate([r["xout"] for r in res.results], axis=1)

    res = run_bass_kernel_spmd(_get_nc("M1", build_M1), m1_inputs(x_fm, ins), core_ids=cores)
    attn = np.zeros((D, T_ALL), dtype=ml_dtypes.bfloat16)
    for c in cores:
        b, hh2 = c // 2, c % 2
        attn[hh2 * 512:(hh2 + 1) * 512, b * S:(b + 1) * S] = np.asarray(res.results[c]["attn_o"])
    res = run_bass_kernel_spmd(_get_nc("T2", build_T2), t2_inputs(x_fm, attn, ins), core_ids=cores)
    x_fm = np.concatenate([r["xout"] for r in res.results], axis=1)

    res = run_bass_kernel_spmd(_get_nc("M2", build_M2), m2_inputs(x_fm, ins), core_ids=cores)
    attn = np.zeros((D, T_ALL), dtype=ml_dtypes.bfloat16)
    for c in cores:
        b, hh2 = c // 2, c % 2
        attn[hh2 * 512:(hh2 + 1) * 512, b * S:(b + 1) * S] = np.asarray(res.results[c]["attn_o"])
    res = run_bass_kernel_spmd(_get_nc("T3", build_T3), t3_inputs(x_fm, attn, ins), core_ids=cores)
    x_fm = np.concatenate([r["xout"] for r in res.results], axis=1)

    res = run_bass_kernel_spmd(_get_nc("M3", build_M3), m3_inputs(x_fm, ins), core_ids=cores)
    attn = np.zeros((D, T_ALL), dtype=ml_dtypes.bfloat16)
    for c in cores:
        b, hh2 = c // 2, c % 2
        attn[hh2 * 512:(hh2 + 1) * 512, b * S:(b + 1) * S] = np.asarray(res.results[c]["attn_o"])
    res = run_bass_kernel_spmd(_get_nc("T4", build_T4), t4_inputs(x_fm, attn, ins), core_ids=cores)
    x_fm = np.concatenate([r["xout"] for r in res.results], axis=1)

    return np.ascontiguousarray(x_fm.T.reshape(B, S, D)).astype(np.float32)


# version 02 (final)
# speedup vs baseline: 1.1104x; 1.1104x over previous
import contextlib
import math
import numpy as np

import concourse.bass as bass
import concourse.mybir as mybir
from concourse.bass_utils import run_bass_kernel_spmd

F32 = mybir.dt.float32
BF16 = mybir.dt.bfloat16
AF = mybir.ActivationFunctionType
ALU = mybir.AluOpType
AX = mybir.AxisListType

NCORES = 8
B, S, D = 4, 4096, 1024
T_ALL = B * S
T_C = T_ALL // NCORES
DT = D // 128

ENGS = ("tensor", "vector", "scalar", "sync", "gpsimd")


class Prog:

    def __init__(self, nc):
        self.nc = nc
        self.stages = []
        self.cur = None
        self.new_stage()

    def new_stage(self, name=""):
        self.cur = {e: [] for e in ENGS}
        self.cur["_name"] = name
        self.stages.append(self.cur)

    def on(self, eng, fn):
        self.cur[eng].append(("op", fn))

    def dma(self, queue, out, in_, **kw):
        def fn(e, out=out, in_=in_, kw=kw):
            return e.dma_start(out=out, in_=in_, **kw)
        self.cur[queue].append(("dma", fn))

    def cc(self, fn):
        self.cur["gpsimd"].append(("cc", fn))

    def finalize(self):
        nc = self.nc
        nstage = len(self.stages)
        incs = {e: [0] * nstage for e in ENGS}
        cc_events = []
        for k, st in enumerate(self.stages):
            for e in ENGS:
                kinds = {kind for kind, _ in st[e]}
                if "op" in kinds and len(kinds) > 1:
                    raise AssertionError(f"stage {k} engine {e}: mixing op and dma/cc unsupported")
                if "cc" in kinds and "dma" in kinds:
                    raise AssertionError(f"stage {k} engine {e}: mixing cc and dma unsupported")
                for kind, _ in st[e]:
                    if kind == "dma":
                        incs[e][k] += 16
                    elif kind == "cc":
                        cc_events.append((k, len(cc_events)))
                if st[e] and all(kind == "op" for kind, _ in st[e]):
                    incs[e][k] = 1
        cum = {e: np.cumsum([0] + incs[e]).tolist() for e in ENGS}

        with contextlib.ExitStack() as ctx:
            sems = {e: ctx.enter_context(nc.semaphore(f"sem_{e}")) for e in ENGS}
            cc_sems = [ctx.enter_context(nc.semaphore(f"cc_{i}")) for _, i in cc_events]
            cc_stage = [k for k, _ in cc_events]
            block = ctx.enter_context(nc.Block())
            stages = self.stages
            for eng_name in ENGS:
                def body(eng, eng_name=eng_name):
                    last_wait = {e2: 0 for e2 in ENGS}
                    cc_waited = [False] * len(cc_sems)
                    cc_idx = 0

                    def do_waits(k):
                        for e2 in ENGS:
                            if e2 == eng_name:
                                continue
                            tgt = cum[e2][k]
                            if tgt > last_wait[e2]:
                                eng.wait_ge(sems[e2], tgt)
                                last_wait[e2] = tgt
                        for j in range(len(cc_sems)):
                            if cc_stage[j] < k and not cc_waited[j] and eng_name != "gpsimd":
                                eng.wait_ge(cc_sems[j], 1)
                                cc_waited[j] = True

                    for k in range(nstage):
                        items = stages[k][eng_name]
                        if not items:
                            continue
                        do_waits(k)
                        n = len(items)
                        for i, (kind, fn) in enumerate(items):
                            inst = fn(eng)
                            if kind == "dma":
                                inst.then_inc(sems[eng_name], 16)
                            elif kind == "cc":
                                inst.then_inc(cc_sems[cc_idx])
                                cc_idx += 1
                            elif i == n - 1:
                                inst.then_inc(sems[eng_name], 1)
                            if kind == "cc":
                                eng.wait_ge(cc_sems[cc_idx - 1], 1)
                    do_waits(nstage)
                getattr(block, eng_name)(body)



def ceil_div(a, b):
    return (a + b - 1) // b


def chunk_starts(total, size):
    if total <= size:
        return [0]
    starts = list(range(0, total - size + 1, size))
    if starts[-1] + size < total:
        starts.append(total - size)
    return starts


def emit_mm_group(P, psum, lhsT_tiles, rhs_tiles, n0, nn, accumulate=False, pn0=None):
    nk = len(lhsT_tiles)
    p0 = n0 if pn0 is None else pn0
    for ki in range(nk):
        def fn(t, ki=ki):
            return t.matmul(psum[:, p0:p0 + nn], lhsT_tiles[ki],
                            rhs_tiles[ki][:, n0:n0 + nn],
                            start=(ki == 0 and not accumulate),
                            stop=(ki == nk - 1))
        P.on("tensor", fn)


def act(P, out, in_, func, bias=None, scale=None, accum_out=None):
    def fn(sc, out=out, in_=in_, func=func, bias=bias, scale=scale, accum_out=accum_out):
        kw = {}
        if bias is not None:
            kw["bias"] = bias
        if scale is not None:
            kw["scale"] = scale
        if accum_out is not None:
            kw["accum_out"] = accum_out
        return sc.activation(out, in_, func, **kw)
    P.on("scalar", fn)


def vec_tt(P, out, a, b_, op):
    P.on("vector", lambda v, out=out, a=a, b_=b_, op=op: v.tensor_tensor(out, a, b_, op=op))


def vec_ts(P, out, a, s1, s2, op0, op1=None):
    def fn(v, out=out, a=a, s1=s1, s2=s2, op0=op0, op1=op1):
        if op1 is None:
            return v.tensor_scalar(out, a, s1, None, op0=op0)
        return v.tensor_scalar(out, a, s1, s2, op0=op0, op1=op1)
    P.on("vector", fn)


def vec_stt(P, out, in0, scalar, in1, op0, op1):
    P.on("vector", lambda v, out=out, in0=in0, s=scalar, in1=in1, op0=op0, op1=op1:
         v.scalar_tensor_tensor(out, in0, s, in1, op0=op0, op1=op1))


def vec_copy(P, out, in_):
    P.on("vector", lambda v, out=out, in_=in_: v.tensor_copy(out, in_))


def vec_red(P, out, in_, op, negate=False):
    def fn(v, out=out, in_=in_, op=op, negate=negate):
        kw = {"negate": True} if negate else {}
        return v.tensor_reduce(out, in_, axis=AX.X, op=op, **kw)
    P.on("vector", fn)


def vec_memset(P, ap, val):
    P.on("vector", lambda v, ap=ap, val=val: v.memset(ap, val))


def vec_scan(P, out, a, b_, init):
    P.on("vector", lambda v, out=out, a=a, b_=b_, init=init:
         v.tensor_tensor_scan(out, a, b_, init, op0=ALU.mult, op1=ALU.add))



def emit_ln(P, xt, xnt, sq, mu_b, rstd_b, ps_rows, ps_bc, mu_row, sq_row,
            ones_col_f32, ones_row_f32, Tn, eps=1e-5, gam=None, bet=None,
            out_f32=None, nt=DT, dim=None, rms=False):
    dim = dim or (nt * 128)
    P.new_stage("ln_sq")
    for i in range(nt):
        act(P, sq[i][:], xt[i][:], AF.Square)
    for j in chunk_starts(Tn, 512):
        nn = min(512, Tn - j)
        P.new_stage("ln_sums")
        if not rms:
            emit_mm_group(P, ps_rows[0], [ones_col_f32] * nt,
                          [xt[i][:, j:j + nn] for i in range(nt)], 0, nn)
        emit_mm_group(P, ps_rows[1], [ones_col_f32] * nt,
                      [sq[i][:, j:j + nn] for i in range(nt)], 0, nn)
        P.new_stage("ln_rm1")
        if not rms:
            vec_ts(P, mu_row[:, j:j + nn], ps_rows[0][0:1, 0:nn], 1.0 / dim, None, ALU.mult)
        else:
            vec_ts(P, sq_row[:, j:j + nn], ps_rows[1][0:1, 0:nn], 1.0 / dim, eps, ALU.mult, ALU.add)
        if not rms:
            P.new_stage("ln_rm2")
            vec_stt(P, sq_row[:, j:j + nn], ps_rows[0][0:1, 0:nn], 1.0 / dim,
                    mu_row[:, j:j + nn], ALU.mult, ALU.mult)
            P.new_stage("ln_rm3")
            vec_stt(P, sq_row[:, j:j + nn], ps_rows[1][0:1, 0:nn], 1.0 / dim,
                    sq_row[:, j:j + nn], ALU.mult, ALU.subtract)
            P.new_stage("ln_rm4")
            vec_ts(P, sq_row[:, j:j + nn], sq_row[:, j:j + nn], eps, None, ALU.add)
        P.new_stage("ln_rm5")
        act(P, sq_row[:, j:j + nn], sq_row[:, j:j + nn], AF.Ln)
        P.new_stage("ln_rm6")
        act(P, sq_row[:, j:j + nn], sq_row[:, j:j + nn], AF.Exp, scale=-0.5)
        P.new_stage("ln_bc")
        if not rms:
            emit_mm_group(P, ps_bc[:, 0:512], [ones_row_f32], [mu_row[:]], j, nn, pn0=0)
        emit_mm_group(P, ps_bc[:, 512:1024], [ones_row_f32], [sq_row[:]], j, nn, pn0=0)
        P.new_stage("ln_bcev")
        if not rms:
            vec_copy(P, mu_b[:, j:j + nn], ps_bc[:, 0:nn])
        vec_copy(P, rstd_b[:, j:j + nn], ps_bc[:, 512:512 + nn])
    if not rms:
        P.new_stage("ln_apply")
        for i in range(nt):
            vec_tt(P, sq[i][:], xt[i][:], mu_b[:], ALU.subtract)
    src0 = sq if not rms else xt
    P.new_stage("ln_apply2")
    for i in range(nt):
        dst = out_f32[i][:] if out_f32 is not None else sq[i][:]
        vec_tt(P, dst, src0[i][:], rstd_b[:], ALU.mult)
    src = out_f32 if out_f32 is not None else sq
    if gam is not None:
        P.new_stage("ln_affine")
        for i in range(nt):
            if bet is not None:
                vec_ts(P, xnt[i][:], src[i][:], gam[i], bet[i], ALU.mult, ALU.add)
            else:
                vec_ts(P, xnt[i][:], src[i][:], gam[i], None, ALU.mult)
    elif xnt is not src:
        P.new_stage("ln_cast")
        for i in range(nt):
            vec_copy(P, xnt[i][:], src[i][:])



def fm(a):
    a = np.asarray(a)
    a2 = a.reshape(-1, a.shape[-1])
    return np.ascontiguousarray(a2.T)


ONES_COL = np.ones((128, 1), np.float32)
ONES_ROW = np.ones((1, 128), np.float32)
IDENT = np.eye(128, dtype=np.float32)



def build_T0():
    nc = bass.Bass("TRN2", target_bir_lowering=False)
    xin = nc.dram_tensor("xfm", [D, T_C], F32, kind="ExternalInput").ap()
    onc = nc.dram_tensor("ones_col", [128, 1], F32, kind="ExternalInput").ap()
    onr = nc.dram_tensor("ones_row", [1, 128], F32, kind="ExternalInput").ap()
    xno = nc.dram_tensor("xnfm", [D, T_C], F32, kind="ExternalOutput").ap()
    Tn = T_C
    with contextlib.ExitStack() as ctx:
        E = ctx.enter_context
        xt = [E(nc.sbuf_tensor(f"x{i}", [128, Tn], F32)) for i in range(DT)]
        sq = [E(nc.sbuf_tensor(f"sq{i}", [128, Tn], F32)) for i in range(DT)]
        xnt = sq
        mu_b = E(nc.sbuf_tensor("mu_b", [128, Tn], F32))
        rstd_b = E(nc.sbuf_tensor("rstd_b", [128, Tn], F32))
        mu_row = E(nc.sbuf_tensor("mu_row", [1, Tn], F32))
        sq_row = E(nc.sbuf_tensor("sq_row", [1, Tn], F32))
        onescol = E(nc.sbuf_tensor("onescol", [128, 1], F32))
        onesrow = E(nc.sbuf_tensor("onesrow", [1, 128], F32))
        pr_mu = E(nc.psum_tensor("pr_mu", [1, 512], F32))
        pr_sq = E(nc.psum_tensor("pr_sq", [1, 512], F32))
        ps_bc = E(nc.psum_tensor("ps_bc", [128, 1024], F32))

        P = Prog(nc)
        P.new_stage("loads")
        for i in range(DT):
            P.dma("sync", xt[i][:], xin[i * 128:(i + 1) * 128, :])
        P.dma("sync", onescol[:], onc)
        P.dma("sync", onesrow[:], onr)
        emit_ln(P, xt, xnt, sq, mu_b, rstd_b, (pr_mu, pr_sq), ps_bc, mu_row, sq_row,
                onescol[:], onesrow[:], Tn)
        P.new_stage("store")
        for i in range(DT):
            P.dma("sync", xno[i * 128:(i + 1) * 128, :], xnt[i][:])
        P.finalize()
    return nc


def run_T0(x_fm_full):
    nc = build_T0()
    in_maps = []
    for c in range(NCORES):
        in_maps.append({
            "xfm": np.ascontiguousarray(x_fm_full[:, c * T_C:(c + 1) * T_C]),
            "ones_col": ONES_COL, "ones_row": ONES_ROW,
        })
    res = run_bass_kernel_spmd(nc, in_maps, core_ids=list(range(NCORES)))
    return np.concatenate([r["xnfm"] for r in res.results], axis=1)



M0_DEBUG = False
M0_TC = 512
M0_NCH = S // M0_TC
XB = 2048
XBT = XB // 128
HT = 1024
HTT = HT // 128
STATE = 16


def build_M0():
    nc = bass.Bass("TRN2", target_bir_lowering=False)
    ap = lambda n, s, d=F32, k="ExternalInput": nc.dram_tensor(n, s, d, kind=k).ap()
    xraw = ap("xraw", [D, S])
    wi_xb = ap("wi_xb", [D, XB])
    wi_z = ap("wi_z", [D, HT])
    wdt = ap("wdt", [XB, HT])
    wbc = ap("wbc", [XB, 32])
    alog = ap("alog", [128, HTT * STATE])
    convrows = ap("convrows", [1, 3 * XB])
    convb = ap("convb", [128, XBT])
    bdt = ap("bdt", [128, HTT])
    gam = ap("gam", [128, DT])
    bet = ap("bet", [128, DT])
    dpar = ap("dpar", [128, HTT])
    sel16 = ap("sel16", [16, 16 * 128])
    onc = ap("ones_col", [128, 1])
    onr = ap("ones_row", [1, 128])
    yout = ap("ygated", [HT, S], F32, "ExternalOutput")
    if M0_DEBUG:
        d_xn2 = ap("d_xn2", [D, S], BF16, "ExternalOutput")
        d_xbc = ap("d_xbc", [XB, S], F32, "ExternalOutput")
        d_dt = ap("d_dt", [HT, S], F32, "ExternalOutput")
        d_zs = ap("d_zs", [HT, S], F32, "ExternalOutput")
        d_ys = ap("d_ys", [HT, S], F32, "ExternalOutput")
        d_bc = ap("d_bc", [32, S], F32, "ExternalOutput")
    wsc = nc.dram_tensor("wsc", [XBT, 128, 3 * DT * 128], BF16)
    wsc_r = wsc.ap()

    W = M0_TC + 2
    TC = M0_TC
    with contextlib.ExitStack() as ctx:
        E = ctx.enter_context
        P = Prog(nc)

        wiz_sb = E(nc.sbuf_tensor("wiz_sb", [128, DT * HT], BF16))
        wdt_sb = E(nc.sbuf_tensor("wdt_sb", [128, XBT * HT], BF16))
        wbc_sb = E(nc.sbuf_tensor("wbc_sb", [128, XBT * 32], BF16))
        A_sb = E(nc.sbuf_tensor("A_sb", [128, HTT * STATE], F32))
        sel_sb = E(nc.sbuf_tensor("sel_sb", [16, 16 * 128], F32))
        cols = E(nc.sbuf_tensor("cols", [128, 2 * XBT + 4 * HTT + 2 * DT], F32))
        convb_c = cols[:, 0:XBT]
        bdt_c = cols[:, XBT:XBT + HTT]
        gam_c = cols[:, XBT + HTT:XBT + HTT + DT]
        bet_c = cols[:, XBT + HTT + DT:XBT + HTT + 2 * DT]
        dpar_c = cols[:, XBT + HTT + 2 * DT:XBT + 2 * HTT + 2 * DT]
        onescol = E(nc.sbuf_tensor("onescol", [128, 1], F32))
        onesrow = E(nc.sbuf_tensor("onesrow", [1, 128], F32))

        PS_A = E(nc.psum_tensor("PS_A", [128, 1024], F32))
        PS_B = E(nc.psum_tensor("PS_B", [128, 1024], F32))
        PS_C = E(nc.psum_tensor("PS_C", [128, 1024], F32))
        PS_R0 = E(nc.psum_tensor("PS_R0", [1, 512], F32))
        PS_R1 = E(nc.psum_tensor("PS_R1", [1, 512], F32))

        P.new_stage("prep_loads")
        P.dma("sync", onescol[:], onc)
        P.dma("sync", onesrow[:], onr)
        P.dma("sync", sel_sb[:], sel16)
        P.dma("sync", cols[:, 0:XBT], convb)
        P.dma("sync", cols[:, XBT:XBT + HTT], bdt)
        P.dma("sync", cols[:, XBT + HTT:XBT + HTT + DT], gam)
        P.dma("sync", cols[:, XBT + HTT + DT:XBT + HTT + 2 * DT], bet)
        P.dma("sync", cols[:, XBT + HTT + 2 * DT:XBT + 2 * HTT + 2 * DT], dpar)
        for k in range(DT):
            P.dma("gpsimd", wiz_sb[:, k * HT:(k + 1) * HT], wi_z[k * 128:(k + 1) * 128, :])
        for k in range(XBT):
            P.dma("gpsimd", wdt_sb[:, k * HT:(k + 1) * HT], wdt[k * 128:(k + 1) * 128, :])
            P.dma("gpsimd", wbc_sb[:, k * 32:(k + 1) * 32], wbc[k * 128:(k + 1) * 128, :])
        with contextlib.ExitStack() as pctx:
            Ep = pctx.enter_context
            convrows_sb = Ep(nc.sbuf_tensor("convrows_sb", [1, 3 * XB], F32))
            alog_sb = Ep(nc.sbuf_tensor("alog_sb", [128, HTT * STATE], F32))
            wjb = Ep(nc.sbuf_tensor("wjb", [128, XB], F32))
            wtmp = Ep(nc.sbuf_tensor("wtmp", [128, XB], F32))
            wbf = Ep(nc.sbuf_tensor("wbf", [128, XB], BF16))
            P.dma("sync", convrows_sb[:], convrows)
            P.dma("sync", alog_sb[:], alog)
            P.new_stage("prep_A")
            act(P, A_sb[:], alog_sb[:], AF.Exp)
            P.new_stage("prep_A2")
            P.on("scalar", lambda sc: sc.mul(A_sb[:], A_sb[:], -1.0))
            for j in range(3):
                for q in range(4):
                    P.new_stage("prep_bc_mm")
                    emit_mm_group(P, PS_A[:, (q % 2) * 512:(q % 2) * 512 + 512],
                                  [onesrow[:]], [convrows_sb[:, j * XB:(j + 1) * XB]],
                                  q * 512, 512, pn0=0)
                    P.new_stage("prep_bc_ev")
                    vec_copy(P, wjb[:, q * 512:(q + 1) * 512], PS_A[:, (q % 2) * 512:(q % 2) * 512 + 512])
                for k in range(DT):
                    P.new_stage("prep_wld")
                    P.dma("sync", wtmp[:], wi_xb[k * 128:(k + 1) * 128, :])
                    P.new_stage("prep_wsc")
                    vec_tt(P, wbf[:], wtmp[:], wjb[:], ALU.mult)
                    P.new_stage("prep_wst")
                    cs = (j * DT + k) * 128
                    P.dma("sync",
                          wsc_r[:, :, cs:cs + 128].rearrange("g p c -> p g c"),
                          wbf[:].rearrange("p (g c) -> p g c", c=128))

        xnt = [E(nc.sbuf_tensor(f"xn{i}", [128, W], F32)) for i in range(DT)]
        sqt = [E(nc.sbuf_tensor(f"sq{i}", [128, W], F32)) for i in range(DT)]
        xn2 = [E(nc.sbuf_tensor(f"xn2_{i}", [128, W], BF16)) for i in range(DT)]
        mu_b = E(nc.sbuf_tensor("mu_b", [128, W], F32))
        rstd_b = E(nc.sbuf_tensor("rstd_b", [128, W], F32))
        mu_row = E(nc.sbuf_tensor("mu_row", [1, W], F32))
        sq_row = E(nc.sbuf_tensor("sq_row", [1, W], F32))
        xbc_f = [E(nc.sbuf_tensor(f"xbcf{i}", [128, TC], F32)) for i in range(XBT)]
        xbc_b = [E(nc.sbuf_tensor(f"xbcb{i}", [128, TC], BF16)) for i in range(XBT)]
        dt_f = [E(nc.sbuf_tensor(f"dtf{i}", [128, TC], F32)) for i in range(HTT)]
        zs_f = [E(nc.sbuf_tensor(f"zsf{i}", [128, TC], F32)) for i in range(HTT)]
        brow = E(nc.sbuf_tensor("brow", [16, TC], F32))
        crow = E(nc.sbuf_tensor("crow", [16, TC], F32))
        hstate = E(nc.sbuf_tensor("hstate", [128, HTT * STATE], F32))
        hout2 = [E(nc.sbuf_tensor(f"hout{i}", [128, TC], F32)) for i in range(2)]
        abar2 = [E(nc.sbuf_tensor(f"abar{i}", [128, TC], F32)) for i in range(2)]
        abar = abar2[0]
        utmp = E(nc.sbuf_tensor("utmp", [128, TC], F32))
        wg_buf = E(nc.sbuf_tensor("wg_buf", [128, 3 * DT * 128], BF16))
        yacc = sqt
        dtx = xnt

        P.new_stage("zero_state")
        vec_memset(P, hstate[:], 0.0)

        for ch in range(M0_NCH):
            t0 = ch * TC
            first, last = ch == 0, ch == M0_NCH - 1
            P.new_stage("xn_load")
            lo = 0 if first else t0 - 1
            hi = S if last else t0 + TC + 1
            off = 1 if first else 0
            for i in range(DT):
                P.dma("sync", xnt[i][:, off:off + (hi - lo)], xraw[i * 128:(i + 1) * 128, lo:hi])
                if first:
                    vec_memset(P, xnt[i][:, 0:1], 0.0)
                if last:
                    vec_memset(P, xnt[i][:, W - 1:W], 0.0)
            emit_ln(P, xnt, xnt, sqt, mu_b, rstd_b, (PS_R0, PS_R1), PS_A,
                    mu_row, sq_row, onescol[:], onesrow[:], W, out_f32=xnt)
            emit_ln(P, xnt, xn2, sqt, mu_b, rstd_b, (PS_R0, PS_R1), PS_A,
                    mu_row, sq_row, onescol[:], onesrow[:], W,
                    gam=[gam_c[:, i:i + 1] for i in range(DT)],
                    bet=[bet_c[:, i:i + 1] for i in range(DT)])
            if first or last:
                P.new_stage("halo_zero")
                col = 0 if first else W - 1
                for i in range(DT):
                    vec_memset(P, xn2[i][:, col:col + 1], 0.0)
            P.new_stage("wg_load0")
            P.dma("sync", wg_buf[:], wsc_r[0])
            for g in range(XBT):
                P.new_stage(f"xb_mm{g}")
                pp = PS_B[:, (g % 2) * 512:(g % 2) * 512 + 512]
                for j in range(3):
                    for k in range(DT):
                        def fn(t, j=j, k=k, pp=pp, g=g):
                            return t.matmul(pp[:, 0:TC],
                                            wg_buf[:, (j * DT + k) * 128:(j * DT + k) * 128 + 128],
                                            xn2[k][:, j:j + TC],
                                            start=(j == 0 and k == 0), stop=(j == 2 and k == DT - 1))
                        P.on("tensor", fn)
                P.new_stage(f"xb_ev{g}")
                act(P, xbc_f[g][:], pp[:, 0:TC], AF.Silu, bias=convb_c[:, g:g + 1])
                if g + 1 < XBT:
                    P.dma("sync", wg_buf[:], wsc_r[g + 1])
            P.new_stage("xb_cast")
            for g in range(XBT):
                vec_copy(P, xbc_b[g][:], xbc_f[g][:])
            for m in range(HTT):
                P.new_stage(f"z_mm{m}")
                pp = PS_B[:, (m % 2) * 512:(m % 2) * 512 + 512]
                emit_mm_group(P, pp, [wiz_sb[:, k * HT + m * 128:k * HT + m * 128 + 128] for k in range(DT)],
                              [xn2[k][:, 1:1 + TC] for k in range(DT)], 0, TC)
                P.new_stage(f"z_ev{m}")
                act(P, zs_f[m][:], pp[:, 0:TC], AF.Silu)
            for m in range(HTT):
                P.new_stage(f"dt_mm{m}")
                pp = PS_B[:, (m % 2) * 512:(m % 2) * 512 + 512]
                emit_mm_group(P, pp, [wdt_sb[:, k * HT + m * 128:k * HT + m * 128 + 128] for k in range(XBT)],
                              [xbc_b[k][:] for k in range(XBT)], 0, TC)
                P.new_stage(f"dt_abs{m}")
                act(P, abar[:], pp[:, 0:TC], AF.Abs, bias=bdt_c[:, m:m + 1])
                P.new_stage(f"dt_exp{m}")
                act(P, abar[:], abar[:], AF.Exp, scale=-1.0)
                P.new_stage(f"dt_ln{m}")
                act(P, abar[:], abar[:], AF.Ln, bias=1.0)
                P.new_stage(f"dt_relu{m}")
                act(P, dt_f[m][:], pp[:, 0:TC], AF.Relu, bias=bdt_c[:, m:m + 1])
                P.new_stage(f"dt_add{m}")
                vec_tt(P, dt_f[m][:], dt_f[m][:], abar[:], ALU.add)
            P.new_stage("bc_mm")
            emit_mm_group(P, PS_C[0:16, 0:512], [wbc_sb[:, k * 32:k * 32 + 16] for k in range(XBT)],
                          [xbc_b[k][:] for k in range(XBT)], 0, TC, pn0=0)
            emit_mm_group(P, PS_C[0:16, 512:1024], [wbc_sb[:, k * 32 + 16:k * 32 + 32] for k in range(XBT)],
                          [xbc_b[k][:] for k in range(XBT)], 0, TC, pn0=0)
            P.new_stage("bc_ev")
            vec_copy(P, brow[:], PS_C[0:16, 0:TC])
            vec_copy(P, crow[:], PS_C[0:16, 512:512 + TC])
            P.new_stage("dtx")
            for g in range(HTT):
                vec_tt(P, dtx[g][:, 0:TC], dt_f[g][:], xbc_f[g][:], ALU.mult)
            P.new_stage("yacc0")
            for g in range(HTT):
                vec_memset(P, yacc[g][:, 0:TC], 0.0)
            for n in range(STATE):
                P.new_stage(f"bbc_mm{n}")
                emit_mm_group(P, PS_C[:, 0:512], [sel_sb[:, n * 128:(n + 1) * 128]], [brow[:]], 0, TC, pn0=0)
                emit_mm_group(P, PS_C[:, 512:1024], [sel_sb[:, n * 128:(n + 1) * 128]], [crow[:]], 0, TC, pn0=0)
                act(P, abar2[0][:], dt_f[0][:], AF.Exp, scale=A_sb[:, 0 * STATE + n:0 * STATE + n + 1])
                for g in range(HTT):
                    col = g * STATE + n
                    P.new_stage(f"scan{n}_{g}")
                    vec_tt(P, utmp[:], dtx[g][:, 0:TC], PS_C[:, 0:TC], ALU.mult)
                    vec_scan(P, hout2[g % 2][:], abar2[g % 2][:], utmp[:], hstate[:, col:col + 1])
                    vec_tt(P, utmp[:], hout2[g % 2][:], PS_C[:, 512:512 + TC], ALU.mult)
                    vec_tt(P, yacc[g][:, 0:TC], yacc[g][:, 0:TC], utmp[:], ALU.add)
                    if g + 1 < HTT:
                        act(P, abar2[(g + 1) % 2][:], dt_f[g + 1][:], AF.Exp,
                            scale=A_sb[:, (g + 1) * STATE + n:(g + 1) * STATE + n + 1])
                    if g >= 1:
                        pc = (g - 1) * STATE + n
                        act(P, hstate[:, pc:pc + 1], hout2[(g - 1) % 2][:, TC - 1:TC], AF.Copy)
                P.new_stage(f"scan{n}_tail")
                pc = (HTT - 1) * STATE + n
                act(P, hstate[:, pc:pc + 1], hout2[(HTT - 1) % 2][:, TC - 1:TC], AF.Copy)
            P.new_stage("yg1")
            for g in range(HTT):
                vec_ts(P, hout2[0][:], xbc_f[g][:], dpar_c[:, g:g + 1], None, ALU.mult)
                vec_tt(P, yacc[g][:, 0:TC], yacc[g][:, 0:TC], hout2[0][:], ALU.add)
                vec_tt(P, dt_f[g][:], yacc[g][:, 0:TC], zs_f[g][:], ALU.mult)
            P.new_stage("yg_store")
            for g in range(HTT):
                P.dma("sync", yout[g * 128:(g + 1) * 128, t0:t0 + TC], dt_f[g][:])
        P.finalize()
    return nc


def m0_inputs(x_fm, ins):
    W_in = np.asarray(ins["L0_attn_W_in"])
    W_dt = np.asarray(ins["L0_attn_W_dt"])
    conv_w = np.asarray(ins["L0_attn_conv_w"])
    A_log = np.asarray(ins["L0_attn_A_log"])
    sel = np.zeros((16, 16 * 128), np.float32)
    for n in range(16):
        sel[n, n * 128:(n + 1) * 128] = 1.0
    maps = []
    for c in range(NCORES):
        b, dh = c // 2, c % 2
        hsl = slice(dh * HT, (dh + 1) * HT)
        xbperm = np.concatenate([np.arange(dh * HT, (dh + 1) * HT),
                                 np.arange((1 - dh) * HT, (2 - dh) * HT)])
        alog_r = np.ascontiguousarray(
            A_log[hsl].reshape(HTT, 128, STATE).transpose(1, 0, 2).reshape(128, HTT * STATE))
        maps.append({
            "xraw": np.ascontiguousarray(x_fm[:, b * S:(b + 1) * S]),
            "wi_xb": np.ascontiguousarray(W_in[:, :XB][:, xbperm]),
            "wi_z": np.ascontiguousarray(W_in[:, XB + dh * HT:XB + (dh + 1) * HT]),
            "wdt": np.ascontiguousarray(W_dt[xbperm][:, hsl]),
            "wbc": np.ascontiguousarray(np.concatenate(
                [ins["L0_attn_W_B"], ins["L0_attn_W_C"]], axis=1)[xbperm]),
            "alog": alog_r,
            "convrows": np.ascontiguousarray(conv_w[xbperm].T.reshape(1, 3 * XB)),
            "convb": np.ascontiguousarray(np.asarray(ins["L0_attn_conv_b"])[xbperm].reshape(XBT, 128).T),
            "bdt": np.ascontiguousarray(np.asarray(ins["L0_attn_b_dt"])[hsl].reshape(HTT, 128).T),
            "gam": np.ascontiguousarray(np.asarray(ins["L0_attn_norm_scale"]).reshape(DT, 128).T),
            "bet": np.ascontiguousarray(np.asarray(ins["L0_attn_norm_bias"]).reshape(DT, 128).T),
            "dpar": np.ascontiguousarray(np.asarray(ins["L0_attn_D_param"])[hsl].reshape(HTT, 128).T),
            "sel16": sel,
            "ones_col": ONES_COL, "ones_row": ONES_ROW,
        })
    return maps


def run_M0(x_fm, ins):
    nc = build_M0()
    res = run_bass_kernel_spmd(nc, m0_inputs(x_fm, ins), core_ids=list(range(NCORES)))
    out = np.zeros((XB, T_ALL), np.float32)
    for c in range(NCORES):
        b, dh = c // 2, c % 2
        out[dh * HT:(dh + 1) * HT, b * S:(b + 1) * S] = res.results[c]["ygated"]
    return out



T1_STOP = "full"
T1_DEBUG = False
T1_TH = 1024
T1_TC = 512
NEXP = 8
FF = 4096
FFT = FF // 128


def scol(P, dst, src):
    act(P, dst, src, AF.Copy)


def vdrain(P):
    P.on("vector", lambda v: v.drain())


def sdrain(P):
    P.on("scalar", lambda sc: sc.drain())


def build_T1():
    nc = bass.Bass("TRN2", target_bir_lowering=False)
    ap = lambda n, s, d=F32, k="ExternalInput": nc.dram_tensor(n, s, d, kind=k).ap()
    xin = ap("xin", [D, T_C])
    ygin = ap("ygin", [XB, T_C])
    wout = ap("wout", [XB, D])
    wr = ap("wr", [D, NEXP])
    w1 = ap("w1", [NEXP, D, FF])
    w2 = ap("w2", [NEXP, FF, D])
    b1c_in = ap("b1c", [128, NEXP * FFT])
    b2_in = ap("b2", [NEXP, D])
    sel8 = ap("sel8", [NEXP, NEXP * 128])
    ident = ap("ident", [128, 128])
    onc = ap("ones_col", [128, 1])
    onr = ap("ones_row", [1, 128])
    xout = ap("xout", [D, T_C], F32, "ExternalOutput")
    d_wrow = ap("d_wrow", [NEXP, T_C], F32, "ExternalOutput") if T1_DEBUG else None
    d_lg = ap("d_lg", [128, 8], F32, "ExternalOutput") if T1_DEBUG else None
    d_pt = ap("d_pt", [128, 8], F32, "ExternalOutput") if T1_DEBUG else None
    d_tr = ap("d_tr", [8, 128], F32, "ExternalOutput") if T1_DEBUG else None
    d_h = ap("d_h", [128, 512], BF16, "ExternalOutput") if T1_DEBUG else None
    d_wb = ap("d_wb", [128, 512], F32, "ExternalOutput") if T1_DEBUG else None

    TH, TC = T1_TH, T1_TC
    with contextlib.ExitStack() as ctx:
        E = ctx.enter_context
        P = Prog(nc)
        out_sb = [E(nc.sbuf_tensor(f"out{i}", [128, TH], F32)) for i in range(DT)]
        xn_bf = [E(nc.sbuf_tensor(f"xnb{i}", [128, TH], BF16)) for i in range(DT)]
        wrow = E(nc.sbuf_tensor("wrow", [NEXP, TH], F32))
        wr_sb = E(nc.sbuf_tensor("wr_sb", [128, DT * NEXP], BF16))
        b1c = E(nc.sbuf_tensor("b1c_sb", [128, NEXP * FFT], F32))
        b2_sb = E(nc.sbuf_tensor("b2_sb", [NEXP, D], F32))
        sel_sb = E(nc.sbuf_tensor("sel_sb", [NEXP, NEXP * 128], F32))
        id_sb = E(nc.sbuf_tensor("id_sb", [128, 128], F32))
        onescol = E(nc.sbuf_tensor("onescol", [128, 1], F32))
        onesrow = E(nc.sbuf_tensor("onesrow", [1, 128], F32))
        ptile = E(nc.sbuf_tensor("ptile", [128, NEXP], F32))
        ptmp = E(nc.sbuf_tensor("ptmp", [128, NEXP], F32))
        t4c = E(nc.sbuf_tensor("t4c", [128, 4], F32))
        t2c = E(nc.sbuf_tensor("t2c", [128, 4], F32))
        w8 = E(nc.sbuf_tensor("w8", [128, NEXP], F32))
        PS_A = E(nc.psum_tensor("PS_A", [128, 1024], F32))
        PS_O = E(nc.psum_tensor("PS_O", [128, 2048], F32))
        PS_W = E(nc.psum_tensor("PS_W", [128, 512], F32))
        PS_X = E(nc.psum_tensor("PS_X", [128, 512], F32))

        P.new_stage("const_loads")
        P.dma("sync", b1c[:], b1c_in)
        P.dma("sync", b2_sb[:], b2_in)
        P.dma("sync", sel_sb[:], sel8)
        P.dma("sync", id_sb[:], ident)
        P.dma("sync", onescol[:], onc)
        P.dma("sync", onesrow[:], onr)
        for k in range(DT):
            P.dma("gpsimd", wr_sb[:, k * NEXP:(k + 1) * NEXP], wr[k * 128:(k + 1) * 128, :])

        for hh in range(2):
            t0 = hh * TH
            with contextlib.ExitStack() as sctx:
                Es = sctx.enter_context
                xt = [Es(nc.sbuf_tensor(f"xt{hh}_{i}", [128, TH], F32)) for i in range(DT)]
                sq = [Es(nc.sbuf_tensor(f"sq{hh}_{i}", [128, TH], F32)) for i in range(DT)]
                mu_b = Es(nc.sbuf_tensor(f"mu_b{hh}", [128, TH], F32))
                rstd_b = Es(nc.sbuf_tensor(f"rstd_b{hh}", [128, TH], F32))
                mu_row = Es(nc.sbuf_tensor(f"mu_row{hh}", [1, TH], F32))
                sq_row = Es(nc.sbuf_tensor(f"sq_row{hh}", [1, TH], F32))
                P.new_stage("x_load")
                for i in range(DT):
                    P.dma("sync", xt[i][:], xin[i * 128:(i + 1) * 128, t0:t0 + TH])
                emit_ln(P, xt, xn_bf, sq, mu_b, rstd_b,
                        (PS_A[0:1, 0:512], PS_A[0:1, 512:1024]), PS_O[:, 0:1024],
                        mu_row, sq_row, onescol[:], onesrow[:], TH, out_f32=out_sb)
                P.new_stage("base_addx")
                for i in range(DT):
                    vec_tt(P, out_sb[i][:], out_sb[i][:], xt[i][:], ALU.add)
            with contextlib.ExitStack() as sctx:
                Es = sctx.enter_context
                yg_bf = [Es(nc.sbuf_tensor(f"yg{hh}_{k}", [128, TH], BF16)) for k in range(XBT)]
                wo_sb = Es(nc.sbuf_tensor(f"wo_sb{hh}", [128, XBT * 1024], BF16))
                P.new_stage("yg_load")
                for k in range(XBT):
                    P.dma("gpsimd", yg_bf[k][:], ygin[k * 128:(k + 1) * 128, t0:t0 + TH])
                    P.dma("gpsimd", wo_sb[:, k * 1024:(k + 1) * 1024], wout[k * 128:(k + 1) * 128, :])
                for m in range(DT):
                    for j in range(2):
                        P.new_stage(f"wo_mm{m}_{j}")
                        pp = PS_A[:, (j % 2) * 512:(j % 2) * 512 + 512]
                        emit_mm_group(P, pp, [wo_sb[:, k * 1024 + m * 128:k * 1024 + m * 128 + 128] for k in range(XBT)],
                                      [yg_bf[k][:, j * 512:(j + 1) * 512] for k in range(XBT)], 0, 512)
                        P.new_stage(f"wo_ev{m}_{j}")
                        vec_tt(P, out_sb[m][:, j * 512:(j + 1) * 512],
                               out_sb[m][:, j * 512:(j + 1) * 512], pp[:, 0:512], ALU.add)
            def tourn8(src_t, op):
                P.new_stage("tn_a")
                vec_tt(P, t4c[:], src_t[:, 0:4], src_t[:, 4:8], op)
                vdrain(P)
                P.new_stage("tn_b")
                vec_tt(P, t2c[:, 0:2], t4c[:, 0:2], t4c[:, 2:4], op)
                vdrain(P)
                P.new_stage("tn_c")
                vec_tt(P, t2c[:, 2:3], t2c[:, 0:1], t2c[:, 1:2], op)
                vdrain(P)
                return t2c[:, 2:3]
            for tt in range(TH // 128 if T1_STOP != "base" else 0):
                P.new_stage(f"rt_mm{tt}")
                emit_mm_group(P, PS_W[:, 0:NEXP], [xn_bf[k][:, tt * 128:(tt + 1) * 128] for k in range(DT)],
                              [wr_sb[:, k * NEXP:(k + 1) * NEXP] for k in range(DT)], 0, NEXP, pn0=0)
                P.new_stage(f"rt_ev{tt}")
                vec_copy(P, ptile[:], PS_W[:, 0:NEXP])
                vdrain(P)
                if T1_DEBUG and hh == 0 and tt == 0:
                    P.new_stage("dbg_lg")
                    P.dma("sync", d_lg, ptile[:])
                mc = tourn8(ptile, ALU.max)
                P.new_stage(f"rt_sub{tt}")
                vec_tt(P, ptile[:], ptile[:], mc.broadcast_to([128, NEXP]), ALU.subtract)
                vdrain(P)
                P.new_stage(f"rt_exp{tt}")
                act(P, ptile[:], ptile[:], AF.Exp)
                sdrain(P)
                sc_ = tourn8(ptile, ALU.add)
                P.new_stage(f"rt_rc{tt}")
                P.on("vector", lambda v, sc_=sc_: v.reciprocal(t2c[:, 3:4], sc_))
                vdrain(P)
                P.new_stage(f"rt_nm{tt}")
                vec_tt(P, ptile[:], ptile[:], t2c[:, 3:4].broadcast_to([128, NEXP]), ALU.mult)
                vdrain(P)
                mc = tourn8(ptile, ALU.max)
                P.new_stage(f"rt_t{tt}d")
                vec_tt(P, ptmp[:], ptile[:], mc.broadcast_to([128, NEXP]), ALU.is_ge)
                vdrain(P)
                P.new_stage(f"rt_t{tt}e")
                vec_stt(P, ptmp[:], ptmp[:], -1e30, ptile[:], ALU.mult, ALU.add)
                vdrain(P)
                mc = tourn8(ptmp, ALU.max)
                P.new_stage(f"rt_t{tt}i")
                vec_tt(P, ptmp[:], ptile[:], mc.broadcast_to([128, NEXP]), ALU.is_ge)
                vdrain(P)
                P.new_stage(f"rt_t{tt}j")
                vec_tt(P, ptile[:], ptile[:], ptmp[:], ALU.mult)
                vdrain(P)
                if T1_DEBUG and hh == 0 and tt == 0:
                    P.new_stage("dbg_pt")
                    P.dma("sync", d_pt, ptile[:])
                P.new_stage(f"rt_tr{tt}")
                P.on("tensor", lambda t, tt=tt: t.transpose(PS_W[0:NEXP, 0:128], ptile[:], id_sb[:]))
                P.new_stage(f"rt_trev{tt}")
                vec_copy(P, wrow[:, tt * 128:(tt + 1) * 128], PS_W[0:NEXP, 0:128])
                vdrain(P)
                if T1_DEBUG and hh == 0 and tt in (0, 1, 8, 15):
                    P.new_stage(f"dbg_tr{tt}")
                    P.dma("sync", d_tr[:, (0 if tt == 0 else 1) * 0:128], wrow[:, 0:128])
            with contextlib.ExitStack() as sctx:
                Es = sctx.enter_context
                poolA = Es(nc.sbuf_tensor(f"poolA{hh}", [128, DT * FF], BF16))
                poolB = Es(nc.sbuf_tensor(f"poolB{hh}", [128, 2 * 8192], BF16))
                h_bf = Es(nc.sbuf_tensor(f"h_bf{hh}", [128, FFT * TC], BF16))
                wbce = Es(nc.sbuf_tensor(f"wbce{hh}", [128, TC], F32))
                opsum = [PS_O[:, 0:512], PS_O[:, 512:1024], PS_O[:, 1024:1536], PS_O[:, 1536:2048],
                         PS_A[:, 0:512], PS_A[:, 512:1024], PS_W[:, 0:512], PS_X[:, 0:512]]
                for e in range(NEXP if T1_STOP == "full" else 0):
                    P.new_stage(f"w1_load{e}")
                    for k in range(DT):
                        P.dma("gpsimd", poolA[:, k * FF:(k + 1) * FF], w1[e, k * 128:(k + 1) * 128, :])
                    for tci, tcs in enumerate(range(0, TH, TC)):
                        for ht in range(FFT):
                            P.new_stage(f"moe_h{e}_{tci}_{ht}")
                            pp = PS_A[:, (ht % 2) * 512:(ht % 2) * 512 + 512]
                            emit_mm_group(P, pp, [poolA[:, k * FF + ht * 128:k * FF + ht * 128 + 128] for k in range(DT)],
                                          [xn_bf[k][:, tcs:tcs + TC] for k in range(DT)], 0, TC)
                            if ht == 0:
                                P.dma("gpsimd",
                                      poolB[:, 0:8192].rearrange("p (kk d) -> p kk d", d=1024),
                                      w2[e, 0:1024, :].rearrange("(kk p) d -> p kk d", p=128))
                            P.new_stage(f"moe_g{e}_{tci}_{ht}")
                            act(P, h_bf[:, ht * TC:(ht + 1) * TC], pp[:, 0:TC], AF.Gelu_apprx_tanh,
                                bias=b1c[:, e * FFT + ht:e * FFT + ht + 1])
                        P.new_stage(f"moe_w{e}_{tci}")
                        emit_mm_group(P, PS_A[:, 0:512], [sel_sb[:, e * 128:(e + 1) * 128]],
                                      [wrow[:, tcs:tcs + TC]], 0, TC, pn0=0)
                        P.new_stage(f"moe_wev{e}_{tci}")
                        vec_copy(P, wbce[:], PS_A[:, 0:TC])
                        for gi in range(4):
                            P.new_stage(f"moe_o{e}_{tci}_{gi}")
                            for ki in range(8):
                                kk = gi * 8 + ki
                                for mm in range(8):
                                    def fn(t, kk=kk, ki=ki, gi=gi, mm=mm):
                                        return t.matmul(opsum[mm][:, 0:TC],
                                                        poolB[:, (gi % 2) * 8192 + ki * 1024 + mm * 128:
                                                              (gi % 2) * 8192 + ki * 1024 + mm * 128 + 128],
                                                        h_bf[:, kk * TC:(kk + 1) * TC],
                                                        start=(kk == 0), stop=(kk == FFT - 1))
                                    P.on("tensor", fn)
                            if gi + 1 < 4:
                                gl = gi + 1
                                P.dma("gpsimd",
                                      poolB[:, (gl % 2) * 8192:(gl % 2) * 8192 + 8192].rearrange("p (kk d) -> p kk d", d=1024),
                                      w2[e, gl * 1024:(gl + 1) * 1024, :].rearrange("(kk p) d -> p kk d", p=128))
                        P.new_stage(f"moe_acc{e}_{tci}")
                        for mm in range(8):
                            vec_tt(P, opsum[mm][:, 0:TC], opsum[mm][:, 0:TC], wbce[:], ALU.mult)
                        P.new_stage(f"moe_acc2{e}_{tci}")
                        for mm in range(8):
                            vec_tt(P, out_sb[mm][:, tcs:tcs + TC], out_sb[mm][:, tcs:tcs + TC],
                                   opsum[mm][:, 0:TC], ALU.add)
            if T1_DEBUG:
                P.new_stage("dbg_moe")
                P.dma("sync", d_h, h_bf[:, 0:512])
                P.dma("sync", d_wb, wbce[:])
            for m in range(DT if T1_STOP == "full" else 0):
                P.new_stage(f"b2_mm{m}")
                emit_mm_group(P, PS_A[:, 0:512], [b2_sb[:, m * 128:(m + 1) * 128]], [wrow[:]], 0, 512, pn0=0)
                emit_mm_group(P, PS_A[:, 512:1024], [b2_sb[:, m * 128:(m + 1) * 128]], [wrow[:]], 512, 512, pn0=0)
                P.new_stage(f"b2_ev{m}")
                vec_tt(P, out_sb[m][:, 0:512], out_sb[m][:, 0:512], PS_A[:, 0:512], ALU.add)
                vec_tt(P, out_sb[m][:, 512:1024], out_sb[m][:, 512:1024], PS_A[:, 512:1024], ALU.add)
            P.new_stage("store")
            for i in range(DT):
                P.dma("sync", xout[i * 128:(i + 1) * 128, t0:t0 + TH], out_sb[i][:])
        P.finalize()
    return nc


def t1_inputs(x_fm, yg_fm, ins):
    w_router = np.asarray(ins["L0_ffn_w_router"])
    b1 = np.asarray(ins["L0_ffn_b1"])
    b1c = np.ascontiguousarray(b1.reshape(NEXP, FFT, 128).transpose(2, 0, 1).reshape(128, NEXP * FFT))
    sel = np.zeros((NEXP, NEXP * 128), np.float32)
    for e in range(NEXP):
        sel[e, e * 128:(e + 1) * 128] = 1.0
    maps = []
    for c in range(NCORES):
        ts = slice(c * T_C, (c + 1) * T_C)
        maps.append({
            "xin": np.ascontiguousarray(x_fm[:, ts]),
            "ygin": np.ascontiguousarray(yg_fm[:, ts]),
            "wout": np.asarray(ins["L0_attn_W_out"]),
            "wr": w_router,
            "w1": np.asarray(ins["L0_ffn_w1"]),
            "w2": np.asarray(ins["L0_ffn_w2"]),
            "b1c": b1c,
            "b2": np.asarray(ins["L0_ffn_b2"]),
            "sel8": sel,
            "ident": IDENT,
            "ones_col": ONES_COL, "ones_row": ONES_ROW,
        })
    return maps


def run_T1(x_fm, yg_fm, ins):
    nc = build_T1()
    res = run_bass_kernel_spmd(nc, t1_inputs(x_fm, yg_fm, ins), core_ids=list(range(NCORES)))
    return np.concatenate([r["xout"] for r in res.results], axis=1)



SHIFT_C = 20.0
M1_DEBUG = False
NH_C = 8
HD = 64


def nc_psum(nc, E, name, shape):
    return E(nc.psum_tensor(name, shape, F32))


def build_M1():
    nc = bass.Bass("TRN2", target_bir_lowering=False)
    ap = lambda n, s, d=F32, k="ExternalInput": nc.dram_tensor(n, s, d, kind=k).ap()
    xb_in = ap("xb_in", [D, S])
    wq = ap("wq", [D, 512])
    wk = ap("wk", [D, 512])
    wv = ap("wv", [D, 512])
    bq = ap("bq", [128, 4])
    bk = ap("bk", [128, 4])
    bv_r = ap("bv_r", [1, 512])
    cosm = ap("cosm", [32, S])
    sinm = ap("sinm", [32, S])
    maskd = ap("maskd", [128, 128])
    ident = ap("ident", [128, 128])
    attn_o = ap("attn_o", [512, S], BF16, "ExternalOutput")
    if M1_DEBUG:
        d_q = ap("d_q", [128, 512], BF16, "ExternalOutput")
        d_k = ap("d_k", [128, 512], BF16, "ExternalOutput")
        d_v = ap("d_v", [128, 65], BF16, "ExternalOutput")
        d_st = ap("d_st", [128, 512], BF16, "ExternalOutput")
    TCn = 512

    with contextlib.ExitStack() as ctx:
        E = ctx.enter_context
        P = Prog(nc)
        q_bf = [E(nc.sbuf_tensor(f"q{m}", [128, S], BF16)) for m in range(4)]
        k_bf = [E(nc.sbuf_tensor(f"k{m}", [128, S], BF16)) for m in range(4)]
        v_tm = [E(nc.sbuf_tensor(f"v{t}", [128, NH_C * 65], BF16)) for t in range(S // 128)]
        mask_sb = E(nc.sbuf_tensor("mask_sb", [128, 128], BF16))
        id_sb = E(nc.sbuf_tensor("id_sb", [128, 128], F32))
        bcols = E(nc.sbuf_tensor("bcols", [128, 8], F32))
        bvrow = E(nc.sbuf_tensor("bvrow", [1, 512], F32))
        st_bf = E(nc.sbuf_tensor("st_bf", [128, 512], BF16))
        ot_tm = E(nc.sbuf_tensor("ot_tm", [128, 512], F32))
        rec_c = E(nc.sbuf_tensor("rec_c", [128, 4], F32))
        oh_fm = E(nc.sbuf_tensor("oh_fm", [64, S], BF16))

        wq_sb = E(nc.sbuf_tensor("wq_sb", [128, DT * 512], BF16))
        wk_sb = E(nc.sbuf_tensor("wk_sb", [128, DT * 512], BF16))
        wv_sb = E(nc.sbuf_tensor("wv_sb", [128, DT * 512], BF16))
        P.new_stage("m1_const")
        for k in range(DT):
            P.dma("gpsimd", wq_sb[:, k * 512:(k + 1) * 512], wq[k * 128:(k + 1) * 128, :])
            P.dma("gpsimd", wk_sb[:, k * 512:(k + 1) * 512], wk[k * 128:(k + 1) * 128, :])
            P.dma("gpsimd", wv_sb[:, k * 512:(k + 1) * 512], wv[k * 128:(k + 1) * 128, :])
        P.dma("gpsimd", mask_sb[:], maskd)
        P.dma("sync", id_sb[:], ident)
        P.dma("sync", bcols[:, 0:4], bq)
        P.dma("sync", bcols[:, 4:8], bk)
        P.dma("sync", bvrow[:], bv_r)
        P.new_stage("m1_vones")
        for t in range(S // 128):
            P.on("vector", lambda v, t=t: v.memset(v_tm[t][:], 0.0))

        with contextlib.ExitStack() as sctx:
            Es = sctx.enter_context
            xt = [Es(nc.sbuf_tensor(f"pjx{i}", [128, TCn], F32)) for i in range(DT)]
            sq = [Es(nc.sbuf_tensor(f"pjs{i}", [128, TCn], F32)) for i in range(DT)]
            xn2 = [Es(nc.sbuf_tensor(f"pjn{i}", [128, TCn], BF16)) for i in range(DT)]
            mu_b = Es(nc.sbuf_tensor("pjmu", [128, TCn], F32))
            rstd_b = Es(nc.sbuf_tensor("pjrs", [128, TCn], F32))
            mu_row = Es(nc.sbuf_tensor("pjmr", [1, TCn], F32))
            sq_row = Es(nc.sbuf_tensor("pjsr", [1, TCn], F32))
            qf4 = [Es(nc.sbuf_tensor(f"pjqf{m}", [128, TCn], F32)) for m in range(1)]
            rt_a = Es(nc.sbuf_tensor("pjra", [32, TCn], F32))
            rt_b = Es(nc.sbuf_tensor("pjrb", [32, TCn], F32))
            rt_c = Es(nc.sbuf_tensor("pjrc", [32, TCn], F32))
            rt_d = Es(nc.sbuf_tensor("pjrd", [32, TCn], F32))
            cos_sb = Es(nc.sbuf_tensor("cos_sb", [32, TCn], F32))
            sin_sb = Es(nc.sbuf_tensor("sin_sb", [32, TCn], F32))
            onescol = Es(nc.sbuf_tensor("pjoc", [128, 1], F32))
            onesrow = Es(nc.sbuf_tensor("pjor", [1, 128], F32))
            PS_A = nc_psum(nc, Es, "pjA", [128, 1024])
            PS_B = nc_psum(nc, Es, "pjB", [128, 1024])
            PS_R0 = nc_psum(nc, Es, "pjR0", [1, 512])
            PS_R1 = nc_psum(nc, Es, "pjR1", [1, 512])
            P.new_stage("pj_const")
            P.on("vector", lambda v: v.memset(onescol[:], 1.0))
            P.new_stage("pj_const2")
            vec_memset(P, onesrow[:], 1.0)
            vdrain(P)
            for ci, t0 in enumerate(range(0, S, TCn)):
                P.new_stage("pj_load")
                for i in range(DT):
                    P.dma("sync", xt[i][:], xb_in[i * 128:(i + 1) * 128, t0:t0 + TCn])
                P.dma("sync", cos_sb[:], cosm[:, t0:t0 + TCn])
                P.dma("sync", sin_sb[:], sinm[:, t0:t0 + TCn])
                emit_ln(P, xt, xn2, sq, mu_b, rstd_b, (PS_R0, PS_R1), PS_A,
                        mu_row, sq_row, onescol[:], onesrow[:], TCn)
                for which, w_in, dst, bofs in (("q", wq_sb, q_bf, 0), ("k", wk_sb, k_bf, 4)):
                    for m in range(4):
                        P.new_stage(f"pj_{which}mm{m}")
                        pp = PS_B[:, (m % 2) * 512:(m % 2) * 512 + 512]
                        emit_mm_group(P, pp, [w_in[:, k * 512 + m * 128:k * 512 + (m + 1) * 128] for k in range(DT)],
                                      [xn2[k][:] for k in range(DT)], 0, TCn)
                        P.new_stage(f"pj_{which}ev{m}")
                        vec_ts(P, qf4[0][:], pp[:, 0:TCn], bcols[:, bofs + m:bofs + m + 1], None, ALU.add)
                        for hb in (0, 64):
                            P.new_stage(f"pj_{which}r1{m}{hb}")
                            vec_copy(P, rt_a[:], qf4[0][hb:hb + 32, :])
                            vec_copy(P, rt_b[:], qf4[0][hb + 32:hb + 64, :])
                            P.new_stage(f"pj_{which}r2{m}{hb}")
                            vec_tt(P, rt_c[:], rt_a[:], cos_sb[0:32, :], ALU.mult)
                            vec_tt(P, rt_d[:], rt_b[:], sin_sb[0:32, :], ALU.mult)
                            P.new_stage(f"pj_{which}r3{m}{hb}")
                            vec_tt(P, rt_c[:], rt_c[:], rt_d[:], ALU.subtract)
                            vec_tt(P, rt_d[:], rt_a[:], sin_sb[0:32, :], ALU.mult)
                            P.new_stage(f"pj_{which}r4{m}{hb}")
                            vec_tt(P, rt_b[:], rt_b[:], cos_sb[0:32, :], ALU.mult)
                            P.new_stage(f"pj_{which}r5{m}{hb}")
                            vec_tt(P, rt_d[:], rt_d[:], rt_b[:], ALU.add)
                            P.new_stage(f"pj_{which}r6{m}{hb}")
                            vec_copy(P, qf4[0][hb:hb + 32, :], rt_c[:])
                            vec_copy(P, qf4[0][hb + 32:hb + 64, :], rt_d[:])
                        P.new_stage(f"pj_{which}cp{m}")
                        vec_copy(P, dst[m][:, t0:t0 + TCn], qf4[0][:])
                for tb in range(TCn // 128):
                    tg = (t0 + tb * 128) // 128
                    P.new_stage(f"pj_vmm{tb}")
                    pvof = lambda h: h * 65 if h < 4 else 512 + (h - 4) * 65
                    for h in range(NH_C):
                        for k in range(DT):
                            def fn(t, tb=tb, h=h, k=k, pvof=pvof):
                                return t.matmul(PS_B[:, pvof(h):pvof(h) + 64],
                                                xn2[k][:, tb * 128:(tb + 1) * 128],
                                                wv_sb[:, k * 512 + h * 64:k * 512 + (h + 1) * 64],
                                                start=(k == 0), stop=(k == DT - 1))
                            P.on("tensor", fn)
                    P.new_stage(f"pj_vev{tb}")
                    for h in range(NH_C):
                        act(P, v_tm[tg][:, h * 65:h * 65 + 64], PS_B[:, pvof(h):pvof(h) + 64], AF.Copy)

        with contextlib.ExitStack() as sctx:
            Es = sctx.enter_context
            onesr = Es(nc.sbuf_tensor("m1_onesr", [1, 128], F32))
            bvb = Es(nc.sbuf_tensor("m1_bvb", [128, NH_C * 65], F32))
            PS_V = nc_psum(nc, Es, "m1psv", [128, 1024])
            P.new_stage("m1_fixv0")
            vec_memset(P, onesr[:], 1.0)
            vdrain(P)
            P.new_stage("m1_fixv1")
            pvof2 = lambda h: h * 65 if h < 4 else 512 + (h - 4) * 65
            for h in range(NH_C):
                def fn(t, h=h):
                    return t.matmul(PS_V[:, pvof2(h):pvof2(h) + 64], onesr[0:1, :],
                                    bvrow[0:1, h * 64:(h + 1) * 64], start=True, stop=True)
                P.on("tensor", fn)
            P.new_stage("m1_fixv2")
            for h in range(NH_C):
                vec_copy(P, bvb[:, h * 65:h * 65 + 64], PS_V[:, pvof2(h):pvof2(h) + 64])
            P.new_stage("m1_fixv3")
            for h in range(NH_C):
                vec_memset(P, bvb[:, h * 65 + 64:h * 65 + 65], 1.0)
            P.new_stage("m1_fixv4")
            for t in range(S // 128):
                vec_tt(P, v_tm[t][:], v_tm[t][:], bvb[:], ALU.add)

        if M1_DEBUG:
            P.new_stage("m1_dbg")
            P.dma("sync", d_q[0:128, :], q_bf[0][:, 0:512])
            P.dma("sync", d_k[0:128, :], k_bf[0][:, 0:512])
            P.dma("sync", d_v, v_tm[0][:, 455:520])
        with contextlib.ExitStack() as sctx:
            Es = sctx.enter_context
            PS_S = nc_psum(nc, Es, "m1pss", [128, 1024])
            PS_P = nc_psum(nc, Es, "m1psp", [128, 512])
            PS_T = nc_psum(nc, Es, "m1pst", [128, 512])
            for h in range(NH_C):
                qt = q_bf[h // 2]
                kt_ = k_bf[h // 2]
                hb = (h % 2) * 64
                for band in range(S // 512):
                    nkt = 4 * band + 4
                    for kti in range(nkt):
                        pp = PS_S[:, (kti % 2) * 512:(kti % 2) * 512 + 512]
                        P.new_stage(f"at_s{h}_{band}_{kti}")
                        def fn(t, pp=pp, kti=kti, band=band, qt=qt, kt_=kt_, hb=hb):
                            return t.matmul(pp, kt_[hb:hb + 64, kti * 128:(kti + 1) * 128],
                                            qt[hb:hb + 64, band * 512:(band + 1) * 512],
                                            start=True, stop=True)
                        P.on("tensor", fn)
                        P.new_stage(f"at_e{h}_{band}_{kti}")
                        act(P, st_bf[:], pp, AF.Exp, scale=0.125)
                        dq = kti - 4 * band
                        if dq >= 0:
                            P.new_stage(f"at_m{h}_{band}_{kti}")
                            vec_tt(P, st_bf[:, dq * 128:(dq + 1) * 128],
                                   st_bf[:, dq * 128:(dq + 1) * 128], mask_sb[:], ALU.mult)
                        P.new_stage(f"at_p{h}_{band}_{kti}")
                        for qb in range(4):
                            if 4 * band + qb < kti:
                                continue
                            def fn(t, qb=qb, kti=kti, band=band, h=h, nkt=nkt):
                                return t.matmul(PS_P[:, qb * 128:qb * 128 + 65],
                                                st_bf[:, qb * 128:(qb + 1) * 128],
                                                v_tm[kti][:, h * 65:h * 65 + 65],
                                                start=(kti == 0 and qb == 0), stop=(kti == 4 * band + qb))
                            P.on("tensor", fn)
                    if M1_DEBUG and h == 7 and band == 0:
                        P.new_stage("m1_dbgst")
                        P.dma("sync", d_st, st_bf[:])
                    P.new_stage(f"at_r{h}_{band}")
                    for qb in range(4):
                        P.on("vector", lambda v, qb=qb: v.reciprocal(
                            rec_c[:, qb:qb + 1], PS_P[:, qb * 128 + 64:qb * 128 + 65]))
                    vdrain(P)
                    P.new_stage(f"at_n{h}_{band}")
                    for qb in range(4):
                        vec_tt(P, ot_tm[:, qb * 128:qb * 128 + 64], PS_P[:, qb * 128:qb * 128 + 64],
                               rec_c[:, qb:qb + 1].broadcast_to([128, 64]), ALU.mult)
                    vdrain(P)
                    P.new_stage(f"at_t{h}_{band}")
                    for qb in range(4):
                        def fn(t, qb=qb):
                            return t.transpose(PS_T[0:64, qb * 128:(qb + 1) * 128],
                                               ot_tm[:, qb * 128:qb * 128 + 64], id_sb[:])
                        P.on("tensor", fn)
                    P.new_stage(f"at_o{h}_{band}")
                    vec_copy(P, oh_fm[:, band * 512:(band + 1) * 512], PS_T[0:64, 0:512])
                P.new_stage(f"at_st{h}")
                P.dma("sync", attn_o[h * 64:(h + 1) * 64, :], oh_fm[:])
        P.finalize()
    return nc


def m1_inputs(x_fm, ins):
    wq = np.asarray(ins["L1_attn_wq"])
    wk = np.asarray(ins["L1_attn_wk"])
    wv = np.asarray(ins["L1_attn_wv"])
    bq = np.asarray(ins["L1_attn_bq"])
    bk = np.asarray(ins["L1_attn_bk"])
    bv = np.asarray(ins["L1_attn_bv"])
    pos = np.arange(S, dtype=np.float64)[:, None]
    dim = np.arange(32, dtype=np.float64)[None, :]
    theta = pos / np.exp(2.0 * dim / 64 * math.log(10000.0))
    cosm = np.ascontiguousarray(np.cos(theta).T.astype(np.float32))
    sinm = np.ascontiguousarray(np.sin(theta).T.astype(np.float32))
    maskd = np.triu(np.ones((128, 128), np.float32))
    maps = []
    for c in range(NCORES):
        b, hh = c // 2, c % 2
        hsl = slice(hh * 8 * 64, (hh + 1) * 8 * 64)
        wqT = wq.T[:, hsl]
        wkT = wk.T[:, hsl]
        wvT = wv.T[:, hsl]
        pq = np.concatenate(
            [np.concatenate([np.arange(h * 64, (h + 1) * 64, 2),
                             np.arange(h * 64 + 1, (h + 1) * 64, 2)]) for h in range(8)])
        wq_c = np.ascontiguousarray(wqT[:, pq])
        wk_c = np.ascontiguousarray(wkT[:, pq])
        bq_c = np.ascontiguousarray(bq[hsl][pq].reshape(4, 128).T)
        bk_c = np.ascontiguousarray(bk[hsl][pq].reshape(4, 128).T)
        maps.append({
            "xb_in": np.ascontiguousarray(x_fm[:, b * S:(b + 1) * S]),
            "wq": wq_c, "wk": wk_c, "wv": np.ascontiguousarray(wvT),
            "bq": bq_c, "bk": bk_c,
            "bv_r": np.ascontiguousarray(bv[hsl].reshape(1, 512)),
            "cosm": cosm, "sinm": sinm, "maskd": maskd, "ident": IDENT,
        })
    return maps


def run_M1(x_fm, ins):
    nc = build_M1()
    res = run_bass_kernel_spmd(nc, m1_inputs(x_fm, ins), core_ids=list(range(NCORES)))
    out = np.zeros((D, T_ALL), np.float32)
    import ml_dtypes
    for c in range(NCORES):
        b, hh = c // 2, c % 2
        a = np.asarray(res.results[c]["attn_o"])
        if a.dtype != np.float32:
            a = a.view(ml_dtypes.bfloat16).astype(np.float32) if a.dtype == np.uint16 else a.astype(np.float32)
        out[hh * 512:(hh + 1) * 512, b * S:(b + 1) * S] = a
    return out



def build_T2():
    nc = bass.Bass("TRN2", target_bir_lowering=False)
    ap = lambda n, s, d=F32, k="ExternalInput": nc.dram_tensor(n, s, d, kind=k).ap()
    xin = ap("xin", [D, T_C + 1])
    attn_in = ap("attn_in", [D, T_C], BF16)
    woT = ap("woT", [D, D])
    wrT = ap("wrT", [D, D])
    wkT = ap("wkT", [D, D])
    wvT = ap("wvT", [D, D])
    bo_c = ap("bo_c", [128, DT])
    mur_c = ap("mur_c", [128, DT])
    muk_c = ap("muk_c", [128, DT])
    xout = ap("xout", [D, T_C], F32, "ExternalOutput")
    TH = 1024

    with contextlib.ExitStack() as ctx:
        E = ctx.enter_context
        P = Prog(nc)
        out_sb = [E(nc.sbuf_tensor(f"t2o{i}", [128, TH], F32)) for i in range(DT)]
        xr_bf = [E(nc.sbuf_tensor(f"t2xr{i}", [128, TH], BF16)) for i in range(DT)]
        xk_bf = [E(nc.sbuf_tensor(f"t2xk{i}", [128, TH], BF16)) for i in range(DT)]
        r_bf = [E(nc.sbuf_tensor(f"t2r{i}", [128, TH], BF16)) for i in range(DT)]
        ka_bf = [E(nc.sbuf_tensor(f"t2ka{i}", [128, TH], BF16)) for i in range(DT)]
        cols = E(nc.sbuf_tensor("t2cols", [128, 4 * DT], F32))
        bo_sb = cols[:, 0:DT]
        mur_sb = cols[:, DT:2 * DT]
        muk_sb = cols[:, 2 * DT:3 * DT]
        om_sb = E(nc.sbuf_tensor("t2om", [128, 2 * DT], F32))
        PS_A = E(nc.psum_tensor("t2psA", [128, 1024], F32))
        PS_B = E(nc.psum_tensor("t2psB", [128, 1024], F32))
        PS_R0 = E(nc.psum_tensor("t2psR0", [1, 512], F32))
        PS_R1 = E(nc.psum_tensor("t2psR1", [1, 512], F32))

        P.new_stage("t2_cols")
        P.dma("sync", bo_sb, bo_c)
        P.dma("sync", mur_sb, mur_c)
        P.dma("sync", muk_sb, muk_c)
        P.new_stage("t2_om")
        vec_ts(P, om_sb[:, 0:DT], mur_sb, -1.0, 1.0, ALU.mult, ALU.add)
        vec_ts(P, om_sb[:, DT:2 * DT], muk_sb, -1.0, 1.0, ALU.mult, ALU.add)
        vdrain(P)

        for hh in range(2):
            t0 = hh * TH
            W = TH + 1
            with contextlib.ExitStack() as sctx:
                Es = sctx.enter_context
                xt = [Es(nc.sbuf_tensor(f"t2x{hh}_{i}", [128, W], F32)) for i in range(DT)]
                sq = [Es(nc.sbuf_tensor(f"t2s{hh}_{i}", [128, W], F32)) for i in range(DT)]
                mu_b = Es(nc.sbuf_tensor(f"t2mu{hh}", [128, W], F32))
                rstd_b = Es(nc.sbuf_tensor(f"t2rs{hh}", [128, W], F32))
                mu_row = Es(nc.sbuf_tensor(f"t2mr{hh}", [1, W], F32))
                sq_row = Es(nc.sbuf_tensor(f"t2sr{hh}", [1, W], F32))
                onescol = Es(nc.sbuf_tensor(f"t2oc{hh}", [128, 1], F32))
                onesrow = Es(nc.sbuf_tensor(f"t2or{hh}", [1, 128], F32))
                P.new_stage("t2_c2")
                P.on("vector", lambda v, onescol=onescol: v.memset(onescol[:], 1.0))
                P.new_stage("t2_c3")
                vec_memset(P, onesrow[:], 1.0)
                vdrain(P)
                P.new_stage("t2_xload")
                for i in range(DT):
                    P.dma("sync", xt[i][:], xin[i * 128:(i + 1) * 128, t0:t0 + W])
                emit_ln(P, xt, sq, sq, mu_b, rstd_b, (PS_R0, PS_R1), PS_A,
                        mu_row, sq_row, onescol[:], onesrow[:], W)
                P.new_stage("t2_base")
                for i in range(DT):
                    vec_copy(P, out_sb[i][:], xt[i][:, 1:1 + TH])
                P.new_stage("t2_mix1")
                for i in range(DT):
                    vec_ts(P, mu_b[:, 0:TH], sq[i][:, 1:1 + TH], mur_sb[:, i:i + 1], None, ALU.mult)
                    vec_stt(P, xr_bf[i][:], sq[i][:, 0:TH], om_sb[:, i:i + 1], mu_b[:, 0:TH],
                            ALU.mult, ALU.add)
                    vec_ts(P, mu_b[:, 0:TH], sq[i][:, 1:1 + TH], muk_sb[:, i:i + 1], None, ALU.mult)
                    vec_stt(P, xk_bf[i][:], sq[i][:, 0:TH], om_sb[:, DT + i:DT + i + 1], mu_b[:, 0:TH],
                            ALU.mult, ALU.add)
            with contextlib.ExitStack() as sctx:
                Es = sctx.enter_context
                w_sb = Es(nc.sbuf_tensor(f"t2w{hh}", [128, DT * 1024], BF16))
                attn_bf = [Es(nc.sbuf_tensor(f"t2a{hh}_{i}", [128, TH], BF16)) for i in range(DT)]
                tmp = Es(nc.sbuf_tensor(f"t2tmp{hh}", [128, 512], F32))
                for wname, w_in, rhs_t, dstl, kind in (
                        ("r", wrT, xr_bf, r_bf, "sig"),
                        ("k", wkT, xk_bf, ka_bf, "relsq"),
                        ("v", wvT, ka_bf, None, "vout"),
                        ("ao", woT, attn_bf, None, "attn")):
                    P.new_stage(f"t2_{wname}wl")
                    for k in range(DT):
                        P.dma("gpsimd", w_sb[:, k * 1024:(k + 1) * 1024],
                              w_in[k * 128:(k + 1) * 128, :])
                    if kind == "attn":
                        for i in range(DT):
                            P.dma("gpsimd", attn_bf[i][:], attn_in[i * 128:(i + 1) * 128, t0:t0 + TH])
                    for m in range(DT):
                        for j in range(2):
                            P.new_stage(f"t2_{wname}mm{m}{j}")
                            pp = PS_B[:, (j % 2) * 512:(j % 2) * 512 + 512]
                            emit_mm_group(P, pp, [w_sb[:, k * 1024 + m * 128:k * 1024 + (m + 1) * 128]
                                                  for k in range(DT)],
                                          [rhs_t[k][:, j * 512:(j + 1) * 512] for k in range(DT)], 0, 512)
                            P.new_stage(f"t2_{wname}ev{m}{j}")
                            sl = slice(j * 512, (j + 1) * 512)
                            if kind == "sig":
                                act(P, dstl[m][:, sl], pp[:, 0:512], AF.Sigmoid)
                            elif kind == "relsq":
                                act(P, tmp[:], pp[:, 0:512], AF.Relu)
                                P.new_stage(f"t2_{wname}ev2{m}{j}")
                                act(P, dstl[m][:, sl], tmp[:], AF.Square)
                            elif kind == "vout":
                                vec_tt(P, tmp[:], pp[:, 0:512], r_bf[m][:, sl], ALU.mult)
                                P.new_stage(f"t2_{wname}ac{m}{j}")
                                vec_tt(P, out_sb[m][:, sl], out_sb[m][:, sl], tmp[:], ALU.add)
                            elif kind == "attn":
                                vec_ts(P, tmp[:], pp[:, 0:512], bo_sb[:, m:m + 1], None, ALU.add)
                                P.new_stage(f"t2_{wname}ac{m}{j}")
                                vec_tt(P, out_sb[m][:, sl], out_sb[m][:, sl], tmp[:], ALU.add)
            P.new_stage("t2_store")
            for i in range(DT):
                P.dma("sync", xout[i * 128:(i + 1) * 128, t0:t0 + TH], out_sb[i][:])
        P.finalize()
    return nc


def t2_inputs(x_fm, attn_fm_bf, ins):
    woT = np.ascontiguousarray(np.asarray(ins["L1_attn_wo"]).T)
    wrT = np.ascontiguousarray(np.asarray(ins["L1_ffn_W_r"]).T)
    wkT = np.ascontiguousarray(np.asarray(ins["L1_ffn_W_k"]).T)
    wvT = np.ascontiguousarray(np.asarray(ins["L1_ffn_W_v"]).T)
    cz = lambda a: np.ascontiguousarray(np.asarray(a).reshape(DT, 128).T)
    maps = []
    for c in range(NCORES):
        t0 = c * T_C
        if c % 2 == 0:
            prev = np.zeros((D, 1), np.float32)
        else:
            prev = x_fm[:, t0 - 1:t0]
        xin = np.ascontiguousarray(np.concatenate([prev, x_fm[:, t0:t0 + T_C]], axis=1))
        maps.append({
            "xin": xin,
            "attn_in": np.ascontiguousarray(attn_fm_bf[:, t0:t0 + T_C]),
            "woT": woT, "wrT": wrT, "wkT": wkT, "wvT": wvT,
            "bo_c": cz(ins["L1_attn_bo"]),
            "mur_c": cz(ins["L1_ffn_mu_r"]),
            "muk_c": cz(ins["L1_ffn_mu_k"]),
        })
    return maps


def run_T2(x_fm, attn_fm, ins):
    import ml_dtypes
    nc = build_T2()
    attn_bf = attn_fm.astype(ml_dtypes.bfloat16) if attn_fm.dtype == np.float32 else attn_fm
    res = run_bass_kernel_spmd(nc, t2_inputs(x_fm, attn_bf, ins), core_ids=list(range(NCORES)))
    return np.concatenate([r["xout"] for r in res.results], axis=1)



L2EPS = 0.001


def build_M2():
    nc = bass.Bass("TRN2", target_bir_lowering=False)
    ap = lambda n, s, d=F32, k="ExternalInput": nc.dram_tensor(n, s, d, kind=k).ap()
    xb_in = ap("xb_in", [D, S])
    wq = ap("wq", [D, 512])
    wk = ap("wk", [D, 512])
    wv = ap("wv", [D, 512])
    maskd = ap("maskd", [128, 128])
    ident = ap("ident", [128, 128])
    ident64 = ap("ident64", [128, 64])
    attn_o = ap("attn_o", [512, S], BF16, "ExternalOutput")
    TCn = 512

    with contextlib.ExitStack() as ctx:
        E = ctx.enter_context
        P = Prog(nc)
        q_bf = [E(nc.sbuf_tensor(f"m2q{m}", [128, S], BF16)) for m in range(4)]
        k_bf = [E(nc.sbuf_tensor(f"m2k{m}", [128, S], BF16)) for m in range(4)]
        v_tm = [E(nc.sbuf_tensor(f"m2v{t}", [128, NH_C * 65], BF16)) for t in range(S // 128)]
        ktb = E(nc.sbuf_tensor("m2ktb", [128, 64], BF16))
        mask_sb = E(nc.sbuf_tensor("m2mask", [128, 128], BF16))
        id_sb = E(nc.sbuf_tensor("m2id", [128, 128], F32))
        id_bf = E(nc.sbuf_tensor("m2idb", [128, 64], BF16))
        s_f = E(nc.sbuf_tensor("m2sf", [64, 65], F32))
        s_b = E(nc.sbuf_tensor("m2sb", [128, 65], BF16))
        abuf = E(nc.sbuf_tensor("m2ab", [128, 128], BF16))
        ot_tm = E(nc.sbuf_tensor("m2ot", [128, 64], F32))
        rec_c = E(nc.sbuf_tensor("m2rc", [128, 4], F32))
        oh_fm = E(nc.sbuf_tensor("m2oh", [64, S], BF16))
        wq_sb = E(nc.sbuf_tensor("m2wq", [128, DT * 512], BF16))
        wk_sb = E(nc.sbuf_tensor("m2wk", [128, DT * 512], BF16))
        wv_sb = E(nc.sbuf_tensor("m2wv", [128, DT * 512], BF16))

        P.new_stage("m2_const")
        P.dma("gpsimd", mask_sb[:], maskd)
        P.dma("sync", id_sb[:], ident)
        P.dma("gpsimd", id_bf[:], ident64)
        for k in range(DT):
            P.dma("gpsimd", wq_sb[:, k * 512:(k + 1) * 512], wq[k * 128:(k + 1) * 128, :])
            P.dma("gpsimd", wk_sb[:, k * 512:(k + 1) * 512], wk[k * 128:(k + 1) * 128, :])
            P.dma("gpsimd", wv_sb[:, k * 512:(k + 1) * 512], wv[k * 128:(k + 1) * 128, :])

        with contextlib.ExitStack() as sctx:
            Es = sctx.enter_context
            xt = [Es(nc.sbuf_tensor(f"m2x{i}", [128, TCn], F32)) for i in range(DT)]
            sq = [Es(nc.sbuf_tensor(f"m2s{i}", [128, TCn], F32)) for i in range(DT)]
            xn2 = [Es(nc.sbuf_tensor(f"m2n{i}", [128, TCn], BF16)) for i in range(DT)]
            mu_b = Es(nc.sbuf_tensor("m2mu", [128, TCn], F32))
            rstd_b = Es(nc.sbuf_tensor("m2rs", [128, TCn], F32))
            mu_row = Es(nc.sbuf_tensor("m2mr", [1, TCn], F32))
            sq_row = Es(nc.sbuf_tensor("m2sr", [1, TCn], F32))
            onescol = Es(nc.sbuf_tensor("m2oc", [128, 1], F32))
            onesrow = Es(nc.sbuf_tensor("m2or", [1, 128], F32))
            PS_A = nc_psum(nc, Es, "m2A", [128, 1024])
            PS_B = nc_psum(nc, Es, "m2B", [128, 1024])
            PS_R0 = nc_psum(nc, Es, "m2R0", [1, 512])
            PS_R1 = nc_psum(nc, Es, "m2R1", [1, 512])
            P.new_stage("m2_c1")
            P.on("vector", lambda v: v.memset(onescol[:], 1.0))
            P.new_stage("m2_c2")
            vec_memset(P, onesrow[:], 1.0)
            vdrain(P)
            for ci, t0 in enumerate(range(0, S, TCn)):
                P.new_stage("m2_load")
                for i in range(DT):
                    P.dma("sync", xt[i][:], xb_in[i * 128:(i + 1) * 128, t0:t0 + TCn])
                emit_ln(P, xt, xn2, sq, mu_b, rstd_b, (PS_R0, PS_R1), PS_A,
                        mu_row, sq_row, onescol[:], onesrow[:], TCn)
                for which, w_in, dst in (("q", wq_sb, q_bf), ("k", wk_sb, k_bf)):
                    for m in range(4):
                        P.new_stage(f"m2_{which}mm{m}")
                        pp = PS_B[:, (m % 2) * 512:(m % 2) * 512 + 512]
                        emit_mm_group(P, pp, [w_in[:, k * 512 + m * 128:k * 512 + (m + 1) * 128] for k in range(DT)],
                                      [xn2[k][:] for k in range(DT)], 0, TCn)
                        P.new_stage(f"m2_{which}ev{m}")
                        vec_ts(P, dst[m][:, t0:t0 + TCn], pp[:, 0:TCn], 0.0, L2EPS, ALU.max, ALU.add)
                pvof = lambda h: h * 65 if h < 4 else 512 + (h - 4) * 65
                for tb in range(TCn // 128):
                    tg = (t0 + tb * 128) // 128
                    P.new_stage(f"m2_vt{tb}")
                    for h in range(NH_C):
                        for k in range(DT):
                            def fn(t, tb=tb, h=h, k=k, pvof=pvof):
                                return t.matmul(PS_B[:, pvof(h):pvof(h) + 64],
                                                xn2[k][:, tb * 128:(tb + 1) * 128],
                                                wv_sb[:, k * 512 + h * 64:k * 512 + (h + 1) * 64],
                                                start=(k == 0), stop=(k == DT - 1))
                            P.on("tensor", fn)
                    P.new_stage(f"m2_vtev{tb}")
                    for h in range(NH_C):
                        act(P, v_tm[tg][:, h * 65:h * 65 + 64], PS_B[:, pvof(h):pvof(h) + 64], AF.Copy)
        P.new_stage("m2_vaug")
        for t in range(S // 128):
            for h in range(NH_C):
                vec_memset(P, v_tm[t][:, h * 65 + 64:h * 65 + 65], 1.0)
        vdrain(P)

        with contextlib.ExitStack() as sctx:
            Es = sctx.enter_context
            PS_S = nc_psum(nc, Es, "m2pss", [128, 512])
            PS_P = nc_psum(nc, Es, "m2psp", [128, 512])
            PS_U = nc_psum(nc, Es, "m2psu", [128, 512])
            PS_T = nc_psum(nc, Es, "m2pst", [128, 512])
            PS_KT = Es(nc.psum_tensor("m2pskt", [128, 64], BF16))
            for h in range(NH_C):
                qf = q_bf[h // 2]
                kf = k_bf[h // 2]
                hb = (h % 2) * 64
                P.new_stage(f"m2_z{h}")
                vec_memset(P, s_f[:], 0.0)
                vec_memset(P, s_b[:], 0.0)
                vdrain(P)
                for ci in range(S // 128):
                    csl = slice(ci * 128, (ci + 1) * 128)
                    P.new_stage(f"m2a_{h}_{ci}")
                    def fn(t, qf=qf, kf=kf, hb=hb, csl=csl):
                        return t.matmul(PS_S[:, 0:128], kf[hb:hb + 64, csl], qf[hb:hb + 64, csl],
                                        start=True, stop=True)
                    P.on("tensor", fn)
                    P.new_stage(f"m2b_{h}_{ci}")
                    vec_tt(P, abuf[:], PS_S[:, 0:128], mask_sb[:], ALU.mult)
                    P.new_stage(f"m2c_{h}_{ci}")
                    def fn(t, ci=ci, h=h, qf=qf, hb=hb, csl=csl):
                        t.matmul(PS_P[:, 0:65], abuf[:], v_tm[ci][:, h * 65:h * 65 + 65],
                                 start=True, stop=False)
                        return t.matmul(PS_P[:, 0:65], qf[hb:hb + 64, csl], s_b[hb:hb + 64, :],
                                        start=False, stop=True)
                    P.on("tensor", fn)
                    P.new_stage(f"m2d_{h}_{ci}")
                    P.on("vector", lambda v, ci=ci: v.reciprocal(rec_c[:, 0:1], PS_P[:, 64:65]))
                    vdrain(P)
                    P.new_stage(f"m2e_{h}_{ci}")
                    vec_tt(P, ot_tm[:], PS_P[:, 0:64], rec_c[:, 0:1].broadcast_to([128, 64]), ALU.mult)
                    P.new_stage(f"m2f_{h}_{ci}")
                    def fn(t, ci=ci):
                        return t.transpose(PS_T[0:64, 0:128], ot_tm[:], id_sb[:])
                    P.on("tensor", fn)
                    P.new_stage(f"m2g_{h}_{ci}")
                    vec_copy(P, oh_fm[:, csl], PS_T[0:64, 0:128])
                    P.new_stage(f"m2h0_{h}_{ci}")
                    def fn(t, ci=ci, kf=kf, hb=hb):
                        return t.transpose(PS_KT[:], kf[hb:hb + 64, ci * 128:(ci + 1) * 128], id_bf[hb:hb + 64, :])
                    P.on("tensor", fn)
                    P.new_stage(f"m2h1_{h}_{ci}")
                    vec_copy(P, ktb[:], PS_KT[:])
                    vdrain(P)
                    P.new_stage(f"m2h_{h}_{ci}")
                    def fn(t, ci=ci, h=h):
                        return t.matmul(PS_U[0:64, 0:65], ktb[:],
                                        v_tm[ci][:, h * 65:h * 65 + 65], start=True, stop=True)
                    P.on("tensor", fn)
                    P.new_stage(f"m2i_{h}_{ci}")
                    vec_tt(P, s_f[:], s_f[:], PS_U[0:64, 0:65], ALU.add)
                    P.new_stage(f"m2j_{h}_{ci}")
                    vec_copy(P, s_b[0:64, :], s_f[:])
                    vec_copy(P, s_b[64:128, :], s_f[:])
                    vdrain(P)
                P.new_stage(f"m2st{h}")
                P.dma("sync", attn_o[h * 64:(h + 1) * 64, :], oh_fm[:])
        P.finalize()
    return nc


def m2_inputs(x_fm, ins):
    Wq = np.asarray(ins["L2_attn_Wq"])
    Wk = np.asarray(ins["L2_attn_Wk"])
    Wv = np.asarray(ins["L2_attn_Wv"])
    maskd = np.triu(np.ones((128, 128), np.float32))
    maps = []
    for c in range(NCORES):
        b, hh = c // 2, c % 2
        hsl = slice(hh * 512, (hh + 1) * 512)
        maps.append({
            "xb_in": np.ascontiguousarray(x_fm[:, b * S:(b + 1) * S]),
            "wq": np.ascontiguousarray(Wq[:, hsl]),
            "wk": np.ascontiguousarray(Wk[:, hsl]),
            "wv": np.ascontiguousarray(Wv[:, hsl]),
            "maskd": maskd, "ident": IDENT,
            "ident64": np.ascontiguousarray(np.vstack([np.eye(64, dtype=np.float32)] * 2)),
        })
    return maps


def run_M2(x_fm, ins):
    nc = build_M2()
    res = run_bass_kernel_spmd(nc, m2_inputs(x_fm, ins), core_ids=list(range(NCORES)))
    out = np.zeros((D, T_ALL), np.float32)
    for c in range(NCORES):
        b, hh = c // 2, c % 2
        out[hh * 512:(hh + 1) * 512, b * S:(b + 1) * S] = np.asarray(res.results[c]["attn_o"]).astype(np.float32)
    return out



T3_DEBUG = False


def build_T3():
    nc = bass.Bass("TRN2", target_bir_lowering=False)
    ap = lambda n, s, d=F32, k="ExternalInput": nc.dram_tensor(n, s, d, kind=k).ap()
    xin = ap("xin", [D, T_C])
    attn_in = ap("attn_in", [D, T_C], BF16)
    wo = ap("wo", [D, D])
    w1T = ap("w1T", [D, FF])
    w2T = ap("w2T", [FF, D])
    bo_c = ap("bo_c", [128, DT])
    b1_c = ap("b1_c", [128, FFT])
    b2_c = ap("b2_c", [128, DT])
    hm_col = ap("hm_col", [128, 2])
    onr = ap("ones_row", [1, 128])
    xout = ap("xout", [D, T_C], F32, "ExternalOutput")
    d_sc = ap("d_sc", [128, 8], F32, "ExternalOutput") if T3_DEBUG else None
    d_w1 = ap("d_w1", [128, FF], BF16, "ExternalOutput") if T3_DEBUG else None
    w1t_d = nc.dram_tensor("w1t_d", [D, FF], BF16).ap()
    w2t_d = nc.dram_tensor("w2t_d", [FF, D], BF16).ap()
    TH = 1024

    with contextlib.ExitStack() as ctx:
        E = ctx.enter_context
        P = Prog(nc)
        out_sb = [E(nc.sbuf_tensor(f"t3o{i}", [128, TH], F32)) for i in range(DT)]
        xn_bf = [E(nc.sbuf_tensor(f"t3n{i}", [128, TH], BF16)) for i in range(DT)]
        cols = E(nc.sbuf_tensor("t3cols", [128, 2 * DT + FFT + 8], F32))
        bo_sb = cols[:, 0:DT]
        b1_sb = cols[:, DT:DT + FFT]
        b2_sb = cols[:, DT + FFT:2 * DT + FFT]
        sc_sb = cols[:, 2 * DT + FFT:2 * DT + FFT + 8]
        hmc = E(nc.sbuf_tensor("t3hm", [128, 2], F32))
        onesrow = E(nc.sbuf_tensor("t3or", [1, 128], F32))
        srow = E(nc.sbuf_tensor("t3srow", [1, 2], F32))
        PS_A = E(nc.psum_tensor("t3psA", [128, 1024], F32))
        PS_B = E(nc.psum_tensor("t3psB", [128, 1024], F32))
        PS_R0 = E(nc.psum_tensor("t3psR0", [1, 512], F32))
        PS_R1 = E(nc.psum_tensor("t3psR1", [1, 512], F32))

        P.new_stage("t3_c")
        P.dma("sync", bo_sb, bo_c)
        P.dma("sync", cols[:, DT:DT + FFT], b1_c)
        P.dma("sync", cols[:, DT + FFT:2 * DT + FFT], b2_c)
        P.dma("sync", hmc[:], hm_col)
        P.dma("sync", onesrow[:], onr)

        with contextlib.ExitStack() as sctx:
            Es = sctx.enter_context
            wtmp = Es(nc.sbuf_tensor("t3wt", [128, FF], F32))
            wabs = Es(nc.sbuf_tensor("t3wa", [128, FF], F32))
            wsgn = Es(nc.sbuf_tensor("t3wg", [128, FF], F32))
            wbf = Es(nc.sbuf_tensor("t3wb", [128, FF], BF16))
            acc = Es(nc.sbuf_tensor("t3ac", [128, 64], F32))
            for wi, (w_in, wd, ntile, wid, thr_i) in enumerate(
                    ((w1T, w1t_d, DT, FF, 0), (w2T, w2t_d, FFT, D, 1))):
                for k in range(ntile):
                    P.new_stage(f"t3_s{wi}_{k}")
                    P.dma("sync", wtmp[:, 0:wid], w_in[k * 128:(k + 1) * 128, :])
                    P.new_stage(f"t3_s{wi}_{k}b")
                    act(P, wabs[:, 0:wid], wtmp[:, 0:wid], AF.Abs,
                        accum_out=acc[:, 8 + wi * 16 + k:8 + wi * 16 + k + 1])
                    sdrain(P)
                P.new_stage(f"t3_t{wi}")
                nt = ntile
                base = 8 + wi * 16
                step = 1
                while step < nt:
                    for j in range(0, nt, 2 * step):
                        if j + step < nt:
                            vec_tt(P, acc[:, base + j:base + j + 1], acc[:, base + j:base + j + 1],
                                   acc[:, base + j + step:base + j + step + 1], ALU.add)
                    vdrain(P)
                    P.new_stage(f"t3_t{wi}_{step}")
                    step *= 2
                vec_copy(P, acc[:, wi:wi + 1], acc[:, base:base + 1])
                vdrain(P)
                P.new_stage(f"t3_th{wi}")
                def fn(t, wi=wi, thr_i=thr_i):
                    return t.matmul(PS_R0[0:1, 0:1], hmc[:, thr_i:thr_i + 1],
                                    acc[:, wi:wi + 1], start=True, stop=True)
                P.on("tensor", fn)
                P.new_stage(f"t3_th{wi}b")
                vec_ts(P, srow[:, wi:wi + 1], PS_R0[0:1, 0:1], 5e-9, None, ALU.max)
                vdrain(P)
                P.new_stage(f"t3_th{wi}c")
                def fn(t, wi=wi):
                    return t.matmul(PS_R1[0:1, 0:128].rearrange("a b -> a b") if False else PS_A[:, 0:1],
                                    onesrow[0:1, :], srow[0:1, wi:wi + 1], start=True, stop=True)
                P.on("tensor", fn)
                P.new_stage(f"t3_th{wi}d")
                vec_copy(P, sc_sb[:, wi:wi + 1], PS_A[:, 0:1])
                vec_ts(P, sc_sb[:, 2 + wi:3 + wi], PS_A[:, 0:1], 2.0, None, ALU.mult)
                vdrain(P)
                for k in range(ntile):
                    P.new_stage(f"t3_z{wi}_{k}")
                    P.dma("sync", wtmp[:, 0:wid], w_in[k * 128:(k + 1) * 128, :])
                    P.new_stage(f"t3_z{wi}_{k}b")
                    act(P, wabs[:, 0:wid], wtmp[:, 0:wid], AF.Abs)
                    P.new_stage(f"t3_z{wi}_{k}c")
                    act(P, wsgn[:, 0:wid], wtmp[:, 0:wid], AF.Sign)
                    P.new_stage(f"t3_z{wi}_{k}d")
                    vec_tt(P, wabs[:, 0:wid], wabs[:, 0:wid],
                           sc_sb[:, wi:wi + 1].broadcast_to([128, wid]), ALU.is_gt)
                    P.new_stage(f"t3_z{wi}_{k}e")
                    vec_tt(P, wbf[:, 0:wid], wsgn[:, 0:wid], wabs[:, 0:wid], ALU.mult)
                    P.new_stage(f"t3_z{wi}_{k}f")
                    P.dma("sync", wd[k * 128:(k + 1) * 128, :], wbf[:, 0:wid])

        if T3_DEBUG:
            P.new_stage("t3_dbg")
            P.dma("sync", d_sc, cols[:, 2 * DT + FFT:2 * DT + FFT + 8])
            P.dma("sync", d_w1, w1t_d[0:128, :])
        for hh in range(2):
            t0 = hh * TH
            with contextlib.ExitStack() as sctx:
                Es = sctx.enter_context
                xt = [Es(nc.sbuf_tensor(f"t3x{hh}_{i}", [128, TH], F32)) for i in range(DT)]
                sq = [Es(nc.sbuf_tensor(f"t3s{hh}_{i}", [128, TH], F32)) for i in range(DT)]
                mu_b = Es(nc.sbuf_tensor(f"t3mu{hh}", [128, TH], F32))
                rstd_b = Es(nc.sbuf_tensor(f"t3rs{hh}", [128, TH], F32))
                mu_row = Es(nc.sbuf_tensor(f"t3mr{hh}", [1, TH], F32))
                sq_row = Es(nc.sbuf_tensor(f"t3sr{hh}", [1, TH], F32))
                onescol = Es(nc.sbuf_tensor(f"t3oc{hh}", [128, 1], F32))
                P.new_stage("t3_c2")
                P.on("vector", lambda v, onescol=onescol: v.memset(onescol[:], 1.0))
                P.new_stage("t3_xload")
                for i in range(DT):
                    P.dma("sync", xt[i][:], xin[i * 128:(i + 1) * 128, t0:t0 + TH])
                emit_ln(P, xt, xn_bf, sq, mu_b, rstd_b, (PS_R0, PS_R1), PS_A,
                        mu_row, sq_row, onescol[:], onesrow[:], TH)
                P.new_stage("t3_base")
                for i in range(DT):
                    vec_copy(P, out_sb[i][:], xt[i][:])
            with contextlib.ExitStack() as sctx:
                Es = sctx.enter_context
                w_sb = Es(nc.sbuf_tensor(f"t3w{hh}", [128, DT * 1024], BF16))
                attn_bf = [Es(nc.sbuf_tensor(f"t3a{hh}_{i}", [128, TH], BF16)) for i in range(DT)]
                tmp = Es(nc.sbuf_tensor(f"t3tmp{hh}", [128, 512], F32))
                P.new_stage("t3_wol")
                for k in range(DT):
                    P.dma("gpsimd", w_sb[:, k * 1024:(k + 1) * 1024], wo[k * 128:(k + 1) * 128, :])
                    P.dma("gpsimd", attn_bf[k][:], attn_in[k * 128:(k + 1) * 128, t0:t0 + TH])
                for m in range(DT):
                    for j in range(2):
                        P.new_stage(f"t3_aomm{m}{j}")
                        pp = PS_B[:, (j % 2) * 512:(j % 2) * 512 + 512]
                        emit_mm_group(P, pp, [w_sb[:, k * 1024 + m * 128:k * 1024 + (m + 1) * 128] for k in range(DT)],
                                      [attn_bf[k][:, j * 512:(j + 1) * 512] for k in range(DT)], 0, 512)
                        P.new_stage(f"t3_aoev{m}{j}")
                        sl = slice(j * 512, (j + 1) * 512)
                        vec_ts(P, tmp[:], pp[:, 0:512], bo_sb[:, m:m + 1], None, ALU.add)
                        P.new_stage(f"t3_aoac{m}{j}")
                        vec_tt(P, out_sb[m][:, sl], out_sb[m][:, sl], tmp[:], ALU.add)
            with contextlib.ExitStack() as sctx:
                Es = sctx.enter_context
                poolA = Es(nc.sbuf_tensor(f"t3pA{hh}", [128, DT * FF], BF16))
                poolB = Es(nc.sbuf_tensor(f"t3pB{hh}", [128, 2 * 8192], BF16))
                h_bf = Es(nc.sbuf_tensor(f"t3h{hh}", [128, FFT * 512], BF16))
                tmp = Es(nc.sbuf_tensor(f"t3tm2{hh}", [128, 512], F32))
                opsum = [PS_A[:, 0:512], PS_A[:, 512:1024], PS_B[:, 0:512], PS_B[:, 512:1024]]
                P.new_stage("t3_w1l")
                for k in range(DT):
                    P.dma("sync", poolA[:, k * FF:(k + 1) * FF], w1t_d[k * 128:(k + 1) * 128, :])
                for tci, tcs in enumerate(range(0, TH, 512)):
                    for ht in range(FFT):
                        P.new_stage(f"t3_h{hh}_{tci}_{ht}")
                        pp = PS_B[:, (ht % 2) * 512:(ht % 2) * 512 + 512]
                        emit_mm_group(P, pp, [poolA[:, k * FF + ht * 128:k * FF + ht * 128 + 128] for k in range(DT)],
                                      [xn_bf[k][:, tcs:tcs + 512] for k in range(DT)], 0, 512)
                        if ht == 0:
                            P.dma("gpsimd", poolB[:, 0:8192].rearrange("p (kk d) -> p kk d", d=1024),
                                  w2t_d[0:1024, :].rearrange("(kk p) d -> p kk d", p=128))
                        P.new_stage(f"t3_hb{hh}_{tci}_{ht}")
                        vec_ts(P, tmp[:], pp[:, 0:512], b1_sb[:, ht:ht + 1], None, ALU.add)
                        P.new_stage(f"t3_hg{hh}_{tci}_{ht}")
                        act(P, h_bf[:, ht * 512:(ht + 1) * 512], tmp[:], AF.Gelu)
                    for gi in range(4):
                        P.new_stage(f"t3_o{hh}_{tci}_{gi}")
                        for ki in range(8):
                            kk = gi * 8 + ki
                            for mm in range(4):
                                def fn(t, kk=kk, ki=ki, gi=gi, mm=mm, hh2=hh):
                                    return t.matmul(opsum[mm][:, 0:512],
                                                    poolB[:, (gi % 2) * 8192 + ki * 1024 + (mm + 0) * 128:
                                                          (gi % 2) * 8192 + ki * 1024 + (mm + 0) * 128 + 128],
                                                    h_bf[:, kk * 512:(kk + 1) * 512],
                                                    start=(kk == 0), stop=(kk == FFT - 1))
                                P.on("tensor", fn)
                        if gi + 1 < 4:
                            gl = gi + 1
                            P.dma("gpsimd", poolB[:, (gl % 2) * 8192:(gl % 2) * 8192 + 8192].rearrange("p (kk d) -> p kk d", d=1024),
                                  w2t_d[gl * 1024:(gl + 1) * 1024, :].rearrange("(kk p) d -> p kk d", p=128))
                    P.new_stage(f"t3_oe{hh}_{tci}")
                    for mm in range(4):
                        vec_ts(P, tmp[:], opsum[mm][:, 0:512], b2_sb[:, mm:mm + 1], None, ALU.add)
                        P.new_stage(f"t3_oe3{hh}_{tci}_{mm}")
                        vec_tt(P, out_sb[mm][:, tcs:tcs + 512], out_sb[mm][:, tcs:tcs + 512], tmp[:], ALU.add)
                        if mm < 3:
                            P.new_stage(f"t3_oe4{hh}_{tci}_{mm}")
                    P.new_stage(f"t3_o2pre{hh}_{tci}")
                    P.dma("gpsimd", poolB[:, 0:8192].rearrange("p (kk d) -> p kk d", d=1024),
                          w2t_d[0:1024, :].rearrange("(kk p) d -> p kk d", p=128))
                    for gi in range(4):
                        P.new_stage(f"t3_o2{hh}_{tci}_{gi}")
                        for ki in range(8):
                            kk = gi * 8 + ki
                            for mm in range(4):
                                def fn(t, kk=kk, ki=ki, gi=gi, mm=mm):
                                    return t.matmul(opsum[mm][:, 0:512],
                                                    poolB[:, (gi % 2) * 8192 + ki * 1024 + (mm + 4) * 128:
                                                          (gi % 2) * 8192 + ki * 1024 + (mm + 4) * 128 + 128],
                                                    h_bf[:, kk * 512:(kk + 1) * 512],
                                                    start=(kk == 0), stop=(kk == FFT - 1))
                                P.on("tensor", fn)
                        if gi + 1 < 4:
                            gl = gi + 1
                            P.dma("gpsimd", poolB[:, (gl % 2) * 8192:(gl % 2) * 8192 + 8192].rearrange("p (kk d) -> p kk d", d=1024),
                                  w2t_d[gl * 1024:(gl + 1) * 1024, :].rearrange("(kk p) d -> p kk d", p=128))
                    P.new_stage(f"t3_o2e{hh}_{tci}")
                    for mm in range(4):
                        vec_ts(P, tmp[:], opsum[mm][:, 0:512], b2_sb[:, 4 + mm:5 + mm], None, ALU.add)
                        P.new_stage(f"t3_o2e3{hh}_{tci}_{mm}")
                        vec_tt(P, out_sb[4 + mm][:, tcs:tcs + 512], out_sb[4 + mm][:, tcs:tcs + 512], tmp[:], ALU.add)
                        if mm < 3:
                            P.new_stage(f"t3_o2e4{hh}_{tci}_{mm}")
            P.new_stage("t3_store")
            for i in range(DT):
                P.dma("sync", xout[i * 128:(i + 1) * 128, t0:t0 + TH], out_sb[i][:])
        P.finalize()
    return nc


def t3_inputs(x_fm, attn_fm_bf, ins):
    cz = lambda a, n: np.ascontiguousarray(np.asarray(a).reshape(n, 128).T)
    w1 = np.asarray(ins["L2_ffn_W1"])
    w2 = np.asarray(ins["L2_ffn_W2"])
    hm = np.zeros((128, 2), np.float32)
    hm[:, 0] = 0.5 / w1.size
    hm[:, 1] = 0.5 / w2.size
    maps = []
    for c in range(NCORES):
        t0 = c * T_C
        maps.append({
            "xin": np.ascontiguousarray(x_fm[:, t0:t0 + T_C]),
            "attn_in": np.ascontiguousarray(attn_fm_bf[:, t0:t0 + T_C]),
            "wo": np.asarray(ins["L2_attn_Wo"]),
            "w1T": np.ascontiguousarray(w1.T),
            "w2T": np.ascontiguousarray(w2.T),
            "bo_c": cz(ins["L2_attn_bo"], DT),
            "b1_c": cz(ins["L2_ffn_b1"], FFT),
            "b2_c": cz(ins["L2_ffn_b2"], DT),
            "hm_col": hm,
            "ones_row": ONES_ROW,
        })
    return maps


def run_T3(x_fm, attn_fm, ins):
    import ml_dtypes
    nc = build_T3()
    attn_bf = attn_fm.astype(ml_dtypes.bfloat16) if attn_fm.dtype == np.float32 else attn_fm
    res = run_bass_kernel_spmd(nc, t3_inputs(x_fm, attn_bf, ins), core_ids=list(range(NCORES)))
    return np.concatenate([r["xout"] for r in res.results], axis=1)



def build_M3():
    nc = bass.Bass("TRN2", target_bir_lowering=False)
    ap = lambda n, s, d=F32, k="ExternalInput": nc.dram_tensor(n, s, d, kind=k).ap()
    xb_in = ap("xb_in", [D, S])
    wqT = ap("wqT", [D, 512])
    wdkvT = ap("wdkvT", [D, 512])
    wukT = ap("wukT", [512, 512])
    wuvT = ap("wuvT", [512, 512])
    n1w_c = ap("n1w_c", [128, DT])
    lng_c = ap("lng_c", [128, 4])
    lnb_c = ap("lnb_c", [128, 4])
    cosm = ap("cosm", [32, S])
    sinm = ap("sinm", [32, S])
    maskd = ap("maskd", [128, 128])
    ident = ap("ident", [128, 128])
    attn_o = ap("attn_o", [512, S], BF16, "ExternalOutput")
    TCn = 512

    with contextlib.ExitStack() as ctx:
        E = ctx.enter_context
        P = Prog(nc)
        q_bf = [E(nc.sbuf_tensor(f"m3q{m}", [128, S], BF16)) for m in range(4)]
        k_bf = [E(nc.sbuf_tensor(f"m3k{m}", [128, S], BF16)) for m in range(4)]
        v_tm = [E(nc.sbuf_tensor(f"m3v{t}", [128, NH_C * 65], BF16)) for t in range(S // 128)]
        mask_sb = E(nc.sbuf_tensor("m3mask", [128, 128], BF16))
        id_sb = E(nc.sbuf_tensor("m3id", [128, 128], F32))
        st_bf = E(nc.sbuf_tensor("m3st", [128, 512], BF16))
        ot_tm = E(nc.sbuf_tensor("m3ot", [128, 512], F32))
        rec_c = E(nc.sbuf_tensor("m3rc", [128, 4], F32))
        oh_fm = E(nc.sbuf_tensor("m3oh", [64, S], BF16))
        wq_sb = E(nc.sbuf_tensor("m3wq", [128, DT * 512], BF16))
        wdkv_sb = E(nc.sbuf_tensor("m3wd", [128, DT * 512], BF16))
        wuk_sb = E(nc.sbuf_tensor("m3wu", [128, 4 * 512], BF16))
        wuv_sb = E(nc.sbuf_tensor("m3wv", [128, 4 * 512], BF16))
        ccols = E(nc.sbuf_tensor("m3cc", [128, DT + 8], F32))
        n1w_sb = ccols[:, 0:DT]
        lng_sb = ccols[:, DT:DT + 4]
        lnb_sb = ccols[:, DT + 4:DT + 8]

        P.new_stage("m3_const")
        P.dma("gpsimd", mask_sb[:], maskd)
        P.dma("sync", id_sb[:], ident)
        P.dma("sync", ccols[:, 0:DT], n1w_c)
        P.dma("sync", ccols[:, DT:DT + 4], lng_c)
        P.dma("sync", ccols[:, DT + 4:DT + 8], lnb_c)
        for k in range(DT):
            P.dma("gpsimd", wq_sb[:, k * 512:(k + 1) * 512], wqT[k * 128:(k + 1) * 128, :])
            P.dma("gpsimd", wdkv_sb[:, k * 512:(k + 1) * 512], wdkvT[k * 128:(k + 1) * 128, :])
        for k in range(4):
            P.dma("gpsimd", wuk_sb[:, k * 512:(k + 1) * 512], wukT[k * 128:(k + 1) * 128, :])
            P.dma("gpsimd", wuv_sb[:, k * 512:(k + 1) * 512], wuvT[k * 128:(k + 1) * 128, :])
        P.new_stage("m3_vz")
        for t in range(S // 128):
            P.on("vector", lambda v, t=t: v.memset(v_tm[t][:], 0.0))
        P.new_stage("m3_vone")
        for t in range(S // 128):
            for h in range(NH_C):
                vec_memset(P, v_tm[t][:, h * 65 + 64:h * 65 + 65], 1.0)
        vdrain(P)

        with contextlib.ExitStack() as sctx:
            Es = sctx.enter_context
            xt = [Es(nc.sbuf_tensor(f"m3x{i}", [128, TCn], F32)) for i in range(DT)]
            sq = [Es(nc.sbuf_tensor(f"m3s{i}", [128, TCn], F32)) for i in range(DT)]
            xn2 = [Es(nc.sbuf_tensor(f"m3n{i}", [128, TCn], BF16)) for i in range(DT)]
            lat_f = [Es(nc.sbuf_tensor(f"m3l{i}", [128, TCn], F32)) for i in range(4)]
            lat_bf = [Es(nc.sbuf_tensor(f"m3lb{i}", [128, TCn], BF16)) for i in range(4)]
            mu_b = Es(nc.sbuf_tensor("m3mu", [128, TCn], F32))
            rstd_b = Es(nc.sbuf_tensor("m3rs", [128, TCn], F32))
            mu_row = Es(nc.sbuf_tensor("m3mr", [1, TCn], F32))
            sq_row = Es(nc.sbuf_tensor("m3sr", [1, TCn], F32))
            qf4 = Es(nc.sbuf_tensor("m3qf", [128, TCn], F32))
            rt_a = Es(nc.sbuf_tensor("m3ra", [32, TCn], F32))
            rt_b = Es(nc.sbuf_tensor("m3rb", [32, TCn], F32))
            rt_c = Es(nc.sbuf_tensor("m3rcs", [32, TCn], F32))
            rt_d = Es(nc.sbuf_tensor("m3rd", [32, TCn], F32))
            cos_sb = Es(nc.sbuf_tensor("m3cos", [32, TCn], F32))
            sin_sb = Es(nc.sbuf_tensor("m3sin", [32, TCn], F32))
            onescol = Es(nc.sbuf_tensor("m3oc", [128, 1], F32))
            onesrow = Es(nc.sbuf_tensor("m3or", [1, 128], F32))
            PS_A = nc_psum(nc, Es, "m3A", [128, 1024])
            PS_B = nc_psum(nc, Es, "m3B", [128, 1024])
            PS_R0 = nc_psum(nc, Es, "m3R0", [1, 512])
            PS_R1 = nc_psum(nc, Es, "m3R1", [1, 512])
            P.new_stage("m3_c1")
            P.on("vector", lambda v: v.memset(onescol[:], 1.0))
            P.new_stage("m3_c2")
            vec_memset(P, onesrow[:], 1.0)
            vdrain(P)
            for ci, t0 in enumerate(range(0, S, TCn)):
                P.new_stage("m3_load")
                for i in range(DT):
                    P.dma("sync", xt[i][:], xb_in[i * 128:(i + 1) * 128, t0:t0 + TCn])
                P.dma("sync", cos_sb[:], cosm[:, t0:t0 + TCn])
                P.dma("sync", sin_sb[:], sinm[:, t0:t0 + TCn])
                emit_ln(P, xt, sq, sq, mu_b, rstd_b, (PS_R0, PS_R1), PS_A,
                        mu_row, sq_row, onescol[:], onesrow[:], TCn)
                emit_ln(P, sq, xn2, xt, mu_b, rstd_b, (PS_R0, PS_R1), PS_A,
                        mu_row, sq_row, onescol[:], onesrow[:], TCn, eps=1e-6,
                        gam=[n1w_sb[:, i:i + 1] for i in range(DT)], bet=None, rms=True)
                for m in range(4):
                    P.new_stage(f"m3_qmm{m}")
                    pp = PS_B[:, (m % 2) * 512:(m % 2) * 512 + 512]
                    emit_mm_group(P, pp, [wq_sb[:, k * 512 + m * 128:k * 512 + (m + 1) * 128] for k in range(DT)],
                                  [xn2[k][:] for k in range(DT)], 0, TCn)
                    P.new_stage(f"m3_qev{m}")
                    vec_copy(P, qf4[:], pp[:, 0:TCn])
                    for hb in (0, 64):
                        P.new_stage(f"m3_qr1{m}{hb}")
                        vec_copy(P, rt_a[:], qf4[hb:hb + 32, :])
                        vec_copy(P, rt_b[:], qf4[hb + 32:hb + 64, :])
                        P.new_stage(f"m3_qr2{m}{hb}")
                        vec_tt(P, rt_c[:], rt_a[:], cos_sb[:], ALU.mult)
                        vec_tt(P, rt_d[:], rt_b[:], sin_sb[:], ALU.mult)
                        P.new_stage(f"m3_qr3{m}{hb}")
                        vec_tt(P, rt_c[:], rt_c[:], rt_d[:], ALU.subtract)
                        vec_tt(P, rt_d[:], rt_a[:], sin_sb[:], ALU.mult)
                        P.new_stage(f"m3_qr4{m}{hb}")
                        vec_tt(P, rt_b[:], rt_b[:], cos_sb[:], ALU.mult)
                        P.new_stage(f"m3_qr5{m}{hb}")
                        vec_tt(P, rt_d[:], rt_d[:], rt_b[:], ALU.add)
                        P.new_stage(f"m3_qr6{m}{hb}")
                        vec_copy(P, qf4[hb:hb + 32, :], rt_c[:])
                        vec_copy(P, qf4[hb + 32:hb + 64, :], rt_d[:])
                    P.new_stage(f"m3_qcp{m}")
                    vec_copy(P, q_bf[m][:, t0:t0 + TCn], qf4[:])
                for m in range(4):
                    P.new_stage(f"m3_lmm{m}")
                    pp = PS_B[:, (m % 2) * 512:(m % 2) * 512 + 512]
                    emit_mm_group(P, pp, [wdkv_sb[:, k * 512 + m * 128:k * 512 + (m + 1) * 128] for k in range(DT)],
                                  [xn2[k][:] for k in range(DT)], 0, TCn)
                    P.new_stage(f"m3_lev{m}")
                    vec_copy(P, lat_f[m][:], pp[:, 0:TCn])
                emit_ln(P, lat_f, lat_bf, xt, mu_b, rstd_b, (PS_R0, PS_R1), PS_A,
                        mu_row, sq_row, onescol[:], onesrow[:], TCn, nt=4, dim=512,
                        gam=[lng_sb[:, i:i + 1] for i in range(4)],
                        bet=[lnb_sb[:, i:i + 1] for i in range(4)])
                for m in range(4):
                    P.new_stage(f"m3_kmm{m}")
                    pp = PS_B[:, (m % 2) * 512:(m % 2) * 512 + 512]
                    emit_mm_group(P, pp, [wuk_sb[:, k * 512 + m * 128:k * 512 + (m + 1) * 128] for k in range(4)],
                                  [lat_bf[k][:] for k in range(4)], 0, TCn)
                    P.new_stage(f"m3_kev{m}")
                    vec_copy(P, qf4[:], pp[:, 0:TCn])
                    for hb in (0, 64):
                        P.new_stage(f"m3_kr1{m}{hb}")
                        vec_copy(P, rt_a[:], qf4[hb:hb + 32, :])
                        vec_copy(P, rt_b[:], qf4[hb + 32:hb + 64, :])
                        P.new_stage(f"m3_kr2{m}{hb}")
                        vec_tt(P, rt_c[:], rt_a[:], cos_sb[:], ALU.mult)
                        vec_tt(P, rt_d[:], rt_b[:], sin_sb[:], ALU.mult)
                        P.new_stage(f"m3_kr3{m}{hb}")
                        vec_tt(P, rt_c[:], rt_c[:], rt_d[:], ALU.subtract)
                        vec_tt(P, rt_d[:], rt_a[:], sin_sb[:], ALU.mult)
                        P.new_stage(f"m3_kr4{m}{hb}")
                        vec_tt(P, rt_b[:], rt_b[:], cos_sb[:], ALU.mult)
                        P.new_stage(f"m3_kr5{m}{hb}")
                        vec_tt(P, rt_d[:], rt_d[:], rt_b[:], ALU.add)
                        P.new_stage(f"m3_kr6{m}{hb}")
                        vec_copy(P, qf4[hb:hb + 32, :], rt_c[:])
                        vec_copy(P, qf4[hb + 32:hb + 64, :], rt_d[:])
                    P.new_stage(f"m3_kcp{m}")
                    vec_copy(P, k_bf[m][:, t0:t0 + TCn], qf4[:])
                pvof = lambda h: h * 65 if h < 4 else 512 + (h - 4) * 65
                for tb in range(TCn // 128):
                    tg = (t0 + tb * 128) // 128
                    P.new_stage(f"m3_vt{tb}")
                    for h in range(NH_C):
                        for k in range(4):
                            def fn(t, tb=tb, h=h, k=k, pvof=pvof):
                                return t.matmul(PS_B[:, pvof(h):pvof(h) + 64],
                                                lat_bf[k][:, tb * 128:(tb + 1) * 128],
                                                wuv_sb[:, k * 512 + h * 64:k * 512 + (h + 1) * 64],
                                                start=(k == 0), stop=(k == 3))
                            P.on("tensor", fn)
                    P.new_stage(f"m3_vtev{tb}")
                    for h in range(NH_C):
                        act(P, v_tm[tg][:, h * 65:h * 65 + 64], PS_B[:, pvof(h):pvof(h) + 64], AF.Copy)

        with contextlib.ExitStack() as sctx:
            Es = sctx.enter_context
            PS_S = nc_psum(nc, Es, "m3pss", [128, 1024])
            PS_P = nc_psum(nc, Es, "m3psp", [128, 512])
            PS_T = nc_psum(nc, Es, "m3pst", [128, 512])
            for h in range(NH_C):
                qt = q_bf[h // 2]
                kt_ = k_bf[h // 2]
                hb = (h % 2) * 64
                for band in range(S // 512):
                    nkt = 4 * band + 4
                    for kti in range(nkt):
                        pp = PS_S[:, (kti % 2) * 512:(kti % 2) * 512 + 512]
                        P.new_stage(f"m3at_s{h}_{band}_{kti}")
                        def fn(t, pp=pp, kti=kti, band=band, qt=qt, kt_=kt_, hb=hb):
                            return t.matmul(pp, kt_[hb:hb + 64, kti * 128:(kti + 1) * 128],
                                            qt[hb:hb + 64, band * 512:(band + 1) * 512],
                                            start=True, stop=True)
                        P.on("tensor", fn)
                        P.new_stage(f"m3at_e{h}_{band}_{kti}")
                        act(P, st_bf[:], pp, AF.Exp, scale=0.125)
                        dq = kti - 4 * band
                        if dq >= 0:
                            P.new_stage(f"m3at_m{h}_{band}_{kti}")
                            vec_tt(P, st_bf[:, dq * 128:(dq + 1) * 128],
                                   st_bf[:, dq * 128:(dq + 1) * 128], mask_sb[:], ALU.mult)
                        P.new_stage(f"m3at_p{h}_{band}_{kti}")
                        for qb in range(4):
                            if 4 * band + qb < kti:
                                continue
                            def fn(t, qb=qb, kti=kti, band=band, h=h):
                                return t.matmul(PS_P[:, qb * 128:qb * 128 + 65],
                                                st_bf[:, qb * 128:(qb + 1) * 128],
                                                v_tm[kti][:, h * 65:h * 65 + 65],
                                                start=(kti == 0 and qb == 0), stop=(kti == 4 * band + qb))
                            P.on("tensor", fn)
                    P.new_stage(f"m3at_r{h}_{band}")
                    for qb in range(4):
                        P.on("vector", lambda v, qb=qb: v.reciprocal(
                            rec_c[:, qb:qb + 1], PS_P[:, qb * 128 + 64:qb * 128 + 65]))
                    vdrain(P)
                    P.new_stage(f"m3at_n{h}_{band}")
                    for qb in range(4):
                        vec_tt(P, ot_tm[:, qb * 128:qb * 128 + 64], PS_P[:, qb * 128:qb * 128 + 64],
                               rec_c[:, qb:qb + 1].broadcast_to([128, 64]), ALU.mult)
                    vdrain(P)
                    P.new_stage(f"m3at_t{h}_{band}")
                    for qb in range(4):
                        def fn(t, qb=qb):
                            return t.transpose(PS_T[0:64, qb * 128:(qb + 1) * 128],
                                               ot_tm[:, qb * 128:qb * 128 + 64], id_sb[:])
                        P.on("tensor", fn)
                    P.new_stage(f"m3at_o{h}_{band}")
                    vec_copy(P, oh_fm[:, band * 512:(band + 1) * 512], PS_T[0:64, 0:512])
                P.new_stage(f"m3at_st{h}")
                P.dma("sync", attn_o[h * 64:(h + 1) * 64, :], oh_fm[:])
        P.finalize()
    return nc


def m3_inputs(x_fm, ins):
    WqT = np.ascontiguousarray(np.asarray(ins["L3_ffn_Wq"]).T)
    WdkvT = np.ascontiguousarray(np.asarray(ins["L3_ffn_Wdkv"]).T)
    WukT = np.ascontiguousarray(np.asarray(ins["L3_ffn_Wuk"]).T)
    WuvT = np.ascontiguousarray(np.asarray(ins["L3_ffn_Wuv"]).T)
    pos = np.arange(S, dtype=np.float64)[:, None]
    inv = 1.0 / 10000.0 ** (np.arange(32, dtype=np.float64) / 32.0)
    ang = pos * inv[None, :]
    cosm = np.ascontiguousarray(np.cos(ang).T.astype(np.float32))
    sinm = np.ascontiguousarray(np.sin(ang).T.astype(np.float32))
    maskd = np.triu(np.ones((128, 128), np.float32))
    cz = lambda a, n: np.ascontiguousarray(np.asarray(a).reshape(n, 128).T)
    maps = []
    for c in range(NCORES):
        b, hh = c // 2, c % 2
        hsl = slice(hh * 512, (hh + 1) * 512)
        maps.append({
            "xb_in": np.ascontiguousarray(x_fm[:, b * S:(b + 1) * S]),
            "wqT": np.ascontiguousarray(WqT[:, hsl]),
            "wdkvT": WdkvT,
            "wukT": np.ascontiguousarray(WukT[:, hsl]),
            "wuvT": np.ascontiguousarray(WuvT[:, hsl]),
            "n1w_c": cz(ins["L3_ffn_norm1_w"], DT),
            "lng_c": cz(ins["L3_ffn_ln_g"], 4),
            "lnb_c": cz(ins["L3_ffn_ln_b"], 4),
            "cosm": cosm, "sinm": sinm, "maskd": maskd, "ident": IDENT,
        })
    return maps


def run_M3(x_fm, ins):
    nc = build_M3()
    res = run_bass_kernel_spmd(nc, m3_inputs(x_fm, ins), core_ids=list(range(NCORES)))
    out = np.zeros((D, T_ALL), np.float32)
    for c in range(NCORES):
        b, hh = c // 2, c % 2
        out[hh * 512:(hh + 1) * 512, b * S:(b + 1) * S] = np.asarray(res.results[c]["attn_o"]).astype(np.float32)
    return out



NBANK = 8192
SQRT_NB = float(np.float32(np.sqrt(np.float32(8192.0))))


def build_T4():
    nc = bass.Bass("TRN2", target_bir_lowering=False)
    ap = lambda n, s, d=F32, k="ExternalInput": nc.dram_tensor(n, s, d, kind=k).ap()
    xin = ap("xin", [D, T_C])
    mla_in = ap("mla_in", [D, T_C], BF16)
    kT = ap("kT", [D, NBANK])
    vbank = ap("vbank", [NBANK, D])
    woT = ap("woT", [D, D])
    n2w_c = ap("n2w_c", [128, DT])
    gwT = ap("gwT", [D, 4])
    gb_t = ap("gb_t", [128, 4])
    w1T = ap("w1T", [4, D, FF])
    w2T = ap("w2T", [4, FF, D])
    b1_c = ap("b1_c", [128, 4 * FFT])
    b2_in = ap("b2", [4, D])
    sel4 = ap("sel4", [4, 4 * 128])
    ident = ap("ident", [128, 128])
    onr = ap("ones_row", [1, 128])
    xout = ap("xout", [D, T_C], F32, "ExternalOutput")
    TH = 1024

    with contextlib.ExitStack() as ctx:
        E = ctx.enter_context
        P = Prog(nc)
        out_sb = [E(nc.sbuf_tensor(f"t4o{i}", [128, TH], F32)) for i in range(DT)]
        xn_bf = [E(nc.sbuf_tensor(f"t4n{i}", [128, TH], BF16)) for i in range(DT)]
        rms2_bf = [E(nc.sbuf_tensor(f"t4r{i}", [128, TH], BF16)) for i in range(DT)]
        wrow = E(nc.sbuf_tensor("t4wr", [4, TH], F32))
        cols = E(nc.sbuf_tensor("t4cols", [128, DT + 4 * FFT + 4], F32))
        n2w_sb = cols[:, 0:DT]
        b1_sb = cols[:, DT:DT + 4 * FFT]
        gb_sb = cols[:, DT + 4 * FFT:DT + 4 * FFT + 4]
        b2_sb = E(nc.sbuf_tensor("t4b2", [4, D], F32))
        sel_sb = E(nc.sbuf_tensor("t4sel", [4, 4 * 128], F32))
        id_sb = E(nc.sbuf_tensor("t4id", [128, 128], F32))
        idb_g = E(nc.sbuf_tensor("t4idb", [128, 128], BF16))
        onesrow = E(nc.sbuf_tensor("t4or", [1, 128], F32))
        gw_sb = E(nc.sbuf_tensor("t4gw", [128, DT * 4], BF16))
        lgt = E(nc.sbuf_tensor("t4lg", [128, 4], F32))
        lg2 = E(nc.sbuf_tensor("t4lg2", [128, 4], F32))
        PS_A = E(nc.psum_tensor("t4psA", [128, 1024], F32))
        PS_B = E(nc.psum_tensor("t4psB", [128, 1024], F32))
        PS_O = E(nc.psum_tensor("t4psO", [128, 1024], F32))
        PS_W = E(nc.psum_tensor("t4psW", [128, 512], F32))
        PS_GT = E(nc.psum_tensor("t4psGT", [128, 1024], BF16))
        PS_R0 = PS_A[0:1, 0:512]
        PS_R1 = PS_A[0:1, 512:1024]

        P.new_stage("t4_c")
        P.dma("sync", cols[:, 0:DT], n2w_c)
        P.dma("sync", cols[:, DT:DT + 4 * FFT], b1_c)
        P.dma("sync", cols[:, DT + 4 * FFT:DT + 4 * FFT + 4], gb_t)
        P.dma("sync", b2_sb[:], b2_in)
        P.dma("sync", sel_sb[:], sel4)
        P.dma("sync", id_sb[:], ident)
        P.dma("gpsimd", idb_g[:], ident)
        P.dma("sync", onesrow[:], onr)
        for k in range(DT):
            P.dma("gpsimd", gw_sb[:, k * 4:(k + 1) * 4], gwT[k * 128:(k + 1) * 128, :])

        for hh in range(2):
          t0 = hh * TH
          with contextlib.ExitStack() as hctx:
            Eh = hctx.enter_context
            yctx = contextlib.ExitStack()
            xn_f = [yctx.enter_context(nc.sbuf_tensor(f"t4y{hh}_{i}", [128, TH], F32)) for i in range(DT)]
            with contextlib.ExitStack() as sctx:
                Es = sctx.enter_context
                xt = [Es(nc.sbuf_tensor(f"t4x{hh}_{i}", [128, TH], F32)) for i in range(DT)]
                sq0 = [Es(nc.sbuf_tensor(f"t4sq0{hh}_{i}", [128, TH], F32)) for i in range(DT)]
                mu_b = Es(nc.sbuf_tensor(f"t4mu{hh}", [128, TH], F32))
                rstd_b = Es(nc.sbuf_tensor(f"t4rs{hh}", [128, TH], F32))
                mu_row = Es(nc.sbuf_tensor(f"t4mr{hh}", [1, TH], F32))
                sq_row = Es(nc.sbuf_tensor(f"t4sr{hh}", [1, TH], F32))
                onescol = Es(nc.sbuf_tensor(f"t4oc{hh}", [128, 1], F32))
                P.new_stage("t4_c2")
                P.on("vector", lambda v, onescol=onescol: v.memset(onescol[:], 1.0))
                P.new_stage("t4_xload")
                for i in range(DT):
                    P.dma("sync", xt[i][:], xin[i * 128:(i + 1) * 128, t0:t0 + TH])
                emit_ln(P, xt, xn_bf, sq0, mu_b, rstd_b, (PS_R0, PS_R1), PS_O[:, 0:1024],
                        mu_row, sq_row, onescol[:], onesrow[:], TH, out_f32=xn_f)
                P.new_stage("t4_base")
                for i in range(DT):
                    vec_tt(P, out_sb[i][:], xt[i][:], xn_f[i][:], ALU.add)
            with contextlib.ExitStack() as sctx:
                Es = sctx.enter_context
                srow = Es(nc.sbuf_tensor(f"t4sw{hh}", [128, NBANK], F32))
                gbuf = Es(nc.sbuf_tensor(f"t4gb{hh}", [128, NBANK], BF16))
                kT_sb = Es(nc.sbuf_tensor(f"t4kt{hh}", [128, 2 * DT * 512], BF16))
                v_sb = Es(nc.sbuf_tensor(f"t4vs{hh}", [128, 2 * 8192], BF16))
                gT = Es(nc.sbuf_tensor(f"t4gT{hh}", [128, 1024], BF16))
                accq = Es(nc.sbuf_tensor(f"t4aq{hh}", [128, 32], F32))
                for tt in range(TH // 128):
                    tsl = slice(tt * 128, (tt + 1) * 128)
                    P.new_stage(f"t4a_l0{hh}{tt}")
                    P.dma("gpsimd", kT_sb[:, 0:DT * 512].rearrange("p (kk d) -> p kk d", d=512),
                          kT[:, 0:512].rearrange("(kk p) d -> p kk d", p=128))
                    for nchi in range(16):
                        P.new_stage(f"t4a_s{hh}{tt}{nchi}")
                        buf = (nchi % 2) * DT * 512
                        if nchi + 1 < 16:
                            b2_ = ((nchi + 1) % 2) * DT * 512
                            P.dma("gpsimd", kT_sb[:, b2_:b2_ + DT * 512].rearrange("p (kk d) -> p kk d", d=512),
                                  kT[:, (nchi + 1) * 512:(nchi + 2) * 512].rearrange("(kk p) d -> p kk d", p=128))
                        pp = PS_A[:, (nchi % 2) * 512:(nchi % 2) * 512 + 512]
                        emit_mm_group(P, pp, [xn_bf[k][:, tt * 128:(tt + 1) * 128] for k in range(DT)],
                                      [kT_sb[:, buf + k * 512:buf + (k + 1) * 512] for k in range(DT)], 0, 512)
                        P.new_stage(f"t4a_e{hh}{tt}{nchi}")
                        act(P, srow[:, nchi * 512:(nchi + 1) * 512], pp[:, 0:512], AF.Square,
                            accum_out=accq[:, nchi:nchi + 1])
                        sdrain(P)
                        P.new_stage(f"t4a_c{hh}{tt}{nchi}")
                        vec_copy(P, srow[:, nchi * 512:(nchi + 1) * 512], pp[:, 0:512])
                    P.new_stage(f"t4a_n{hh}{tt}")
                    step = 1
                    while step < 16:
                        for j in range(0, 16, 2 * step):
                            vec_tt(P, accq[:, j:j + 1], accq[:, j:j + 1], accq[:, j + step:j + step + 1], ALU.add)
                        vdrain(P)
                        P.new_stage(f"t4a_n{hh}{tt}_{step}")
                        step *= 2
                    act(P, accq[:, 16:17], accq[:, 0:1], AF.Ln)
                    sdrain(P)
                    P.new_stage(f"t4a_n2{hh}{tt}")
                    act(P, accq[:, 17:18], accq[:, 16:17], AF.Exp, scale=-0.5)
                    sdrain(P)
                    P.new_stage(f"t4a_n3{hh}{tt}")
                    vec_ts(P, accq[:, 18:19], accq[:, 17:18], SQRT_NB, None, ALU.mult)
                    vdrain(P)
                    P.new_stage(f"t4a_g{hh}{tt}")
                    vec_tt(P, srow[:], srow[:], accq[:, 18:19].broadcast_to([128, NBANK]), ALU.mult)
                    P.new_stage(f"t4a_g2{hh}{tt}")
                    act(P, gbuf[:], srow[:], AF.Gelu)
                    P.new_stage(f"t4a_vl0{hh}{tt}")
                    P.dma("gpsimd", v_sb[:, 0:8192].rearrange("p (kk d) -> p kk d", d=1024),
                          vbank[0:1024, :].rearrange("(kk p) d -> p kk d", p=128))
                    for vg in range(8):
                        vb = (vg % 2) * 8192
                        P.new_stage(f"t4a_tr{hh}{tt}{vg}")
                        for ki in range(8):
                            n64 = vg * 8 + ki
                            def fn(t, n64=n64, ki=ki):
                                return t.transpose(PS_GT[:, ki * 128:(ki + 1) * 128],
                                                   gbuf[:, n64 * 128:(n64 + 1) * 128], idb_g[:])
                            P.on("tensor", fn)
                        P.new_stage(f"t4a_te{hh}{tt}{vg}")
                        vec_copy(P, gT[:], PS_GT[:])
                        vdrain(P)
                        P.new_stage(f"t4a_vm{hh}{tt}{vg}")
                        if vg + 1 < 8:
                            vb2 = ((vg + 1) % 2) * 8192
                            P.dma("gpsimd", v_sb[:, vb2:vb2 + 8192].rearrange("p (kk d) -> p kk d", d=1024),
                                  vbank[(vg + 1) * 1024:(vg + 2) * 1024, :].rearrange("(kk p) d -> p kk d", p=128))
                        for ki in range(8):
                            n64 = vg * 8 + ki
                            for mm in range(DT):
                                def fn(t, vb=vb, ki=ki, mm=mm, n64=n64):
                                    return t.matmul(PS_O[:, mm * 128:(mm + 1) * 128],
                                                    v_sb[:, vb + ki * 1024 + mm * 128:vb + ki * 1024 + (mm + 1) * 128],
                                                    gT[:, ki * 128:(ki + 1) * 128], start=(n64 == 0), stop=(n64 == 63))
                                P.on("tensor", fn)
                    P.new_stage(f"t4a_oe{hh}{tt}")
                    for mm in range(DT):
                        vec_tt(P, out_sb[mm][:, tsl], out_sb[mm][:, tsl], PS_O[:, mm * 128:(mm + 1) * 128], ALU.add)
            with contextlib.ExitStack() as sctx:
                Es = sctx.enter_context
                w_sb = Es(nc.sbuf_tensor(f"t4w{hh}", [128, DT * 1024], BF16))
                mla_bf = [Es(nc.sbuf_tensor(f"t4m{hh}_{i}", [128, TH], BF16)) for i in range(DT)]
                P.new_stage("t4_mwl")
                for k in range(DT):
                    P.dma("gpsimd", w_sb[:, k * 1024:(k + 1) * 1024], woT[k * 128:(k + 1) * 128, :])
                    P.dma("gpsimd", mla_bf[k][:], mla_in[k * 128:(k + 1) * 128, t0:t0 + TH])
                for m in range(DT):
                    for j in range(2):
                        P.new_stage(f"t4_mm{m}{j}")
                        pp = PS_B[:, (j % 2) * 512:(j % 2) * 512 + 512]
                        emit_mm_group(P, pp, [w_sb[:, k * 1024 + m * 128:k * 1024 + (m + 1) * 128] for k in range(DT)],
                                      [mla_bf[k][:, j * 512:(j + 1) * 512] for k in range(DT)], 0, 512)
                        P.new_stage(f"t4_mev{m}{j}")
                        sl = slice(j * 512, (j + 1) * 512)
                        vec_tt(P, xn_f[m][:, sl], xn_f[m][:, sl], pp[:, 0:512], ALU.add)
                        P.new_stage(f"t4_mac{m}{j}")
                        vec_tt(P, out_sb[m][:, sl], out_sb[m][:, sl], pp[:, 0:512], ALU.add)
            with contextlib.ExitStack() as sctx:
                Es = sctx.enter_context
                sq2 = [Es(nc.sbuf_tensor(f"t4q{hh}_{i}", [128, TH], F32)) for i in range(DT)]
                mu_b = Es(nc.sbuf_tensor(f"t4mu2{hh}", [128, TH], F32))
                rstd_b = Es(nc.sbuf_tensor(f"t4rs2{hh}", [128, TH], F32))
                mu_row = Es(nc.sbuf_tensor(f"t4mr2{hh}", [1, TH], F32))
                sq_row = Es(nc.sbuf_tensor(f"t4sr2{hh}", [1, TH], F32))
                onescol = Es(nc.sbuf_tensor(f"t4oc2{hh}", [128, 1], F32))
                P.new_stage("t4_c3")
                P.on("vector", lambda v, onescol=onescol: v.memset(onescol[:], 1.0))
                emit_ln(P, xn_f, rms2_bf, sq2, mu_b, rstd_b, (PS_R0, PS_R1), PS_O[:, 0:1024],
                        mu_row, sq_row, onescol[:], onesrow[:], TH, eps=1e-6,
                        gam=[n2w_sb[:, i:i + 1] for i in range(DT)], bet=None, rms=True)
            yctx.close()
            for tt in range(TH // 128):
                P.new_stage(f"t4g_mm{tt}")
                emit_mm_group(P, PS_W[:, 0:4], [rms2_bf[k][:, tt * 128:(tt + 1) * 128] for k in range(DT)],
                              [gw_sb[:, k * 4:(k + 1) * 4] for k in range(DT)], 0, 4, pn0=0)
                P.new_stage(f"t4g_ev{tt}")
                vec_copy(P, lgt[:], PS_W[:, 0:4])
                vdrain(P)
                P.new_stage(f"t4g_b{tt}")
                vec_tt(P, lgt[:], lgt[:], gb_sb[:, 0:4] if False else gb_sb, ALU.add)
                vdrain(P)
                P.new_stage(f"t4g_m1{tt}")
                vec_tt(P, lg2[:, 0:2], lgt[:, 0:2], lgt[:, 2:4], ALU.max)
                vdrain(P)
                P.new_stage(f"t4g_m2{tt}")
                vec_tt(P, lg2[:, 2:3], lg2[:, 0:1], lg2[:, 1:2], ALU.max)
                vdrain(P)
                P.new_stage(f"t4g_oh{tt}")
                vec_tt(P, lgt[:], lgt[:], lg2[:, 2:3].broadcast_to([128, 4]), ALU.is_ge)
                vdrain(P)
                P.new_stage(f"t4g_tr{tt}")
                P.on("tensor", lambda t, tt=tt: t.transpose(PS_W[0:4, 0:128], lgt[:], id_sb[:]))
                P.new_stage(f"t4g_tev{tt}")
                vec_copy(P, wrow[:, tt * 128:(tt + 1) * 128], PS_W[0:4, 0:128])
                vdrain(P)
            with contextlib.ExitStack() as sctx:
                Es = sctx.enter_context
                poolA = Es(nc.sbuf_tensor(f"t4pA{hh}", [128, DT * FF], BF16))
                poolB = Es(nc.sbuf_tensor(f"t4pB{hh}", [128, 2 * 8192], BF16))
                h_bf = Es(nc.sbuf_tensor(f"t4h{hh}", [128, FFT * 512], BF16))
                wbce = Es(nc.sbuf_tensor(f"t4wb{hh}", [128, 512], F32))
                opsum = [PS_O[:, 0:512], PS_O[:, 512:1024], PS_A[:, 0:512], PS_A[:, 512:1024],
                         PS_B[:, 0:512], PS_B[:, 512:1024], PS_W[:, 0:512], None]
                for e in range(4):
                    P.new_stage(f"t4w1l{e}")
                    for k in range(DT):
                        P.dma("gpsimd", poolA[:, k * FF:(k + 1) * FF], w1T[e, k * 128:(k + 1) * 128, :])
                    for tci, tcs in enumerate(range(0, TH, 512)):
                        for ht in range(FFT):
                            P.new_stage(f"t4h{e}_{tci}_{ht}")
                            pp = PS_A[:, (ht % 2) * 512:(ht % 2) * 512 + 512]
                            emit_mm_group(P, pp, [poolA[:, k * FF + ht * 128:k * FF + ht * 128 + 128] for k in range(DT)],
                                          [rms2_bf[k][:, tcs:tcs + 512] for k in range(DT)], 0, 512)
                            if ht == 0:
                                P.dma("gpsimd", poolB[:, 0:8192].rearrange("p (kk d) -> p kk d", d=1024),
                                      w2T[e, 0:1024, :].rearrange("(kk p) d -> p kk d", p=128))
                            P.new_stage(f"t4hg{e}_{tci}_{ht}")
                            act(P, h_bf[:, ht * 512:(ht + 1) * 512], pp[:, 0:512], AF.Relu,
                                bias=b1_sb[:, e * FFT + ht:e * FFT + ht + 1])
                        P.new_stage(f"t4w{e}_{tci}")
                        emit_mm_group(P, PS_W, [sel_sb[:, e * 128:(e + 1) * 128]],
                                      [wrow[:, tcs:tcs + 512]], 0, 512, pn0=0)
                        P.new_stage(f"t4wev{e}_{tci}")
                        vec_copy(P, wbce[:], PS_W[:, 0:512])
                        for half in range(2):
                            if half == 1:
                                P.new_stage(f"t4o2p{e}_{tci}")
                                P.dma("gpsimd", poolB[:, 0:8192].rearrange("p (kk d) -> p kk d", d=1024),
                                      w2T[e, 0:1024, :].rearrange("(kk p) d -> p kk d", p=128))
                            for gi in range(4):
                                P.new_stage(f"t4o{e}_{tci}_{half}_{gi}")
                                for ki in range(8):
                                    kk = gi * 8 + ki
                                    for mm in range(4):
                                        def fn(t, kk=kk, ki=ki, gi=gi, mm=mm, half=half):
                                            return t.matmul(opsum[mm][:, 0:512],
                                                            poolB[:, (gi % 2) * 8192 + ki * 1024 + (mm + 4 * half) * 128:
                                                                  (gi % 2) * 8192 + ki * 1024 + (mm + 4 * half) * 128 + 128],
                                                            h_bf[:, kk * 512:(kk + 1) * 512],
                                                            start=(kk == 0), stop=(kk == FFT - 1))
                                        P.on("tensor", fn)
                                if gi + 1 < 4:
                                    gl = gi + 1
                                    P.dma("gpsimd", poolB[:, (gl % 2) * 8192:(gl % 2) * 8192 + 8192].rearrange("p (kk d) -> p kk d", d=1024),
                                          w2T[e, gl * 1024:(gl + 1) * 1024, :].rearrange("(kk p) d -> p kk d", p=128))
                            P.new_stage(f"t4oa{e}_{tci}_{half}")
                            for mm in range(4):
                                vec_tt(P, opsum[mm][:, 0:512], opsum[mm][:, 0:512], wbce[:], ALU.mult)
                            P.new_stage(f"t4ob{e}_{tci}_{half}")
                            for mm in range(4):
                                vec_tt(P, out_sb[4 * half + mm][:, tcs:tcs + 512],
                                       out_sb[4 * half + mm][:, tcs:tcs + 512], opsum[mm][:, 0:512], ALU.add)
            for m in range(DT):
                P.new_stage(f"t4b2_{m}")
                emit_mm_group(P, PS_A[:, 0:512], [b2_sb[:, m * 128:(m + 1) * 128]], [wrow[:]], 0, 512, pn0=0)
                emit_mm_group(P, PS_A[:, 512:1024], [b2_sb[:, m * 128:(m + 1) * 128]], [wrow[:]], 512, 512, pn0=0)
                P.new_stage(f"t4b2e_{m}")
                vec_tt(P, out_sb[m][:, 0:512], out_sb[m][:, 0:512], PS_A[:, 0:512], ALU.add)
                vec_tt(P, out_sb[m][:, 512:1024], out_sb[m][:, 512:1024], PS_A[:, 512:1024], ALU.add)
            P.new_stage("t4_store")
            for i in range(DT):
                P.dma("sync", xout[i * 128:(i + 1) * 128, t0:t0 + TH], out_sb[i][:])
        P.finalize()
    return nc


def t4_inputs(x_fm, mla_fm_bf, ins):
    cz = lambda a, n: np.ascontiguousarray(np.asarray(a).reshape(n, 128).T)
    b1 = np.asarray(ins["L3_ffn_b1"])
    sel = np.zeros((4, 4 * 128), np.float32)
    for e in range(4):
        sel[e, e * 128:(e + 1) * 128] = 1.0
    w1T = np.ascontiguousarray(np.asarray(ins["L3_ffn_W1"]).transpose(0, 2, 1))
    w2T = np.ascontiguousarray(np.asarray(ins["L3_ffn_W2"]).transpose(0, 2, 1))
    kT = np.ascontiguousarray(np.asarray(ins["L3_attn_k"]).T)
    gb = np.asarray(ins["L3_ffn_gate_b"])
    maps = []
    for c in range(NCORES):
        t0 = c * T_C
        maps.append({
            "xin": np.ascontiguousarray(x_fm[:, t0:t0 + T_C]),
            "mla_in": np.ascontiguousarray(mla_fm_bf[:, t0:t0 + T_C]),
            "kT": kT,
            "vbank": np.asarray(ins["L3_attn_v"]),
            "woT": np.ascontiguousarray(np.asarray(ins["L3_ffn_Wo"]).T),
            "n2w_c": cz(ins["L3_ffn_norm2_w"], DT),
            "gwT": np.ascontiguousarray(np.asarray(ins["L3_ffn_gate_w"]).T),
            "gb_t": np.ascontiguousarray(np.tile(gb[None, :], (128, 1))),
            "w1T": w1T, "w2T": w2T,
            "b1_c": np.ascontiguousarray(b1.reshape(4, FFT, 128).transpose(2, 0, 1).reshape(128, 4 * FFT)),
            "b2": np.asarray(ins["L3_ffn_b2"]),
            "sel4": sel, "ident": IDENT, "ones_row": ONES_ROW,
        })
    return maps


def run_T4(x_fm, mla_fm, ins):
    import ml_dtypes
    nc = build_T4()
    mla_bf = mla_fm.astype(ml_dtypes.bfloat16) if mla_fm.dtype == np.float32 else mla_fm
    res = run_bass_kernel_spmd(nc, t4_inputs(x_fm, mla_bf, ins), core_ids=list(range(NCORES)))
    return np.concatenate([r["xout"] for r in res.results], axis=1)



_NC_CACHE = {}


def _get_nc(name, builder):
    if name not in _NC_CACHE:
        _NC_CACHE[name] = builder()
    return _NC_CACHE[name]


def kernel(**inputs):
    import ml_dtypes
    ins = inputs
    x_fm = fm(ins["L0_attn_x"])
    cores = list(range(NCORES))

    res = run_bass_kernel_spmd(_get_nc("M0", build_M0), m0_inputs(x_fm, ins), core_ids=cores)
    yg = np.zeros((XB, T_ALL), np.float32)
    for c in cores:
        b, dh = c // 2, c % 2
        yg[dh * HT:(dh + 1) * HT, b * S:(b + 1) * S] = res.results[c]["ygated"]
    res = run_bass_kernel_spmd(_get_nc("T1", build_T1), t1_inputs(x_fm, yg, ins), core_ids=cores)
    x_fm = np.concatenate([r["xout"] for r in res.results], axis=1)

    res = run_bass_kernel_spmd(_get_nc("M1", build_M1), m1_inputs(x_fm, ins), core_ids=cores)
    attn = np.zeros((D, T_ALL), dtype=ml_dtypes.bfloat16)
    for c in cores:
        b, hh2 = c // 2, c % 2
        attn[hh2 * 512:(hh2 + 1) * 512, b * S:(b + 1) * S] = np.asarray(res.results[c]["attn_o"])
    res = run_bass_kernel_spmd(_get_nc("T2", build_T2), t2_inputs(x_fm, attn, ins), core_ids=cores)
    x_fm = np.concatenate([r["xout"] for r in res.results], axis=1)

    res = run_bass_kernel_spmd(_get_nc("M2", build_M2), m2_inputs(x_fm, ins), core_ids=cores)
    attn = np.zeros((D, T_ALL), dtype=ml_dtypes.bfloat16)
    for c in cores:
        b, hh2 = c // 2, c % 2
        attn[hh2 * 512:(hh2 + 1) * 512, b * S:(b + 1) * S] = np.asarray(res.results[c]["attn_o"])
    res = run_bass_kernel_spmd(_get_nc("T3", build_T3), t3_inputs(x_fm, attn, ins), core_ids=cores)
    x_fm = np.concatenate([r["xout"] for r in res.results], axis=1)

    res = run_bass_kernel_spmd(_get_nc("M3", build_M3), m3_inputs(x_fm, ins), core_ids=cores)
    attn = np.zeros((D, T_ALL), dtype=ml_dtypes.bfloat16)
    for c in cores:
        b, hh2 = c // 2, c % 2
        attn[hh2 * 512:(hh2 + 1) * 512, b * S:(b + 1) * S] = np.asarray(res.results[c]["attn_o"])
    res = run_bass_kernel_spmd(_get_nc("T4", build_T4), t4_inputs(x_fm, attn, ins), core_ids=cores)
    x_fm = np.concatenate([r["xout"] for r in res.results], axis=1)

    return np.ascontiguousarray(x_fm.T.reshape(B, S, D)).astype(np.float32)


# version 03
# speedup vs baseline: 1.1439x; 1.0302x over previous
import contextlib
import math
import numpy as np

import concourse.bass as bass
import concourse.mybir as mybir
from concourse.bass_utils import run_bass_kernel_spmd

F32 = mybir.dt.float32
BF16 = mybir.dt.bfloat16
AF = mybir.ActivationFunctionType
ALU = mybir.AluOpType
AX = mybir.AxisListType

NCORES = 8
B, S, D = 4, 4096, 1024
T_ALL = B * S
T_C = T_ALL // NCORES
DT = D // 128

ENGS = ("tensor", "vector", "scalar", "sync", "gpsimd")


class Prog:

    def __init__(self, nc):
        self.nc = nc
        self.stages = []
        self.cur = None
        self.new_stage()

    def new_stage(self, name=""):
        self.cur = {e: [] for e in ENGS}
        self.cur["_name"] = name
        self.stages.append(self.cur)

    def on(self, eng, fn):
        self.cur[eng].append(("op", fn))

    def dma(self, queue, out, in_, **kw):
        def fn(e, out=out, in_=in_, kw=kw):
            return e.dma_start(out=out, in_=in_, **kw)
        self.cur[queue].append(("dma", fn))

    def cc(self, fn):
        self.cur["gpsimd"].append(("cc", fn))

    def finalize(self):
        nc = self.nc
        nstage = len(self.stages)
        incs = {e: [0] * nstage for e in ENGS}
        cc_events = []
        for k, st in enumerate(self.stages):
            for e in ENGS:
                kinds = {kind for kind, _ in st[e]}
                if "op" in kinds and len(kinds) > 1:
                    raise AssertionError(f"stage {k} engine {e}: mixing op and dma/cc unsupported")
                if "cc" in kinds and "dma" in kinds:
                    raise AssertionError(f"stage {k} engine {e}: mixing cc and dma unsupported")
                for kind, _ in st[e]:
                    if kind == "dma":
                        incs[e][k] += 16
                    elif kind == "cc":
                        cc_events.append((k, len(cc_events)))
                if st[e] and all(kind == "op" for kind, _ in st[e]):
                    incs[e][k] = 1
        cum = {e: np.cumsum([0] + incs[e]).tolist() for e in ENGS}

        with contextlib.ExitStack() as ctx:
            sems = {e: ctx.enter_context(nc.semaphore(f"sem_{e}")) for e in ENGS}
            cc_sems = [ctx.enter_context(nc.semaphore(f"cc_{i}")) for _, i in cc_events]
            cc_stage = [k for k, _ in cc_events]
            block = ctx.enter_context(nc.Block())
            stages = self.stages
            for eng_name in ENGS:
                def body(eng, eng_name=eng_name):
                    last_wait = {e2: 0 for e2 in ENGS}
                    cc_waited = [False] * len(cc_sems)
                    cc_idx = 0

                    def do_waits(k):
                        for e2 in ENGS:
                            if e2 == eng_name:
                                continue
                            tgt = cum[e2][k]
                            if tgt > last_wait[e2]:
                                eng.wait_ge(sems[e2], tgt)
                                last_wait[e2] = tgt
                        for j in range(len(cc_sems)):
                            if cc_stage[j] < k and not cc_waited[j] and eng_name != "gpsimd":
                                eng.wait_ge(cc_sems[j], 1)
                                cc_waited[j] = True

                    for k in range(nstage):
                        items = stages[k][eng_name]
                        if not items:
                            continue
                        do_waits(k)
                        n = len(items)
                        for i, (kind, fn) in enumerate(items):
                            inst = fn(eng)
                            if kind == "dma":
                                inst.then_inc(sems[eng_name], 16)
                            elif kind == "cc":
                                inst.then_inc(cc_sems[cc_idx])
                                cc_idx += 1
                            elif i == n - 1:
                                inst.then_inc(sems[eng_name], 1)
                            if kind == "cc":
                                eng.wait_ge(cc_sems[cc_idx - 1], 1)
                    do_waits(nstage)
                getattr(block, eng_name)(body)



def ceil_div(a, b):
    return (a + b - 1) // b


def chunk_starts(total, size):
    if total <= size:
        return [0]
    starts = list(range(0, total - size + 1, size))
    if starts[-1] + size < total:
        starts.append(total - size)
    return starts


def emit_mm_group(P, psum, lhsT_tiles, rhs_tiles, n0, nn, accumulate=False, pn0=None):
    nk = len(lhsT_tiles)
    p0 = n0 if pn0 is None else pn0
    for ki in range(nk):
        def fn(t, ki=ki):
            return t.matmul(psum[:, p0:p0 + nn], lhsT_tiles[ki],
                            rhs_tiles[ki][:, n0:n0 + nn],
                            start=(ki == 0 and not accumulate),
                            stop=(ki == nk - 1))
        P.on("tensor", fn)


def act(P, out, in_, func, bias=None, scale=None, accum_out=None):
    def fn(sc, out=out, in_=in_, func=func, bias=bias, scale=scale, accum_out=accum_out):
        kw = {}
        if bias is not None:
            kw["bias"] = bias
        if scale is not None:
            kw["scale"] = scale
        if accum_out is not None:
            kw["accum_out"] = accum_out
        return sc.activation(out, in_, func, **kw)
    P.on("scalar", fn)


def vec_tt(P, out, a, b_, op):
    P.on("vector", lambda v, out=out, a=a, b_=b_, op=op: v.tensor_tensor(out, a, b_, op=op))


def vec_ts(P, out, a, s1, s2, op0, op1=None):
    def fn(v, out=out, a=a, s1=s1, s2=s2, op0=op0, op1=op1):
        if op1 is None:
            return v.tensor_scalar(out, a, s1, None, op0=op0)
        return v.tensor_scalar(out, a, s1, s2, op0=op0, op1=op1)
    P.on("vector", fn)


def vec_stt(P, out, in0, scalar, in1, op0, op1):
    P.on("vector", lambda v, out=out, in0=in0, s=scalar, in1=in1, op0=op0, op1=op1:
         v.scalar_tensor_tensor(out, in0, s, in1, op0=op0, op1=op1))


def vec_copy(P, out, in_):
    P.on("vector", lambda v, out=out, in_=in_: v.tensor_copy(out, in_))


def vec_red(P, out, in_, op, negate=False):
    def fn(v, out=out, in_=in_, op=op, negate=negate):
        kw = {"negate": True} if negate else {}
        return v.tensor_reduce(out, in_, axis=AX.X, op=op, **kw)
    P.on("vector", fn)


def vec_memset(P, ap, val):
    P.on("vector", lambda v, ap=ap, val=val: v.memset(ap, val))


def vec_scan(P, out, a, b_, init):
    P.on("vector", lambda v, out=out, a=a, b_=b_, init=init:
         v.tensor_tensor_scan(out, a, b_, init, op0=ALU.mult, op1=ALU.add))



def emit_ln(P, xt, xnt, sq, mu_b, rstd_b, ps_rows, ps_bc, mu_row, sq_row,
            ones_col_f32, ones_row_f32, Tn, eps=1e-5, gam=None, bet=None,
            out_f32=None, nt=DT, dim=None, rms=False):
    dim = dim or (nt * 128)
    P.new_stage("ln_sq")
    for i in range(nt):
        act(P, sq[i][:], xt[i][:], AF.Square)
    for j in chunk_starts(Tn, 512):
        nn = min(512, Tn - j)
        P.new_stage("ln_sums")
        if not rms:
            emit_mm_group(P, ps_rows[0], [ones_col_f32] * nt,
                          [xt[i][:, j:j + nn] for i in range(nt)], 0, nn)
        emit_mm_group(P, ps_rows[1], [ones_col_f32] * nt,
                      [sq[i][:, j:j + nn] for i in range(nt)], 0, nn)
        P.new_stage("ln_rm1")
        if not rms:
            vec_ts(P, mu_row[:, j:j + nn], ps_rows[0][0:1, 0:nn], 1.0 / dim, None, ALU.mult)
        else:
            vec_ts(P, sq_row[:, j:j + nn], ps_rows[1][0:1, 0:nn], 1.0 / dim, eps, ALU.mult, ALU.add)
        if not rms:
            P.new_stage("ln_rm2")
            vec_stt(P, sq_row[:, j:j + nn], ps_rows[0][0:1, 0:nn], 1.0 / dim,
                    mu_row[:, j:j + nn], ALU.mult, ALU.mult)
            P.new_stage("ln_rm3")
            vec_stt(P, sq_row[:, j:j + nn], ps_rows[1][0:1, 0:nn], 1.0 / dim,
                    sq_row[:, j:j + nn], ALU.mult, ALU.subtract)
            P.new_stage("ln_rm4")
            vec_ts(P, sq_row[:, j:j + nn], sq_row[:, j:j + nn], eps, None, ALU.add)
        P.new_stage("ln_rm5")
        act(P, sq_row[:, j:j + nn], sq_row[:, j:j + nn], AF.Ln)
        P.new_stage("ln_rm6")
        act(P, sq_row[:, j:j + nn], sq_row[:, j:j + nn], AF.Exp, scale=-0.5)
        P.new_stage("ln_bc")
        if not rms:
            emit_mm_group(P, ps_bc[:, 0:512], [ones_row_f32], [mu_row[:]], j, nn, pn0=0)
        emit_mm_group(P, ps_bc[:, 512:1024], [ones_row_f32], [sq_row[:]], j, nn, pn0=0)
        P.new_stage("ln_bcev")
        if not rms:
            vec_copy(P, mu_b[:, j:j + nn], ps_bc[:, 0:nn])
        vec_copy(P, rstd_b[:, j:j + nn], ps_bc[:, 512:512 + nn])
    if not rms:
        P.new_stage("ln_apply")
        for i in range(nt):
            vec_tt(P, sq[i][:], xt[i][:], mu_b[:], ALU.subtract)
    src0 = sq if not rms else xt
    P.new_stage("ln_apply2")
    for i in range(nt):
        dst = out_f32[i][:] if out_f32 is not None else sq[i][:]
        vec_tt(P, dst, src0[i][:], rstd_b[:], ALU.mult)
    src = out_f32 if out_f32 is not None else sq
    if gam is not None:
        P.new_stage("ln_affine")
        for i in range(nt):
            if bet is not None:
                vec_ts(P, xnt[i][:], src[i][:], gam[i], bet[i], ALU.mult, ALU.add)
            else:
                vec_ts(P, xnt[i][:], src[i][:], gam[i], None, ALU.mult)
    elif xnt is not src:
        P.new_stage("ln_cast")
        for i in range(nt):
            vec_copy(P, xnt[i][:], src[i][:])



def fm(a):
    a = np.asarray(a)
    a2 = a.reshape(-1, a.shape[-1])
    return np.ascontiguousarray(a2.T)


ONES_COL = np.ones((128, 1), np.float32)
ONES_ROW = np.ones((1, 128), np.float32)
IDENT = np.eye(128, dtype=np.float32)



def build_T0():
    nc = bass.Bass("TRN2", target_bir_lowering=False)
    xin = nc.dram_tensor("xfm", [D, T_C], F32, kind="ExternalInput").ap()
    onc = nc.dram_tensor("ones_col", [128, 1], F32, kind="ExternalInput").ap()
    onr = nc.dram_tensor("ones_row", [1, 128], F32, kind="ExternalInput").ap()
    xno = nc.dram_tensor("xnfm", [D, T_C], F32, kind="ExternalOutput").ap()
    Tn = T_C
    with contextlib.ExitStack() as ctx:
        E = ctx.enter_context
        xt = [E(nc.sbuf_tensor(f"x{i}", [128, Tn], F32)) for i in range(DT)]
        sq = [E(nc.sbuf_tensor(f"sq{i}", [128, Tn], F32)) for i in range(DT)]
        xnt = sq
        mu_b = E(nc.sbuf_tensor("mu_b", [128, Tn], F32))
        rstd_b = E(nc.sbuf_tensor("rstd_b", [128, Tn], F32))
        mu_row = E(nc.sbuf_tensor("mu_row", [1, Tn], F32))
        sq_row = E(nc.sbuf_tensor("sq_row", [1, Tn], F32))
        onescol = E(nc.sbuf_tensor("onescol", [128, 1], F32))
        onesrow = E(nc.sbuf_tensor("onesrow", [1, 128], F32))
        pr_mu = E(nc.psum_tensor("pr_mu", [1, 512], F32))
        pr_sq = E(nc.psum_tensor("pr_sq", [1, 512], F32))
        ps_bc = E(nc.psum_tensor("ps_bc", [128, 1024], F32))

        P = Prog(nc)
        P.new_stage("loads")
        for i in range(DT):
            P.dma("sync", xt[i][:], xin[i * 128:(i + 1) * 128, :])
        P.dma("sync", onescol[:], onc)
        P.dma("sync", onesrow[:], onr)
        emit_ln(P, xt, xnt, sq, mu_b, rstd_b, (pr_mu, pr_sq), ps_bc, mu_row, sq_row,
                onescol[:], onesrow[:], Tn)
        P.new_stage("store")
        for i in range(DT):
            P.dma("sync", xno[i * 128:(i + 1) * 128, :], xnt[i][:])
        P.finalize()
    return nc


def run_T0(x_fm_full):
    nc = build_T0()
    in_maps = []
    for c in range(NCORES):
        in_maps.append({
            "xfm": np.ascontiguousarray(x_fm_full[:, c * T_C:(c + 1) * T_C]),
            "ones_col": ONES_COL, "ones_row": ONES_ROW,
        })
    res = run_bass_kernel_spmd(nc, in_maps, core_ids=list(range(NCORES)))
    return np.concatenate([r["xnfm"] for r in res.results], axis=1)



M0_DEBUG = False
M0_TC = 512
M0_NCH = S // M0_TC
XB = 2048
XBT = XB // 128
HT = 1024
HTT = HT // 128
STATE = 16


def build_M0():
    nc = bass.Bass("TRN2", target_bir_lowering=False)
    ap = lambda n, s, d=F32, k="ExternalInput": nc.dram_tensor(n, s, d, kind=k).ap()
    xraw = ap("xraw", [D, S])
    wi_xb = ap("wi_xb", [D, XB])
    wi_z = ap("wi_z", [D, HT])
    wdt = ap("wdt", [XB, HT])
    wbc = ap("wbc", [XB, 32])
    alog = ap("alog", [128, HTT * STATE])
    convrows = ap("convrows", [1, 3 * XB])
    convb = ap("convb", [128, XBT])
    bdt = ap("bdt", [128, HTT])
    gam = ap("gam", [128, DT])
    bet = ap("bet", [128, DT])
    dpar = ap("dpar", [128, HTT])
    sel16 = ap("sel16", [16, 16 * 128])
    onc = ap("ones_col", [128, 1])
    onr = ap("ones_row", [1, 128])
    yout = ap("ygated", [HT, S], F32, "ExternalOutput")
    if M0_DEBUG:
        d_xn2 = ap("d_xn2", [D, S], BF16, "ExternalOutput")
        d_xbc = ap("d_xbc", [XB, S], F32, "ExternalOutput")
        d_dt = ap("d_dt", [HT, S], F32, "ExternalOutput")
        d_zs = ap("d_zs", [HT, S], F32, "ExternalOutput")
        d_ys = ap("d_ys", [HT, S], F32, "ExternalOutput")
        d_bc = ap("d_bc", [32, S], F32, "ExternalOutput")
    wsc = nc.dram_tensor("wsc", [XBT, 128, 3 * DT * 128], BF16)
    wsc_r = wsc.ap()

    W = M0_TC + 2
    TC = M0_TC
    with contextlib.ExitStack() as ctx:
        E = ctx.enter_context
        P = Prog(nc)

        wiz_sb = E(nc.sbuf_tensor("wiz_sb", [128, DT * HT], BF16))
        wdt_sb = E(nc.sbuf_tensor("wdt_sb", [128, XBT * HT], BF16))
        wbc_sb = E(nc.sbuf_tensor("wbc_sb", [128, XBT * 32], BF16))
        A_sb = E(nc.sbuf_tensor("A_sb", [128, HTT * STATE], F32))
        sel_sb = E(nc.sbuf_tensor("sel_sb", [16, 16 * 128], F32))
        cols = E(nc.sbuf_tensor("cols", [128, 2 * XBT + 4 * HTT + 2 * DT], F32))
        convb_c = cols[:, 0:XBT]
        bdt_c = cols[:, XBT:XBT + HTT]
        gam_c = cols[:, XBT + HTT:XBT + HTT + DT]
        bet_c = cols[:, XBT + HTT + DT:XBT + HTT + 2 * DT]
        dpar_c = cols[:, XBT + HTT + 2 * DT:XBT + 2 * HTT + 2 * DT]
        onescol = E(nc.sbuf_tensor("onescol", [128, 1], F32))
        onesrow = E(nc.sbuf_tensor("onesrow", [1, 128], F32))

        PS_A = E(nc.psum_tensor("PS_A", [128, 1024], F32))
        PS_B = E(nc.psum_tensor("PS_B", [128, 1024], F32))
        PS_C = E(nc.psum_tensor("PS_C", [128, 1024], F32))
        PS_R0 = E(nc.psum_tensor("PS_R0", [1, 512], F32))
        PS_R1 = E(nc.psum_tensor("PS_R1", [1, 512], F32))

        P.new_stage("prep_loads")
        P.dma("sync", onescol[:], onc)
        P.dma("sync", onesrow[:], onr)
        P.dma("sync", sel_sb[:], sel16)
        P.dma("sync", cols[:, 0:XBT], convb)
        P.dma("sync", cols[:, XBT:XBT + HTT], bdt)
        P.dma("sync", cols[:, XBT + HTT:XBT + HTT + DT], gam)
        P.dma("sync", cols[:, XBT + HTT + DT:XBT + HTT + 2 * DT], bet)
        P.dma("sync", cols[:, XBT + HTT + 2 * DT:XBT + 2 * HTT + 2 * DT], dpar)
        for k in range(DT):
            P.dma("gpsimd", wiz_sb[:, k * HT:(k + 1) * HT], wi_z[k * 128:(k + 1) * 128, :])
        for k in range(XBT):
            P.dma("gpsimd", wdt_sb[:, k * HT:(k + 1) * HT], wdt[k * 128:(k + 1) * 128, :])
            P.dma("gpsimd", wbc_sb[:, k * 32:(k + 1) * 32], wbc[k * 128:(k + 1) * 128, :])
        with contextlib.ExitStack() as pctx:
            Ep = pctx.enter_context
            convrows_sb = Ep(nc.sbuf_tensor("convrows_sb", [1, 3 * XB], F32))
            alog_sb = Ep(nc.sbuf_tensor("alog_sb", [128, HTT * STATE], F32))
            wjb = Ep(nc.sbuf_tensor("wjb", [128, XB], F32))
            wtmp = Ep(nc.sbuf_tensor("wtmp", [128, XB], F32))
            wbf = Ep(nc.sbuf_tensor("wbf", [128, XB], BF16))
            P.dma("sync", convrows_sb[:], convrows)
            P.dma("sync", alog_sb[:], alog)
            P.new_stage("prep_A")
            act(P, A_sb[:], alog_sb[:], AF.Exp)
            P.new_stage("prep_A2")
            P.on("scalar", lambda sc: sc.mul(A_sb[:], A_sb[:], -1.0))
            for j in range(3):
                for q in range(4):
                    P.new_stage("prep_bc_mm")
                    emit_mm_group(P, PS_A[:, (q % 2) * 512:(q % 2) * 512 + 512],
                                  [onesrow[:]], [convrows_sb[:, j * XB:(j + 1) * XB]],
                                  q * 512, 512, pn0=0)
                    P.new_stage("prep_bc_ev")
                    vec_copy(P, wjb[:, q * 512:(q + 1) * 512], PS_A[:, (q % 2) * 512:(q % 2) * 512 + 512])
                for k in range(DT):
                    P.new_stage("prep_wld")
                    P.dma("sync", wtmp[:], wi_xb[k * 128:(k + 1) * 128, :])
                    P.new_stage("prep_wsc")
                    vec_tt(P, wbf[:], wtmp[:], wjb[:], ALU.mult)
                    P.new_stage("prep_wst")
                    cs = (j * DT + k) * 128
                    P.dma("sync",
                          wsc_r[:, :, cs:cs + 128].rearrange("g p c -> p g c"),
                          wbf[:].rearrange("p (g c) -> p g c", c=128))

        xnt = [E(nc.sbuf_tensor(f"xn{i}", [128, W], F32)) for i in range(DT)]
        sqt = [E(nc.sbuf_tensor(f"sq{i}", [128, W], F32)) for i in range(DT)]
        xn2 = [E(nc.sbuf_tensor(f"xn2_{i}", [128, W], BF16)) for i in range(DT)]
        mu_b = E(nc.sbuf_tensor("mu_b", [128, W], F32))
        rstd_b = E(nc.sbuf_tensor("rstd_b", [128, W], F32))
        mu_row = E(nc.sbuf_tensor("mu_row", [1, W], F32))
        sq_row = E(nc.sbuf_tensor("sq_row", [1, W], F32))
        xbc_f = [E(nc.sbuf_tensor(f"xbcf{i}", [128, TC], F32)) for i in range(XBT)]
        xbc_b = [E(nc.sbuf_tensor(f"xbcb{i}", [128, TC], BF16)) for i in range(XBT)]
        dt_f = [E(nc.sbuf_tensor(f"dtf{i}", [128, TC], F32)) for i in range(HTT)]
        zs_f = [E(nc.sbuf_tensor(f"zsf{i}", [128, TC], F32)) for i in range(HTT)]
        brow = E(nc.sbuf_tensor("brow", [16, TC], F32))
        crow = E(nc.sbuf_tensor("crow", [16, TC], F32))
        hstate = E(nc.sbuf_tensor("hstate", [128, HTT * STATE], F32))
        hout2 = [E(nc.sbuf_tensor(f"hout{i}", [128, TC], F32)) for i in range(2)]
        abar2 = [E(nc.sbuf_tensor(f"abar{i}", [128, TC], F32)) for i in range(2)]
        abar = abar2[0]
        utmp = E(nc.sbuf_tensor("utmp", [128, TC], F32))
        wg_buf = E(nc.sbuf_tensor("wg_buf", [128, 3 * DT * 128], BF16))
        yacc = sqt
        dtx = xnt

        P.new_stage("zero_state")
        vec_memset(P, hstate[:], 0.0)

        for ch in range(M0_NCH):
            t0 = ch * TC
            first, last = ch == 0, ch == M0_NCH - 1
            P.new_stage("xn_load")
            lo = 0 if first else t0 - 1
            hi = S if last else t0 + TC + 1
            off = 1 if first else 0
            for i in range(DT):
                P.dma("sync", xnt[i][:, off:off + (hi - lo)], xraw[i * 128:(i + 1) * 128, lo:hi])
                if first:
                    vec_memset(P, xnt[i][:, 0:1], 0.0)
                if last:
                    vec_memset(P, xnt[i][:, W - 1:W], 0.0)
            emit_ln(P, xnt, xnt, sqt, mu_b, rstd_b, (PS_R0, PS_R1), PS_A,
                    mu_row, sq_row, onescol[:], onesrow[:], W, out_f32=xnt)
            emit_ln(P, xnt, xn2, sqt, mu_b, rstd_b, (PS_R0, PS_R1), PS_A,
                    mu_row, sq_row, onescol[:], onesrow[:], W,
                    gam=[gam_c[:, i:i + 1] for i in range(DT)],
                    bet=[bet_c[:, i:i + 1] for i in range(DT)])
            if first or last:
                P.new_stage("halo_zero")
                col = 0 if first else W - 1
                for i in range(DT):
                    vec_memset(P, xn2[i][:, col:col + 1], 0.0)
            P.new_stage("wg_load0")
            P.dma("sync", wg_buf[:], wsc_r[0])
            for g in range(XBT):
                P.new_stage(f"xb_mm{g}")
                pp = PS_B[:, (g % 2) * 512:(g % 2) * 512 + 512]
                for j in range(3):
                    for k in range(DT):
                        def fn(t, j=j, k=k, pp=pp, g=g):
                            return t.matmul(pp[:, 0:TC],
                                            wg_buf[:, (j * DT + k) * 128:(j * DT + k) * 128 + 128],
                                            xn2[k][:, j:j + TC],
                                            start=(j == 0 and k == 0), stop=(j == 2 and k == DT - 1))
                        P.on("tensor", fn)
                P.new_stage(f"xb_ev{g}")
                act(P, xbc_f[g][:], pp[:, 0:TC], AF.Silu, bias=convb_c[:, g:g + 1])
                if g + 1 < XBT:
                    P.dma("sync", wg_buf[:], wsc_r[g + 1])
            P.new_stage("xb_cast")
            for g in range(XBT):
                vec_copy(P, xbc_b[g][:], xbc_f[g][:])
            for m in range(HTT):
                P.new_stage(f"z_mm{m}")
                pp = PS_B[:, (m % 2) * 512:(m % 2) * 512 + 512]
                emit_mm_group(P, pp, [wiz_sb[:, k * HT + m * 128:k * HT + m * 128 + 128] for k in range(DT)],
                              [xn2[k][:, 1:1 + TC] for k in range(DT)], 0, TC)
                P.new_stage(f"z_ev{m}")
                act(P, zs_f[m][:], pp[:, 0:TC], AF.Silu)
            for m in range(HTT):
                P.new_stage(f"dt_mm{m}")
                pp = PS_B[:, (m % 2) * 512:(m % 2) * 512 + 512]
                emit_mm_group(P, pp, [wdt_sb[:, k * HT + m * 128:k * HT + m * 128 + 128] for k in range(XBT)],
                              [xbc_b[k][:] for k in range(XBT)], 0, TC)
                P.new_stage(f"dt_abs{m}")
                act(P, abar[:], pp[:, 0:TC], AF.Abs, bias=bdt_c[:, m:m + 1])
                P.new_stage(f"dt_exp{m}")
                act(P, abar[:], abar[:], AF.Exp, scale=-1.0)
                P.new_stage(f"dt_ln{m}")
                act(P, abar[:], abar[:], AF.Ln, bias=1.0)
                P.new_stage(f"dt_relu{m}")
                act(P, dt_f[m][:], pp[:, 0:TC], AF.Relu, bias=bdt_c[:, m:m + 1])
                P.new_stage(f"dt_add{m}")
                vec_tt(P, dt_f[m][:], dt_f[m][:], abar[:], ALU.add)
            P.new_stage("bc_mm")
            emit_mm_group(P, PS_C[0:16, 0:512], [wbc_sb[:, k * 32:k * 32 + 16] for k in range(XBT)],
                          [xbc_b[k][:] for k in range(XBT)], 0, TC, pn0=0)
            emit_mm_group(P, PS_C[0:16, 512:1024], [wbc_sb[:, k * 32 + 16:k * 32 + 32] for k in range(XBT)],
                          [xbc_b[k][:] for k in range(XBT)], 0, TC, pn0=0)
            P.new_stage("bc_ev")
            vec_copy(P, brow[:], PS_C[0:16, 0:TC])
            vec_copy(P, crow[:], PS_C[0:16, 512:512 + TC])
            P.new_stage("dtx")
            for g in range(HTT):
                vec_tt(P, dtx[g][:, 0:TC], dt_f[g][:], xbc_f[g][:], ALU.mult)
            P.new_stage("yacc0")
            for g in range(HTT):
                vec_memset(P, yacc[g][:, 0:TC], 0.0)
            for n in range(STATE):
                P.new_stage(f"bbc_mm{n}")
                emit_mm_group(P, PS_C[:, 0:512], [sel_sb[:, n * 128:(n + 1) * 128]], [brow[:]], 0, TC, pn0=0)
                emit_mm_group(P, PS_C[:, 512:1024], [sel_sb[:, n * 128:(n + 1) * 128]], [crow[:]], 0, TC, pn0=0)
                act(P, abar2[0][:], dt_f[0][:], AF.Exp, scale=A_sb[:, 0 * STATE + n:0 * STATE + n + 1])
                for g in range(HTT):
                    col = g * STATE + n
                    P.new_stage(f"scan{n}_{g}")
                    vec_tt(P, utmp[:], dtx[g][:, 0:TC], PS_C[:, 0:TC], ALU.mult)
                    vec_scan(P, hout2[g % 2][:], abar2[g % 2][:], utmp[:], hstate[:, col:col + 1])
                    vec_tt(P, utmp[:], hout2[g % 2][:], PS_C[:, 512:512 + TC], ALU.mult)
                    vec_tt(P, yacc[g][:, 0:TC], yacc[g][:, 0:TC], utmp[:], ALU.add)
                    if g + 1 < HTT:
                        act(P, abar2[(g + 1) % 2][:], dt_f[g + 1][:], AF.Exp,
                            scale=A_sb[:, (g + 1) * STATE + n:(g + 1) * STATE + n + 1])
                    if g >= 1:
                        pc = (g - 1) * STATE + n
                        act(P, hstate[:, pc:pc + 1], hout2[(g - 1) % 2][:, TC - 1:TC], AF.Copy)
                P.new_stage(f"scan{n}_tail")
                pc = (HTT - 1) * STATE + n
                act(P, hstate[:, pc:pc + 1], hout2[(HTT - 1) % 2][:, TC - 1:TC], AF.Copy)
            P.new_stage("yg1")
            for g in range(HTT):
                vec_ts(P, hout2[0][:], xbc_f[g][:], dpar_c[:, g:g + 1], None, ALU.mult)
                vec_tt(P, yacc[g][:, 0:TC], yacc[g][:, 0:TC], hout2[0][:], ALU.add)
                vec_tt(P, dt_f[g][:], yacc[g][:, 0:TC], zs_f[g][:], ALU.mult)
            P.new_stage("yg_store")
            for g in range(HTT):
                P.dma("sync", yout[g * 128:(g + 1) * 128, t0:t0 + TC], dt_f[g][:])
        P.finalize()
    return nc


def m0_inputs(x_fm, ins):
    W_in = np.asarray(ins["L0_attn_W_in"])
    W_dt = np.asarray(ins["L0_attn_W_dt"])
    conv_w = np.asarray(ins["L0_attn_conv_w"])
    A_log = np.asarray(ins["L0_attn_A_log"])
    sel = np.zeros((16, 16 * 128), np.float32)
    for n in range(16):
        sel[n, n * 128:(n + 1) * 128] = 1.0
    maps = []
    for c in range(NCORES):
        b, dh = c // 2, c % 2
        hsl = slice(dh * HT, (dh + 1) * HT)
        xbperm = np.concatenate([np.arange(dh * HT, (dh + 1) * HT),
                                 np.arange((1 - dh) * HT, (2 - dh) * HT)])
        alog_r = np.ascontiguousarray(
            A_log[hsl].reshape(HTT, 128, STATE).transpose(1, 0, 2).reshape(128, HTT * STATE))
        maps.append({
            "xraw": np.ascontiguousarray(x_fm[:, b * S:(b + 1) * S]),
            "wi_xb": np.ascontiguousarray(W_in[:, :XB][:, xbperm]),
            "wi_z": np.ascontiguousarray(W_in[:, XB + dh * HT:XB + (dh + 1) * HT]),
            "wdt": np.ascontiguousarray(W_dt[xbperm][:, hsl]),
            "wbc": np.ascontiguousarray(np.concatenate(
                [ins["L0_attn_W_B"], ins["L0_attn_W_C"]], axis=1)[xbperm]),
            "alog": alog_r,
            "convrows": np.ascontiguousarray(conv_w[xbperm].T.reshape(1, 3 * XB)),
            "convb": np.ascontiguousarray(np.asarray(ins["L0_attn_conv_b"])[xbperm].reshape(XBT, 128).T),
            "bdt": np.ascontiguousarray(np.asarray(ins["L0_attn_b_dt"])[hsl].reshape(HTT, 128).T),
            "gam": np.ascontiguousarray(np.asarray(ins["L0_attn_norm_scale"]).reshape(DT, 128).T),
            "bet": np.ascontiguousarray(np.asarray(ins["L0_attn_norm_bias"]).reshape(DT, 128).T),
            "dpar": np.ascontiguousarray(np.asarray(ins["L0_attn_D_param"])[hsl].reshape(HTT, 128).T),
            "sel16": sel,
            "ones_col": ONES_COL, "ones_row": ONES_ROW,
        })
    return maps


def run_M0(x_fm, ins):
    nc = build_M0()
    res = run_bass_kernel_spmd(nc, m0_inputs(x_fm, ins), core_ids=list(range(NCORES)))
    out = np.zeros((XB, T_ALL), np.float32)
    for c in range(NCORES):
        b, dh = c // 2, c % 2
        out[dh * HT:(dh + 1) * HT, b * S:(b + 1) * S] = res.results[c]["ygated"]
    return out



T1_STOP = "full"
T1_DEBUG = False
T1_TH = 1024
T1_TC = 512
NEXP = 8
FF = 4096
FFT = FF // 128


def scol(P, dst, src):
    act(P, dst, src, AF.Copy)


def vdrain(P):
    P.on("vector", lambda v: v.drain())


def sdrain(P):
    P.on("scalar", lambda sc: sc.drain())


def build_T1():
    nc = bass.Bass("TRN2", target_bir_lowering=False)
    ap = lambda n, s, d=F32, k="ExternalInput": nc.dram_tensor(n, s, d, kind=k).ap()
    xin = ap("xin", [D, T_C])
    ygin = ap("ygin", [XB, T_C])
    wout = ap("wout", [XB, D])
    wr = ap("wr", [D, NEXP])
    w1 = ap("w1", [NEXP, D, FF])
    w2 = ap("w2", [NEXP, FF, D])
    b1c_in = ap("b1c", [128, NEXP * FFT])
    b2_in = ap("b2", [NEXP, D])
    sel8 = ap("sel8", [NEXP, NEXP * 128])
    ident = ap("ident", [128, 128])
    onc = ap("ones_col", [128, 1])
    onr = ap("ones_row", [1, 128])
    xout = ap("xout", [D, T_C], F32, "ExternalOutput")
    d_wrow = ap("d_wrow", [NEXP, T_C], F32, "ExternalOutput") if T1_DEBUG else None
    d_lg = ap("d_lg", [128, 8], F32, "ExternalOutput") if T1_DEBUG else None
    d_pt = ap("d_pt", [128, 8], F32, "ExternalOutput") if T1_DEBUG else None
    d_tr = ap("d_tr", [8, 128], F32, "ExternalOutput") if T1_DEBUG else None
    d_h = ap("d_h", [128, 512], BF16, "ExternalOutput") if T1_DEBUG else None
    d_wb = ap("d_wb", [128, 512], F32, "ExternalOutput") if T1_DEBUG else None

    TH, TC = T1_TH, T1_TC
    with contextlib.ExitStack() as ctx:
        E = ctx.enter_context
        P = Prog(nc)
        out_sb = [E(nc.sbuf_tensor(f"out{i}", [128, TH], F32)) for i in range(DT)]
        xn_bf = [E(nc.sbuf_tensor(f"xnb{i}", [128, TH], BF16)) for i in range(DT)]
        wrow = E(nc.sbuf_tensor("wrow", [NEXP, TH], F32))
        wr_sb = E(nc.sbuf_tensor("wr_sb", [128, DT * NEXP], BF16))
        b1c = E(nc.sbuf_tensor("b1c_sb", [128, NEXP * FFT], F32))
        b2_sb = E(nc.sbuf_tensor("b2_sb", [NEXP, D], F32))
        sel_sb = E(nc.sbuf_tensor("sel_sb", [NEXP, NEXP * 128], F32))
        id_sb = E(nc.sbuf_tensor("id_sb", [128, 128], F32))
        onescol = E(nc.sbuf_tensor("onescol", [128, 1], F32))
        onesrow = E(nc.sbuf_tensor("onesrow", [1, 128], F32))
        ptile = E(nc.sbuf_tensor("ptile", [128, NEXP], F32))
        ptmp = E(nc.sbuf_tensor("ptmp", [128, NEXP], F32))
        t4c = E(nc.sbuf_tensor("t4c", [128, 4], F32))
        t2c = E(nc.sbuf_tensor("t2c", [128, 4], F32))
        w8 = E(nc.sbuf_tensor("w8", [128, NEXP], F32))
        PS_A = E(nc.psum_tensor("PS_A", [128, 1024], F32))
        PS_O = E(nc.psum_tensor("PS_O", [128, 2048], F32))
        PS_W = E(nc.psum_tensor("PS_W", [128, 512], F32))
        PS_X = E(nc.psum_tensor("PS_X", [128, 512], F32))

        P.new_stage("const_loads")
        P.dma("sync", b1c[:], b1c_in)
        P.dma("sync", b2_sb[:], b2_in)
        P.dma("sync", sel_sb[:], sel8)
        P.dma("sync", id_sb[:], ident)
        P.dma("sync", onescol[:], onc)
        P.dma("sync", onesrow[:], onr)
        for k in range(DT):
            P.dma("gpsimd", wr_sb[:, k * NEXP:(k + 1) * NEXP], wr[k * 128:(k + 1) * 128, :])

        for hh in range(2):
            t0 = hh * TH
            with contextlib.ExitStack() as sctx:
                Es = sctx.enter_context
                xt = [Es(nc.sbuf_tensor(f"xt{hh}_{i}", [128, TH], F32)) for i in range(DT)]
                sq = [Es(nc.sbuf_tensor(f"sq{hh}_{i}", [128, TH], F32)) for i in range(DT)]
                mu_b = Es(nc.sbuf_tensor(f"mu_b{hh}", [128, TH], F32))
                rstd_b = Es(nc.sbuf_tensor(f"rstd_b{hh}", [128, TH], F32))
                mu_row = Es(nc.sbuf_tensor(f"mu_row{hh}", [1, TH], F32))
                sq_row = Es(nc.sbuf_tensor(f"sq_row{hh}", [1, TH], F32))
                P.new_stage("x_load")
                for i in range(DT):
                    P.dma("sync", xt[i][:], xin[i * 128:(i + 1) * 128, t0:t0 + TH])
                emit_ln(P, xt, xn_bf, sq, mu_b, rstd_b,
                        (PS_A[0:1, 0:512], PS_A[0:1, 512:1024]), PS_O[:, 0:1024],
                        mu_row, sq_row, onescol[:], onesrow[:], TH, out_f32=out_sb)
                P.new_stage("base_addx")
                for i in range(DT):
                    vec_tt(P, out_sb[i][:], out_sb[i][:], xt[i][:], ALU.add)
            with contextlib.ExitStack() as sctx:
                Es = sctx.enter_context
                yg_bf = [Es(nc.sbuf_tensor(f"yg{hh}_{k}", [128, TH], BF16)) for k in range(XBT)]
                wo_sb = Es(nc.sbuf_tensor(f"wo_sb{hh}", [128, XBT * 1024], BF16))
                P.new_stage("yg_load")
                for k in range(XBT):
                    P.dma("gpsimd", yg_bf[k][:], ygin[k * 128:(k + 1) * 128, t0:t0 + TH])
                    P.dma("gpsimd", wo_sb[:, k * 1024:(k + 1) * 1024], wout[k * 128:(k + 1) * 128, :])
                for m in range(DT):
                    for j in range(2):
                        P.new_stage(f"wo_mm{m}_{j}")
                        pp = PS_A[:, (j % 2) * 512:(j % 2) * 512 + 512]
                        emit_mm_group(P, pp, [wo_sb[:, k * 1024 + m * 128:k * 1024 + m * 128 + 128] for k in range(XBT)],
                                      [yg_bf[k][:, j * 512:(j + 1) * 512] for k in range(XBT)], 0, 512)
                        P.new_stage(f"wo_ev{m}_{j}")
                        vec_tt(P, out_sb[m][:, j * 512:(j + 1) * 512],
                               out_sb[m][:, j * 512:(j + 1) * 512], pp[:, 0:512], ALU.add)
            def tourn8(src_t, op):
                P.new_stage("tn_a")
                vec_tt(P, t4c[:], src_t[:, 0:4], src_t[:, 4:8], op)
                vdrain(P)
                P.new_stage("tn_b")
                vec_tt(P, t2c[:, 0:2], t4c[:, 0:2], t4c[:, 2:4], op)
                vdrain(P)
                P.new_stage("tn_c")
                vec_tt(P, t2c[:, 2:3], t2c[:, 0:1], t2c[:, 1:2], op)
                vdrain(P)
                return t2c[:, 2:3]
            for tt in range(TH // 128 if T1_STOP != "base" else 0):
                P.new_stage(f"rt_mm{tt}")
                emit_mm_group(P, PS_W[:, 0:NEXP], [xn_bf[k][:, tt * 128:(tt + 1) * 128] for k in range(DT)],
                              [wr_sb[:, k * NEXP:(k + 1) * NEXP] for k in range(DT)], 0, NEXP, pn0=0)
                P.new_stage(f"rt_ev{tt}")
                vec_copy(P, ptile[:], PS_W[:, 0:NEXP])
                vdrain(P)
                if T1_DEBUG and hh == 0 and tt == 0:
                    P.new_stage("dbg_lg")
                    P.dma("sync", d_lg, ptile[:])
                mc = tourn8(ptile, ALU.max)
                P.new_stage(f"rt_sub{tt}")
                vec_tt(P, ptile[:], ptile[:], mc.broadcast_to([128, NEXP]), ALU.subtract)
                vdrain(P)
                P.new_stage(f"rt_exp{tt}")
                act(P, ptile[:], ptile[:], AF.Exp)
                sdrain(P)
                sc_ = tourn8(ptile, ALU.add)
                P.new_stage(f"rt_rc{tt}")
                P.on("vector", lambda v, sc_=sc_: v.reciprocal(t2c[:, 3:4], sc_))
                vdrain(P)
                P.new_stage(f"rt_nm{tt}")
                vec_tt(P, ptile[:], ptile[:], t2c[:, 3:4].broadcast_to([128, NEXP]), ALU.mult)
                vdrain(P)
                mc = tourn8(ptile, ALU.max)
                P.new_stage(f"rt_t{tt}d")
                vec_tt(P, ptmp[:], ptile[:], mc.broadcast_to([128, NEXP]), ALU.is_ge)
                vdrain(P)
                P.new_stage(f"rt_t{tt}e")
                vec_stt(P, ptmp[:], ptmp[:], -1e30, ptile[:], ALU.mult, ALU.add)
                vdrain(P)
                mc = tourn8(ptmp, ALU.max)
                P.new_stage(f"rt_t{tt}i")
                vec_tt(P, ptmp[:], ptile[:], mc.broadcast_to([128, NEXP]), ALU.is_ge)
                vdrain(P)
                P.new_stage(f"rt_t{tt}j")
                vec_tt(P, ptile[:], ptile[:], ptmp[:], ALU.mult)
                vdrain(P)
                if T1_DEBUG and hh == 0 and tt == 0:
                    P.new_stage("dbg_pt")
                    P.dma("sync", d_pt, ptile[:])
                P.new_stage(f"rt_tr{tt}")
                P.on("tensor", lambda t, tt=tt: t.transpose(PS_W[0:NEXP, 0:128], ptile[:], id_sb[:]))
                P.new_stage(f"rt_trev{tt}")
                vec_copy(P, wrow[:, tt * 128:(tt + 1) * 128], PS_W[0:NEXP, 0:128])
                vdrain(P)
                if T1_DEBUG and hh == 0 and tt in (0, 1, 8, 15):
                    P.new_stage(f"dbg_tr{tt}")
                    P.dma("sync", d_tr[:, (0 if tt == 0 else 1) * 0:128], wrow[:, 0:128])
            with contextlib.ExitStack() as sctx:
                Es = sctx.enter_context
                poolA = Es(nc.sbuf_tensor(f"poolA{hh}", [128, DT * FF], BF16))
                poolB = Es(nc.sbuf_tensor(f"poolB{hh}", [128, 2 * 8192], BF16))
                h_bf = Es(nc.sbuf_tensor(f"h_bf{hh}", [128, FFT * TC], BF16))
                wbce = Es(nc.sbuf_tensor(f"wbce{hh}", [128, TC], F32))
                opsum = [PS_O[:, 0:512], PS_O[:, 512:1024], PS_O[:, 1024:1536], PS_O[:, 1536:2048],
                         PS_A[:, 0:512], PS_A[:, 512:1024], PS_W[:, 0:512], PS_X[:, 0:512]]
                for e in range(NEXP if T1_STOP == "full" else 0):
                    P.new_stage(f"w1_load{e}")
                    for k in range(DT):
                        P.dma("gpsimd", poolA[:, k * FF:(k + 1) * FF], w1[e, k * 128:(k + 1) * 128, :])
                    for tci, tcs in enumerate(range(0, TH, TC)):
                        for ht in range(FFT):
                            P.new_stage(f"moe_h{e}_{tci}_{ht}")
                            pp = PS_A[:, (ht % 2) * 512:(ht % 2) * 512 + 512]
                            emit_mm_group(P, pp, [poolA[:, k * FF + ht * 128:k * FF + ht * 128 + 128] for k in range(DT)],
                                          [xn_bf[k][:, tcs:tcs + TC] for k in range(DT)], 0, TC)
                            if ht == 0:
                                P.dma("gpsimd",
                                      poolB[:, 0:8192].rearrange("p (kk d) -> p kk d", d=1024),
                                      w2[e, 0:1024, :].rearrange("(kk p) d -> p kk d", p=128))
                            P.new_stage(f"moe_g{e}_{tci}_{ht}")
                            act(P, h_bf[:, ht * TC:(ht + 1) * TC], pp[:, 0:TC], AF.Gelu_apprx_tanh,
                                bias=b1c[:, e * FFT + ht:e * FFT + ht + 1])
                        P.new_stage(f"moe_w{e}_{tci}")
                        emit_mm_group(P, PS_A[:, 0:512], [sel_sb[:, e * 128:(e + 1) * 128]],
                                      [wrow[:, tcs:tcs + TC]], 0, TC, pn0=0)
                        P.new_stage(f"moe_wev{e}_{tci}")
                        vec_copy(P, wbce[:], PS_A[:, 0:TC])
                        for gi in range(4):
                            P.new_stage(f"moe_o{e}_{tci}_{gi}")
                            for ki in range(8):
                                kk = gi * 8 + ki
                                for mm in range(8):
                                    def fn(t, kk=kk, ki=ki, gi=gi, mm=mm):
                                        return t.matmul(opsum[mm][:, 0:TC],
                                                        poolB[:, (gi % 2) * 8192 + ki * 1024 + mm * 128:
                                                              (gi % 2) * 8192 + ki * 1024 + mm * 128 + 128],
                                                        h_bf[:, kk * TC:(kk + 1) * TC],
                                                        start=(kk == 0), stop=(kk == FFT - 1))
                                    P.on("tensor", fn)
                            if gi + 1 < 4:
                                gl = gi + 1
                                P.dma("gpsimd",
                                      poolB[:, (gl % 2) * 8192:(gl % 2) * 8192 + 8192].rearrange("p (kk d) -> p kk d", d=1024),
                                      w2[e, gl * 1024:(gl + 1) * 1024, :].rearrange("(kk p) d -> p kk d", p=128))
                        P.new_stage(f"moe_acc{e}_{tci}")
                        for mm in range(8):
                            vec_tt(P, opsum[mm][:, 0:TC], opsum[mm][:, 0:TC], wbce[:], ALU.mult)
                        P.new_stage(f"moe_acc2{e}_{tci}")
                        for mm in range(8):
                            vec_tt(P, out_sb[mm][:, tcs:tcs + TC], out_sb[mm][:, tcs:tcs + TC],
                                   opsum[mm][:, 0:TC], ALU.add)
            if T1_DEBUG:
                P.new_stage("dbg_moe")
                P.dma("sync", d_h, h_bf[:, 0:512])
                P.dma("sync", d_wb, wbce[:])
            for m in range(DT if T1_STOP == "full" else 0):
                P.new_stage(f"b2_mm{m}")
                emit_mm_group(P, PS_A[:, 0:512], [b2_sb[:, m * 128:(m + 1) * 128]], [wrow[:]], 0, 512, pn0=0)
                emit_mm_group(P, PS_A[:, 512:1024], [b2_sb[:, m * 128:(m + 1) * 128]], [wrow[:]], 512, 512, pn0=0)
                P.new_stage(f"b2_ev{m}")
                vec_tt(P, out_sb[m][:, 0:512], out_sb[m][:, 0:512], PS_A[:, 0:512], ALU.add)
                vec_tt(P, out_sb[m][:, 512:1024], out_sb[m][:, 512:1024], PS_A[:, 512:1024], ALU.add)
            P.new_stage("store")
            for i in range(DT):
                P.dma("sync", xout[i * 128:(i + 1) * 128, t0:t0 + TH], out_sb[i][:])
        P.finalize()
    return nc


def t1_inputs(x_fm, yg_fm, ins):
    w_router = np.asarray(ins["L0_ffn_w_router"])
    b1 = np.asarray(ins["L0_ffn_b1"])
    b1c = np.ascontiguousarray(b1.reshape(NEXP, FFT, 128).transpose(2, 0, 1).reshape(128, NEXP * FFT))
    sel = np.zeros((NEXP, NEXP * 128), np.float32)
    for e in range(NEXP):
        sel[e, e * 128:(e + 1) * 128] = 1.0
    maps = []
    for c in range(NCORES):
        ts = slice(c * T_C, (c + 1) * T_C)
        maps.append({
            "xin": np.ascontiguousarray(x_fm[:, ts]),
            "ygin": np.ascontiguousarray(yg_fm[:, ts]),
            "wout": np.asarray(ins["L0_attn_W_out"]),
            "wr": w_router,
            "w1": np.asarray(ins["L0_ffn_w1"]),
            "w2": np.asarray(ins["L0_ffn_w2"]),
            "b1c": b1c,
            "b2": np.asarray(ins["L0_ffn_b2"]),
            "sel8": sel,
            "ident": IDENT,
            "ones_col": ONES_COL, "ones_row": ONES_ROW,
        })
    return maps


def run_T1(x_fm, yg_fm, ins):
    nc = build_T1()
    res = run_bass_kernel_spmd(nc, t1_inputs(x_fm, yg_fm, ins), core_ids=list(range(NCORES)))
    return np.concatenate([r["xout"] for r in res.results], axis=1)



SHIFT_C = 20.0
M1_DEBUG = False
NH_C = 8
HD = 64


def nc_psum(nc, E, name, shape):
    return E(nc.psum_tensor(name, shape, F32))


def build_M1():
    nc = bass.Bass("TRN2", target_bir_lowering=False)
    ap = lambda n, s, d=F32, k="ExternalInput": nc.dram_tensor(n, s, d, kind=k).ap()
    xb_in = ap("xb_in", [D, S])
    wq = ap("wq", [D, 512])
    wk = ap("wk", [D, 512])
    wv = ap("wv", [D, 512])
    bq = ap("bq", [128, 4])
    bk = ap("bk", [128, 4])
    bv_r = ap("bv_r", [1, 512])
    cosm = ap("cosm", [32, S])
    sinm = ap("sinm", [32, S])
    maskd = ap("maskd", [128, 128])
    ident = ap("ident", [128, 128])
    attn_o = ap("attn_o", [512, S], BF16, "ExternalOutput")
    if M1_DEBUG:
        d_q = ap("d_q", [128, 512], BF16, "ExternalOutput")
        d_k = ap("d_k", [128, 512], BF16, "ExternalOutput")
        d_v = ap("d_v", [128, 65], BF16, "ExternalOutput")
        d_st = ap("d_st", [128, 512], BF16, "ExternalOutput")
    TCn = 512

    with contextlib.ExitStack() as ctx:
        E = ctx.enter_context
        P = Prog(nc)
        q_bf = [E(nc.sbuf_tensor(f"q{m}", [128, S], BF16)) for m in range(4)]
        k_bf = [E(nc.sbuf_tensor(f"k{m}", [128, S], BF16)) for m in range(4)]
        v_tm = [E(nc.sbuf_tensor(f"v{t}", [128, NH_C * 65], BF16)) for t in range(S // 128)]
        mask_sb = E(nc.sbuf_tensor("mask_sb", [128, 128], BF16))
        id_sb = E(nc.sbuf_tensor("id_sb", [128, 128], F32))
        bcols = E(nc.sbuf_tensor("bcols", [128, 8], F32))
        bvrow = E(nc.sbuf_tensor("bvrow", [1, 512], F32))
        st_bf = E(nc.sbuf_tensor("st_bf", [128, 512], BF16))
        ot_tm = E(nc.sbuf_tensor("ot_tm", [128, 512], F32))
        rec_c = E(nc.sbuf_tensor("rec_c", [128, 4], F32))
        oh_fm = E(nc.sbuf_tensor("oh_fm", [64, S], BF16))

        wq_sb = E(nc.sbuf_tensor("wq_sb", [128, DT * 512], BF16))
        wk_sb = E(nc.sbuf_tensor("wk_sb", [128, DT * 512], BF16))
        wv_sb = E(nc.sbuf_tensor("wv_sb", [128, DT * 512], BF16))
        P.new_stage("m1_const")
        for k in range(DT):
            P.dma("gpsimd", wq_sb[:, k * 512:(k + 1) * 512], wq[k * 128:(k + 1) * 128, :])
            P.dma("gpsimd", wk_sb[:, k * 512:(k + 1) * 512], wk[k * 128:(k + 1) * 128, :])
            P.dma("gpsimd", wv_sb[:, k * 512:(k + 1) * 512], wv[k * 128:(k + 1) * 128, :])
        P.dma("gpsimd", mask_sb[:], maskd)
        P.dma("sync", id_sb[:], ident)
        P.dma("sync", bcols[:, 0:4], bq)
        P.dma("sync", bcols[:, 4:8], bk)
        P.dma("sync", bvrow[:], bv_r)
        P.new_stage("m1_vones")
        for t in range(S // 128):
            P.on("vector", lambda v, t=t: v.memset(v_tm[t][:], 0.0))

        with contextlib.ExitStack() as sctx:
            Es = sctx.enter_context
            xt = [Es(nc.sbuf_tensor(f"pjx{i}", [128, TCn], F32)) for i in range(DT)]
            sq = [Es(nc.sbuf_tensor(f"pjs{i}", [128, TCn], F32)) for i in range(DT)]
            xn2 = [Es(nc.sbuf_tensor(f"pjn{i}", [128, TCn], BF16)) for i in range(DT)]
            mu_b = Es(nc.sbuf_tensor("pjmu", [128, TCn], F32))
            rstd_b = Es(nc.sbuf_tensor("pjrs", [128, TCn], F32))
            mu_row = Es(nc.sbuf_tensor("pjmr", [1, TCn], F32))
            sq_row = Es(nc.sbuf_tensor("pjsr", [1, TCn], F32))
            qf4 = [Es(nc.sbuf_tensor(f"pjqf{m}", [128, TCn], F32)) for m in range(1)]
            rt_a = Es(nc.sbuf_tensor("pjra", [32, TCn], F32))
            rt_b = Es(nc.sbuf_tensor("pjrb", [32, TCn], F32))
            rt_c = Es(nc.sbuf_tensor("pjrc", [32, TCn], F32))
            rt_d = Es(nc.sbuf_tensor("pjrd", [32, TCn], F32))
            cos_sb = Es(nc.sbuf_tensor("cos_sb", [32, TCn], F32))
            sin_sb = Es(nc.sbuf_tensor("sin_sb", [32, TCn], F32))
            onescol = Es(nc.sbuf_tensor("pjoc", [128, 1], F32))
            onesrow = Es(nc.sbuf_tensor("pjor", [1, 128], F32))
            PS_A = nc_psum(nc, Es, "pjA", [128, 1024])
            PS_B = nc_psum(nc, Es, "pjB", [128, 1024])
            PS_R0 = nc_psum(nc, Es, "pjR0", [1, 512])
            PS_R1 = nc_psum(nc, Es, "pjR1", [1, 512])
            P.new_stage("pj_const")
            P.on("vector", lambda v: v.memset(onescol[:], 1.0))
            P.new_stage("pj_const2")
            vec_memset(P, onesrow[:], 1.0)
            vdrain(P)
            for ci, t0 in enumerate(range(0, S, TCn)):
                P.new_stage("pj_load")
                for i in range(DT):
                    P.dma("sync", xt[i][:], xb_in[i * 128:(i + 1) * 128, t0:t0 + TCn])
                P.dma("sync", cos_sb[:], cosm[:, t0:t0 + TCn])
                P.dma("sync", sin_sb[:], sinm[:, t0:t0 + TCn])
                emit_ln(P, xt, xn2, sq, mu_b, rstd_b, (PS_R0, PS_R1), PS_A,
                        mu_row, sq_row, onescol[:], onesrow[:], TCn)
                for which, w_in, dst, bofs in (("q", wq_sb, q_bf, 0), ("k", wk_sb, k_bf, 4)):
                    for m in range(4):
                        P.new_stage(f"pj_{which}mm{m}")
                        pp = PS_B[:, (m % 2) * 512:(m % 2) * 512 + 512]
                        emit_mm_group(P, pp, [w_in[:, k * 512 + m * 128:k * 512 + (m + 1) * 128] for k in range(DT)],
                                      [xn2[k][:] for k in range(DT)], 0, TCn)
                        P.new_stage(f"pj_{which}ev{m}")
                        vec_ts(P, qf4[0][:], pp[:, 0:TCn], bcols[:, bofs + m:bofs + m + 1], None, ALU.add)
                        for hb in (0, 64):
                            P.new_stage(f"pj_{which}r1{m}{hb}")
                            vec_copy(P, rt_a[:], qf4[0][hb:hb + 32, :])
                            vec_copy(P, rt_b[:], qf4[0][hb + 32:hb + 64, :])
                            P.new_stage(f"pj_{which}r2{m}{hb}")
                            vec_tt(P, rt_c[:], rt_a[:], cos_sb[0:32, :], ALU.mult)
                            vec_tt(P, rt_d[:], rt_b[:], sin_sb[0:32, :], ALU.mult)
                            P.new_stage(f"pj_{which}r3{m}{hb}")
                            vec_tt(P, rt_c[:], rt_c[:], rt_d[:], ALU.subtract)
                            vec_tt(P, rt_d[:], rt_a[:], sin_sb[0:32, :], ALU.mult)
                            P.new_stage(f"pj_{which}r4{m}{hb}")
                            vec_tt(P, rt_b[:], rt_b[:], cos_sb[0:32, :], ALU.mult)
                            P.new_stage(f"pj_{which}r5{m}{hb}")
                            vec_tt(P, rt_d[:], rt_d[:], rt_b[:], ALU.add)
                            P.new_stage(f"pj_{which}r6{m}{hb}")
                            vec_copy(P, qf4[0][hb:hb + 32, :], rt_c[:])
                            vec_copy(P, qf4[0][hb + 32:hb + 64, :], rt_d[:])
                        P.new_stage(f"pj_{which}cp{m}")
                        vec_copy(P, dst[m][:, t0:t0 + TCn], qf4[0][:])
                for tb in range(TCn // 128):
                    tg = (t0 + tb * 128) // 128
                    P.new_stage(f"pj_vmm{tb}")
                    pvof = lambda h: h * 65 if h < 4 else 512 + (h - 4) * 65
                    for h in range(NH_C):
                        for k in range(DT):
                            def fn(t, tb=tb, h=h, k=k, pvof=pvof):
                                return t.matmul(PS_B[:, pvof(h):pvof(h) + 64],
                                                xn2[k][:, tb * 128:(tb + 1) * 128],
                                                wv_sb[:, k * 512 + h * 64:k * 512 + (h + 1) * 64],
                                                start=(k == 0), stop=(k == DT - 1))
                            P.on("tensor", fn)
                    P.new_stage(f"pj_vev{tb}")
                    for h in range(NH_C):
                        act(P, v_tm[tg][:, h * 65:h * 65 + 64], PS_B[:, pvof(h):pvof(h) + 64], AF.Copy)

        with contextlib.ExitStack() as sctx:
            Es = sctx.enter_context
            onesr = Es(nc.sbuf_tensor("m1_onesr", [1, 128], F32))
            bvb = Es(nc.sbuf_tensor("m1_bvb", [128, NH_C * 65], F32))
            PS_V = nc_psum(nc, Es, "m1psv", [128, 1024])
            P.new_stage("m1_fixv0")
            vec_memset(P, onesr[:], 1.0)
            vdrain(P)
            P.new_stage("m1_fixv1")
            pvof2 = lambda h: h * 65 if h < 4 else 512 + (h - 4) * 65
            for h in range(NH_C):
                def fn(t, h=h):
                    return t.matmul(PS_V[:, pvof2(h):pvof2(h) + 64], onesr[0:1, :],
                                    bvrow[0:1, h * 64:(h + 1) * 64], start=True, stop=True)
                P.on("tensor", fn)
            P.new_stage("m1_fixv2")
            for h in range(NH_C):
                vec_copy(P, bvb[:, h * 65:h * 65 + 64], PS_V[:, pvof2(h):pvof2(h) + 64])
            P.new_stage("m1_fixv3")
            for h in range(NH_C):
                vec_memset(P, bvb[:, h * 65 + 64:h * 65 + 65], 1.0)
            P.new_stage("m1_fixv4")
            for t in range(S // 128):
                vec_tt(P, v_tm[t][:], v_tm[t][:], bvb[:], ALU.add)

        if M1_DEBUG:
            P.new_stage("m1_dbg")
            P.dma("sync", d_q[0:128, :], q_bf[0][:, 0:512])
            P.dma("sync", d_k[0:128, :], k_bf[0][:, 0:512])
            P.dma("sync", d_v, v_tm[0][:, 455:520])
        with contextlib.ExitStack() as sctx:
            Es = sctx.enter_context
            PS_S = nc_psum(nc, Es, "m1pss", [128, 1024])
            PS_P = nc_psum(nc, Es, "m1psp", [128, 512])
            PS_T = nc_psum(nc, Es, "m1pst", [128, 512])
            for h in range(NH_C):
                qt = q_bf[h // 2]
                kt_ = k_bf[h // 2]
                hb = (h % 2) * 64
                for band in range(S // 512):
                    nkt = 4 * band + 4
                    for kti in range(nkt):
                        pp = PS_S[:, (kti % 2) * 512:(kti % 2) * 512 + 512]
                        P.new_stage(f"at_s{h}_{band}_{kti}")
                        def fn(t, pp=pp, kti=kti, band=band, qt=qt, kt_=kt_, hb=hb):
                            return t.matmul(pp, kt_[hb:hb + 64, kti * 128:(kti + 1) * 128],
                                            qt[hb:hb + 64, band * 512:(band + 1) * 512],
                                            start=True, stop=True)
                        P.on("tensor", fn)
                        P.new_stage(f"at_e{h}_{band}_{kti}")
                        act(P, st_bf[:], pp, AF.Exp, scale=0.125)
                        dq = kti - 4 * band
                        if dq >= 0:
                            P.new_stage(f"at_m{h}_{band}_{kti}")
                            vec_tt(P, st_bf[:, dq * 128:(dq + 1) * 128],
                                   st_bf[:, dq * 128:(dq + 1) * 128], mask_sb[:], ALU.mult)
                        P.new_stage(f"at_p{h}_{band}_{kti}")
                        for qb in range(4):
                            if 4 * band + qb < kti:
                                continue
                            def fn(t, qb=qb, kti=kti, band=band, h=h, nkt=nkt):
                                return t.matmul(PS_P[:, qb * 128:qb * 128 + 65],
                                                st_bf[:, qb * 128:(qb + 1) * 128],
                                                v_tm[kti][:, h * 65:h * 65 + 65],
                                                start=(kti == 0 and qb == 0), stop=(kti == 4 * band + qb))
                            P.on("tensor", fn)
                    if M1_DEBUG and h == 7 and band == 0:
                        P.new_stage("m1_dbgst")
                        P.dma("sync", d_st, st_bf[:])
                    P.new_stage(f"at_r{h}_{band}")
                    for qb in range(4):
                        P.on("vector", lambda v, qb=qb: v.reciprocal(
                            rec_c[:, qb:qb + 1], PS_P[:, qb * 128 + 64:qb * 128 + 65]))
                    vdrain(P)
                    P.new_stage(f"at_n{h}_{band}")
                    for qb in range(4):
                        vec_tt(P, ot_tm[:, qb * 128:qb * 128 + 64], PS_P[:, qb * 128:qb * 128 + 64],
                               rec_c[:, qb:qb + 1].broadcast_to([128, 64]), ALU.mult)
                    vdrain(P)
                    P.new_stage(f"at_t{h}_{band}")
                    for qb in range(4):
                        def fn(t, qb=qb):
                            return t.transpose(PS_T[0:64, qb * 128:(qb + 1) * 128],
                                               ot_tm[:, qb * 128:qb * 128 + 64], id_sb[:])
                        P.on("tensor", fn)
                    P.new_stage(f"at_o{h}_{band}")
                    vec_copy(P, oh_fm[:, band * 512:(band + 1) * 512], PS_T[0:64, 0:512])
                P.new_stage(f"at_st{h}")
                P.dma("sync", attn_o[h * 64:(h + 1) * 64, :], oh_fm[:])
        P.finalize()
    return nc


def m1_inputs(x_fm, ins):
    wq = np.asarray(ins["L1_attn_wq"])
    wk = np.asarray(ins["L1_attn_wk"])
    wv = np.asarray(ins["L1_attn_wv"])
    bq = np.asarray(ins["L1_attn_bq"])
    bk = np.asarray(ins["L1_attn_bk"])
    bv = np.asarray(ins["L1_attn_bv"])
    pos = np.arange(S, dtype=np.float64)[:, None]
    dim = np.arange(32, dtype=np.float64)[None, :]
    theta = pos / np.exp(2.0 * dim / 64 * math.log(10000.0))
    cosm = np.ascontiguousarray(np.cos(theta).T.astype(np.float32))
    sinm = np.ascontiguousarray(np.sin(theta).T.astype(np.float32))
    maskd = np.triu(np.ones((128, 128), np.float32))
    maps = []
    for c in range(NCORES):
        b, hh = c // 2, c % 2
        hsl = slice(hh * 8 * 64, (hh + 1) * 8 * 64)
        wqT = wq.T[:, hsl]
        wkT = wk.T[:, hsl]
        wvT = wv.T[:, hsl]
        pq = np.concatenate(
            [np.concatenate([np.arange(h * 64, (h + 1) * 64, 2),
                             np.arange(h * 64 + 1, (h + 1) * 64, 2)]) for h in range(8)])
        wq_c = np.ascontiguousarray(wqT[:, pq])
        wk_c = np.ascontiguousarray(wkT[:, pq])
        bq_c = np.ascontiguousarray(bq[hsl][pq].reshape(4, 128).T)
        bk_c = np.ascontiguousarray(bk[hsl][pq].reshape(4, 128).T)
        maps.append({
            "xb_in": np.ascontiguousarray(x_fm[:, b * S:(b + 1) * S]),
            "wq": wq_c, "wk": wk_c, "wv": np.ascontiguousarray(wvT),
            "bq": bq_c, "bk": bk_c,
            "bv_r": np.ascontiguousarray(bv[hsl].reshape(1, 512)),
            "cosm": cosm, "sinm": sinm, "maskd": maskd, "ident": IDENT,
        })
    return maps


def run_M1(x_fm, ins):
    nc = build_M1()
    res = run_bass_kernel_spmd(nc, m1_inputs(x_fm, ins), core_ids=list(range(NCORES)))
    out = np.zeros((D, T_ALL), np.float32)
    import ml_dtypes
    for c in range(NCORES):
        b, hh = c // 2, c % 2
        a = np.asarray(res.results[c]["attn_o"])
        if a.dtype != np.float32:
            a = a.view(ml_dtypes.bfloat16).astype(np.float32) if a.dtype == np.uint16 else a.astype(np.float32)
        out[hh * 512:(hh + 1) * 512, b * S:(b + 1) * S] = a
    return out



def build_T2():
    nc = bass.Bass("TRN2", target_bir_lowering=False)
    ap = lambda n, s, d=F32, k="ExternalInput": nc.dram_tensor(n, s, d, kind=k).ap()
    xin = ap("xin", [D, T_C + 1])
    attn_in = ap("attn_in", [D, T_C], BF16)
    woT = ap("woT", [D, D])
    wrT = ap("wrT", [D, D])
    wkT = ap("wkT", [D, D])
    wvT = ap("wvT", [D, D])
    bo_c = ap("bo_c", [128, DT])
    mur_c = ap("mur_c", [128, DT])
    muk_c = ap("muk_c", [128, DT])
    xout = ap("xout", [D, T_C], F32, "ExternalOutput")
    TH = 1024

    with contextlib.ExitStack() as ctx:
        E = ctx.enter_context
        P = Prog(nc)
        out_sb = [E(nc.sbuf_tensor(f"t2o{i}", [128, TH], F32)) for i in range(DT)]
        xr_bf = [E(nc.sbuf_tensor(f"t2xr{i}", [128, TH], BF16)) for i in range(DT)]
        xk_bf = [E(nc.sbuf_tensor(f"t2xk{i}", [128, TH], BF16)) for i in range(DT)]
        r_bf = [E(nc.sbuf_tensor(f"t2r{i}", [128, TH], BF16)) for i in range(DT)]
        ka_bf = [E(nc.sbuf_tensor(f"t2ka{i}", [128, TH], BF16)) for i in range(DT)]
        cols = E(nc.sbuf_tensor("t2cols", [128, 4 * DT], F32))
        bo_sb = cols[:, 0:DT]
        mur_sb = cols[:, DT:2 * DT]
        muk_sb = cols[:, 2 * DT:3 * DT]
        om_sb = E(nc.sbuf_tensor("t2om", [128, 2 * DT], F32))
        PS_A = E(nc.psum_tensor("t2psA", [128, 1024], F32))
        PS_B = E(nc.psum_tensor("t2psB", [128, 1024], F32))
        PS_R0 = E(nc.psum_tensor("t2psR0", [1, 512], F32))
        PS_R1 = E(nc.psum_tensor("t2psR1", [1, 512], F32))

        P.new_stage("t2_cols")
        P.dma("sync", bo_sb, bo_c)
        P.dma("sync", mur_sb, mur_c)
        P.dma("sync", muk_sb, muk_c)
        P.new_stage("t2_om")
        vec_ts(P, om_sb[:, 0:DT], mur_sb, -1.0, 1.0, ALU.mult, ALU.add)
        vec_ts(P, om_sb[:, DT:2 * DT], muk_sb, -1.0, 1.0, ALU.mult, ALU.add)
        vdrain(P)

        for hh in range(2):
            t0 = hh * TH
            W = TH + 1
            with contextlib.ExitStack() as sctx:
                Es = sctx.enter_context
                xt = [Es(nc.sbuf_tensor(f"t2x{hh}_{i}", [128, W], F32)) for i in range(DT)]
                sq = [Es(nc.sbuf_tensor(f"t2s{hh}_{i}", [128, W], F32)) for i in range(DT)]
                mu_b = Es(nc.sbuf_tensor(f"t2mu{hh}", [128, W], F32))
                rstd_b = Es(nc.sbuf_tensor(f"t2rs{hh}", [128, W], F32))
                mu_row = Es(nc.sbuf_tensor(f"t2mr{hh}", [1, W], F32))
                sq_row = Es(nc.sbuf_tensor(f"t2sr{hh}", [1, W], F32))
                onescol = Es(nc.sbuf_tensor(f"t2oc{hh}", [128, 1], F32))
                onesrow = Es(nc.sbuf_tensor(f"t2or{hh}", [1, 128], F32))
                P.new_stage("t2_c2")
                P.on("vector", lambda v, onescol=onescol: v.memset(onescol[:], 1.0))
                P.new_stage("t2_c3")
                vec_memset(P, onesrow[:], 1.0)
                vdrain(P)
                P.new_stage("t2_xload")
                for i in range(DT):
                    P.dma("sync", xt[i][:], xin[i * 128:(i + 1) * 128, t0:t0 + W])
                emit_ln(P, xt, sq, sq, mu_b, rstd_b, (PS_R0, PS_R1), PS_A,
                        mu_row, sq_row, onescol[:], onesrow[:], W)
                P.new_stage("t2_base")
                for i in range(DT):
                    vec_copy(P, out_sb[i][:], xt[i][:, 1:1 + TH])
                P.new_stage("t2_mix1")
                for i in range(DT):
                    vec_ts(P, mu_b[:, 0:TH], sq[i][:, 1:1 + TH], mur_sb[:, i:i + 1], None, ALU.mult)
                    vec_stt(P, xr_bf[i][:], sq[i][:, 0:TH], om_sb[:, i:i + 1], mu_b[:, 0:TH],
                            ALU.mult, ALU.add)
                    vec_ts(P, mu_b[:, 0:TH], sq[i][:, 1:1 + TH], muk_sb[:, i:i + 1], None, ALU.mult)
                    vec_stt(P, xk_bf[i][:], sq[i][:, 0:TH], om_sb[:, DT + i:DT + i + 1], mu_b[:, 0:TH],
                            ALU.mult, ALU.add)
            with contextlib.ExitStack() as sctx:
                Es = sctx.enter_context
                w_sb = Es(nc.sbuf_tensor(f"t2w{hh}", [128, DT * 1024], BF16))
                attn_bf = [Es(nc.sbuf_tensor(f"t2a{hh}_{i}", [128, TH], BF16)) for i in range(DT)]
                tmp = Es(nc.sbuf_tensor(f"t2tmp{hh}", [128, 512], F32))
                for wname, w_in, rhs_t, dstl, kind in (
                        ("r", wrT, xr_bf, r_bf, "sig"),
                        ("k", wkT, xk_bf, ka_bf, "relsq"),
                        ("v", wvT, ka_bf, None, "vout"),
                        ("ao", woT, attn_bf, None, "attn")):
                    P.new_stage(f"t2_{wname}wl")
                    for k in range(DT):
                        P.dma("gpsimd", w_sb[:, k * 1024:(k + 1) * 1024],
                              w_in[k * 128:(k + 1) * 128, :])
                    if kind == "attn":
                        for i in range(DT):
                            P.dma("gpsimd", attn_bf[i][:], attn_in[i * 128:(i + 1) * 128, t0:t0 + TH])
                    for m in range(DT):
                        for j in range(2):
                            P.new_stage(f"t2_{wname}mm{m}{j}")
                            pp = PS_B[:, (j % 2) * 512:(j % 2) * 512 + 512]
                            emit_mm_group(P, pp, [w_sb[:, k * 1024 + m * 128:k * 1024 + (m + 1) * 128]
                                                  for k in range(DT)],
                                          [rhs_t[k][:, j * 512:(j + 1) * 512] for k in range(DT)], 0, 512)
                            P.new_stage(f"t2_{wname}ev{m}{j}")
                            sl = slice(j * 512, (j + 1) * 512)
                            if kind == "sig":
                                act(P, dstl[m][:, sl], pp[:, 0:512], AF.Sigmoid)
                            elif kind == "relsq":
                                act(P, tmp[:], pp[:, 0:512], AF.Relu)
                                P.new_stage(f"t2_{wname}ev2{m}{j}")
                                act(P, dstl[m][:, sl], tmp[:], AF.Square)
                            elif kind == "vout":
                                vec_tt(P, tmp[:], pp[:, 0:512], r_bf[m][:, sl], ALU.mult)
                                P.new_stage(f"t2_{wname}ac{m}{j}")
                                vec_tt(P, out_sb[m][:, sl], out_sb[m][:, sl], tmp[:], ALU.add)
                            elif kind == "attn":
                                vec_ts(P, tmp[:], pp[:, 0:512], bo_sb[:, m:m + 1], None, ALU.add)
                                P.new_stage(f"t2_{wname}ac{m}{j}")
                                vec_tt(P, out_sb[m][:, sl], out_sb[m][:, sl], tmp[:], ALU.add)
            P.new_stage("t2_store")
            for i in range(DT):
                P.dma("sync", xout[i * 128:(i + 1) * 128, t0:t0 + TH], out_sb[i][:])
        P.finalize()
    return nc


def t2_inputs(x_fm, attn_fm_bf, ins):
    woT = np.ascontiguousarray(np.asarray(ins["L1_attn_wo"]).T)
    wrT = np.ascontiguousarray(np.asarray(ins["L1_ffn_W_r"]).T)
    wkT = np.ascontiguousarray(np.asarray(ins["L1_ffn_W_k"]).T)
    wvT = np.ascontiguousarray(np.asarray(ins["L1_ffn_W_v"]).T)
    cz = lambda a: np.ascontiguousarray(np.asarray(a).reshape(DT, 128).T)
    maps = []
    for c in range(NCORES):
        t0 = c * T_C
        if c % 2 == 0:
            prev = np.zeros((D, 1), np.float32)
        else:
            prev = x_fm[:, t0 - 1:t0]
        xin = np.ascontiguousarray(np.concatenate([prev, x_fm[:, t0:t0 + T_C]], axis=1))
        maps.append({
            "xin": xin,
            "attn_in": np.ascontiguousarray(attn_fm_bf[:, t0:t0 + T_C]),
            "woT": woT, "wrT": wrT, "wkT": wkT, "wvT": wvT,
            "bo_c": cz(ins["L1_attn_bo"]),
            "mur_c": cz(ins["L1_ffn_mu_r"]),
            "muk_c": cz(ins["L1_ffn_mu_k"]),
        })
    return maps


def run_T2(x_fm, attn_fm, ins):
    import ml_dtypes
    nc = build_T2()
    attn_bf = attn_fm.astype(ml_dtypes.bfloat16) if attn_fm.dtype == np.float32 else attn_fm
    res = run_bass_kernel_spmd(nc, t2_inputs(x_fm, attn_bf, ins), core_ids=list(range(NCORES)))
    return np.concatenate([r["xout"] for r in res.results], axis=1)



L2EPS = 0.001


def build_M2():
    nc = bass.Bass("TRN2", target_bir_lowering=False)
    ap = lambda n, s, d=F32, k="ExternalInput": nc.dram_tensor(n, s, d, kind=k).ap()
    xb_in = ap("xb_in", [D, S])
    wq = ap("wq", [D, 512])
    wk = ap("wk", [D, 512])
    wv = ap("wv", [D, 512])
    maskd = ap("maskd", [128, 128])
    ident = ap("ident", [128, 128])
    ident64 = ap("ident64", [128, 64])
    attn_o = ap("attn_o", [512, S], BF16, "ExternalOutput")
    TCn = 512

    with contextlib.ExitStack() as ctx:
        E = ctx.enter_context
        P = Prog(nc)
        q_bf = [E(nc.sbuf_tensor(f"m2q{m}", [128, S], BF16)) for m in range(4)]
        k_bf = [E(nc.sbuf_tensor(f"m2k{m}", [128, S], BF16)) for m in range(4)]
        v_tm = [E(nc.sbuf_tensor(f"m2v{t}", [128, NH_C * 65], BF16)) for t in range(S // 128)]
        ktb = E(nc.sbuf_tensor("m2ktb", [128, 64], BF16))
        mask_sb = E(nc.sbuf_tensor("m2mask", [128, 128], BF16))
        id_sb = E(nc.sbuf_tensor("m2id", [128, 128], F32))
        id_bf = E(nc.sbuf_tensor("m2idb", [128, 64], BF16))
        s_f = E(nc.sbuf_tensor("m2sf", [64, 65], F32))
        s_b = E(nc.sbuf_tensor("m2sb", [128, 65], BF16))
        abuf = E(nc.sbuf_tensor("m2ab", [128, 128], BF16))
        ot_tm = E(nc.sbuf_tensor("m2ot", [128, 64], F32))
        rec_c = E(nc.sbuf_tensor("m2rc", [128, 4], F32))
        oh_fm = E(nc.sbuf_tensor("m2oh", [64, S], BF16))
        wq_sb = E(nc.sbuf_tensor("m2wq", [128, DT * 512], BF16))
        wk_sb = E(nc.sbuf_tensor("m2wk", [128, DT * 512], BF16))
        wv_sb = E(nc.sbuf_tensor("m2wv", [128, DT * 512], BF16))

        P.new_stage("m2_const")
        P.dma("gpsimd", mask_sb[:], maskd)
        P.dma("sync", id_sb[:], ident)
        P.dma("gpsimd", id_bf[:], ident64)
        for k in range(DT):
            P.dma("gpsimd", wq_sb[:, k * 512:(k + 1) * 512], wq[k * 128:(k + 1) * 128, :])
            P.dma("gpsimd", wk_sb[:, k * 512:(k + 1) * 512], wk[k * 128:(k + 1) * 128, :])
            P.dma("gpsimd", wv_sb[:, k * 512:(k + 1) * 512], wv[k * 128:(k + 1) * 128, :])

        with contextlib.ExitStack() as sctx:
            Es = sctx.enter_context
            xt = [Es(nc.sbuf_tensor(f"m2x{i}", [128, TCn], F32)) for i in range(DT)]
            sq = [Es(nc.sbuf_tensor(f"m2s{i}", [128, TCn], F32)) for i in range(DT)]
            xn2 = [Es(nc.sbuf_tensor(f"m2n{i}", [128, TCn], BF16)) for i in range(DT)]
            mu_b = Es(nc.sbuf_tensor("m2mu", [128, TCn], F32))
            rstd_b = Es(nc.sbuf_tensor("m2rs", [128, TCn], F32))
            mu_row = Es(nc.sbuf_tensor("m2mr", [1, TCn], F32))
            sq_row = Es(nc.sbuf_tensor("m2sr", [1, TCn], F32))
            onescol = Es(nc.sbuf_tensor("m2oc", [128, 1], F32))
            onesrow = Es(nc.sbuf_tensor("m2or", [1, 128], F32))
            PS_A = nc_psum(nc, Es, "m2A", [128, 1024])
            PS_B = nc_psum(nc, Es, "m2B", [128, 1024])
            PS_R0 = nc_psum(nc, Es, "m2R0", [1, 512])
            PS_R1 = nc_psum(nc, Es, "m2R1", [1, 512])
            P.new_stage("m2_c1")
            P.on("vector", lambda v: v.memset(onescol[:], 1.0))
            P.new_stage("m2_c2")
            vec_memset(P, onesrow[:], 1.0)
            vdrain(P)
            for ci, t0 in enumerate(range(0, S, TCn)):
                P.new_stage("m2_load")
                for i in range(DT):
                    P.dma("sync", xt[i][:], xb_in[i * 128:(i + 1) * 128, t0:t0 + TCn])
                emit_ln(P, xt, xn2, sq, mu_b, rstd_b, (PS_R0, PS_R1), PS_A,
                        mu_row, sq_row, onescol[:], onesrow[:], TCn)
                for which, w_in, dst in (("q", wq_sb, q_bf), ("k", wk_sb, k_bf)):
                    for m in range(4):
                        P.new_stage(f"m2_{which}mm{m}")
                        pp = PS_B[:, (m % 2) * 512:(m % 2) * 512 + 512]
                        emit_mm_group(P, pp, [w_in[:, k * 512 + m * 128:k * 512 + (m + 1) * 128] for k in range(DT)],
                                      [xn2[k][:] for k in range(DT)], 0, TCn)
                        P.new_stage(f"m2_{which}ev{m}")
                        vec_ts(P, dst[m][:, t0:t0 + TCn], pp[:, 0:TCn], 0.0, L2EPS, ALU.max, ALU.add)
                pvof = lambda h: h * 65 if h < 4 else 512 + (h - 4) * 65
                for tb in range(TCn // 128):
                    tg = (t0 + tb * 128) // 128
                    P.new_stage(f"m2_vt{tb}")
                    for h in range(NH_C):
                        for k in range(DT):
                            def fn(t, tb=tb, h=h, k=k, pvof=pvof):
                                return t.matmul(PS_B[:, pvof(h):pvof(h) + 64],
                                                xn2[k][:, tb * 128:(tb + 1) * 128],
                                                wv_sb[:, k * 512 + h * 64:k * 512 + (h + 1) * 64],
                                                start=(k == 0), stop=(k == DT - 1))
                            P.on("tensor", fn)
                    P.new_stage(f"m2_vtev{tb}")
                    for h in range(NH_C):
                        act(P, v_tm[tg][:, h * 65:h * 65 + 64], PS_B[:, pvof(h):pvof(h) + 64], AF.Copy)
        P.new_stage("m2_vaug")
        for t in range(S // 128):
            for h in range(NH_C):
                vec_memset(P, v_tm[t][:, h * 65 + 64:h * 65 + 65], 1.0)
        vdrain(P)

        with contextlib.ExitStack() as sctx:
            Es = sctx.enter_context
            PS_S = nc_psum(nc, Es, "m2pss", [128, 512])
            PS_P = nc_psum(nc, Es, "m2psp", [128, 512])
            PS_U = nc_psum(nc, Es, "m2psu", [128, 512])
            PS_T = nc_psum(nc, Es, "m2pst", [128, 512])
            PS_KT = Es(nc.psum_tensor("m2pskt", [128, 64], BF16))
            for h in range(NH_C):
                qf = q_bf[h // 2]
                kf = k_bf[h // 2]
                hb = (h % 2) * 64
                P.new_stage(f"m2_z{h}")
                vec_memset(P, s_f[:], 0.0)
                vec_memset(P, s_b[:], 0.0)
                vdrain(P)
                for ci in range(S // 128):
                    csl = slice(ci * 128, (ci + 1) * 128)
                    P.new_stage(f"m2a_{h}_{ci}")
                    def fn(t, qf=qf, kf=kf, hb=hb, csl=csl):
                        return t.matmul(PS_S[:, 0:128], kf[hb:hb + 64, csl], qf[hb:hb + 64, csl],
                                        start=True, stop=True)
                    P.on("tensor", fn)
                    P.new_stage(f"m2b_{h}_{ci}")
                    vec_tt(P, abuf[:], PS_S[:, 0:128], mask_sb[:], ALU.mult)
                    P.new_stage(f"m2c_{h}_{ci}")
                    def fn(t, ci=ci, h=h, qf=qf, hb=hb, csl=csl):
                        t.matmul(PS_P[:, 0:65], abuf[:], v_tm[ci][:, h * 65:h * 65 + 65],
                                 start=True, stop=False)
                        return t.matmul(PS_P[:, 0:65], qf[hb:hb + 64, csl], s_b[hb:hb + 64, :],
                                        start=False, stop=True)
                    P.on("tensor", fn)
                    P.new_stage(f"m2d_{h}_{ci}")
                    P.on("vector", lambda v, ci=ci: v.reciprocal(rec_c[:, 0:1], PS_P[:, 64:65]))
                    vdrain(P)
                    P.new_stage(f"m2e_{h}_{ci}")
                    vec_tt(P, ot_tm[:], PS_P[:, 0:64], rec_c[:, 0:1].broadcast_to([128, 64]), ALU.mult)
                    P.new_stage(f"m2f_{h}_{ci}")
                    def fn(t, ci=ci):
                        return t.transpose(PS_T[0:64, 0:128], ot_tm[:], id_sb[:])
                    P.on("tensor", fn)
                    P.new_stage(f"m2g_{h}_{ci}")
                    vec_copy(P, oh_fm[:, csl], PS_T[0:64, 0:128])
                    P.new_stage(f"m2h0_{h}_{ci}")
                    def fn(t, ci=ci, kf=kf, hb=hb):
                        return t.transpose(PS_KT[:], kf[hb:hb + 64, ci * 128:(ci + 1) * 128], id_bf[hb:hb + 64, :])
                    P.on("tensor", fn)
                    P.new_stage(f"m2h1_{h}_{ci}")
                    vec_copy(P, ktb[:], PS_KT[:])
                    vdrain(P)
                    P.new_stage(f"m2h_{h}_{ci}")
                    def fn(t, ci=ci, h=h):
                        return t.matmul(PS_U[0:64, 0:65], ktb[:],
                                        v_tm[ci][:, h * 65:h * 65 + 65], start=True, stop=True)
                    P.on("tensor", fn)
                    P.new_stage(f"m2i_{h}_{ci}")
                    vec_tt(P, s_f[:], s_f[:], PS_U[0:64, 0:65], ALU.add)
                    P.new_stage(f"m2j_{h}_{ci}")
                    vec_copy(P, s_b[0:64, :], s_f[:])
                    vec_copy(P, s_b[64:128, :], s_f[:])
                    vdrain(P)
                P.new_stage(f"m2st{h}")
                P.dma("sync", attn_o[h * 64:(h + 1) * 64, :], oh_fm[:])
        P.finalize()
    return nc


def m2_inputs(x_fm, ins):
    Wq = np.asarray(ins["L2_attn_Wq"])
    Wk = np.asarray(ins["L2_attn_Wk"])
    Wv = np.asarray(ins["L2_attn_Wv"])
    maskd = np.triu(np.ones((128, 128), np.float32))
    maps = []
    for c in range(NCORES):
        b, hh = c // 2, c % 2
        hsl = slice(hh * 512, (hh + 1) * 512)
        maps.append({
            "xb_in": np.ascontiguousarray(x_fm[:, b * S:(b + 1) * S]),
            "wq": np.ascontiguousarray(Wq[:, hsl]),
            "wk": np.ascontiguousarray(Wk[:, hsl]),
            "wv": np.ascontiguousarray(Wv[:, hsl]),
            "maskd": maskd, "ident": IDENT,
            "ident64": np.ascontiguousarray(np.vstack([np.eye(64, dtype=np.float32)] * 2)),
        })
    return maps


def run_M2(x_fm, ins):
    nc = build_M2()
    res = run_bass_kernel_spmd(nc, m2_inputs(x_fm, ins), core_ids=list(range(NCORES)))
    out = np.zeros((D, T_ALL), np.float32)
    for c in range(NCORES):
        b, hh = c // 2, c % 2
        out[hh * 512:(hh + 1) * 512, b * S:(b + 1) * S] = np.asarray(res.results[c]["attn_o"]).astype(np.float32)
    return out



T3_DEBUG = False


def build_T3():
    nc = bass.Bass("TRN2", target_bir_lowering=False)
    ap = lambda n, s, d=F32, k="ExternalInput": nc.dram_tensor(n, s, d, kind=k).ap()
    xin = ap("xin", [D, T_C])
    attn_in = ap("attn_in", [D, T_C], BF16)
    wo = ap("wo", [D, D])
    w1T = ap("w1T", [D, FF])
    w2T = ap("w2T", [FF, D])
    bo_c = ap("bo_c", [128, DT])
    b1_c = ap("b1_c", [128, FFT])
    b2_c = ap("b2_c", [128, DT])
    hm_col = ap("hm_col", [128, 2])
    onr = ap("ones_row", [1, 128])
    xout = ap("xout", [D, T_C], F32, "ExternalOutput")
    d_sc = ap("d_sc", [128, 8], F32, "ExternalOutput") if T3_DEBUG else None
    d_w1 = ap("d_w1", [128, FF], BF16, "ExternalOutput") if T3_DEBUG else None
    w1t_d = nc.dram_tensor("w1t_d", [D, FF], BF16).ap()
    w2t_d = nc.dram_tensor("w2t_d", [FF, D], BF16).ap()
    TH = 1024

    with contextlib.ExitStack() as ctx:
        E = ctx.enter_context
        P = Prog(nc)
        out_sb = [E(nc.sbuf_tensor(f"t3o{i}", [128, TH], F32)) for i in range(DT)]
        xn_bf = [E(nc.sbuf_tensor(f"t3n{i}", [128, TH], BF16)) for i in range(DT)]
        cols = E(nc.sbuf_tensor("t3cols", [128, 2 * DT + FFT + 8], F32))
        bo_sb = cols[:, 0:DT]
        b1_sb = cols[:, DT:DT + FFT]
        b2_sb = cols[:, DT + FFT:2 * DT + FFT]
        sc_sb = cols[:, 2 * DT + FFT:2 * DT + FFT + 8]
        hmc = E(nc.sbuf_tensor("t3hm", [128, 2], F32))
        onesrow = E(nc.sbuf_tensor("t3or", [1, 128], F32))
        srow = E(nc.sbuf_tensor("t3srow", [1, 2], F32))
        PS_A = E(nc.psum_tensor("t3psA", [128, 1024], F32))
        PS_B = E(nc.psum_tensor("t3psB", [128, 1024], F32))
        PS_R0 = E(nc.psum_tensor("t3psR0", [1, 512], F32))
        PS_R1 = E(nc.psum_tensor("t3psR1", [1, 512], F32))

        P.new_stage("t3_c")
        P.dma("sync", bo_sb, bo_c)
        P.dma("sync", cols[:, DT:DT + FFT], b1_c)
        P.dma("sync", cols[:, DT + FFT:2 * DT + FFT], b2_c)
        P.dma("sync", hmc[:], hm_col)
        P.dma("sync", onesrow[:], onr)

        with contextlib.ExitStack() as sctx:
            Es = sctx.enter_context
            wtmp = Es(nc.sbuf_tensor("t3wt", [128, FF], F32))
            wabs = Es(nc.sbuf_tensor("t3wa", [128, FF], F32))
            wsgn = Es(nc.sbuf_tensor("t3wg", [128, FF], F32))
            wbf = Es(nc.sbuf_tensor("t3wb", [128, FF], BF16))
            acc = Es(nc.sbuf_tensor("t3ac", [128, 64], F32))
            for wi, (w_in, wd, ntile, wid, thr_i) in enumerate(
                    ((w1T, w1t_d, DT, FF, 0), (w2T, w2t_d, FFT, D, 1))):
                for k in range(ntile):
                    P.new_stage(f"t3_s{wi}_{k}")
                    P.dma("sync", wtmp[:, 0:wid], w_in[k * 128:(k + 1) * 128, :])
                    P.new_stage(f"t3_s{wi}_{k}b")
                    act(P, wabs[:, 0:wid], wtmp[:, 0:wid], AF.Abs,
                        accum_out=acc[:, 8 + wi * 16 + k:8 + wi * 16 + k + 1])
                    sdrain(P)
                P.new_stage(f"t3_t{wi}")
                nt = ntile
                base = 8 + wi * 16
                step = 1
                while step < nt:
                    for j in range(0, nt, 2 * step):
                        if j + step < nt:
                            vec_tt(P, acc[:, base + j:base + j + 1], acc[:, base + j:base + j + 1],
                                   acc[:, base + j + step:base + j + step + 1], ALU.add)
                    vdrain(P)
                    P.new_stage(f"t3_t{wi}_{step}")
                    step *= 2
                vec_copy(P, acc[:, wi:wi + 1], acc[:, base:base + 1])
                vdrain(P)
                P.new_stage(f"t3_th{wi}")
                def fn(t, wi=wi, thr_i=thr_i):
                    return t.matmul(PS_R0[0:1, 0:1], hmc[:, thr_i:thr_i + 1],
                                    acc[:, wi:wi + 1], start=True, stop=True)
                P.on("tensor", fn)
                P.new_stage(f"t3_th{wi}b")
                vec_ts(P, srow[:, wi:wi + 1], PS_R0[0:1, 0:1], 5e-9, None, ALU.max)
                vdrain(P)
                P.new_stage(f"t3_th{wi}c")
                def fn(t, wi=wi):
                    return t.matmul(PS_R1[0:1, 0:128].rearrange("a b -> a b") if False else PS_A[:, 0:1],
                                    onesrow[0:1, :], srow[0:1, wi:wi + 1], start=True, stop=True)
                P.on("tensor", fn)
                P.new_stage(f"t3_th{wi}d")
                vec_copy(P, sc_sb[:, wi:wi + 1], PS_A[:, 0:1])
                vec_ts(P, sc_sb[:, 2 + wi:3 + wi], PS_A[:, 0:1], 2.0, None, ALU.mult)
                vdrain(P)
                for k in range(ntile):
                    P.new_stage(f"t3_z{wi}_{k}")
                    P.dma("sync", wtmp[:, 0:wid], w_in[k * 128:(k + 1) * 128, :])
                    P.new_stage(f"t3_z{wi}_{k}b")
                    act(P, wabs[:, 0:wid], wtmp[:, 0:wid], AF.Abs)
                    P.new_stage(f"t3_z{wi}_{k}c")
                    act(P, wsgn[:, 0:wid], wtmp[:, 0:wid], AF.Sign)
                    P.new_stage(f"t3_z{wi}_{k}d")
                    vec_tt(P, wabs[:, 0:wid], wabs[:, 0:wid],
                           sc_sb[:, wi:wi + 1].broadcast_to([128, wid]), ALU.is_gt)
                    P.new_stage(f"t3_z{wi}_{k}e")
                    vec_tt(P, wbf[:, 0:wid], wsgn[:, 0:wid], wabs[:, 0:wid], ALU.mult)
                    P.new_stage(f"t3_z{wi}_{k}f")
                    P.dma("sync", wd[k * 128:(k + 1) * 128, :], wbf[:, 0:wid])

        if T3_DEBUG:
            P.new_stage("t3_dbg")
            P.dma("sync", d_sc, cols[:, 2 * DT + FFT:2 * DT + FFT + 8])
            P.dma("sync", d_w1, w1t_d[0:128, :])
        for hh in range(2):
            t0 = hh * TH
            with contextlib.ExitStack() as sctx:
                Es = sctx.enter_context
                xt = [Es(nc.sbuf_tensor(f"t3x{hh}_{i}", [128, TH], F32)) for i in range(DT)]
                sq = [Es(nc.sbuf_tensor(f"t3s{hh}_{i}", [128, TH], F32)) for i in range(DT)]
                mu_b = Es(nc.sbuf_tensor(f"t3mu{hh}", [128, TH], F32))
                rstd_b = Es(nc.sbuf_tensor(f"t3rs{hh}", [128, TH], F32))
                mu_row = Es(nc.sbuf_tensor(f"t3mr{hh}", [1, TH], F32))
                sq_row = Es(nc.sbuf_tensor(f"t3sr{hh}", [1, TH], F32))
                onescol = Es(nc.sbuf_tensor(f"t3oc{hh}", [128, 1], F32))
                P.new_stage("t3_c2")
                P.on("vector", lambda v, onescol=onescol: v.memset(onescol[:], 1.0))
                P.new_stage("t3_xload")
                for i in range(DT):
                    P.dma("sync", xt[i][:], xin[i * 128:(i + 1) * 128, t0:t0 + TH])
                emit_ln(P, xt, xn_bf, sq, mu_b, rstd_b, (PS_R0, PS_R1), PS_A,
                        mu_row, sq_row, onescol[:], onesrow[:], TH)
                P.new_stage("t3_base")
                for i in range(DT):
                    vec_copy(P, out_sb[i][:], xt[i][:])
            with contextlib.ExitStack() as sctx:
                Es = sctx.enter_context
                w_sb = Es(nc.sbuf_tensor(f"t3w{hh}", [128, DT * 1024], BF16))
                attn_bf = [Es(nc.sbuf_tensor(f"t3a{hh}_{i}", [128, TH], BF16)) for i in range(DT)]
                tmp = Es(nc.sbuf_tensor(f"t3tmp{hh}", [128, 512], F32))
                P.new_stage("t3_wol")
                for k in range(DT):
                    P.dma("gpsimd", w_sb[:, k * 1024:(k + 1) * 1024], wo[k * 128:(k + 1) * 128, :])
                    P.dma("gpsimd", attn_bf[k][:], attn_in[k * 128:(k + 1) * 128, t0:t0 + TH])
                for m in range(DT):
                    for j in range(2):
                        P.new_stage(f"t3_aomm{m}{j}")
                        pp = PS_B[:, (j % 2) * 512:(j % 2) * 512 + 512]
                        emit_mm_group(P, pp, [w_sb[:, k * 1024 + m * 128:k * 1024 + (m + 1) * 128] for k in range(DT)],
                                      [attn_bf[k][:, j * 512:(j + 1) * 512] for k in range(DT)], 0, 512)
                        P.new_stage(f"t3_aoev{m}{j}")
                        sl = slice(j * 512, (j + 1) * 512)
                        vec_ts(P, tmp[:], pp[:, 0:512], bo_sb[:, m:m + 1], None, ALU.add)
                        P.new_stage(f"t3_aoac{m}{j}")
                        vec_tt(P, out_sb[m][:, sl], out_sb[m][:, sl], tmp[:], ALU.add)
            with contextlib.ExitStack() as sctx:
                Es = sctx.enter_context
                poolA = Es(nc.sbuf_tensor(f"t3pA{hh}", [128, DT * FF], BF16))
                poolB = Es(nc.sbuf_tensor(f"t3pB{hh}", [128, 2 * 8192], BF16))
                h_bf = Es(nc.sbuf_tensor(f"t3h{hh}", [128, FFT * 512], BF16))
                tmp = Es(nc.sbuf_tensor(f"t3tm2{hh}", [128, 512], F32))
                opsum = [PS_A[:, 0:512], PS_A[:, 512:1024], PS_B[:, 0:512], PS_B[:, 512:1024]]
                P.new_stage("t3_w1l")
                for k in range(DT):
                    P.dma("sync", poolA[:, k * FF:(k + 1) * FF], w1t_d[k * 128:(k + 1) * 128, :])
                for tci, tcs in enumerate(range(0, TH, 512)):
                    for ht in range(FFT):
                        P.new_stage(f"t3_h{hh}_{tci}_{ht}")
                        pp = PS_B[:, (ht % 2) * 512:(ht % 2) * 512 + 512]
                        emit_mm_group(P, pp, [poolA[:, k * FF + ht * 128:k * FF + ht * 128 + 128] for k in range(DT)],
                                      [xn_bf[k][:, tcs:tcs + 512] for k in range(DT)], 0, 512)
                        if ht == 0:
                            P.dma("gpsimd", poolB[:, 0:8192].rearrange("p (kk d) -> p kk d", d=1024),
                                  w2t_d[0:1024, :].rearrange("(kk p) d -> p kk d", p=128))
                        P.new_stage(f"t3_hb{hh}_{tci}_{ht}")
                        vec_ts(P, tmp[:], pp[:, 0:512], b1_sb[:, ht:ht + 1], None, ALU.add)
                        P.new_stage(f"t3_hg{hh}_{tci}_{ht}")
                        act(P, h_bf[:, ht * 512:(ht + 1) * 512], tmp[:], AF.Gelu)
                    for gi in range(4):
                        P.new_stage(f"t3_o{hh}_{tci}_{gi}")
                        for ki in range(8):
                            kk = gi * 8 + ki
                            for mm in range(4):
                                def fn(t, kk=kk, ki=ki, gi=gi, mm=mm, hh2=hh):
                                    return t.matmul(opsum[mm][:, 0:512],
                                                    poolB[:, (gi % 2) * 8192 + ki * 1024 + (mm + 0) * 128:
                                                          (gi % 2) * 8192 + ki * 1024 + (mm + 0) * 128 + 128],
                                                    h_bf[:, kk * 512:(kk + 1) * 512],
                                                    start=(kk == 0), stop=(kk == FFT - 1))
                                P.on("tensor", fn)
                        if gi + 1 < 4:
                            gl = gi + 1
                            P.dma("gpsimd", poolB[:, (gl % 2) * 8192:(gl % 2) * 8192 + 8192].rearrange("p (kk d) -> p kk d", d=1024),
                                  w2t_d[gl * 1024:(gl + 1) * 1024, :].rearrange("(kk p) d -> p kk d", p=128))
                    P.new_stage(f"t3_oe{hh}_{tci}")
                    for mm in range(4):
                        vec_ts(P, tmp[:], opsum[mm][:, 0:512], b2_sb[:, mm:mm + 1], None, ALU.add)
                        P.new_stage(f"t3_oe3{hh}_{tci}_{mm}")
                        vec_tt(P, out_sb[mm][:, tcs:tcs + 512], out_sb[mm][:, tcs:tcs + 512], tmp[:], ALU.add)
                        if mm < 3:
                            P.new_stage(f"t3_oe4{hh}_{tci}_{mm}")
                    P.new_stage(f"t3_o2pre{hh}_{tci}")
                    P.dma("gpsimd", poolB[:, 0:8192].rearrange("p (kk d) -> p kk d", d=1024),
                          w2t_d[0:1024, :].rearrange("(kk p) d -> p kk d", p=128))
                    for gi in range(4):
                        P.new_stage(f"t3_o2{hh}_{tci}_{gi}")
                        for ki in range(8):
                            kk = gi * 8 + ki
                            for mm in range(4):
                                def fn(t, kk=kk, ki=ki, gi=gi, mm=mm):
                                    return t.matmul(opsum[mm][:, 0:512],
                                                    poolB[:, (gi % 2) * 8192 + ki * 1024 + (mm + 4) * 128:
                                                          (gi % 2) * 8192 + ki * 1024 + (mm + 4) * 128 + 128],
                                                    h_bf[:, kk * 512:(kk + 1) * 512],
                                                    start=(kk == 0), stop=(kk == FFT - 1))
                                P.on("tensor", fn)
                        if gi + 1 < 4:
                            gl = gi + 1
                            P.dma("gpsimd", poolB[:, (gl % 2) * 8192:(gl % 2) * 8192 + 8192].rearrange("p (kk d) -> p kk d", d=1024),
                                  w2t_d[gl * 1024:(gl + 1) * 1024, :].rearrange("(kk p) d -> p kk d", p=128))
                    P.new_stage(f"t3_o2e{hh}_{tci}")
                    for mm in range(4):
                        vec_ts(P, tmp[:], opsum[mm][:, 0:512], b2_sb[:, 4 + mm:5 + mm], None, ALU.add)
                        P.new_stage(f"t3_o2e3{hh}_{tci}_{mm}")
                        vec_tt(P, out_sb[4 + mm][:, tcs:tcs + 512], out_sb[4 + mm][:, tcs:tcs + 512], tmp[:], ALU.add)
                        if mm < 3:
                            P.new_stage(f"t3_o2e4{hh}_{tci}_{mm}")
            P.new_stage("t3_store")
            for i in range(DT):
                P.dma("sync", xout[i * 128:(i + 1) * 128, t0:t0 + TH], out_sb[i][:])
        P.finalize()
    return nc


def t3_inputs(x_fm, attn_fm_bf, ins):
    cz = lambda a, n: np.ascontiguousarray(np.asarray(a).reshape(n, 128).T)
    w1 = np.asarray(ins["L2_ffn_W1"])
    w2 = np.asarray(ins["L2_ffn_W2"])
    hm = np.zeros((128, 2), np.float32)
    hm[:, 0] = 0.5 / w1.size
    hm[:, 1] = 0.5 / w2.size
    maps = []
    for c in range(NCORES):
        t0 = c * T_C
        maps.append({
            "xin": np.ascontiguousarray(x_fm[:, t0:t0 + T_C]),
            "attn_in": np.ascontiguousarray(attn_fm_bf[:, t0:t0 + T_C]),
            "wo": np.asarray(ins["L2_attn_Wo"]),
            "w1T": np.ascontiguousarray(w1.T),
            "w2T": np.ascontiguousarray(w2.T),
            "bo_c": cz(ins["L2_attn_bo"], DT),
            "b1_c": cz(ins["L2_ffn_b1"], FFT),
            "b2_c": cz(ins["L2_ffn_b2"], DT),
            "hm_col": hm,
            "ones_row": ONES_ROW,
        })
    return maps


def run_T3(x_fm, attn_fm, ins):
    import ml_dtypes
    nc = build_T3()
    attn_bf = attn_fm.astype(ml_dtypes.bfloat16) if attn_fm.dtype == np.float32 else attn_fm
    res = run_bass_kernel_spmd(nc, t3_inputs(x_fm, attn_bf, ins), core_ids=list(range(NCORES)))
    return np.concatenate([r["xout"] for r in res.results], axis=1)



def build_M3():
    nc = bass.Bass("TRN2", target_bir_lowering=False)
    ap = lambda n, s, d=F32, k="ExternalInput": nc.dram_tensor(n, s, d, kind=k).ap()
    xb_in = ap("xb_in", [D, S])
    wqT = ap("wqT", [D, 512])
    wdkvT = ap("wdkvT", [D, 512])
    wukT = ap("wukT", [512, 512])
    wuvT = ap("wuvT", [512, 512])
    n1w_c = ap("n1w_c", [128, DT])
    lng_c = ap("lng_c", [128, 4])
    lnb_c = ap("lnb_c", [128, 4])
    cosm = ap("cosm", [32, S])
    sinm = ap("sinm", [32, S])
    maskd = ap("maskd", [128, 128])
    ident = ap("ident", [128, 128])
    attn_o = ap("attn_o", [512, S], BF16, "ExternalOutput")
    TCn = 512

    with contextlib.ExitStack() as ctx:
        E = ctx.enter_context
        P = Prog(nc)
        q_bf = [E(nc.sbuf_tensor(f"m3q{m}", [128, S], BF16)) for m in range(4)]
        k_bf = [E(nc.sbuf_tensor(f"m3k{m}", [128, S], BF16)) for m in range(4)]
        v_tm = [E(nc.sbuf_tensor(f"m3v{t}", [128, NH_C * 65], BF16)) for t in range(S // 128)]
        mask_sb = E(nc.sbuf_tensor("m3mask", [128, 128], BF16))
        id_sb = E(nc.sbuf_tensor("m3id", [128, 128], F32))
        st_bf = E(nc.sbuf_tensor("m3st", [128, 512], BF16))
        ot_tm = E(nc.sbuf_tensor("m3ot", [128, 512], F32))
        rec_c = E(nc.sbuf_tensor("m3rc", [128, 4], F32))
        oh_fm = E(nc.sbuf_tensor("m3oh", [64, S], BF16))
        wq_sb = E(nc.sbuf_tensor("m3wq", [128, DT * 512], BF16))
        wdkv_sb = E(nc.sbuf_tensor("m3wd", [128, DT * 512], BF16))
        wuk_sb = E(nc.sbuf_tensor("m3wu", [128, 4 * 512], BF16))
        wuv_sb = E(nc.sbuf_tensor("m3wv", [128, 4 * 512], BF16))
        ccols = E(nc.sbuf_tensor("m3cc", [128, DT + 8], F32))
        n1w_sb = ccols[:, 0:DT]
        lng_sb = ccols[:, DT:DT + 4]
        lnb_sb = ccols[:, DT + 4:DT + 8]

        P.new_stage("m3_const")
        P.dma("gpsimd", mask_sb[:], maskd)
        P.dma("sync", id_sb[:], ident)
        P.dma("sync", ccols[:, 0:DT], n1w_c)
        P.dma("sync", ccols[:, DT:DT + 4], lng_c)
        P.dma("sync", ccols[:, DT + 4:DT + 8], lnb_c)
        for k in range(DT):
            P.dma("gpsimd", wq_sb[:, k * 512:(k + 1) * 512], wqT[k * 128:(k + 1) * 128, :])
            P.dma("gpsimd", wdkv_sb[:, k * 512:(k + 1) * 512], wdkvT[k * 128:(k + 1) * 128, :])
        for k in range(4):
            P.dma("gpsimd", wuk_sb[:, k * 512:(k + 1) * 512], wukT[k * 128:(k + 1) * 128, :])
            P.dma("gpsimd", wuv_sb[:, k * 512:(k + 1) * 512], wuvT[k * 128:(k + 1) * 128, :])
        P.new_stage("m3_vz")
        for t in range(S // 128):
            P.on("vector", lambda v, t=t: v.memset(v_tm[t][:], 0.0))
        P.new_stage("m3_vone")
        for t in range(S // 128):
            for h in range(NH_C):
                vec_memset(P, v_tm[t][:, h * 65 + 64:h * 65 + 65], 1.0)
        vdrain(P)

        with contextlib.ExitStack() as sctx:
            Es = sctx.enter_context
            xt = [Es(nc.sbuf_tensor(f"m3x{i}", [128, TCn], F32)) for i in range(DT)]
            sq = [Es(nc.sbuf_tensor(f"m3s{i}", [128, TCn], F32)) for i in range(DT)]
            xn2 = [Es(nc.sbuf_tensor(f"m3n{i}", [128, TCn], BF16)) for i in range(DT)]
            lat_f = [Es(nc.sbuf_tensor(f"m3l{i}", [128, TCn], F32)) for i in range(4)]
            lat_bf = [Es(nc.sbuf_tensor(f"m3lb{i}", [128, TCn], BF16)) for i in range(4)]
            mu_b = Es(nc.sbuf_tensor("m3mu", [128, TCn], F32))
            rstd_b = Es(nc.sbuf_tensor("m3rs", [128, TCn], F32))
            mu_row = Es(nc.sbuf_tensor("m3mr", [1, TCn], F32))
            sq_row = Es(nc.sbuf_tensor("m3sr", [1, TCn], F32))
            qf4 = Es(nc.sbuf_tensor("m3qf", [128, TCn], F32))
            rt_a = Es(nc.sbuf_tensor("m3ra", [32, TCn], F32))
            rt_b = Es(nc.sbuf_tensor("m3rb", [32, TCn], F32))
            rt_c = Es(nc.sbuf_tensor("m3rcs", [32, TCn], F32))
            rt_d = Es(nc.sbuf_tensor("m3rd", [32, TCn], F32))
            cos_sb = Es(nc.sbuf_tensor("m3cos", [32, TCn], F32))
            sin_sb = Es(nc.sbuf_tensor("m3sin", [32, TCn], F32))
            onescol = Es(nc.sbuf_tensor("m3oc", [128, 1], F32))
            onesrow = Es(nc.sbuf_tensor("m3or", [1, 128], F32))
            PS_A = nc_psum(nc, Es, "m3A", [128, 1024])
            PS_B = nc_psum(nc, Es, "m3B", [128, 1024])
            PS_R0 = nc_psum(nc, Es, "m3R0", [1, 512])
            PS_R1 = nc_psum(nc, Es, "m3R1", [1, 512])
            P.new_stage("m3_c1")
            P.on("vector", lambda v: v.memset(onescol[:], 1.0))
            P.new_stage("m3_c2")
            vec_memset(P, onesrow[:], 1.0)
            vdrain(P)
            for ci, t0 in enumerate(range(0, S, TCn)):
                P.new_stage("m3_load")
                for i in range(DT):
                    P.dma("sync", xt[i][:], xb_in[i * 128:(i + 1) * 128, t0:t0 + TCn])
                P.dma("sync", cos_sb[:], cosm[:, t0:t0 + TCn])
                P.dma("sync", sin_sb[:], sinm[:, t0:t0 + TCn])
                emit_ln(P, xt, sq, sq, mu_b, rstd_b, (PS_R0, PS_R1), PS_A,
                        mu_row, sq_row, onescol[:], onesrow[:], TCn)
                emit_ln(P, sq, xn2, xt, mu_b, rstd_b, (PS_R0, PS_R1), PS_A,
                        mu_row, sq_row, onescol[:], onesrow[:], TCn, eps=1e-6,
                        gam=[n1w_sb[:, i:i + 1] for i in range(DT)], bet=None, rms=True)
                for m in range(4):
                    P.new_stage(f"m3_qmm{m}")
                    pp = PS_B[:, (m % 2) * 512:(m % 2) * 512 + 512]
                    emit_mm_group(P, pp, [wq_sb[:, k * 512 + m * 128:k * 512 + (m + 1) * 128] for k in range(DT)],
                                  [xn2[k][:] for k in range(DT)], 0, TCn)
                    P.new_stage(f"m3_qev{m}")
                    vec_copy(P, qf4[:], pp[:, 0:TCn])
                    for hb in (0, 64):
                        P.new_stage(f"m3_qr1{m}{hb}")
                        vec_copy(P, rt_a[:], qf4[hb:hb + 32, :])
                        vec_copy(P, rt_b[:], qf4[hb + 32:hb + 64, :])
                        P.new_stage(f"m3_qr2{m}{hb}")
                        vec_tt(P, rt_c[:], rt_a[:], cos_sb[:], ALU.mult)
                        vec_tt(P, rt_d[:], rt_b[:], sin_sb[:], ALU.mult)
                        P.new_stage(f"m3_qr3{m}{hb}")
                        vec_tt(P, rt_c[:], rt_c[:], rt_d[:], ALU.subtract)
                        vec_tt(P, rt_d[:], rt_a[:], sin_sb[:], ALU.mult)
                        P.new_stage(f"m3_qr4{m}{hb}")
                        vec_tt(P, rt_b[:], rt_b[:], cos_sb[:], ALU.mult)
                        P.new_stage(f"m3_qr5{m}{hb}")
                        vec_tt(P, rt_d[:], rt_d[:], rt_b[:], ALU.add)
                        P.new_stage(f"m3_qr6{m}{hb}")
                        vec_copy(P, qf4[hb:hb + 32, :], rt_c[:])
                        vec_copy(P, qf4[hb + 32:hb + 64, :], rt_d[:])
                    P.new_stage(f"m3_qcp{m}")
                    vec_copy(P, q_bf[m][:, t0:t0 + TCn], qf4[:])
                for m in range(4):
                    P.new_stage(f"m3_lmm{m}")
                    pp = PS_B[:, (m % 2) * 512:(m % 2) * 512 + 512]
                    emit_mm_group(P, pp, [wdkv_sb[:, k * 512 + m * 128:k * 512 + (m + 1) * 128] for k in range(DT)],
                                  [xn2[k][:] for k in range(DT)], 0, TCn)
                    P.new_stage(f"m3_lev{m}")
                    vec_copy(P, lat_f[m][:], pp[:, 0:TCn])
                emit_ln(P, lat_f, lat_bf, xt, mu_b, rstd_b, (PS_R0, PS_R1), PS_A,
                        mu_row, sq_row, onescol[:], onesrow[:], TCn, nt=4, dim=512,
                        gam=[lng_sb[:, i:i + 1] for i in range(4)],
                        bet=[lnb_sb[:, i:i + 1] for i in range(4)])
                for m in range(4):
                    P.new_stage(f"m3_kmm{m}")
                    pp = PS_B[:, (m % 2) * 512:(m % 2) * 512 + 512]
                    emit_mm_group(P, pp, [wuk_sb[:, k * 512 + m * 128:k * 512 + (m + 1) * 128] for k in range(4)],
                                  [lat_bf[k][:] for k in range(4)], 0, TCn)
                    P.new_stage(f"m3_kev{m}")
                    vec_copy(P, qf4[:], pp[:, 0:TCn])
                    for hb in (0, 64):
                        P.new_stage(f"m3_kr1{m}{hb}")
                        vec_copy(P, rt_a[:], qf4[hb:hb + 32, :])
                        vec_copy(P, rt_b[:], qf4[hb + 32:hb + 64, :])
                        P.new_stage(f"m3_kr2{m}{hb}")
                        vec_tt(P, rt_c[:], rt_a[:], cos_sb[:], ALU.mult)
                        vec_tt(P, rt_d[:], rt_b[:], sin_sb[:], ALU.mult)
                        P.new_stage(f"m3_kr3{m}{hb}")
                        vec_tt(P, rt_c[:], rt_c[:], rt_d[:], ALU.subtract)
                        vec_tt(P, rt_d[:], rt_a[:], sin_sb[:], ALU.mult)
                        P.new_stage(f"m3_kr4{m}{hb}")
                        vec_tt(P, rt_b[:], rt_b[:], cos_sb[:], ALU.mult)
                        P.new_stage(f"m3_kr5{m}{hb}")
                        vec_tt(P, rt_d[:], rt_d[:], rt_b[:], ALU.add)
                        P.new_stage(f"m3_kr6{m}{hb}")
                        vec_copy(P, qf4[hb:hb + 32, :], rt_c[:])
                        vec_copy(P, qf4[hb + 32:hb + 64, :], rt_d[:])
                    P.new_stage(f"m3_kcp{m}")
                    vec_copy(P, k_bf[m][:, t0:t0 + TCn], qf4[:])
                pvof = lambda h: h * 65 if h < 4 else 512 + (h - 4) * 65
                for tb in range(TCn // 128):
                    tg = (t0 + tb * 128) // 128
                    P.new_stage(f"m3_vt{tb}")
                    for h in range(NH_C):
                        for k in range(4):
                            def fn(t, tb=tb, h=h, k=k, pvof=pvof):
                                return t.matmul(PS_B[:, pvof(h):pvof(h) + 64],
                                                lat_bf[k][:, tb * 128:(tb + 1) * 128],
                                                wuv_sb[:, k * 512 + h * 64:k * 512 + (h + 1) * 64],
                                                start=(k == 0), stop=(k == 3))
                            P.on("tensor", fn)
                    P.new_stage(f"m3_vtev{tb}")
                    for h in range(NH_C):
                        act(P, v_tm[tg][:, h * 65:h * 65 + 64], PS_B[:, pvof(h):pvof(h) + 64], AF.Copy)

        with contextlib.ExitStack() as sctx:
            Es = sctx.enter_context
            PS_S = nc_psum(nc, Es, "m3pss", [128, 1024])
            PS_P = nc_psum(nc, Es, "m3psp", [128, 512])
            PS_T = nc_psum(nc, Es, "m3pst", [128, 512])
            for h in range(NH_C):
                qt = q_bf[h // 2]
                kt_ = k_bf[h // 2]
                hb = (h % 2) * 64
                for band in range(S // 512):
                    nkt = 4 * band + 4
                    for kti in range(nkt):
                        pp = PS_S[:, (kti % 2) * 512:(kti % 2) * 512 + 512]
                        P.new_stage(f"m3at_s{h}_{band}_{kti}")
                        def fn(t, pp=pp, kti=kti, band=band, qt=qt, kt_=kt_, hb=hb):
                            return t.matmul(pp, kt_[hb:hb + 64, kti * 128:(kti + 1) * 128],
                                            qt[hb:hb + 64, band * 512:(band + 1) * 512],
                                            start=True, stop=True)
                        P.on("tensor", fn)
                        P.new_stage(f"m3at_e{h}_{band}_{kti}")
                        act(P, st_bf[:], pp, AF.Exp, scale=0.125)
                        dq = kti - 4 * band
                        if dq >= 0:
                            P.new_stage(f"m3at_m{h}_{band}_{kti}")
                            vec_tt(P, st_bf[:, dq * 128:(dq + 1) * 128],
                                   st_bf[:, dq * 128:(dq + 1) * 128], mask_sb[:], ALU.mult)
                        P.new_stage(f"m3at_p{h}_{band}_{kti}")
                        for qb in range(4):
                            if 4 * band + qb < kti:
                                continue
                            def fn(t, qb=qb, kti=kti, band=band, h=h):
                                return t.matmul(PS_P[:, qb * 128:qb * 128 + 65],
                                                st_bf[:, qb * 128:(qb + 1) * 128],
                                                v_tm[kti][:, h * 65:h * 65 + 65],
                                                start=(kti == 0 and qb == 0), stop=(kti == 4 * band + qb))
                            P.on("tensor", fn)
                    P.new_stage(f"m3at_r{h}_{band}")
                    for qb in range(4):
                        P.on("vector", lambda v, qb=qb: v.reciprocal(
                            rec_c[:, qb:qb + 1], PS_P[:, qb * 128 + 64:qb * 128 + 65]))
                    vdrain(P)
                    P.new_stage(f"m3at_n{h}_{band}")
                    for qb in range(4):
                        vec_tt(P, ot_tm[:, qb * 128:qb * 128 + 64], PS_P[:, qb * 128:qb * 128 + 64],
                               rec_c[:, qb:qb + 1].broadcast_to([128, 64]), ALU.mult)
                    vdrain(P)
                    P.new_stage(f"m3at_t{h}_{band}")
                    for qb in range(4):
                        def fn(t, qb=qb):
                            return t.transpose(PS_T[0:64, qb * 128:(qb + 1) * 128],
                                               ot_tm[:, qb * 128:qb * 128 + 64], id_sb[:])
                        P.on("tensor", fn)
                    P.new_stage(f"m3at_o{h}_{band}")
                    vec_copy(P, oh_fm[:, band * 512:(band + 1) * 512], PS_T[0:64, 0:512])
                P.new_stage(f"m3at_st{h}")
                P.dma("sync", attn_o[h * 64:(h + 1) * 64, :], oh_fm[:])
        P.finalize()
    return nc


def m3_inputs(x_fm, ins):
    WqT = np.ascontiguousarray(np.asarray(ins["L3_ffn_Wq"]).T)
    WdkvT = np.ascontiguousarray(np.asarray(ins["L3_ffn_Wdkv"]).T)
    WukT = np.ascontiguousarray(np.asarray(ins["L3_ffn_Wuk"]).T)
    WuvT = np.ascontiguousarray(np.asarray(ins["L3_ffn_Wuv"]).T)
    pos = np.arange(S, dtype=np.float64)[:, None]
    inv = 1.0 / 10000.0 ** (np.arange(32, dtype=np.float64) / 32.0)
    ang = pos * inv[None, :]
    cosm = np.ascontiguousarray(np.cos(ang).T.astype(np.float32))
    sinm = np.ascontiguousarray(np.sin(ang).T.astype(np.float32))
    maskd = np.triu(np.ones((128, 128), np.float32))
    cz = lambda a, n: np.ascontiguousarray(np.asarray(a).reshape(n, 128).T)
    maps = []
    for c in range(NCORES):
        b, hh = c // 2, c % 2
        hsl = slice(hh * 512, (hh + 1) * 512)
        maps.append({
            "xb_in": np.ascontiguousarray(x_fm[:, b * S:(b + 1) * S]),
            "wqT": np.ascontiguousarray(WqT[:, hsl]),
            "wdkvT": WdkvT,
            "wukT": np.ascontiguousarray(WukT[:, hsl]),
            "wuvT": np.ascontiguousarray(WuvT[:, hsl]),
            "n1w_c": cz(ins["L3_ffn_norm1_w"], DT),
            "lng_c": cz(ins["L3_ffn_ln_g"], 4),
            "lnb_c": cz(ins["L3_ffn_ln_b"], 4),
            "cosm": cosm, "sinm": sinm, "maskd": maskd, "ident": IDENT,
        })
    return maps


def run_M3(x_fm, ins):
    nc = build_M3()
    res = run_bass_kernel_spmd(nc, m3_inputs(x_fm, ins), core_ids=list(range(NCORES)))
    out = np.zeros((D, T_ALL), np.float32)
    for c in range(NCORES):
        b, hh = c // 2, c % 2
        out[hh * 512:(hh + 1) * 512, b * S:(b + 1) * S] = np.asarray(res.results[c]["attn_o"]).astype(np.float32)
    return out



NBANK = 8192
SQRT_NB = float(np.float32(np.sqrt(np.float32(8192.0))))


def build_T4():
    nc = bass.Bass("TRN2", target_bir_lowering=False)
    ap = lambda n, s, d=F32, k="ExternalInput": nc.dram_tensor(n, s, d, kind=k).ap()
    xin = ap("xin", [D, T_C])
    mla_in = ap("mla_in", [D, T_C], BF16)
    kT = ap("kT", [D, NBANK])
    vbank = ap("vbank", [NBANK, D])
    woT = ap("woT", [D, D])
    n2w_c = ap("n2w_c", [128, DT])
    gwT = ap("gwT", [D, 4])
    gb_t = ap("gb_t", [128, 4])
    w1T = ap("w1T", [4, D, FF])
    w2T = ap("w2T", [4, FF, D])
    b1_c = ap("b1_c", [128, 4 * FFT])
    b2_in = ap("b2", [4, D])
    sel4 = ap("sel4", [4, 4 * 128])
    ident = ap("ident", [128, 128])
    onr = ap("ones_row", [1, 128])
    xout = ap("xout", [D, T_C], F32, "ExternalOutput")
    kT_bf = nc.dram_tensor("t4ktbf", [D, NBANK], BF16).ap()
    v_bfd = nc.dram_tensor("t4vbf", [NBANK, D], BF16).ap()
    TH = 1024

    with contextlib.ExitStack() as ctx:
        E = ctx.enter_context
        P = Prog(nc)
        out_sb = [E(nc.sbuf_tensor(f"t4o{i}", [128, TH], F32)) for i in range(DT)]
        xn_bf = [E(nc.sbuf_tensor(f"t4n{i}", [128, TH], BF16)) for i in range(DT)]
        rms2_bf = [E(nc.sbuf_tensor(f"t4r{i}", [128, TH], BF16)) for i in range(DT)]
        wrow = E(nc.sbuf_tensor("t4wr", [4, TH], F32))
        cols = E(nc.sbuf_tensor("t4cols", [128, DT + 4 * FFT + 4], F32))
        n2w_sb = cols[:, 0:DT]
        b1_sb = cols[:, DT:DT + 4 * FFT]
        gb_sb = cols[:, DT + 4 * FFT:DT + 4 * FFT + 4]
        b2_sb = E(nc.sbuf_tensor("t4b2", [4, D], F32))
        sel_sb = E(nc.sbuf_tensor("t4sel", [4, 4 * 128], F32))
        id_sb = E(nc.sbuf_tensor("t4id", [128, 128], F32))
        idb_g = E(nc.sbuf_tensor("t4idb", [128, 128], BF16))
        onesrow = E(nc.sbuf_tensor("t4or", [1, 128], F32))
        gw_sb = E(nc.sbuf_tensor("t4gw", [128, DT * 4], BF16))
        lgt = E(nc.sbuf_tensor("t4lg", [128, 4], F32))
        lg2 = E(nc.sbuf_tensor("t4lg2", [128, 4], F32))
        PS_A = E(nc.psum_tensor("t4psA", [128, 1024], F32))
        PS_B = E(nc.psum_tensor("t4psB", [128, 1024], F32))
        PS_O = E(nc.psum_tensor("t4psO", [128, 1024], F32))
        PS_W = E(nc.psum_tensor("t4psW", [128, 512], F32))
        PS_GT = E(nc.psum_tensor("t4psGT", [128, 1024], BF16))
        PS_R0 = PS_A[0:1, 0:512]
        PS_R1 = PS_A[0:1, 512:1024]

        P.new_stage("t4_c")
        P.dma("sync", cols[:, 0:DT], n2w_c)
        P.dma("sync", cols[:, DT:DT + 4 * FFT], b1_c)
        P.dma("sync", cols[:, DT + 4 * FFT:DT + 4 * FFT + 4], gb_t)
        P.dma("sync", b2_sb[:], b2_in)
        P.dma("sync", sel_sb[:], sel4)
        P.dma("sync", id_sb[:], ident)
        P.dma("gpsimd", idb_g[:], ident)
        P.dma("sync", onesrow[:], onr)
        for k in range(DT):
            P.dma("gpsimd", gw_sb[:, k * 4:(k + 1) * 4], gwT[k * 128:(k + 1) * 128, :])

        with contextlib.ExitStack() as cctx:
            wcast = cctx.enter_context(nc.sbuf_tensor("t4wc", [128, NBANK], BF16))
            for k in range(DT):
                P.new_stage(f"t4pc_k{k}")
                P.dma("gpsimd", wcast[:], kT[k * 128:(k + 1) * 128, :])
                P.new_stage(f"t4pc_k{k}b")
                P.dma("sync", kT_bf[k * 128:(k + 1) * 128, :], wcast[:])
            for k in range(NBANK // 128):
                P.new_stage(f"t4pc_v{k}")
                P.dma("gpsimd", wcast[:, 0:D], vbank[k * 128:(k + 1) * 128, :])
                P.new_stage(f"t4pc_v{k}b")
                P.dma("sync", v_bfd[k * 128:(k + 1) * 128, :], wcast[:, 0:D])

        for hh in range(2):
          t0 = hh * TH
          with contextlib.ExitStack() as hctx:
            Eh = hctx.enter_context
            yctx = contextlib.ExitStack()
            xn_f = [yctx.enter_context(nc.sbuf_tensor(f"t4y{hh}_{i}", [128, TH], F32)) for i in range(DT)]
            with contextlib.ExitStack() as sctx:
                Es = sctx.enter_context
                xt = [Es(nc.sbuf_tensor(f"t4x{hh}_{i}", [128, TH], F32)) for i in range(DT)]
                sq0 = [Es(nc.sbuf_tensor(f"t4sq0{hh}_{i}", [128, TH], F32)) for i in range(DT)]
                mu_b = Es(nc.sbuf_tensor(f"t4mu{hh}", [128, TH], F32))
                rstd_b = Es(nc.sbuf_tensor(f"t4rs{hh}", [128, TH], F32))
                mu_row = Es(nc.sbuf_tensor(f"t4mr{hh}", [1, TH], F32))
                sq_row = Es(nc.sbuf_tensor(f"t4sr{hh}", [1, TH], F32))
                onescol = Es(nc.sbuf_tensor(f"t4oc{hh}", [128, 1], F32))
                P.new_stage("t4_c2")
                P.on("vector", lambda v, onescol=onescol: v.memset(onescol[:], 1.0))
                P.new_stage("t4_xload")
                for i in range(DT):
                    P.dma("sync", xt[i][:], xin[i * 128:(i + 1) * 128, t0:t0 + TH])
                emit_ln(P, xt, xn_bf, sq0, mu_b, rstd_b, (PS_R0, PS_R1), PS_O[:, 0:1024],
                        mu_row, sq_row, onescol[:], onesrow[:], TH, out_f32=xn_f)
                P.new_stage("t4_base")
                for i in range(DT):
                    vec_tt(P, out_sb[i][:], xt[i][:], xn_f[i][:], ALU.add)
            with contextlib.ExitStack() as sctx:
                Es = sctx.enter_context
                srow = Es(nc.sbuf_tensor(f"t4sw{hh}", [128, NBANK], F32))
                gbuf = Es(nc.sbuf_tensor(f"t4gb{hh}", [128, NBANK], BF16))
                kT_sb = Es(nc.sbuf_tensor(f"t4kt{hh}", [128, 2 * DT * 512], BF16))
                v_sb = Es(nc.sbuf_tensor(f"t4vs{hh}", [128, 2 * 8192], BF16))
                gT = Es(nc.sbuf_tensor(f"t4gT{hh}", [128, 1024], BF16))
                accq = Es(nc.sbuf_tensor(f"t4aq{hh}", [128, 32], F32))
                for tt in range(TH // 128):
                    tsl = slice(tt * 128, (tt + 1) * 128)
                    P.new_stage(f"t4a_l0{hh}{tt}")
                    P.dma("sync", kT_sb[:, 0:DT * 512].rearrange("p (kk d) -> p kk d", d=512),
                          kT_bf[:, 0:512].rearrange("(kk p) d -> p kk d", p=128))
                    for nchi in range(16):
                        P.new_stage(f"t4a_s{hh}{tt}{nchi}")
                        buf = (nchi % 2) * DT * 512
                        if nchi + 1 < 16:
                            b2_ = ((nchi + 1) % 2) * DT * 512
                            P.dma("sync", kT_sb[:, b2_:b2_ + DT * 512].rearrange("p (kk d) -> p kk d", d=512),
                                  kT_bf[:, (nchi + 1) * 512:(nchi + 2) * 512].rearrange("(kk p) d -> p kk d", p=128))
                        pp = PS_A[:, (nchi % 2) * 512:(nchi % 2) * 512 + 512]
                        emit_mm_group(P, pp, [xn_bf[k][:, tt * 128:(tt + 1) * 128] for k in range(DT)],
                                      [kT_sb[:, buf + k * 512:buf + (k + 1) * 512] for k in range(DT)], 0, 512)
                        P.new_stage(f"t4a_e{hh}{tt}{nchi}")
                        act(P, srow[:, nchi * 512:(nchi + 1) * 512], pp[:, 0:512], AF.Square,
                            accum_out=accq[:, nchi:nchi + 1])
                        sdrain(P)
                        P.new_stage(f"t4a_c{hh}{tt}{nchi}")
                        vec_copy(P, srow[:, nchi * 512:(nchi + 1) * 512], pp[:, 0:512])
                    P.new_stage(f"t4a_n{hh}{tt}")
                    step = 1
                    while step < 16:
                        for j in range(0, 16, 2 * step):
                            vec_tt(P, accq[:, j:j + 1], accq[:, j:j + 1], accq[:, j + step:j + step + 1], ALU.add)
                        vdrain(P)
                        P.new_stage(f"t4a_n{hh}{tt}_{step}")
                        step *= 2
                    act(P, accq[:, 16:17], accq[:, 0:1], AF.Ln)
                    sdrain(P)
                    P.new_stage(f"t4a_n2{hh}{tt}")
                    act(P, accq[:, 17:18], accq[:, 16:17], AF.Exp, scale=-0.5)
                    sdrain(P)
                    P.new_stage(f"t4a_n3{hh}{tt}")
                    vec_ts(P, accq[:, 18:19], accq[:, 17:18], SQRT_NB, None, ALU.mult)
                    vdrain(P)
                    P.new_stage(f"t4a_g{hh}{tt}")
                    vec_tt(P, srow[:], srow[:], accq[:, 18:19].broadcast_to([128, NBANK]), ALU.mult)
                    P.new_stage(f"t4a_g2{hh}{tt}")
                    act(P, gbuf[:], srow[:], AF.Gelu)
                    P.new_stage(f"t4a_vl0{hh}{tt}")
                    P.dma("sync", v_sb[:, 0:8192].rearrange("p (kk d) -> p kk d", d=1024),
                          v_bfd[0:1024, :].rearrange("(kk p) d -> p kk d", p=128))
                    for vg in range(8):
                        vb = (vg % 2) * 8192
                        P.new_stage(f"t4a_tr{hh}{tt}{vg}")
                        for ki in range(8):
                            n64 = vg * 8 + ki
                            def fn(t, n64=n64, ki=ki):
                                return t.transpose(PS_GT[:, ki * 128:(ki + 1) * 128],
                                                   gbuf[:, n64 * 128:(n64 + 1) * 128], idb_g[:])
                            P.on("tensor", fn)
                        P.new_stage(f"t4a_te{hh}{tt}{vg}")
                        vec_copy(P, gT[:], PS_GT[:])
                        vdrain(P)
                        P.new_stage(f"t4a_vm{hh}{tt}{vg}")
                        if vg + 1 < 8:
                            vb2 = ((vg + 1) % 2) * 8192
                            P.dma("sync", v_sb[:, vb2:vb2 + 8192].rearrange("p (kk d) -> p kk d", d=1024),
                                  v_bfd[(vg + 1) * 1024:(vg + 2) * 1024, :].rearrange("(kk p) d -> p kk d", p=128))
                        for ki in range(8):
                            n64 = vg * 8 + ki
                            for mm in range(DT):
                                def fn(t, vb=vb, ki=ki, mm=mm, n64=n64):
                                    return t.matmul(PS_O[:, mm * 128:(mm + 1) * 128],
                                                    v_sb[:, vb + ki * 1024 + mm * 128:vb + ki * 1024 + (mm + 1) * 128],
                                                    gT[:, ki * 128:(ki + 1) * 128], start=(n64 == 0), stop=(n64 == 63))
                                P.on("tensor", fn)
                    P.new_stage(f"t4a_oe{hh}{tt}")
                    for mm in range(DT):
                        vec_tt(P, out_sb[mm][:, tsl], out_sb[mm][:, tsl], PS_O[:, mm * 128:(mm + 1) * 128], ALU.add)
            with contextlib.ExitStack() as sctx:
                Es = sctx.enter_context
                w_sb = Es(nc.sbuf_tensor(f"t4w{hh}", [128, DT * 1024], BF16))
                mla_bf = [Es(nc.sbuf_tensor(f"t4m{hh}_{i}", [128, TH], BF16)) for i in range(DT)]
                P.new_stage("t4_mwl")
                for k in range(DT):
                    P.dma("gpsimd", w_sb[:, k * 1024:(k + 1) * 1024], woT[k * 128:(k + 1) * 128, :])
                    P.dma("gpsimd", mla_bf[k][:], mla_in[k * 128:(k + 1) * 128, t0:t0 + TH])
                for m in range(DT):
                    for j in range(2):
                        P.new_stage(f"t4_mm{m}{j}")
                        pp = PS_B[:, (j % 2) * 512:(j % 2) * 512 + 512]
                        emit_mm_group(P, pp, [w_sb[:, k * 1024 + m * 128:k * 1024 + (m + 1) * 128] for k in range(DT)],
                                      [mla_bf[k][:, j * 512:(j + 1) * 512] for k in range(DT)], 0, 512)
                        P.new_stage(f"t4_mev{m}{j}")
                        sl = slice(j * 512, (j + 1) * 512)
                        vec_tt(P, xn_f[m][:, sl], xn_f[m][:, sl], pp[:, 0:512], ALU.add)
                        P.new_stage(f"t4_mac{m}{j}")
                        vec_tt(P, out_sb[m][:, sl], out_sb[m][:, sl], pp[:, 0:512], ALU.add)
            with contextlib.ExitStack() as sctx:
                Es = sctx.enter_context
                sq2 = [Es(nc.sbuf_tensor(f"t4q{hh}_{i}", [128, TH], F32)) for i in range(DT)]
                mu_b = Es(nc.sbuf_tensor(f"t4mu2{hh}", [128, TH], F32))
                rstd_b = Es(nc.sbuf_tensor(f"t4rs2{hh}", [128, TH], F32))
                mu_row = Es(nc.sbuf_tensor(f"t4mr2{hh}", [1, TH], F32))
                sq_row = Es(nc.sbuf_tensor(f"t4sr2{hh}", [1, TH], F32))
                onescol = Es(nc.sbuf_tensor(f"t4oc2{hh}", [128, 1], F32))
                P.new_stage("t4_c3")
                P.on("vector", lambda v, onescol=onescol: v.memset(onescol[:], 1.0))
                emit_ln(P, xn_f, rms2_bf, sq2, mu_b, rstd_b, (PS_R0, PS_R1), PS_O[:, 0:1024],
                        mu_row, sq_row, onescol[:], onesrow[:], TH, eps=1e-6,
                        gam=[n2w_sb[:, i:i + 1] for i in range(DT)], bet=None, rms=True)
            yctx.close()
            for tt in range(TH // 128):
                P.new_stage(f"t4g_mm{tt}")
                emit_mm_group(P, PS_W[:, 0:4], [rms2_bf[k][:, tt * 128:(tt + 1) * 128] for k in range(DT)],
                              [gw_sb[:, k * 4:(k + 1) * 4] for k in range(DT)], 0, 4, pn0=0)
                P.new_stage(f"t4g_ev{tt}")
                vec_copy(P, lgt[:], PS_W[:, 0:4])
                vdrain(P)
                P.new_stage(f"t4g_b{tt}")
                vec_tt(P, lgt[:], lgt[:], gb_sb[:, 0:4] if False else gb_sb, ALU.add)
                vdrain(P)
                P.new_stage(f"t4g_m1{tt}")
                vec_tt(P, lg2[:, 0:2], lgt[:, 0:2], lgt[:, 2:4], ALU.max)
                vdrain(P)
                P.new_stage(f"t4g_m2{tt}")
                vec_tt(P, lg2[:, 2:3], lg2[:, 0:1], lg2[:, 1:2], ALU.max)
                vdrain(P)
                P.new_stage(f"t4g_oh{tt}")
                vec_tt(P, lgt[:], lgt[:], lg2[:, 2:3].broadcast_to([128, 4]), ALU.is_ge)
                vdrain(P)
                P.new_stage(f"t4g_tr{tt}")
                P.on("tensor", lambda t, tt=tt: t.transpose(PS_W[0:4, 0:128], lgt[:], id_sb[:]))
                P.new_stage(f"t4g_tev{tt}")
                vec_copy(P, wrow[:, tt * 128:(tt + 1) * 128], PS_W[0:4, 0:128])
                vdrain(P)
            with contextlib.ExitStack() as sctx:
                Es = sctx.enter_context
                poolA = Es(nc.sbuf_tensor(f"t4pA{hh}", [128, DT * FF], BF16))
                poolB = Es(nc.sbuf_tensor(f"t4pB{hh}", [128, 2 * 8192], BF16))
                h_bf = Es(nc.sbuf_tensor(f"t4h{hh}", [128, FFT * 512], BF16))
                wbce = Es(nc.sbuf_tensor(f"t4wb{hh}", [128, 512], F32))
                opsum = [PS_O[:, 0:512], PS_O[:, 512:1024], PS_A[:, 0:512], PS_A[:, 512:1024],
                         PS_B[:, 0:512], PS_B[:, 512:1024], PS_W[:, 0:512], None]
                for e in range(4):
                    P.new_stage(f"t4w1l{e}")
                    for k in range(DT):
                        P.dma("gpsimd", poolA[:, k * FF:(k + 1) * FF], w1T[e, k * 128:(k + 1) * 128, :])
                    for tci, tcs in enumerate(range(0, TH, 512)):
                        for ht in range(FFT):
                            P.new_stage(f"t4h{e}_{tci}_{ht}")
                            pp = PS_A[:, (ht % 2) * 512:(ht % 2) * 512 + 512]
                            emit_mm_group(P, pp, [poolA[:, k * FF + ht * 128:k * FF + ht * 128 + 128] for k in range(DT)],
                                          [rms2_bf[k][:, tcs:tcs + 512] for k in range(DT)], 0, 512)
                            if ht == 0:
                                P.dma("gpsimd", poolB[:, 0:8192].rearrange("p (kk d) -> p kk d", d=1024),
                                      w2T[e, 0:1024, :].rearrange("(kk p) d -> p kk d", p=128))
                            P.new_stage(f"t4hg{e}_{tci}_{ht}")
                            act(P, h_bf[:, ht * 512:(ht + 1) * 512], pp[:, 0:512], AF.Relu,
                                bias=b1_sb[:, e * FFT + ht:e * FFT + ht + 1])
                        P.new_stage(f"t4w{e}_{tci}")
                        emit_mm_group(P, PS_W, [sel_sb[:, e * 128:(e + 1) * 128]],
                                      [wrow[:, tcs:tcs + 512]], 0, 512, pn0=0)
                        P.new_stage(f"t4wev{e}_{tci}")
                        vec_copy(P, wbce[:], PS_W[:, 0:512])
                        for half in range(2):
                            if half == 1:
                                P.new_stage(f"t4o2p{e}_{tci}")
                                P.dma("gpsimd", poolB[:, 0:8192].rearrange("p (kk d) -> p kk d", d=1024),
                                      w2T[e, 0:1024, :].rearrange("(kk p) d -> p kk d", p=128))
                            for gi in range(4):
                                P.new_stage(f"t4o{e}_{tci}_{half}_{gi}")
                                for ki in range(8):
                                    kk = gi * 8 + ki
                                    for mm in range(4):
                                        def fn(t, kk=kk, ki=ki, gi=gi, mm=mm, half=half):
                                            return t.matmul(opsum[mm][:, 0:512],
                                                            poolB[:, (gi % 2) * 8192 + ki * 1024 + (mm + 4 * half) * 128:
                                                                  (gi % 2) * 8192 + ki * 1024 + (mm + 4 * half) * 128 + 128],
                                                            h_bf[:, kk * 512:(kk + 1) * 512],
                                                            start=(kk == 0), stop=(kk == FFT - 1))
                                        P.on("tensor", fn)
                                if gi + 1 < 4:
                                    gl = gi + 1
                                    P.dma("gpsimd", poolB[:, (gl % 2) * 8192:(gl % 2) * 8192 + 8192].rearrange("p (kk d) -> p kk d", d=1024),
                                          w2T[e, gl * 1024:(gl + 1) * 1024, :].rearrange("(kk p) d -> p kk d", p=128))
                            P.new_stage(f"t4oa{e}_{tci}_{half}")
                            for mm in range(4):
                                vec_tt(P, opsum[mm][:, 0:512], opsum[mm][:, 0:512], wbce[:], ALU.mult)
                            P.new_stage(f"t4ob{e}_{tci}_{half}")
                            for mm in range(4):
                                vec_tt(P, out_sb[4 * half + mm][:, tcs:tcs + 512],
                                       out_sb[4 * half + mm][:, tcs:tcs + 512], opsum[mm][:, 0:512], ALU.add)
            for m in range(DT):
                P.new_stage(f"t4b2_{m}")
                emit_mm_group(P, PS_A[:, 0:512], [b2_sb[:, m * 128:(m + 1) * 128]], [wrow[:]], 0, 512, pn0=0)
                emit_mm_group(P, PS_A[:, 512:1024], [b2_sb[:, m * 128:(m + 1) * 128]], [wrow[:]], 512, 512, pn0=0)
                P.new_stage(f"t4b2e_{m}")
                vec_tt(P, out_sb[m][:, 0:512], out_sb[m][:, 0:512], PS_A[:, 0:512], ALU.add)
                vec_tt(P, out_sb[m][:, 512:1024], out_sb[m][:, 512:1024], PS_A[:, 512:1024], ALU.add)
            P.new_stage("t4_store")
            for i in range(DT):
                P.dma("sync", xout[i * 128:(i + 1) * 128, t0:t0 + TH], out_sb[i][:])
        P.finalize()
    return nc


def t4_inputs(x_fm, mla_fm_bf, ins):
    cz = lambda a, n: np.ascontiguousarray(np.asarray(a).reshape(n, 128).T)
    b1 = np.asarray(ins["L3_ffn_b1"])
    sel = np.zeros((4, 4 * 128), np.float32)
    for e in range(4):
        sel[e, e * 128:(e + 1) * 128] = 1.0
    w1T = np.ascontiguousarray(np.asarray(ins["L3_ffn_W1"]).transpose(0, 2, 1))
    w2T = np.ascontiguousarray(np.asarray(ins["L3_ffn_W2"]).transpose(0, 2, 1))
    kT = np.ascontiguousarray(np.asarray(ins["L3_attn_k"]).T)
    gb = np.asarray(ins["L3_ffn_gate_b"])
    maps = []
    for c in range(NCORES):
        t0 = c * T_C
        maps.append({
            "xin": np.ascontiguousarray(x_fm[:, t0:t0 + T_C]),
            "mla_in": np.ascontiguousarray(mla_fm_bf[:, t0:t0 + T_C]),
            "kT": kT,
            "vbank": np.asarray(ins["L3_attn_v"]),
            "woT": np.ascontiguousarray(np.asarray(ins["L3_ffn_Wo"]).T),
            "n2w_c": cz(ins["L3_ffn_norm2_w"], DT),
            "gwT": np.ascontiguousarray(np.asarray(ins["L3_ffn_gate_w"]).T),
            "gb_t": np.ascontiguousarray(np.tile(gb[None, :], (128, 1))),
            "w1T": w1T, "w2T": w2T,
            "b1_c": np.ascontiguousarray(b1.reshape(4, FFT, 128).transpose(2, 0, 1).reshape(128, 4 * FFT)),
            "b2": np.asarray(ins["L3_ffn_b2"]),
            "sel4": sel, "ident": IDENT, "ones_row": ONES_ROW,
        })
    return maps


def run_T4(x_fm, mla_fm, ins):
    import ml_dtypes
    nc = build_T4()
    mla_bf = mla_fm.astype(ml_dtypes.bfloat16) if mla_fm.dtype == np.float32 else mla_fm
    res = run_bass_kernel_spmd(nc, t4_inputs(x_fm, mla_bf, ins), core_ids=list(range(NCORES)))
    return np.concatenate([r["xout"] for r in res.results], axis=1)



_NC_CACHE = {}


def _get_nc(name, builder):
    if name not in _NC_CACHE:
        _NC_CACHE[name] = builder()
    return _NC_CACHE[name]


def kernel(**inputs):
    import ml_dtypes
    ins = inputs
    x_fm = fm(ins["L0_attn_x"])
    cores = list(range(NCORES))

    res = run_bass_kernel_spmd(_get_nc("M0", build_M0), m0_inputs(x_fm, ins), core_ids=cores)
    yg = np.zeros((XB, T_ALL), np.float32)
    for c in cores:
        b, dh = c // 2, c % 2
        yg[dh * HT:(dh + 1) * HT, b * S:(b + 1) * S] = res.results[c]["ygated"]
    res = run_bass_kernel_spmd(_get_nc("T1", build_T1), t1_inputs(x_fm, yg, ins), core_ids=cores)
    x_fm = np.concatenate([r["xout"] for r in res.results], axis=1)

    res = run_bass_kernel_spmd(_get_nc("M1", build_M1), m1_inputs(x_fm, ins), core_ids=cores)
    attn = np.zeros((D, T_ALL), dtype=ml_dtypes.bfloat16)
    for c in cores:
        b, hh2 = c // 2, c % 2
        attn[hh2 * 512:(hh2 + 1) * 512, b * S:(b + 1) * S] = np.asarray(res.results[c]["attn_o"])
    res = run_bass_kernel_spmd(_get_nc("T2", build_T2), t2_inputs(x_fm, attn, ins), core_ids=cores)
    x_fm = np.concatenate([r["xout"] for r in res.results], axis=1)

    res = run_bass_kernel_spmd(_get_nc("M2", build_M2), m2_inputs(x_fm, ins), core_ids=cores)
    attn = np.zeros((D, T_ALL), dtype=ml_dtypes.bfloat16)
    for c in cores:
        b, hh2 = c // 2, c % 2
        attn[hh2 * 512:(hh2 + 1) * 512, b * S:(b + 1) * S] = np.asarray(res.results[c]["attn_o"])
    res = run_bass_kernel_spmd(_get_nc("T3", build_T3), t3_inputs(x_fm, attn, ins), core_ids=cores)
    x_fm = np.concatenate([r["xout"] for r in res.results], axis=1)

    res = run_bass_kernel_spmd(_get_nc("M3", build_M3), m3_inputs(x_fm, ins), core_ids=cores)
    attn = np.zeros((D, T_ALL), dtype=ml_dtypes.bfloat16)
    for c in cores:
        b, hh2 = c // 2, c % 2
        attn[hh2 * 512:(hh2 + 1) * 512, b * S:(b + 1) * S] = np.asarray(res.results[c]["attn_o"])
    res = run_bass_kernel_spmd(_get_nc("T4", build_T4), t4_inputs(x_fm, attn, ins), core_ids=cores)
    x_fm = np.concatenate([r["xout"] for r in res.results], axis=1)

    return np.ascontiguousarray(x_fm.T.reshape(B, S, D)).astype(np.float32)
